# version 15
# baseline (speedup 1.0000x reference)
import numpy as np
import ml_dtypes
from contextlib import ExitStack

import concourse.bass as bass
import concourse.mybir as mybir
from concourse.bass_utils import run_bass_kernel_spmd

F32 = mybir.dt.float32
BF16 = mybir.dt.bfloat16
AF = mybir.ActivationFunctionType
ALU = mybir.AluOpType
AX = mybir.AxisListType

B = 2
S = 8192
D = 2048
H = 16
INTER = 5632
NCORES = 8
NSPAN = 4
SPAN = S // NSPAN
EPS = 1e-5
THETA = 10000.0
SCALE_BASE = 512.0
GAMMAS = [1.0 - 2.0 ** (-5.0 - h) for h in range(H)]
GC = [float(np.float32(np.float64(g) ** 128)) for g in GAMMAS]
KC = D // 128
IC = INTER // 128


class Sem:
    def __init__(self, nc, ctx, name):
        self.h = ctx.enter_context(nc.semaphore(name))
        self.n = 0

    def inc(self, instr, k=1):
        instr.then_inc(self.h, k)
        self.n += k
        return self.n


class DStream:

    def __init__(self, nc, ctx, name):
        self.s = [Sem(nc, ctx, name + "a"), Sem(nc, ctx, name + "b")]
        self.rec = {}
        self.gi = 0

    def group(self, key=None):
        p = self.gi % 2
        self.gi += 1
        if key is None:
            key = self.gi - 1
        sem = self.s[p]

        def emit(e, out, in_):
            v = sem.inc(e.dma_start(out=out, in_=in_), 16)
            self.rec[key] = (sem, v)
        return emit

    def wait(self, e, key):
        sem, v = self.rec[key]
        e.wait_ge(sem.h, v)

    def drain_all(self, e):
        for sem in self.s:
            if sem.n:
                e.wait_ge(sem.h, sem.n)


def build(T=SPAN, debug=False):
    NT = T // 128
    NTG = T // 512
    TPH = NT // 2
    nc = bass.Bass("TRN2", target_bir_lowering=False)

    dram = lambda n, shp, dt=F32, kind="ExternalInput": nc.dram_tensor(n, shp, dt, kind=kind).ap()
    sk = "ExternalOutput" if debug else "Internal"

    x_in = dram("x", [T, D])
    Wq = dram("Wq", [D, D]); Wk = dram("Wk", [D, D]); Wv = dram("Wv", [D, D])
    Wg = dram("Wg", [D, D]); Wo = dram("Wo", [D, D])
    gateT = dram("gateT", [D, INTER]); upT = dram("upT", [D, INTER]); downT = dram("downT", [INTER, D])
    bqr = dram("bqr", [128, 16]); bkr = dram("bkr", [128, 16]); bgr = dram("bgr", [128, 16])
    bv_bc = dram("bv_bc", [128, D]); bo_bc = dram("bo_bc", [128, D])
    gss = dram("gss", [128, IC]); uss = dram("uss", [128, IC]); dss_bc = dram("dss_bc", [128, D])
    cs_t = dram("cs_t", [128, T]); ss_t = dram("ss_t", [128, T])
    ics_t = dram("ics_t", [128, T]); iss_t = dram("iss_t", [128, T])
    d2_t = dram("d2_t", [128, H * 128])
    grp1_bc = dram("grp1_bc", [128, H * 128])
    grp1v_t = dram("grp1v_t", [128, H])
    gcm_t = dram("gcm_t", [128, H])
    wm_bc = dram("wm_bc", [128, NCORES * H])
    p_in = dram("p_rot", [128, 128]); id_in = dram("ident", [128, 128])

    out = dram("out", [T, D], F32, kind="ExternalOutput")

    q_sp = dram("q_sp", [NT, 128, H, 128], BF16, kind=sk)
    k_sp = dram("k_sp", [NT, 128, H, 128], BF16, kind=sk)
    kg_sp = dram("kg_sp", [NT, 128, H, 128], BF16, kind=sk)
    v_sp = dram("v_sp", [NT, 128, H, 128], BF16, kind=sk)
    g_sp = dram("g_sp", [NT, 128, H, 128], BF16, kind=sk)
    xn_sp = dram("xn_sp", [KC, 128, T], BF16, kind=sk)
    s_sp = dram("s_sp", [NT, 128, D], BF16, kind=sk)
    hdn_sp = dram("hdn_sp", [NT, 128, IC, 128], BF16, kind=sk)
    wo_scr = dram("wo_scr", [D, D], BF16, kind="Internal")
    wq_scr = dram("wq_scr", [D, D], BF16, kind="Internal")
    wg_scr = dram("wg_scr", [D, D], BF16, kind="Internal")
    cc_in = nc.dram_tensor("cc_in", [H * 128, 128], F32, kind="Internal")
    cc_out = nc.dram_tensor("cc_out", [NCORES * H * 128, 128], F32, kind="Internal")

    ctx = ExitStack()
    sb = lambda n, shp, dt: ctx.enter_context(nc.sbuf_tensor(n, shp, dt))

    LD = Sem(nc, ctx, "LD")
    SPST = DStream(nc, ctx, "SP")
    KSP = DStream(nc, ctx, "KP")
    RST = DStream(nc, ctx, "RS")
    WST = DStream(nc, ctx, "WS")
    DST = DStream(nc, ctx, "DS")
    MST = DStream(nc, ctx, "MS")
    WOS = DStream(nc, ctx, "WO")
    WOC = Sem(nc, ctx, "WOC")
    CCS = Sem(nc, ctx, "CCS")
    GPC = Sem(nc, ctx, "GPC")
    PE = Sem(nc, ctx, "PE")
    VE = Sem(nc, ctx, "VE")
    SE = Sem(nc, ctx, "SE")
    ENGS = ("sy", "gp", "te", "se", "ve")
    BAR = {e: Sem(nc, ctx, f"B_{e}") for e in ENGS}
    nphase = [0]

    def bs(e):
        if nphase[0] > 0:
            for s_ in BAR.values():
                e.wait_ge(s_.h, nphase[0])

    def be(e, which):
        if which == "sy":
            if LD.n:
                e.wait_ge(LD.h, LD.n)
            SPST.drain_all(e)
            KSP.drain_all(e)
            RST.drain_all(e)
            WOS.drain_all(e)
        elif which == "gp":
            WST.drain_all(e)
            DST.drain_all(e)
            MST.drain_all(e)
            if GPC.n:
                e.wait_ge(GPC.h, GPC.n)
        else:
            st = {"te": PE, "se": SE, "ve": VE}[which]
            if st.n > 0:
                e.wait_ge(st.h, st.n)
        BAR[which].inc(e.nop(), 1)

    bqr_sb = sb("bqr_sb", [128, 16], F32); bkr_sb = sb("bkr_sb", [128, 16], F32)
    bgr_sb = sb("bgr_sb", [128, 16], F32)
    gcm_sb = sb("gcm_sb", [128, H], F32)
    gss_sb = sb("gss_sb", [128, IC], F32); uss_sb = sb("uss_sb", [128, IC], F32)
    wm_sb = sb("wm_sb", [128, NCORES * H], F32)
    p_bf = sb("p_bf", [128, 128], BF16); id_bf = sb("id_bf", [128, 128], BF16)
    m_sb = sb("m_sb", [128, H * 128], F32)
    px = ExitStack()
    xn_fm = px.enter_context(nc.sbuf_tensor("xn_fm", [128, KC * T], BF16))

    with nc.Block() as block:
        def p0_sy(e):
            for dst, src in [(bqr_sb, bqr), (bkr_sb, bkr), (bgr_sb, bgr), (gcm_sb, gcm_t),
                             (gss_sb, gss), (uss_sb, uss), (wm_sb, wm_bc)]:
                LD.inc(e.dma_start(out=dst[:], in_=src), 16)
            be(e, "sy")
        def p0_gp(e):
            g = WST.group("p0")
            g(e, p_bf[:], p_in)
            g(e, id_bf[:], id_in)
            be(e, "gp")
        block.sync(p0_sy); block.gpsimd(p0_gp)
        block.tensor(lambda e: be(e, "te")); block.scalar(lambda e: be(e, "se"))
        block.vector(lambda e: be(e, "ve"))
        nphase[0] += 1

        pa_scope = ExitStack()
        sbA = lambda n, shp, dt: pa_scope.enter_context(nc.sbuf_tensor(n, shp, dt, side="right"))
        wvb = [sbA(f"wvb{i}", [128, KC * 512], BF16) for i in range(2)]
        wkb = [sbA(f"wkb{i}", [128, KC * 512], BF16) for i in range(2)]

        p1 = ExitStack()
        sb1 = lambda n, shp, dt: p1.enter_context(nc.sbuf_tensor(n, shp, dt))
        ps1 = lambda n, shp, dt=F32: p1.enter_context(nc.psum_tensor(n, shp, dt))
        NGRP = 4
        TPH = max(NT // NGRP, 1)
        NGRP = NT // TPH
        x_b = [sb1(f"x_b{i}", [128, D], F32) for i in range(TPH)]
        xn_tok = [sb1(f"xn_tok{i}", [128, D], BF16) for i in range(2)]
        sq_sc = sb1("sq_sc", [128, D], F32)
        st_sum = sb1("st_sum", [128, NT], F32); st_ssq = sb1("st_ssq", [128, NT], F32)
        st_mu = sb1("st_mu", [128, NT], F32); st_rs = sb1("st_rs", [128, NT], F32)
        st_nv = sb1("st_nv", [128, NT], F32)
        ps_tp = [ps1(f"ps_tp{i}", [128, 128], BF16) for i in range(2)]

        bLD, bPE, bVE, bSE = LD.n, PE.n, VE.n, SE.n
        ld_half = lambda hh: bLD + (hh + 1) * TPH * 16
        se_sq = lambda t: bSE + (t // TPH) * (TPH + 1) + (t % TPH) + 1
        se_rstd = lambda hh: bSE + hh * (TPH + 1) + TPH + 1
        ve_negv = lambda hh: bVE + hh * (1 + TPH * 17) + 1
        ve_xn = lambda t: ve_negv(t // TPH) + (t % TPH) * 17 + 1
        ve_evac = lambda t, j: ve_xn(t) + j + 1
        pe_tp1 = lambda t, j: bPE + (t * 16 + j) + 1

        def p1_sy(e):
            bs(e)
            for t in range(NT):
                if t >= TPH:
                    e.wait_ge(VE.h, ve_xn(t - TPH))
                    e.wait_ge(SE.h, se_sq(t - TPH))
                LD.inc(e.dma_start(out=x_b[t % TPH][:], in_=x_in[t * 128:(t + 1) * 128, :]), 16)
            be(e, "sy")

        def p1_gp(e):
            bs(e)
            WST.group(("wv", 0))(
                e, wvb[0][:].rearrange("p (k m) -> p k m", k=KC),
                Wv[:, 0:512].rearrange("(k p) m -> p k m", p=128))
            WST.group(("wk", 0))(
                e, wkb[0][:].rearrange("p (k m) -> p k m", k=KC),
                Wk[:, 0:512].rearrange("(k p) m -> p k m", p=128))
            WOC.inc(e.dma_start(out=wq_scr, in_=Wq), 16)
            WOC.inc(e.dma_start(out=wg_scr, in_=Wg), 16)
            WOC.inc(e.dma_start(out=wo_scr, in_=Wo), 16)
            be(e, "gp")

        def p1_te(e):
            bs(e)
            for t in range(NT):
                e.wait_ge(VE.h, ve_xn(t))
                for j in range(16):
                    f = t * 16 + j
                    if f >= 2:
                        pt, pj = divmod(f - 2, 16)
                        e.wait_ge(VE.h, ve_evac(pt, pj))
                    PE.inc(e.transpose(ps_tp[f % 2][:], xn_tok[t % 2][:, j * 128:(j + 1) * 128],
                                       id_bf[:]))
            be(e, "te")

        def p1_se(e):
            bs(e)
            for hh in range(NGRP):
                e.wait_ge(LD.h, ld_half(hh))
                for i in range(TPH):
                    t = hh * TPH + i
                    e.drain()
                    SE.inc(e.activation(sq_sc[:], x_b[i][:], AF.Square,
                                        accum_out=st_ssq[:, t:t + 1]))
                sl = slice(hh * TPH, hh * TPH + TPH)
                e.wait_ge(VE.h, ve_negv(hh))
                e.activation(st_nv[:, sl], st_nv[:, sl], AF.Ln, scale=-1.0)
                e.drain()
                SE.inc(e.activation(st_rs[:, sl], st_nv[:, sl], AF.Exp, scale=-0.5))
            be(e, "se")

        def p1_ve(e):
            bs(e)
            for hh in range(NGRP):
                sl = slice(hh * TPH, hh * TPH + TPH)
                e.wait_ge(LD.h, ld_half(hh))
                for i in range(TPH):
                    t = hh * TPH + i
                    e.reduce_sum(st_sum[:, t:t + 1], x_b[i][:], axis=AX.X)
                e.wait_ge(SE.h, se_sq(hh * TPH + TPH - 1))
                e.drain()
                e.tensor_scalar(st_mu[:, sl], st_sum[:, sl], 1.0 / D, None, ALU.mult)
                e.tensor_scalar(st_ssq[:, sl], st_ssq[:, sl], 1.0 / D, None, ALU.mult)
                e.drain()
                e.tensor_tensor(st_nv[:, sl], st_mu[:, sl], st_mu[:, sl], ALU.mult)
                e.drain()
                e.tensor_tensor(st_nv[:, sl], st_nv[:, sl], st_ssq[:, sl], ALU.subtract)
                e.drain()
                VE.inc(e.tensor_scalar(st_nv[:, sl], st_nv[:, sl], EPS, None, ALU.subtract))
                for i in range(TPH):
                    t = hh * TPH + i
                    e.wait_ge(SE.h, se_rstd(hh))
                    e.drain()
                    VE.inc(e.tensor_scalar(xn_tok[t % 2][:], x_b[i][:], st_mu[:, t:t + 1],
                                           st_rs[:, t:t + 1], ALU.subtract, ALU.mult))
                    for j in range(16):
                        e.wait_ge(PE.h, pe_tp1(t, j))
                        VE.inc(e.tensor_copy(xn_fm[:, j * T + t * 128: j * T + (t + 1) * 128],
                                             ps_tp[(t * 16 + j) % 2][:]))
            be(e, "ve")

        block.sync(p1_sy); block.gpsimd(p1_gp); block.tensor(p1_te)
        block.scalar(p1_se); block.vector(p1_ve)
        assert SE.n == bSE + NGRP * (TPH + 1) and VE.n == bVE + NGRP * (1 + TPH * 17)
        assert PE.n == bPE + NT * 16 and LD.n == bLD + NT * 16
        p1.close()
        nphase[0] += 1

        pa = ExitStack()
        sbp = lambda n, shp, dt: pa.enter_context(nc.sbuf_tensor(n, shp, dt))
        psp = lambda n, shp, dt=F32: pa.enter_context(nc.psum_tensor(n, shp, dt))
        c1_sb = sbp("c1_sb", [128, T], F32); c2_sb = sbp("c2_sb", [128, T], F32)
        bv_sb = sbp("bv_sb", [128, D], F32)
        vqb = [sbp(f"vqb{i}", [128, NT * 512], BF16) for i in range(2)]
        kbf = [sbp(f"kbf{i}", [128, 512], BF16) for i in range(2)]
        ta = sbp("ta", [128, 512], F32); tb_ = sbp("tb", [128, 512], F32)
        fmb = [sbp(f"fmb{i}", [128, 512], BF16) for i in range(2)]
        kgb = [sbp(f"kgb{i}", [128, 512], BF16) for i in range(2)]
        psV = [psp(f"psV{i}", [128, 512]) for i in range(2)]
        psK = [psp(f"psK{i}", [128, 512]) for i in range(2)]
        psR = psp("psR", [128, 512])
        psT = psp("psT", [128, 512], BF16)
        psU = psp("psU", [128, 512])

        NIT = 16 * NTG
        bPE, bVE, bSE = PE.n, VE.n, SE.n
        pe_ev = {}; _pectr = [bPE]
        ve_ev = {}; _vectr = [bVE]
        se_ev = {}; _sectr = [bSE]
        def _mk(ctr, ev):
            def f(key):
                ctr[0] += 1
                ev[key] = ctr[0]
            return f
        _pe = _mk(_pectr, pe_ev); _ve = _mk(_vectr, ve_ev); _se = _mk(_sectr, se_ev)
        SHIFT = 16 * NTG // 4
        SHIFT = NT
        for i in range(NIT + SHIFT):
            if i < NIT:
                _pe(("v", i))
            if i >= SHIFT:
                it = i - SHIFT
                _pe(("kproj", it)); _pe(("krot", it)); _pe(("ktp", it)); _pe(("kum", it))
        for i in range(NIT + SHIFT):
            if i < NIT:
                _ve(("vev", i))
            if i >= SHIFT:
                it = i - SHIFT
                _ve(("fm", it)); _ve(("kg", it)); _ve(("mu", it))
        for it in range(NIT):
            _se(("kbf", it))

        def pa_sy(e):
            bs(e)
            LD.inc(e.dma_start(out=c1_sb[:], in_=ics_t), 16)
            LD.inc(e.dma_start(out=c2_sb[:], in_=iss_t), 16)
            LD.inc(e.dma_start(out=bv_sb[:], in_=bv_bc), 16)
            for i in range(NIT + SHIFT):
                if i < NIT:
                    vg, t = divmod(i, NT)
                    e.wait_ge(VE.h, ve_ev[("vev", i)])
                    SPST.group(("v", i))(
                        e, v_sp[t, :, 4 * vg:4 * vg + 4, :],
                        vqb[vg % 2][:, t * 512:(t + 1) * 512])
                if i >= SHIFT:
                    it = i - SHIFT
                    qd, rem = divmod(it, 4 * NTG)
                    h, tg = divmod(rem, NTG)
                    head = qd * 4 + h
                    g = KSP.group(("sp", it))
                    e.wait_ge(VE.h, ve_ev[("fm", it)])
                    g(e, k_sp[tg * 4:(tg + 1) * 4, :, head, :].rearrange("n k t -> k n t"),
                      fmb[it % 2][:].rearrange("p (n t) -> p n t", n=4))
                    e.wait_ge(VE.h, ve_ev[("kg", it)])
                    g(e, kg_sp[tg * 4:(tg + 1) * 4, :, head, :].rearrange("n t k -> t n k"),
                      kgb[it % 2][:].rearrange("p (n k) -> p n k", n=4))
            e.wait_ge(VE.h, ve_ev[("mu", NIT - 1)])
            LD.inc(e.dma_start(out=cc_in.ap().rearrange("(h k) v -> k h v", h=H),
                               in_=m_sb[:].rearrange("p (h v) -> p h v", h=H)), 16)
            be(e, "sy")

        ld_m_total = [None]

        def pa_gp(e):
            bs(e)
            for q in range(1, 4):
                e.wait_ge(PE.h, pe_ev[("v", (q - 1) * NT)])
                WST.group(("wv", q))(
                    e, wvb[q % 2][:].rearrange("p (k m) -> p k m", k=KC),
                    Wv[:, q * 512:(q + 1) * 512].rearrange("(k p) m -> p k m", p=128))
                e.wait_ge(PE.h, pe_ev[("kproj", max((q - 1) * 4 * NTG, 0))])
                WST.group(("wk", q))(
                    e, wkb[q % 2][:].rearrange("p (k m) -> p k m", k=KC),
                    Wk[:, q * 512:(q + 1) * 512].rearrange("(k p) m -> p k m", p=128))
            e.wait_ge(LD.h, ld_m_total[0])
            CCS.inc(e.collective_compute(
                "AllGather", ALU.bypass, replica_groups=[list(range(NCORES))],
                ins=[cc_in.ap().opt()], outs=[cc_out.ap().opt()]), 1)
            be(e, "gp")

        def pa_te(e):
            bs(e)
            for i in range(NIT + SHIFT):
                if i < NIT:
                    vg, t = divmod(i, NT)
                    WST.wait(e, ("wv", vg))
                    if i >= 2:
                        e.wait_ge(VE.h, ve_ev[("vev", i - 2)])
                    last = None
                    for k in range(KC):
                        last = e.matmul(psV[i % 2][:],
                                        xn_fm[:, k * T + t * 128: k * T + (t + 1) * 128],
                                        wvb[vg % 2][:, k * 512:(k + 1) * 512],
                                        start=(k == 0), stop=(k == KC - 1))
                    PE.inc(last)
                if i >= SHIFT:
                    it = i - SHIFT
                    qd, rem = divmod(it, 4 * NTG)
                    h, tg = divmod(rem, NTG)
                    WST.wait(e, ("wk", qd))
                    if it >= 2:
                        e.wait_ge(SE.h, se_ev[("kbf", it - 2)])
                    last = None
                    for k in range(KC):
                        last = e.matmul(psK[it % 2][:],
                                        wkb[qd % 2][:, k * 512 + h * 128: k * 512 + (h + 1) * 128],
                                        xn_fm[:, k * T + tg * 512: k * T + (tg + 1) * 512],
                                        start=(k == 0), stop=(k == KC - 1))
                    PE.inc(last)
                    e.wait_ge(SE.h, se_ev[("kbf", it)])
                    if it >= 1:
                        e.wait_ge(VE.h, ve_ev[("fm", it - 1)])
                    PE.inc(e.matmul(psR[:], p_bf[:], kbf[it % 2][:], start=True, stop=True))
                    e.wait_ge(VE.h, ve_ev[("fm", it)])
                    if it >= 1:
                        e.wait_ge(VE.h, ve_ev[("kg", it - 1)])
                    last = None
                    for j in range(4):
                        last = e.transpose(psT[:, j * 128:(j + 1) * 128],
                                           fmb[it % 2][:, j * 128:(j + 1) * 128], id_bf[:])
                    PE.inc(last)
                    e.wait_ge(VE.h, ve_ev[("kg", it)])
                    e.wait_ge(VE.h, ve_ev[("vev", (qd + 1) * NT - 1)])
                    if it >= 1:
                        e.wait_ge(VE.h, ve_ev[("mu", it - 1)])
                    last = None
                    for j in range(4):
                        n_ = tg * 4 + j
                        last = e.matmul(psU[:, j * 128:(j + 1) * 128],
                                        kgb[it % 2][:, j * 128:(j + 1) * 128],
                                        vqb[qd % 2][:, n_ * 512 + h * 128: n_ * 512 + (h + 1) * 128],
                                        start=(j == 0), stop=(j == 3))
                    PE.inc(last)
            be(e, "te")

        def pa_se(e):
            bs(e)
            for it in range(NIT):
                qd, rem = divmod(it, 4 * NTG)
                h, tg = divmod(rem, NTG)
                head = qd * 4 + h
                e.wait_ge(PE.h, pe_ev[("kproj", it)])
                if it >= 2:
                    e.wait_ge(PE.h, pe_ev[("krot", it - 2)])
                    e.wait_ge(VE.h, ve_ev[("fm", it - 2)])
                SE.inc(e.activation(kbf[it % 2][:], psK[it % 2][:], AF.Identity,
                                    bias=bkr_sb[:, head:head + 1]))
            be(e, "se")

        def pa_ve(e):
            bs(e)
            e.memset(m_sb[:], 0.0)
            e.wait_ge(LD.h, bLD_pa + 48)
            for i in range(NIT + SHIFT):
                if i < NIT:
                    vg, t = divmod(i, NT)
                    e.wait_ge(PE.h, pe_ev[("v", i)])
                    if i >= 2:
                        SPST.wait(e, ("v", i - 2))
                    if i >= 2 * NT and t == 0:
                        e.wait_ge(PE.h, pe_ev[("kum", (vg - 1) * 4 * NTG - 1)])
                    VE.inc(e.tensor_tensor(vqb[vg % 2][:, t * 512:(t + 1) * 512], psV[i % 2][:],
                                           bv_sb[:, vg * 512:(vg + 1) * 512], ALU.add))
                if i >= SHIFT:
                    it = i - SHIFT
                    qd, rem = divmod(it, 4 * NTG)
                    h, tg = divmod(rem, NTG)
                    head = qd * 4 + h
                    tsl = slice(tg * 512, (tg + 1) * 512)
                    e.wait_ge(PE.h, pe_ev[("krot", it)])
                    if it >= 2:
                        KSP.wait(e, ("sp", it - 2))
                        e.wait_ge(PE.h, pe_ev[("ktp", it - 2)])
                    e.tensor_tensor(ta[:], kbf[it % 2][:], c1_sb[:, tsl], ALU.mult)
                    e.tensor_tensor(tb_[:], psR[:], c2_sb[:, tsl], ALU.mult)
                    e.drain()
                    VE.inc(e.tensor_tensor(fmb[it % 2][:], ta[:], tb_[:], ALU.add))
                    e.wait_ge(PE.h, pe_ev[("ktp", it)])
                    if it >= 2:
                        e.wait_ge(PE.h, pe_ev[("kum", it - 2)])
                    VE.inc(e.tensor_scalar(kgb[it % 2][:], psT[:], gcm_sb[:, head:head + 1],
                                           None, ALU.mult))
                    e.wait_ge(PE.h, pe_ev[("kum", it)])
                    last = None
                    for j in range(4):
                        if j > 0:
                            e.drain()
                        hsl = slice(head * 128, (head + 1) * 128)
                        last = e.scalar_tensor_tensor(m_sb[:, hsl], m_sb[:, hsl], GC[head],
                                                      psU[:, j * 128:(j + 1) * 128],
                                                      ALU.mult, ALU.add)
                    VE.inc(last)
            be(e, "ve")

        bLD_pa = LD.n
        block.sync(pa_sy)
        ld_m_total[0] = LD.n
        block.gpsimd(pa_gp); block.tensor(pa_te)
        block.scalar(pa_se); block.vector(pa_ve)
        assert PE.n == _pectr[0] and VE.n == _vectr[0] and SE.n == _sectr[0]
        pa.close()
        pa_scope.close()
        nphase[0] += 1

        p56 = ExitStack()
        sb56 = lambda n, shp, dt: p56.enter_context(nc.sbuf_tensor(n, shp, dt, side="right"))
        s_f32 = sb56("s_f32", [128, H * 128], F32)
        s_bfT = sb56("s_bfT", [128, H * 128], BF16)

        pb = ExitStack()
        sbq = lambda n, shp, dt: pb.enter_context(nc.sbuf_tensor(n, shp, dt))
        psq = lambda n, shp, dt=F32: pb.enter_context(nc.psum_tensor(n, shp, dt))
        wqb = [sbq(f"wqb{i}", [128, KC * 512], BF16) for i in range(2)]
        wgb = [sbq(f"wgb{i}", [128, KC * 512], BF16) for i in range(2)]
        mg_buf = [sbq(f"mg{i}", [128, H * 128], F32) for i in range(2)]
        c1q = sbq("c1q", [128, T], F32); c2q = sbq("c2q", [128, T], F32)
        qbf = [sbq(f"qbf{i}", [128, 512], BF16) for i in range(2)]
        taq = sbq("taq", [128, 512], F32); tbq = sbq("tbq", [128, 512], F32)
        fmq = [sbq(f"fmq{i}", [128, 512], BF16) for i in range(2)]
        g_buf = [sbq(f"g_buf{i}", [128, 512], BF16) for i in range(2)]
        psQ = [psq(f"psQ{i}", [128, 512]) for i in range(2)]
        psRq = psq("psRq", [128, 512])
        psG5 = [psq(f"psG5{i}", [128, 512]) for i in range(2)]

        bPE, bVE, bSE = PE.n, VE.n, SE.n
        pe_ev = {}; _pectr = [bPE]
        ve_ev = {}; _vectr = [bVE]
        se_ev = {}; _sectr = [bSE]
        _pe = _mk(_pectr, pe_ev); _ve = _mk(_vectr, ve_ev); _se = _mk(_sectr, se_ev)
        for it in range(NIT):
            _pe(("qproj", it)); _pe(("qrot", it)); _pe(("gproj", it))
        S0C = {}
        for c in range(NCORES):
            S0C.setdefault(4 * NTG + (c * 4 * NTG) // NCORES, []).append(c)
        for it in range(NIT):
            _ve(("fm", it))
            for c in S0C.get(it, []):
                _ve(("s0", c))
                if c == NCORES - 1:
                    _ve(("sbf0",))
        for it in range(NIT):
            _se(("qbf", it)); _se(("gev", it))

        def wqg_load(e, q):
            e.wait_ge(WOC.h, 48)
            if q >= 1:
                e.wait_ge(PE.h, pe_ev[("qproj", (q - 1) * 4 * NTG)])
            RST.group(("wq", q))(
                e, wqb[q % 2][:].rearrange("p (k m) -> p k m", k=KC),
                wq_scr[:, q * 512:(q + 1) * 512].rearrange("(k p) m -> p k m", p=128))
            if q >= 1:
                e.wait_ge(PE.h, pe_ev[("gproj", (q - 1) * 4 * NTG)])
            RST.group(("wg", q))(
                e, wgb[q % 2][:].rearrange("p (k m) -> p k m", k=KC),
                wg_scr[:, q * 512:(q + 1) * 512].rearrange("(k p) m -> p k m", p=128))

        def pb_sy(e):
            bs(e)
            LD.inc(e.dma_start(out=c1q[:], in_=cs_t), 16)
            LD.inc(e.dma_start(out=c2q[:], in_=ss_t), 16)
            wqg_load(e, 0)
            SPST.group("xn")(e, xn_sp.rearrange("k p t -> p k t"),
                             xn_fm[:].rearrange("p (k t) -> p k t", k=KC))
            for it in range(NIT):
                qd, rem = divmod(it, 4 * NTG)
                h, tg = divmod(rem, NTG)
                head = qd * 4 + h
                if it % (4 * NTG) == 0 and qd < 3:
                    wqg_load(e, qd + 1)
                g = KSP.group(("spq", it))
                e.wait_ge(VE.h, ve_ev[("fm", it)])
                g(e, q_sp[tg * 4:(tg + 1) * 4, :, head, :].rearrange("n k t -> k n t"),
                  fmq[it % 2][:].rearrange("p (n t) -> p n t", n=4))
                e.wait_ge(SE.h, se_ev[("gev", it)])
                g(e, g_sp[tg * 4:(tg + 1) * 4, :, head, :].rearrange("n g t -> g n t"),
                  g_buf[it % 2][:].rearrange("p (n t) -> p n t", n=4))
            be(e, "sy")

        def pb_gp(e):
            bs(e)
            e.wait_ge(CCS.h, 1)
            for c in range(NCORES):
                if c >= 2:
                    e.wait_ge(VE.h, ve_ev[("s0", c - 2)])
                MST.group(("mg", c))(
                    e, mg_buf[c % 2][:].rearrange("p (h v) -> p h v", h=H),
                    cc_out.ap().rearrange("(c h k) v -> c k h v", c=NCORES, h=H)[c])
            be(e, "gp")

        def pb_te(e):
            bs(e)
            for it in range(NIT):
                qd, rem = divmod(it, 4 * NTG)
                h, tg = divmod(rem, NTG)
                RST.wait(e, ("wq", qd))
                if it >= 2:
                    e.wait_ge(SE.h, se_ev[("qbf", it - 2)])
                last = None
                for k in range(KC):
                    last = e.matmul(psQ[it % 2][:],
                                    wqb[qd % 2][:, k * 512 + h * 128: k * 512 + (h + 1) * 128],
                                    xn_fm[:, k * T + tg * 512: k * T + (tg + 1) * 512],
                                    start=(k == 0), stop=(k == KC - 1))
                PE.inc(last)
                e.wait_ge(SE.h, se_ev[("qbf", it)])
                if it >= 1:
                    e.wait_ge(VE.h, ve_ev[("fm", it - 1)])
                PE.inc(e.matmul(psRq[:], p_bf[:], qbf[it % 2][:], start=True, stop=True))
                RST.wait(e, ("wg", qd))
                if it >= 2:
                    e.wait_ge(SE.h, se_ev[("gev", it - 2)])
                last = None
                for k in range(KC):
                    last = e.matmul(psG5[it % 2][:],
                                    wgb[qd % 2][:, k * 512 + h * 128: k * 512 + (h + 1) * 128],
                                    xn_fm[:, k * T + tg * 512: k * T + (tg + 1) * 512],
                                    start=(k == 0), stop=(k == KC - 1))
                PE.inc(last)
            be(e, "te")

        def pb_se(e):
            bs(e)
            for it in range(NIT):
                qd, rem = divmod(it, 4 * NTG)
                h, tg = divmod(rem, NTG)
                head = qd * 4 + h
                e.wait_ge(PE.h, pe_ev[("qproj", it)])
                if it >= 2:
                    e.wait_ge(PE.h, pe_ev[("qrot", it - 2)])
                    e.wait_ge(VE.h, ve_ev[("fm", it - 2)])
                SE.inc(e.activation(qbf[it % 2][:], psQ[it % 2][:], AF.Identity,
                                    bias=bqr_sb[:, head:head + 1]))
                e.wait_ge(PE.h, pe_ev[("gproj", it)])
                if it >= 2:
                    KSP.wait(e, ("spq", it - 2))
                SE.inc(e.activation(g_buf[it % 2][:], psG5[it % 2][:], AF.Silu,
                                    bias=bgr_sb[:, head:head + 1]))
            be(e, "se")

        def pb_ve(e):
            bs(e)
            e.wait_ge(LD.h, bLD_pb + 32)
            e.memset(s_f32[:], 0.0)
            for it in range(NIT):
                qd, rem = divmod(it, 4 * NTG)
                h, tg = divmod(rem, NTG)
                tsl = slice(tg * 512, (tg + 1) * 512)
                e.wait_ge(PE.h, pe_ev[("qrot", it)])
                if it >= 2:
                    KSP.wait(e, ("spq", it - 2))
                e.tensor_tensor(taq[:], qbf[it % 2][:], c1q[:, tsl], ALU.mult)
                e.tensor_tensor(tbq[:], psRq[:], c2q[:, tsl], ALU.mult)
                e.drain()
                VE.inc(e.tensor_tensor(fmq[it % 2][:], taq[:], tbq[:], ALU.add))
                for c in S0C.get(it, []):
                    MST.wait(e, ("mg", c))
                    e.drain()
                    last = None
                    for h in range(H):
                        hsl = slice(h * 128, (h + 1) * 128)
                        last = e.scalar_tensor_tensor(s_f32[:, hsl], mg_buf[c % 2][:, hsl],
                                                      wm_sb[:, c * H + h:c * H + h + 1],
                                                      s_f32[:, hsl], ALU.mult, ALU.add)
                    VE.inc(last)
                    if c == NCORES - 1:
                        e.drain()
                        VE.inc(e.tensor_copy(s_bfT[:], s_f32[:]))
            be(e, "ve")

        bLD_pb = LD.n
        block.sync(pb_sy); block.gpsimd(pb_gp); block.tensor(pb_te)
        block.scalar(pb_se); block.vector(pb_ve)
        assert PE.n == _pectr[0] and VE.n == _vectr[0] and SE.n == _sectr[0]
        pb.close()
        px.close()
        nphase[0] += 1


        p6 = ExitStack()
        sb6 = lambda n, shp, dt: p6.enter_context(nc.sbuf_tensor(n, shp, dt))
        ps6 = lambda n, shp, dt=F32: p6.enter_context(nc.psum_tensor(n, shp, dt))
        wo_bf = sb6("wo_bf", [128, KC * D], BF16)
        bo_sb = sb6("bo_sb", [128, D], F32)
        d2_sb = sb6("d2_sb", [128, H * 128], F32)
        grp1v = sb6("grp1v", [128, H], F32)
        chq = [sb6(f"chq{i}", [128, H * 128], BF16) for i in range(2)]
        chk = [sb6(f"chk{i}", [128, H * 128], BF16) for i in range(2)]
        chkg = [sb6(f"chkg{i}", [128, H * 128], BF16) for i in range(2)]
        chv = [sb6(f"chv{i}", [128, H * 128], BF16) for i in range(2)]
        chg = [sb6(f"chg{i}", [128, H * 128], BF16) for i in range(2)]
        a2_sb = [sb6(f"a2_{i}", [128, 512], BF16) for i in range(2)]
        ret1 = sb6("ret1", [128, 512], F32)
        ret_all = [sb6(f"ret_all{i}", [128, H * 128], F32) for i in range(2)]
        sq_all = sb6("sq_all", [128, H * 128], F32)
        gn_all = sb6("gn_all", [128, H * 128], BF16)
        prod_fm = sb6("prod_fm", [128, H * 128], BF16)
        s_buf = [sb6(f"s_buf{i}", [128, D], BF16) for i in range(2)]
        st16 = sb6("st16", [128, 16 * 6], F32)
        psS = [ps6(f"psS{i}", [128, 512]) for i in range(2)]
        psRr = ps6("psRr", [128, 512])
        psC = ps6("psC", [128, 512])
        psU6 = [ps6(f"psU6{i}", [128, 512]) for i in range(2)]
        psT6 = ps6("psT6", [128, 512], BF16)
        psW = ps6("psW", [128, 512])
        c_sum = slice(0, 16); c_ssq = slice(16, 32); c_mu = slice(32, 48)
        c_nv = slice(48, 64); c_rs = slice(64, 80); c_nmr = slice(80, 96)

        bLD6, bPE, bVE, bSE, bGP = LD.n, PE.n, VE.n, SE.n, GPC.n
        ve_sbf0_val = bVE
        pe_ev = {}; _pectr = [bPE]
        ve_ev = {}; _vectr = [bVE]
        se_ev = {}; _sectr = [bSE]
        gp_ev = {}; _gpctr = [bGP]
        def _mk(ctr, ev):
            def f(key):
                ctr[0] += 1
                ev[key] = ctr[0]
            return f
        _pe = _mk(_pectr, pe_ev); _ve = _mk(_vectr, ve_ev)
        _se = _mk(_sectr, se_ev); _gp = _mk(_gpctr, gp_ev)
        CORE_ORD = [("sc", 0), ("sc", 1), ("rt", 0), ("cx", 0), ("um", 0),
                    ("sc", 2), ("rt", 1), ("cx", 1), ("um", 1),
                    ("sc", 3), ("rt", 2), ("cx", 2), ("um", 2),
                    ("rt", 3), ("cx", 3), ("um", 3)]
        for n in range(NT):
            for op, hq in CORE_ORD:
                _pe((op, n, hq))
            if n >= 1:
                for tg4 in range(4):
                    _pe(("tp", n - 1, tg4))
                for dmg in range(4):
                    _pe(("wo", n - 1, dmg))
        for tg4 in range(4):
            _pe(("tp", NT - 1, tg4))
        for dmg in range(4):
            _pe(("wo", NT - 1, dmg))
        for n in range(NT):
            for hq in range(4):
                _ve(("a2", n, hq)); _ve(("retf", n, hq)); _ve(("supd", n, hq))
            if n >= 1:
                for tg4 in range(4):
                    _ve(("prod", n - 1, tg4))
            _ve(("red", n))
            if n >= 1:
                for dmg in range(4):
                    _ve(("sev", n - 1, dmg))
        for tg4 in range(4):
            _ve(("prod", NT - 1, tg4))
        for dmg in range(4):
            _ve(("sev", NT - 1, dmg))
        for n in range(NT):
            for hq in range(4):
                _se(("ev", n, hq))
            _se(("rstd", n))
            for h in range(H):
                _se(("gn", n, h))
        for n in range(NT):
            _gp(("sq", n)); _gp(("stats", n)); _gp(("nmr", n))

        def p6_sy(e):
            bs(e)
            LD.inc(e.dma_start(out=bo_sb[:], in_=bo_bc), 16)
            LD.inc(e.dma_start(out=d2_sb[:], in_=d2_t), 16)
            LD.inc(e.dma_start(out=grp1v[:], in_=grp1v_t), 16)
            def ch_load(n):
                g = RST.group(("ch", n))
                for spill, buf in [(q_sp, chq), (k_sp, chk), (kg_sp, chkg),
                                   (v_sp, chv), (g_sp, chg)]:
                    g(e, buf[n % 2][:].rearrange("p (h t) -> p h t", h=H), spill[n])
            ch_load(0)
            if NT > 1:
                ch_load(1)
            e.wait_ge(WOC.h, 48)
            g = WOS.group("wo")
            for k in range(KC):
                g(e, wo_bf[:, k * D:(k + 1) * D], wo_scr[k * 128:(k + 1) * 128, :])
            for n in range(NT):
                if n + 2 < NT:
                    e.wait_ge(PE.h, pe_ev[("um", n, 3)])
                    e.wait_ge(VE.h, ve_ev[("prod", n, 3)])
                    ch_load(n + 2)
                if n >= 1:
                    e.wait_ge(VE.h, ve_ev[("sev", n - 1, 3)])
                    SPST.group(("s", n - 1))(e, s_sp[n - 1], s_buf[(n - 1) % 2][:])
            e.wait_ge(VE.h, ve_ev[("sev", NT - 1, 3)])
            SPST.group(("s", NT - 1))(e, s_sp[NT - 1], s_buf[(NT - 1) % 2][:])
            be(e, "sy")

        def pk4(e, mk, start=True, stop=True):
            last = None
            for j in range(4):
                o, l, r = mk(j)
                last = e.matmul(o, l, r, start=(start and j == 0), stop=(stop and j == 3))
            return last

        def p6_te_core(e, n):
            def sl4(buf, hq, j):
                return buf[n % 2][:, (hq * 4 + j) * 128:(hq * 4 + j + 1) * 128]
            def mm_sc(hq):
                g = n * 4 + hq
                RST.wait(e, ("ch", n))
                if g >= 2:
                    pn, phq = divmod(g - 2, 4)
                    e.wait_ge(VE.h, ve_ev[("a2", pn, phq)])
                PE.inc(pk4(e, lambda j: (psS[hq % 2][:, j * 128:(j + 1) * 128],
                                         sl4(chk, hq, j), sl4(chq, hq, j))))
            def mm_rt(hq):
                g = n * 4 + hq
                e.wait_ge(VE.h, ve_ev[("a2", n, hq)])
                if g >= 1:
                    pn, phq = divmod(g - 1, 4)
                    e.wait_ge(SE.h, se_ev[("ev", pn, phq)])
                PE.inc(pk4(e, lambda j: (psRr[:, j * 128:(j + 1) * 128],
                                         a2_sb[hq % 2][:, j * 128:(j + 1) * 128],
                                         sl4(chv, hq, j))))
            def mm_cx(hq):
                g = n * 4 + hq
                if n == 0:
                    if hq == 0:
                        e.wait_ge(VE.h, ve_sbf0_val)
                else:
                    e.wait_ge(VE.h, ve_ev[("supd", n - 1, hq)])
                if g >= 1:
                    pn, phq = divmod(g - 1, 4)
                    e.wait_ge(VE.h, ve_ev[("retf", pn, phq)])
                PE.inc(pk4(e, lambda j: (psC[:, j * 128:(j + 1) * 128],
                                         sl4(chq, hq, j),
                                         s_bfT[:, (hq * 4 + j) * 128:(hq * 4 + j + 1) * 128])))
            def mm_um(hq):
                g = n * 4 + hq
                if g >= 2:
                    pn, phq = divmod(g - 2, 4)
                    e.wait_ge(VE.h, ve_ev[("supd", pn, phq)])
                PE.inc(pk4(e, lambda j: (psU6[hq % 2][:, j * 128:(j + 1) * 128],
                                         sl4(chkg, hq, j), sl4(chv, hq, j))))
            mm = {"sc": mm_sc, "rt": mm_rt, "cx": mm_cx, "um": mm_um}
            for op, hq in CORE_ORD:
                mm[op](hq)

        def p6_te_tail(e, n):
            for tg4 in range(4):
                last = None
                for j in range(4):
                    h = tg4 * 4 + j
                    e.wait_ge(SE.h, se_ev[("gn", n, h)])
                    if j == 0:
                        f = n * 4 + tg4
                        if f >= 1:
                            pn, pg = divmod(f - 1, 4)
                            e.wait_ge(VE.h, ve_ev[("prod", pn, pg)])
                    last = e.transpose(psT6[:, j * 128:(j + 1) * 128],
                                       gn_all[:, h * 128:(h + 1) * 128], id_bf[:])
                PE.inc(last)
            e.wait_ge(VE.h, ve_ev[("prod", n, 3)])
            for dmg in range(4):
                if n == 0 and dmg == 0:
                    WOS.wait(e, "wo")
                w = n * 4 + dmg
                if w >= 1:
                    pn, pd = divmod(w - 1, 4)
                    e.wait_ge(VE.h, ve_ev[("sev", pn, pd)])
                last = None
                for hv in range(KC):
                    last = e.matmul(psW[:],
                                    prod_fm[:, hv * 128:(hv + 1) * 128],
                                    wo_bf[:, hv * D + dmg * 512: hv * D + (dmg + 1) * 512],
                                    start=(hv == 0), stop=(hv == KC - 1))
                PE.inc(last)

        def p6_te(e):
            bs(e)
            for n in range(NT):
                p6_te_core(e, n)
                if n >= 1:
                    p6_te_tail(e, n - 1)
            p6_te_tail(e, NT - 1)
            be(e, "te")

        def p6_se(e):
            bs(e)
            for n in range(NT):
                for hq in range(4):
                    e.wait_ge(PE.h, pe_ev[("rt", n, hq)])
                    if n * 4 + hq >= 1:
                        g = n * 4 + hq - 1
                        pn, phq = divmod(g, 4)
                        e.wait_ge(VE.h, ve_ev[("retf", pn, phq)])
                    e.drain()
                    SE.inc(e.copy(ret1[:], psRr[:]))
                e.wait_ge(GPC.h, gp_ev[("stats", n)])
                e.drain()
                e.activation(st16[:, c_nv], st16[:, c_nv], AF.Ln, scale=-1.0)
                e.drain()
                SE.inc(e.activation(st16[:, c_rs], st16[:, c_nv], AF.Exp, scale=-0.5))
                e.wait_ge(GPC.h, gp_ev[("nmr", n)])
                e.drain()
                for h in range(H):
                    if n >= 1:
                        e.wait_ge(PE.h, pe_ev[("tp", n - 1, h // 4)])
                    SE.inc(e.activation(gn_all[:, h * 128:(h + 1) * 128],
                                        ret_all[n % 2][:, h * 128:(h + 1) * 128], AF.Identity,
                                        scale=st16[:, 64 + h:64 + h + 1],
                                        bias=st16[:, 80 + h:80 + h + 1]))
            be(e, "se")

        def p6_ve(e):
            bs(e)
            e.wait_ge(LD.h, bLD6 + 48)
            for n in range(NT):
                for hq in range(4):
                    hsl = slice(hq * 512, (hq + 1) * 512)
                    e.wait_ge(PE.h, pe_ev[("sc", n, hq)])
                    if n * 4 + hq >= 2:
                        pn, phq = divmod(n * 4 + hq - 2, 4)
                        e.wait_ge(PE.h, pe_ev[("rt", pn, phq)])
                    if n >= 2 and hq == 0:
                        e.wait_ge(SE.h, se_ev[("gn", n - 2, 15)])
                    VE.inc(e.tensor_tensor(a2_sb[hq % 2][:], psS[hq % 2][:], d2_sb[:, hsl], ALU.mult))
                    e.wait_ge(SE.h, se_ev[("ev", n, hq)])
                    e.wait_ge(PE.h, pe_ev[("cx", n, hq)])
                    last = None
                    for j in range(4):
                        h = hq * 4 + j
                        last = e.scalar_tensor_tensor(
                            ret_all[n % 2][:, h * 128:(h + 1) * 128],
                            psC[:, j * 128:(j + 1) * 128], grp1v[:, h:h + 1],
                            ret1[:, j * 128:(j + 1) * 128], ALU.mult, ALU.add)
                    VE.inc(last)
                    e.wait_ge(PE.h, pe_ev[("um", n, hq)])
                    for j in range(4):
                        h = hq * 4 + j
                        hsl2 = slice(h * 128, (h + 1) * 128)
                        e.scalar_tensor_tensor(s_f32[:, hsl2], s_f32[:, hsl2], GC[h],
                                               psU6[hq % 2][:, j * 128:(j + 1) * 128],
                                               ALU.mult, ALU.add)
                    e.drain()
                    VE.inc(e.tensor_copy(s_bfT[:, hq * 512:(hq + 1) * 512],
                                         s_f32[:, hq * 512:(hq + 1) * 512]))
                if n >= 1:
                    for tg4 in range(4):
                        e.wait_ge(PE.h, pe_ev[("tp", n - 1, tg4)])
                        if n >= 2 and tg4 == 0:
                            e.wait_ge(PE.h, pe_ev[("wo", n - 2, 3)])
                        VE.inc(e.tensor_tensor(prod_fm[:, tg4 * 512:(tg4 + 1) * 512],
                                               psT6[:],
                                               chg[(n - 1) % 2][:, tg4 * 512:(tg4 + 1) * 512],
                                               ALU.mult))
                e.drain()
                e.reduce_sum(st16[:, c_sum].rearrange("p (h o) -> p h o", o=1),
                             ret_all[n % 2][:].rearrange("p (h t) -> p h t", h=H), axis=AX.X)
                e.wait_ge(GPC.h, gp_ev[("sq", n)])
                VE.inc(e.reduce_sum(st16[:, c_ssq].rearrange("p (h o) -> p h o", o=1),
                                    sq_all[:].rearrange("p (h t) -> p h t", h=H), axis=AX.X))
                if n >= 1:
                    for dmg in range(4):
                        dsl = slice(dmg * 512, (dmg + 1) * 512)
                        e.wait_ge(PE.h, pe_ev[("wo", n - 1, dmg)])
                        if n >= 3 and dmg == 0:
                            SPST.wait(e, ("s", n - 3))
                        VE.inc(e.tensor_tensor(s_buf[(n - 1) % 2][:, dsl], psW[:],
                                               bo_sb[:, dsl], ALU.add))
            n = NT - 1
            for tg4 in range(4):
                e.wait_ge(PE.h, pe_ev[("tp", n, tg4)])
                if tg4 == 0 and NT >= 2:
                    e.wait_ge(PE.h, pe_ev[("wo", n - 1, 3)])
                VE.inc(e.tensor_tensor(prod_fm[:, tg4 * 512:(tg4 + 1) * 512], psT6[:],
                                       chg[n % 2][:, tg4 * 512:(tg4 + 1) * 512], ALU.mult))
            for dmg in range(4):
                dsl = slice(dmg * 512, (dmg + 1) * 512)
                e.wait_ge(PE.h, pe_ev[("wo", n, dmg)])
                if dmg == 0 and NT >= 3:
                    SPST.wait(e, ("s", NT - 3))
                VE.inc(e.tensor_tensor(s_buf[n % 2][:, dsl], psW[:],
                                       bo_sb[:, dsl], ALU.add))
            be(e, "ve")

        def p6_gp(e):
            bs(e)
            for n in range(NT):
                e.wait_ge(VE.h, ve_ev[("retf", n, 3)])
                e.drain()
                GPC.inc(e.tensor_tensor(sq_all[:], ret_all[n % 2][:], ret_all[n % 2][:], ALU.mult))
                e.wait_ge(VE.h, ve_ev[("red", n)])
                e.drain()
                e.tensor_scalar(st16[:, c_mu], st16[:, c_sum], 1.0 / 128, None, ALU.mult)
                e.tensor_scalar(st16[:, c_ssq], st16[:, c_ssq], 1.0 / 128, None, ALU.mult)
                e.drain()
                e.tensor_tensor(st16[:, c_nv], st16[:, c_mu], st16[:, c_mu], ALU.mult)
                e.drain()
                e.tensor_tensor(st16[:, c_nv], st16[:, c_nv], st16[:, c_ssq], ALU.subtract)
                e.drain()
                GPC.inc(e.tensor_scalar(st16[:, c_nv], st16[:, c_nv], EPS, None, ALU.subtract))
                e.wait_ge(SE.h, se_ev[("rstd", n)])
                e.drain()
                e.tensor_scalar(st16[:, c_nmr], st16[:, c_mu], -1.0, None, ALU.mult)
                e.drain()
                GPC.inc(e.tensor_tensor(st16[:, c_nmr], st16[:, c_nmr], st16[:, c_rs], ALU.mult))
            be(e, "gp")

        block.sync(p6_sy); block.gpsimd(p6_gp); block.tensor(p6_te)
        block.scalar(p6_se); block.vector(p6_ve)
        assert PE.n == _pectr[0] and VE.n == _vectr[0]
        assert SE.n == _sectr[0] and GPC.n == _gpctr[0]
        p6.close()
        p56.close()
        nphase[0] += 1


        p7ab = ExitStack()
        sb7s = lambda n, shp, dt: p7ab.enter_context(nc.sbuf_tensor(n, shp, dt, side="right"))
        dwb = [sb7s(f"dwb{i}", [128, IC * 512], BF16) for i in range(2)]
        p7a = ExitStack()
        sb7 = lambda n, shp, dt: p7a.enter_context(nc.sbuf_tensor(n, shp, dt))
        ps7 = lambda n, shp, dt=F32: p7a.enter_context(nc.psum_tensor(n, shp, dt))
        xn_fm2 = sb7("xn_fm2", [128, KC * T], BF16)
        gwb = [sb7(f"gwb{i}", [128, KC * 128], BF16) for i in range(2)]
        uwb = [sb7(f"uwb{i}", [128, KC * 128], BF16) for i in range(2)]
        ga_sb = [sb7(f"ga_sb{i}", [128, 512], F32) for i in range(2)]
        ua_sb = [sb7(f"ua_sb{i}", [128, 512], F32) for i in range(2)]
        hdn_buf = [sb7(f"hdn_buf{i}", [128, 512], BF16) for i in range(2)]
        psGa = [ps7(f"psGa{i}", [128, 512]) for i in range(2)]
        psUa = [ps7(f"psUa{i}", [128, 512]) for i in range(2)]

        bPE, bVE, bSE = PE.n, VE.n, SE.n
        NF = IC * NTG
        pe_gm = lambda f: bPE + f * 2 + 1
        pe_um7 = lambda f: bPE + f * 2 + 2
        se_ga = lambda f: bSE + f * 2 + 1
        se_ua = lambda f: bSE + f * 2 + 2
        ve_hd = lambda f: bVE + f + 1

        def dn_quarter_load(e, q):
            g = DST.group(("dn", q))
            for ic in range(IC):
                g(e, dwb[q % 2][:, ic * 512:(ic + 1) * 512],
                  downT[ic * 128:(ic + 1) * 128, q * 512:(q + 1) * 512])

        def p7a_sy(e):
            bs(e)
            SPST.wait(e, "xn")
            RST.group("xn2")(e, xn_fm2[:].rearrange("p (k t) -> p k t", k=KC),
                             xn_sp.rearrange("k p t -> p k t"))
            for f in range(NF):
                ic, tg = divmod(f, NTG)
                e.wait_ge(VE.h, ve_hd(f))
                SPST.group(("h", f))(
                    e, hdn_sp[tg * 4:(tg + 1) * 4, :, ic, :].rearrange("b i t -> i b t"),
                    hdn_buf[f % 2][:].rearrange("p (b t) -> p b t", b=4))
            be(e, "sy")

        def p7a_gp(e):
            bs(e)
            for ic in range(IC):
                if ic >= 2:
                    e.wait_ge(PE.h, pe_um7((ic - 1) * NTG - 1))
                g = WST.group(("gu", ic))
                g(e, gwb[ic % 2][:].rearrange("p (k m) -> p k m", k=KC),
                  gateT[:, ic * 128:(ic + 1) * 128].rearrange("(k p) i -> p k i", p=128))
                g(e, uwb[ic % 2][:].rearrange("p (k m) -> p k m", k=KC),
                  upT[:, ic * 128:(ic + 1) * 128].rearrange("(k p) i -> p k i", p=128))
                if ic == IC // 2:
                    dn_quarter_load(e, 0)
                if ic == IC - 1:
                    dn_quarter_load(e, 1)
            be(e, "gp")

        def p7a_te(e):
            bs(e)
            RST.wait(e, "xn2")
            for f in range(NF):
                ic, tg = divmod(f, NTG)
                WST.wait(e, ("gu", ic))
                if f >= 2:
                    e.wait_ge(SE.h, se_ga(f - 2))
                last = None
                for k in range(KC):
                    last = e.matmul(psGa[f % 2][:], gwb[ic % 2][:, k * 128:(k + 1) * 128],
                                    xn_fm2[:, k * T + tg * 512: k * T + (tg + 1) * 512],
                                    start=(k == 0), stop=(k == KC - 1))
                PE.inc(last)
                if f >= 2:
                    e.wait_ge(SE.h, se_ua(f - 2))
                last = None
                for k in range(KC):
                    last = e.matmul(psUa[f % 2][:], uwb[ic % 2][:, k * 128:(k + 1) * 128],
                                    xn_fm2[:, k * T + tg * 512: k * T + (tg + 1) * 512],
                                    start=(k == 0), stop=(k == KC - 1))
                PE.inc(last)
            be(e, "te")

        def p7a_se(e):
            bs(e)
            for f in range(NF):
                ic, tg = divmod(f, NTG)
                e.wait_ge(PE.h, pe_gm(f))
                if f >= 2:
                    e.wait_ge(VE.h, ve_hd(f - 2))
                SE.inc(e.activation(ga_sb[f % 2][:], psGa[f % 2][:], AF.Silu,
                                    scale=gss_sb[:, ic:ic + 1]))
                e.wait_ge(PE.h, pe_um7(f))
                SE.inc(e.activation(ua_sb[f % 2][:], psUa[f % 2][:], AF.Identity,
                                    scale=uss_sb[:, ic:ic + 1]))
            be(e, "se")

        def p7a_ve(e):
            bs(e)
            for f in range(NF):
                e.wait_ge(SE.h, se_ua(f))
                if f >= 2:
                    SPST.wait(e, ("h", f - 2))
                VE.inc(e.tensor_tensor(hdn_buf[f % 2][:], ga_sb[f % 2][:], ua_sb[f % 2][:], ALU.mult))
            be(e, "ve")

        block.sync(p7a_sy); block.gpsimd(p7a_gp); block.tensor(p7a_te)
        block.scalar(p7a_se); block.vector(p7a_ve)
        assert PE.n == bPE + NF * 2 and SE.n == bSE + NF * 2 and VE.n == bVE + NF
        p7a.close()
        nphase[0] += 1

        p7b = ExitStack()
        sb8 = lambda n, shp, dt: p7b.enter_context(nc.sbuf_tensor(n, shp, dt))
        ps8 = lambda n, shp, dt=F32: p7b.enter_context(nc.psum_tensor(n, shp, dt))
        dss_sb = sb8("dss_sb", [128, D], F32)
        htb = [sb8(f"htb{i}", [128, IC * 128], BF16) for i in range(2)]
        x_t7 = [sb8(f"x_t7{i}", [128, 512], F32) for i in range(2)]
        s_t7 = [sb8(f"s_t7{i}", [128, 512], BF16) for i in range(2)]
        f1_sb = [sb8(f"f1_sb{i}", [128, 512], F32) for i in range(2)]
        f2_sb = [sb8(f"f2_sb{i}", [128, 512], F32) for i in range(2)]
        ob = [sb8(f"ob{i}", [128, 512], F32) for i in range(2)]
        psD = [ps8(f"psD{i}", [128, 512]) for i in range(2)]

        bLD, bPE, bVE, bSE = LD.n, PE.n, VE.n, SE.n
        NB = 4 * NT
        pe_dm = lambda p: bPE + p + 1
        ve_f1 = lambda p: bVE + p * 2 + 1
        ve_fin = lambda p: bVE + p * 2 + 2
        se_f2 = lambda p: bSE + p + 1

        def p7b_sy(e):
            bs(e)
            LD.inc(e.dma_start(out=dss_sb[:], in_=dss_bc), 16)
            for p in range(NB):
                q, tb = divmod(p, NT)
                if p >= 2:
                    e.wait_ge(PE.h, pe_dm(p - 2))
                    e.wait_ge(VE.h, ve_fin(p - 2))
                g = RST.group(("l", p))
                g(e, htb[p % 2][:].rearrange("p (c t) -> p c t", c=IC), hdn_sp[tb])
                g(e, x_t7[p % 2][:], x_in[tb * 128:(tb + 1) * 128, q * 512:(q + 1) * 512])
                g(e, s_t7[p % 2][:], s_sp[tb][:, q * 512:(q + 1) * 512])
                if p >= 1:
                    e.wait_ge(VE.h, ve_fin(p - 1))
                    pv = p - 1
                    qv, tv = divmod(pv, NT)
                    SPST.group(("o", pv))(
                        e, out[tv * 128:(tv + 1) * 128, qv * 512:(qv + 1) * 512], ob[pv % 2][:])
            e.wait_ge(VE.h, ve_fin(NB - 1))
            SPST.group(("o", NB - 1))(
                e, out[(NT - 1) * 128:NT * 128, 3 * 512:4 * 512], ob[(NB - 1) % 2][:])
            be(e, "sy")

        def p7b_gp(e):
            bs(e)
            for q in range(2, 4):
                e.wait_ge(PE.h, pe_dm((q - 1) * NT - 1))
                dn_quarter_load(e, q)
            be(e, "gp")

        def p7b_te(e):
            bs(e)
            for p in range(NB):
                q, tb = divmod(p, NT)
                DST.wait(e, ("dn", q))
                RST.wait(e, ("l", p))
                if p >= 2:
                    e.wait_ge(VE.h, ve_f1(p - 2))
                last = None
                for ic in range(IC):
                    last = e.matmul(psD[p % 2][:], htb[p % 2][:, ic * 128:(ic + 1) * 128],
                                    dwb[q % 2][:, ic * 512:(ic + 1) * 512],
                                    start=(ic == 0), stop=(ic == IC - 1))
                PE.inc(last)
            be(e, "te")

        def p7b_se(e):
            bs(e)
            for p in range(NB):
                e.wait_ge(VE.h, ve_f1(p))
                if p >= 1:
                    e.wait_ge(VE.h, ve_fin(p - 1))
                SE.inc(e.activation(f2_sb[p % 2][:], f1_sb[p % 2][:], AF.Silu))
            be(e, "se")

        def p7b_ve(e):
            bs(e)
            e.wait_ge(LD.h, bLD + 16)
            for p in range(NB):
                q, tb = divmod(p, NT)
                RST.wait(e, ("l", p))
                dsl = slice(q * 512, (q + 1) * 512)
                e.wait_ge(PE.h, pe_dm(p))
                if p >= 2:
                    e.wait_ge(SE.h, se_f2(p - 2))
                VE.inc(e.tensor_tensor(f1_sb[p % 2][:], psD[p % 2][:],
                                       dss_sb[:, dsl], ALU.mult))
                e.wait_ge(SE.h, se_f2(p))
                if p >= 2:
                    SPST.wait(e, ("o", p - 2))
                e.drain()
                e.tensor_tensor(f2_sb[p % 2][:], f2_sb[p % 2][:], s_t7[p % 2][:], ALU.mult)
                e.drain()
                VE.inc(e.tensor_tensor(ob[p % 2][:], f2_sb[p % 2][:], x_t7[p % 2][:], ALU.add))
            be(e, "ve")

        block.sync(p7b_sy); block.gpsimd(p7b_gp); block.tensor(p7b_te)
        block.scalar(p7b_se); block.vector(p7b_ve)
        assert PE.n == bPE + NB and VE.n == bVE + NB * 2 and SE.n == bSE + NB
        p7b.close()
        p7ab.close()
        nphase[0] += 1


    ctx.close()
    return nc



def _rope_tables(span_start, T):
    pos = np.arange(span_start, span_start + T, dtype=np.float64)
    j = np.arange(64, dtype=np.float64)
    inv_freq = 1.0 / THETA ** (2.0 * j / 128.0)
    freqs = pos[:, None] * inv_freq[None, :]
    cos = np.cos(freqs); sin = np.sin(freqs)
    sbase = (2.0 * j + 0.4 * 128.0) / (1.4 * 128.0)
    power = (pos - S // 2) / SCALE_BASE
    scale = sbase[None, :] ** power[:, None]
    ex = lambda a: np.repeat(a, 2, axis=1)
    cosf, sinf, scf = ex(cos), ex(sin), ex(scale)
    mk = lambda a: np.ascontiguousarray(a.T.astype(np.float32))
    return mk(cosf * scf), mk(sinf * scf), mk(cosf / scf), mk(sinf / scf)


def _const_tables():
    r = np.arange(128)
    d2 = np.zeros((128, H * 128), np.float32)
    grp1 = np.zeros((128, H * 128), np.float32)
    gcm = np.zeros((128, H), np.float32)
    for h, g in enumerate(GAMMAS):
        pm = r[None, :] - r[:, None]
        d2[:, h * 128:(h + 1) * 128] = np.where(pm >= 0, np.float64(g) ** np.maximum(pm, 0), 0.0)
        grp1[:, h * 128:(h + 1) * 128] = (np.float64(g) ** (r + 1.0))[:, None]
        gcm[:, h] = np.float64(g) ** (127.0 - r)
    p_rot = np.zeros((128, 128), np.float32)
    for i in range(64):
        p_rot[2 * i + 1, 2 * i] = -1.0
        p_rot[2 * i, 2 * i + 1] = 1.0
    return d2, grp1, gcm, p_rot, np.eye(128, dtype=np.float32)


def _wm_table(core, gs=SPAN):
    b, sp = core // NSPAN, core % NSPAN
    wm = np.zeros((NCORES * H,), np.float64)
    for cp in range(NCORES):
        bp, spp = cp // NSPAN, cp % NSPAN
        if bp == b and spp < sp:
            for h, g in enumerate(GAMMAS):
                wm[cp * H + h] = np.float64(g) ** (gs * (sp - spp - 1))
    return np.ascontiguousarray(np.broadcast_to(wm.astype(np.float32), (128, NCORES * H)))


def _rs(v):
    return np.ascontiguousarray(np.asarray(v, np.float32).reshape(-1, 128).T)


def _bc(v):
    v = np.asarray(v, np.float32)
    return np.ascontiguousarray(np.broadcast_to(v, (128, v.shape[-1])))


def make_in_maps(inputs, T=SPAN, geom_span=None):
    gs = SPAN if geom_span is None else geom_span
    x = np.ascontiguousarray(np.asarray(inputs["L0_attn_x"], np.float32))
    cf = lambda a: np.ascontiguousarray(np.asarray(a, np.float32))
    shared = {
        "Wq": cf(inputs["L0_attn_Wq"]), "Wk": cf(inputs["L0_attn_Wk"]),
        "Wv": cf(inputs["L0_attn_Wv"]), "Wg": cf(inputs["L0_attn_Wg"]),
        "Wo": cf(inputs["L0_attn_Wo"]),
        "gateT": cf(np.asarray(inputs["L0_ffn_gate_w"], np.float32).T),
        "upT": cf(np.asarray(inputs["L0_ffn_up_w"], np.float32).T),
        "downT": cf(np.asarray(inputs["L0_ffn_down_w"], np.float32).T),
        "bqr": _rs(inputs["L0_attn_bq"]), "bkr": _rs(inputs["L0_attn_bk"]),
        "bgr": _rs(inputs["L0_attn_bg"]),
        "bv_bc": _bc(inputs["L0_attn_bv"]), "bo_bc": _bc(inputs["L0_attn_bo"]),
        "gss": _rs(inputs["L0_ffn_gate_s"]), "uss": _rs(inputs["L0_ffn_up_s"]),
        "dss_bc": _bc(inputs["L0_ffn_down_s"]),
    }
    d2, grp1, gcm, p_rot, ident = _const_tables()
    r_ = np.arange(128)
    grp1v = np.stack([np.float64(g) ** (r_ + 1.0) for g in GAMMAS], axis=1).astype(np.float32)
    shared.update({"d2_t": d2, "grp1_bc": grp1, "grp1v_t": np.ascontiguousarray(grp1v),
                   "gcm_t": gcm, "p_rot": p_rot, "ident": ident})
    rope = [_rope_tables(sp * gs, T) for sp in range(NSPAN)]
    in_maps = []
    for c in range(NCORES):
        b, sp = c // NSPAN, c % NSPAN
        cs, ss_, ics, iss = rope[sp]
        im = dict(shared)
        im["x"] = np.ascontiguousarray(x[b, sp * gs: sp * gs + T])
        im["cs_t"] = cs; im["ss_t"] = ss_; im["ics_t"] = ics; im["iss_t"] = iss
        im["wm_bc"] = _wm_table(c, gs)
        in_maps.append(im)
    return in_maps


def kernel(**inputs):
    nc = build(T=SPAN, debug=False)
    in_maps = make_in_maps(inputs, T=SPAN)
    res = run_bass_kernel_spmd(nc, in_maps, core_ids=list(range(NCORES)))
    out = np.empty((B, S, D), np.float32)
    for c in range(NCORES):
        b, sp = c // NSPAN, c % NSPAN
        out[b, sp * SPAN:(sp + 1) * SPAN] = res.results[c]["out"]
    return out


# version 16
# speedup vs baseline: 1.0028x; 1.0028x over previous
import numpy as np
import ml_dtypes
from contextlib import ExitStack

import concourse.bass as bass
import concourse.mybir as mybir
from concourse.bass_utils import run_bass_kernel_spmd

F32 = mybir.dt.float32
BF16 = mybir.dt.bfloat16
AF = mybir.ActivationFunctionType
ALU = mybir.AluOpType
AX = mybir.AxisListType

B = 2
S = 8192
D = 2048
H = 16
INTER = 5632
NCORES = 8
NSPAN = 4
SPAN = S // NSPAN
EPS = 1e-5
THETA = 10000.0
SCALE_BASE = 512.0
GAMMAS = [1.0 - 2.0 ** (-5.0 - h) for h in range(H)]
GC = [float(np.float32(np.float64(g) ** 128)) for g in GAMMAS]
KC = D // 128
IC = INTER // 128


class Sem:
    def __init__(self, nc, ctx, name):
        self.h = ctx.enter_context(nc.semaphore(name))
        self.n = 0

    def inc(self, instr, k=1):
        instr.then_inc(self.h, k)
        self.n += k
        return self.n


class DStream:

    def __init__(self, nc, ctx, name):
        self.s = [Sem(nc, ctx, name + "a"), Sem(nc, ctx, name + "b")]
        self.rec = {}
        self.gi = 0

    def group(self, key=None):
        p = self.gi % 2
        self.gi += 1
        if key is None:
            key = self.gi - 1
        sem = self.s[p]

        def emit(e, out, in_):
            v = sem.inc(e.dma_start(out=out, in_=in_), 16)
            self.rec[key] = (sem, v)
        return emit

    def wait(self, e, key):
        sem, v = self.rec[key]
        e.wait_ge(sem.h, v)

    def drain_all(self, e):
        for sem in self.s:
            if sem.n:
                e.wait_ge(sem.h, sem.n)


def build(T=SPAN, debug=False):
    NT = T // 128
    NTG = T // 512
    TPH = NT // 2
    nc = bass.Bass("TRN2", target_bir_lowering=False)

    dram = lambda n, shp, dt=F32, kind="ExternalInput": nc.dram_tensor(n, shp, dt, kind=kind).ap()
    sk = "ExternalOutput" if debug else "Internal"

    x_in = dram("x", [T, D])
    Wq = dram("Wq", [D, D]); Wk = dram("Wk", [D, D]); Wv = dram("Wv", [D, D])
    Wg = dram("Wg", [D, D]); Wo = dram("Wo", [D, D])
    gateT = dram("gateT", [D, INTER]); upT = dram("upT", [D, INTER]); downT = dram("downT", [INTER, D])
    bqr = dram("bqr", [128, 16]); bkr = dram("bkr", [128, 16]); bgr = dram("bgr", [128, 16])
    bv_bc = dram("bv_bc", [128, D]); bo_bc = dram("bo_bc", [128, D])
    gss = dram("gss", [128, IC]); uss = dram("uss", [128, IC]); dss_bc = dram("dss_bc", [128, D])
    cs_t = dram("cs_t", [128, T]); ss_t = dram("ss_t", [128, T])
    ics_t = dram("ics_t", [128, T]); iss_t = dram("iss_t", [128, T])
    d2_t = dram("d2_t", [128, H * 128])
    grp1_bc = dram("grp1_bc", [128, H * 128])
    grp1v_t = dram("grp1v_t", [128, H])
    gcm_t = dram("gcm_t", [128, H])
    wm_bc = dram("wm_bc", [128, NCORES * H])
    p_in = dram("p_rot", [128, 128]); id_in = dram("ident", [128, 128])

    out = dram("out", [T, D], F32, kind="ExternalOutput")

    q_sp = dram("q_sp", [NT, 128, H, 128], BF16, kind=sk)
    k_sp = dram("k_sp", [NT, 128, H, 128], BF16, kind=sk)
    kg_sp = dram("kg_sp", [NT, 128, H, 128], BF16, kind=sk)
    v_sp = dram("v_sp", [NT, 128, H, 128], BF16, kind=sk)
    g_sp = dram("g_sp", [NT, 128, H, 128], BF16, kind=sk)
    xn_sp = dram("xn_sp", [KC, 128, T], BF16, kind=sk)
    s_sp = dram("s_sp", [NT, 128, D], BF16, kind=sk)
    hdn_sp = dram("hdn_sp", [NT, 128, IC, 128], BF16, kind=sk)
    wo_scr = dram("wo_scr", [D, D], BF16, kind="Internal")
    wq0_scr = dram("wq0_scr", [D, 512], BF16, kind="Internal")
    wg0_scr = dram("wg0_scr", [D, 512], BF16, kind="Internal")
    cc_in = nc.dram_tensor("cc_in", [H * 128, 128], F32, kind="Internal")
    cc_out = nc.dram_tensor("cc_out", [NCORES * H * 128, 128], F32, kind="Internal")

    ctx = ExitStack()
    sb = lambda n, shp, dt: ctx.enter_context(nc.sbuf_tensor(n, shp, dt))

    LD = Sem(nc, ctx, "LD")
    SPST = DStream(nc, ctx, "SP")
    KSP = DStream(nc, ctx, "KP")
    RST = DStream(nc, ctx, "RS")
    WST = DStream(nc, ctx, "WS")
    DST = DStream(nc, ctx, "DS")
    MST = DStream(nc, ctx, "MS")
    WOS = DStream(nc, ctx, "WO")
    WOC = Sem(nc, ctx, "WOC")
    CCS = Sem(nc, ctx, "CCS")
    GPC = Sem(nc, ctx, "GPC")
    PE = Sem(nc, ctx, "PE")
    VE = Sem(nc, ctx, "VE")
    SE = Sem(nc, ctx, "SE")
    ENGS = ("sy", "gp", "te", "se", "ve")
    BAR = {e: Sem(nc, ctx, f"B_{e}") for e in ENGS}
    nphase = [0]

    def bs(e):
        if nphase[0] > 0:
            for s_ in BAR.values():
                e.wait_ge(s_.h, nphase[0])

    def be(e, which):
        if which == "sy":
            if LD.n:
                e.wait_ge(LD.h, LD.n)
            SPST.drain_all(e)
            KSP.drain_all(e)
            RST.drain_all(e)
            WOS.drain_all(e)
        elif which == "gp":
            WST.drain_all(e)
            DST.drain_all(e)
            MST.drain_all(e)
            if GPC.n:
                e.wait_ge(GPC.h, GPC.n)
        else:
            st = {"te": PE, "se": SE, "ve": VE}[which]
            if st.n > 0:
                e.wait_ge(st.h, st.n)
        BAR[which].inc(e.nop(), 1)

    bqr_sb = sb("bqr_sb", [128, 16], F32); bkr_sb = sb("bkr_sb", [128, 16], F32)
    bgr_sb = sb("bgr_sb", [128, 16], F32)
    gcm_sb = sb("gcm_sb", [128, H], F32)
    gss_sb = sb("gss_sb", [128, IC], F32); uss_sb = sb("uss_sb", [128, IC], F32)
    wm_sb = sb("wm_sb", [128, NCORES * H], F32)
    p_bf = sb("p_bf", [128, 128], BF16); id_bf = sb("id_bf", [128, 128], BF16)
    m_sb = sb("m_sb", [128, H * 128], F32)
    px = ExitStack()
    xn_fm = px.enter_context(nc.sbuf_tensor("xn_fm", [128, KC * T], BF16))

    with nc.Block() as block:
        def p0_sy(e):
            for dst, src in [(bqr_sb, bqr), (bkr_sb, bkr), (bgr_sb, bgr), (gcm_sb, gcm_t),
                             (gss_sb, gss), (uss_sb, uss), (wm_sb, wm_bc)]:
                LD.inc(e.dma_start(out=dst[:], in_=src), 16)
            be(e, "sy")
        def p0_gp(e):
            g = WST.group("p0")
            g(e, p_bf[:], p_in)
            g(e, id_bf[:], id_in)
            be(e, "gp")
        block.sync(p0_sy); block.gpsimd(p0_gp)
        block.tensor(lambda e: be(e, "te")); block.scalar(lambda e: be(e, "se"))
        block.vector(lambda e: be(e, "ve"))
        nphase[0] += 1

        pa_scope = ExitStack()
        sbA = lambda n, shp, dt: pa_scope.enter_context(nc.sbuf_tensor(n, shp, dt, side="right"))
        wvb = [sbA(f"wvb{i}", [128, KC * 512], BF16) for i in range(2)]
        wkb = [sbA(f"wkb{i}", [128, KC * 512], BF16) for i in range(2)]

        p1 = ExitStack()
        sb1 = lambda n, shp, dt: p1.enter_context(nc.sbuf_tensor(n, shp, dt))
        ps1 = lambda n, shp, dt=F32: p1.enter_context(nc.psum_tensor(n, shp, dt))
        NGRP = 4
        TPH = max(NT // NGRP, 1)
        NGRP = NT // TPH
        x_b = [sb1(f"x_b{i}", [128, D], F32) for i in range(TPH)]
        xn_tok = [sb1(f"xn_tok{i}", [128, D], BF16) for i in range(2)]
        sq_sc = sb1("sq_sc", [128, D], F32)
        st_sum = sb1("st_sum", [128, NT], F32); st_ssq = sb1("st_ssq", [128, NT], F32)
        st_mu = sb1("st_mu", [128, NT], F32); st_rs = sb1("st_rs", [128, NT], F32)
        st_nv = sb1("st_nv", [128, NT], F32)
        ps_tp = [ps1(f"ps_tp{i}", [128, 128], BF16) for i in range(2)]

        bLD, bPE, bVE, bSE = LD.n, PE.n, VE.n, SE.n
        ld_half = lambda hh: bLD + (hh + 1) * TPH * 16
        se_sq = lambda t: bSE + (t // TPH) * (TPH + 1) + (t % TPH) + 1
        se_rstd = lambda hh: bSE + hh * (TPH + 1) + TPH + 1
        ve_negv = lambda hh: bVE + hh * (1 + TPH * 17) + 1
        ve_xn = lambda t: ve_negv(t // TPH) + (t % TPH) * 17 + 1
        ve_evac = lambda t, j: ve_xn(t) + j + 1
        pe_tp1 = lambda t, j: bPE + (t * 16 + j) + 1

        def p1_sy(e):
            bs(e)
            for t in range(NT):
                if t >= TPH:
                    e.wait_ge(VE.h, ve_xn(t - TPH))
                    e.wait_ge(SE.h, se_sq(t - TPH))
                LD.inc(e.dma_start(out=x_b[t % TPH][:], in_=x_in[t * 128:(t + 1) * 128, :]), 16)
            be(e, "sy")

        def p1_gp(e):
            bs(e)
            WST.group(("wv", 0))(
                e, wvb[0][:].rearrange("p (k m) -> p k m", k=KC),
                Wv[:, 0:512].rearrange("(k p) m -> p k m", p=128))
            WST.group(("wk", 0))(
                e, wkb[0][:].rearrange("p (k m) -> p k m", k=KC),
                Wk[:, 0:512].rearrange("(k p) m -> p k m", p=128))
            WOC.inc(e.dma_start(out=wo_scr, in_=Wo), 16)
            WOC.inc(e.dma_start(out=wq0_scr, in_=Wq[:, 0:512]), 16)
            WOC.inc(e.dma_start(out=wg0_scr, in_=Wg[:, 0:512]), 16)
            be(e, "gp")

        def p1_te(e):
            bs(e)
            for t in range(NT):
                e.wait_ge(VE.h, ve_xn(t))
                for j in range(16):
                    f = t * 16 + j
                    if f >= 2:
                        pt, pj = divmod(f - 2, 16)
                        e.wait_ge(VE.h, ve_evac(pt, pj))
                    PE.inc(e.transpose(ps_tp[f % 2][:], xn_tok[t % 2][:, j * 128:(j + 1) * 128],
                                       id_bf[:]))
            be(e, "te")

        def p1_se(e):
            bs(e)
            for hh in range(NGRP):
                e.wait_ge(LD.h, ld_half(hh))
                for i in range(TPH):
                    t = hh * TPH + i
                    e.drain()
                    SE.inc(e.activation(sq_sc[:], x_b[i][:], AF.Square,
                                        accum_out=st_ssq[:, t:t + 1]))
                sl = slice(hh * TPH, hh * TPH + TPH)
                e.wait_ge(VE.h, ve_negv(hh))
                e.activation(st_nv[:, sl], st_nv[:, sl], AF.Ln, scale=-1.0)
                e.drain()
                SE.inc(e.activation(st_rs[:, sl], st_nv[:, sl], AF.Exp, scale=-0.5))
            be(e, "se")

        def p1_ve(e):
            bs(e)
            for hh in range(NGRP):
                sl = slice(hh * TPH, hh * TPH + TPH)
                e.wait_ge(LD.h, ld_half(hh))
                for i in range(TPH):
                    t = hh * TPH + i
                    e.reduce_sum(st_sum[:, t:t + 1], x_b[i][:], axis=AX.X)
                e.wait_ge(SE.h, se_sq(hh * TPH + TPH - 1))
                e.drain()
                e.tensor_scalar(st_mu[:, sl], st_sum[:, sl], 1.0 / D, None, ALU.mult)
                e.tensor_scalar(st_ssq[:, sl], st_ssq[:, sl], 1.0 / D, None, ALU.mult)
                e.drain()
                e.tensor_tensor(st_nv[:, sl], st_mu[:, sl], st_mu[:, sl], ALU.mult)
                e.drain()
                e.tensor_tensor(st_nv[:, sl], st_nv[:, sl], st_ssq[:, sl], ALU.subtract)
                e.drain()
                VE.inc(e.tensor_scalar(st_nv[:, sl], st_nv[:, sl], EPS, None, ALU.subtract))
                for i in range(TPH):
                    t = hh * TPH + i
                    e.wait_ge(SE.h, se_rstd(hh))
                    e.drain()
                    VE.inc(e.tensor_scalar(xn_tok[t % 2][:], x_b[i][:], st_mu[:, t:t + 1],
                                           st_rs[:, t:t + 1], ALU.subtract, ALU.mult))
                    for j in range(16):
                        e.wait_ge(PE.h, pe_tp1(t, j))
                        VE.inc(e.tensor_copy(xn_fm[:, j * T + t * 128: j * T + (t + 1) * 128],
                                             ps_tp[(t * 16 + j) % 2][:]))
            be(e, "ve")

        block.sync(p1_sy); block.gpsimd(p1_gp); block.tensor(p1_te)
        block.scalar(p1_se); block.vector(p1_ve)
        assert SE.n == bSE + NGRP * (TPH + 1) and VE.n == bVE + NGRP * (1 + TPH * 17)
        assert PE.n == bPE + NT * 16 and LD.n == bLD + NT * 16
        p1.close()
        nphase[0] += 1

        pa = ExitStack()
        sbp = lambda n, shp, dt: pa.enter_context(nc.sbuf_tensor(n, shp, dt))
        psp = lambda n, shp, dt=F32: pa.enter_context(nc.psum_tensor(n, shp, dt))
        c1_sb = sbp("c1_sb", [128, T], F32); c2_sb = sbp("c2_sb", [128, T], F32)
        bv_sb = sbp("bv_sb", [128, D], F32)
        vqb = [sbp(f"vqb{i}", [128, NT * 512], BF16) for i in range(2)]
        kbf = [sbp(f"kbf{i}", [128, 512], BF16) for i in range(2)]
        ta = sbp("ta", [128, 512], F32); tb_ = sbp("tb", [128, 512], F32)
        fmb = [sbp(f"fmb{i}", [128, 512], BF16) for i in range(2)]
        kgb = [sbp(f"kgb{i}", [128, 512], BF16) for i in range(2)]
        psV = [psp(f"psV{i}", [128, 512]) for i in range(2)]
        psK = [psp(f"psK{i}", [128, 512]) for i in range(2)]
        psR = psp("psR", [128, 512])
        psT = psp("psT", [128, 512], BF16)
        psU = psp("psU", [128, 512])

        NIT = 16 * NTG
        bPE, bVE, bSE = PE.n, VE.n, SE.n
        pe_ev = {}; _pectr = [bPE]
        ve_ev = {}; _vectr = [bVE]
        se_ev = {}; _sectr = [bSE]
        def _mk(ctr, ev):
            def f(key):
                ctr[0] += 1
                ev[key] = ctr[0]
            return f
        _pe = _mk(_pectr, pe_ev); _ve = _mk(_vectr, ve_ev); _se = _mk(_sectr, se_ev)
        SHIFT = 16 * NTG // 4
        SHIFT = NT
        for i in range(NIT + SHIFT):
            if i < NIT:
                _pe(("v", i))
            if i >= SHIFT:
                it = i - SHIFT
                _pe(("kproj", it)); _pe(("krot", it)); _pe(("ktp", it)); _pe(("kum", it))
        for i in range(NIT + SHIFT):
            if i < NIT:
                _ve(("vev", i))
            if i >= SHIFT:
                it = i - SHIFT
                _ve(("fm", it)); _ve(("kg", it)); _ve(("mu", it))
        for it in range(NIT):
            _se(("kbf", it))

        def pa_sy(e):
            bs(e)
            LD.inc(e.dma_start(out=c1_sb[:], in_=ics_t), 16)
            LD.inc(e.dma_start(out=c2_sb[:], in_=iss_t), 16)
            LD.inc(e.dma_start(out=bv_sb[:], in_=bv_bc), 16)
            for i in range(NIT + SHIFT):
                if i < NIT:
                    vg, t = divmod(i, NT)
                    e.wait_ge(VE.h, ve_ev[("vev", i)])
                    SPST.group(("v", i))(
                        e, v_sp[t, :, 4 * vg:4 * vg + 4, :],
                        vqb[vg % 2][:, t * 512:(t + 1) * 512])
                if i >= SHIFT:
                    it = i - SHIFT
                    qd, rem = divmod(it, 4 * NTG)
                    h, tg = divmod(rem, NTG)
                    head = qd * 4 + h
                    g = KSP.group(("sp", it))
                    e.wait_ge(VE.h, ve_ev[("fm", it)])
                    g(e, k_sp[tg * 4:(tg + 1) * 4, :, head, :].rearrange("n k t -> k n t"),
                      fmb[it % 2][:].rearrange("p (n t) -> p n t", n=4))
                    e.wait_ge(VE.h, ve_ev[("kg", it)])
                    g(e, kg_sp[tg * 4:(tg + 1) * 4, :, head, :].rearrange("n t k -> t n k"),
                      kgb[it % 2][:].rearrange("p (n k) -> p n k", n=4))
            e.wait_ge(VE.h, ve_ev[("mu", NIT - 1)])
            LD.inc(e.dma_start(out=cc_in.ap().rearrange("(h k) v -> k h v", h=H),
                               in_=m_sb[:].rearrange("p (h v) -> p h v", h=H)), 16)
            be(e, "sy")

        ld_m_total = [None]

        def pa_gp(e):
            bs(e)
            for q in range(1, 4):
                e.wait_ge(PE.h, pe_ev[("v", (q - 1) * NT)])
                WST.group(("wv", q))(
                    e, wvb[q % 2][:].rearrange("p (k m) -> p k m", k=KC),
                    Wv[:, q * 512:(q + 1) * 512].rearrange("(k p) m -> p k m", p=128))
                e.wait_ge(PE.h, pe_ev[("kproj", max((q - 1) * 4 * NTG, 0))])
                WST.group(("wk", q))(
                    e, wkb[q % 2][:].rearrange("p (k m) -> p k m", k=KC),
                    Wk[:, q * 512:(q + 1) * 512].rearrange("(k p) m -> p k m", p=128))
            e.wait_ge(LD.h, ld_m_total[0])
            CCS.inc(e.collective_compute(
                "AllGather", ALU.bypass, replica_groups=[list(range(NCORES))],
                ins=[cc_in.ap().opt()], outs=[cc_out.ap().opt()]), 1)
            be(e, "gp")

        def pa_te(e):
            bs(e)
            for i in range(NIT + SHIFT):
                if i < NIT:
                    vg, t = divmod(i, NT)
                    WST.wait(e, ("wv", vg))
                    if i >= 2:
                        e.wait_ge(VE.h, ve_ev[("vev", i - 2)])
                    last = None
                    for k in range(KC):
                        last = e.matmul(psV[i % 2][:],
                                        xn_fm[:, k * T + t * 128: k * T + (t + 1) * 128],
                                        wvb[vg % 2][:, k * 512:(k + 1) * 512],
                                        start=(k == 0), stop=(k == KC - 1))
                    PE.inc(last)
                if i >= SHIFT:
                    it = i - SHIFT
                    qd, rem = divmod(it, 4 * NTG)
                    h, tg = divmod(rem, NTG)
                    WST.wait(e, ("wk", qd))
                    if it >= 2:
                        e.wait_ge(SE.h, se_ev[("kbf", it - 2)])
                    last = None
                    for k in range(KC):
                        last = e.matmul(psK[it % 2][:],
                                        wkb[qd % 2][:, k * 512 + h * 128: k * 512 + (h + 1) * 128],
                                        xn_fm[:, k * T + tg * 512: k * T + (tg + 1) * 512],
                                        start=(k == 0), stop=(k == KC - 1))
                    PE.inc(last)
                    e.wait_ge(SE.h, se_ev[("kbf", it)])
                    if it >= 1:
                        e.wait_ge(VE.h, ve_ev[("fm", it - 1)])
                    PE.inc(e.matmul(psR[:], p_bf[:], kbf[it % 2][:], start=True, stop=True))
                    e.wait_ge(VE.h, ve_ev[("fm", it)])
                    if it >= 1:
                        e.wait_ge(VE.h, ve_ev[("kg", it - 1)])
                    last = None
                    for j in range(4):
                        last = e.transpose(psT[:, j * 128:(j + 1) * 128],
                                           fmb[it % 2][:, j * 128:(j + 1) * 128], id_bf[:])
                    PE.inc(last)
                    e.wait_ge(VE.h, ve_ev[("kg", it)])
                    e.wait_ge(VE.h, ve_ev[("vev", (qd + 1) * NT - 1)])
                    if it >= 1:
                        e.wait_ge(VE.h, ve_ev[("mu", it - 1)])
                    last = None
                    for j in range(4):
                        n_ = tg * 4 + j
                        last = e.matmul(psU[:, j * 128:(j + 1) * 128],
                                        kgb[it % 2][:, j * 128:(j + 1) * 128],
                                        vqb[qd % 2][:, n_ * 512 + h * 128: n_ * 512 + (h + 1) * 128],
                                        start=(j == 0), stop=(j == 3))
                    PE.inc(last)
            be(e, "te")

        def pa_se(e):
            bs(e)
            for it in range(NIT):
                qd, rem = divmod(it, 4 * NTG)
                h, tg = divmod(rem, NTG)
                head = qd * 4 + h
                e.wait_ge(PE.h, pe_ev[("kproj", it)])
                if it >= 2:
                    e.wait_ge(PE.h, pe_ev[("krot", it - 2)])
                    e.wait_ge(VE.h, ve_ev[("fm", it - 2)])
                SE.inc(e.activation(kbf[it % 2][:], psK[it % 2][:], AF.Identity,
                                    bias=bkr_sb[:, head:head + 1]))
            be(e, "se")

        def pa_ve(e):
            bs(e)
            e.memset(m_sb[:], 0.0)
            e.wait_ge(LD.h, bLD_pa + 48)
            for i in range(NIT + SHIFT):
                if i < NIT:
                    vg, t = divmod(i, NT)
                    e.wait_ge(PE.h, pe_ev[("v", i)])
                    if i >= 2:
                        SPST.wait(e, ("v", i - 2))
                    if i >= 2 * NT and t == 0:
                        e.wait_ge(PE.h, pe_ev[("kum", (vg - 1) * 4 * NTG - 1)])
                    VE.inc(e.tensor_tensor(vqb[vg % 2][:, t * 512:(t + 1) * 512], psV[i % 2][:],
                                           bv_sb[:, vg * 512:(vg + 1) * 512], ALU.add))
                if i >= SHIFT:
                    it = i - SHIFT
                    qd, rem = divmod(it, 4 * NTG)
                    h, tg = divmod(rem, NTG)
                    head = qd * 4 + h
                    tsl = slice(tg * 512, (tg + 1) * 512)
                    e.wait_ge(PE.h, pe_ev[("krot", it)])
                    if it >= 2:
                        KSP.wait(e, ("sp", it - 2))
                        e.wait_ge(PE.h, pe_ev[("ktp", it - 2)])
                    e.tensor_tensor(ta[:], kbf[it % 2][:], c1_sb[:, tsl], ALU.mult)
                    e.tensor_tensor(tb_[:], psR[:], c2_sb[:, tsl], ALU.mult)
                    e.drain()
                    VE.inc(e.tensor_tensor(fmb[it % 2][:], ta[:], tb_[:], ALU.add))
                    e.wait_ge(PE.h, pe_ev[("ktp", it)])
                    if it >= 2:
                        e.wait_ge(PE.h, pe_ev[("kum", it - 2)])
                    VE.inc(e.tensor_scalar(kgb[it % 2][:], psT[:], gcm_sb[:, head:head + 1],
                                           None, ALU.mult))
                    e.wait_ge(PE.h, pe_ev[("kum", it)])
                    last = None
                    for j in range(4):
                        if j > 0:
                            e.drain()
                        hsl = slice(head * 128, (head + 1) * 128)
                        last = e.scalar_tensor_tensor(m_sb[:, hsl], m_sb[:, hsl], GC[head],
                                                      psU[:, j * 128:(j + 1) * 128],
                                                      ALU.mult, ALU.add)
                    VE.inc(last)
            be(e, "ve")

        bLD_pa = LD.n
        block.sync(pa_sy)
        ld_m_total[0] = LD.n
        block.gpsimd(pa_gp); block.tensor(pa_te)
        block.scalar(pa_se); block.vector(pa_ve)
        assert PE.n == _pectr[0] and VE.n == _vectr[0] and SE.n == _sectr[0]
        pa.close()
        pa_scope.close()
        nphase[0] += 1

        p56 = ExitStack()
        sb56 = lambda n, shp, dt: p56.enter_context(nc.sbuf_tensor(n, shp, dt, side="right"))
        s_f32 = sb56("s_f32", [128, H * 128], F32)
        s_bfT = sb56("s_bfT", [128, H * 128], BF16)

        pb = ExitStack()
        sbq = lambda n, shp, dt: pb.enter_context(nc.sbuf_tensor(n, shp, dt))
        psq = lambda n, shp, dt=F32: pb.enter_context(nc.psum_tensor(n, shp, dt))
        wqb = [sbq(f"wqb{i}", [128, KC * 512], BF16) for i in range(2)]
        wgb = [sbq(f"wgb{i}", [128, KC * 512], BF16) for i in range(2)]
        mg_buf = [sbq(f"mg{i}", [128, H * 128], F32) for i in range(2)]
        c1q = sbq("c1q", [128, T], F32); c2q = sbq("c2q", [128, T], F32)
        qbf = [sbq(f"qbf{i}", [128, 512], BF16) for i in range(2)]
        taq = sbq("taq", [128, 512], F32); tbq = sbq("tbq", [128, 512], F32)
        fmq = [sbq(f"fmq{i}", [128, 512], BF16) for i in range(2)]
        g_buf = [sbq(f"g_buf{i}", [128, 512], BF16) for i in range(2)]
        psQ = [psq(f"psQ{i}", [128, 512]) for i in range(2)]
        psRq = psq("psRq", [128, 512])
        psG5 = [psq(f"psG5{i}", [128, 512]) for i in range(2)]

        bPE, bVE, bSE = PE.n, VE.n, SE.n
        pe_ev = {}; _pectr = [bPE]
        ve_ev = {}; _vectr = [bVE]
        se_ev = {}; _sectr = [bSE]
        _pe = _mk(_pectr, pe_ev); _ve = _mk(_vectr, ve_ev); _se = _mk(_sectr, se_ev)
        for it in range(NIT):
            _pe(("qproj", it)); _pe(("qrot", it)); _pe(("gproj", it))
        S0C = {}
        for c in range(NCORES):
            S0C.setdefault(4 * NTG + (c * 4 * NTG) // NCORES, []).append(c)
        for it in range(NIT):
            _ve(("fm", it))
            for c in S0C.get(it, []):
                _ve(("s0", c))
                if c == NCORES - 1:
                    _ve(("sbf0",))
        for it in range(NIT):
            _se(("qbf", it)); _se(("gev", it))

        def pb_sy(e):
            bs(e)
            LD.inc(e.dma_start(out=c1q[:], in_=cs_t), 16)
            LD.inc(e.dma_start(out=c2q[:], in_=ss_t), 16)
            e.wait_ge(WOC.h, 48)
            RST.group(("wq", 0))(
                e, wqb[0][:].rearrange("p (k m) -> p k m", k=KC),
                wq0_scr.rearrange("(k p) m -> p k m", p=128))
            RST.group(("wg", 0))(
                e, wgb[0][:].rearrange("p (k m) -> p k m", k=KC),
                wg0_scr.rearrange("(k p) m -> p k m", p=128))
            SPST.group("xn")(e, xn_sp.rearrange("k p t -> p k t"),
                             xn_fm[:].rearrange("p (k t) -> p k t", k=KC))
            for it in range(NIT):
                qd, rem = divmod(it, 4 * NTG)
                h, tg = divmod(rem, NTG)
                head = qd * 4 + h
                g = KSP.group(("spq", it))
                e.wait_ge(VE.h, ve_ev[("fm", it)])
                g(e, q_sp[tg * 4:(tg + 1) * 4, :, head, :].rearrange("n k t -> k n t"),
                  fmq[it % 2][:].rearrange("p (n t) -> p n t", n=4))
                e.wait_ge(SE.h, se_ev[("gev", it)])
                g(e, g_sp[tg * 4:(tg + 1) * 4, :, head, :].rearrange("n g t -> g n t"),
                  g_buf[it % 2][:].rearrange("p (n t) -> p n t", n=4))
            be(e, "sy")

        def pb_gp(e):
            bs(e)
            for q in range(1, 4):
                if q >= 1:
                    e.wait_ge(PE.h, pe_ev[("qproj", (q - 1) * 4 * NTG)])
                WST.group(("wq", q))(
                    e, wqb[q % 2][:].rearrange("p (k m) -> p k m", k=KC),
                    Wq[:, q * 512:(q + 1) * 512].rearrange("(k p) m -> p k m", p=128))
                if q >= 1:
                    e.wait_ge(PE.h, pe_ev[("gproj", (q - 1) * 4 * NTG)])
                WST.group(("wg", q))(
                    e, wgb[q % 2][:].rearrange("p (k m) -> p k m", k=KC),
                    Wg[:, q * 512:(q + 1) * 512].rearrange("(k p) m -> p k m", p=128))
                if q == 1:
                    e.wait_ge(CCS.h, 1)
                    for c in range(NCORES):
                        if c >= 2:
                            e.wait_ge(VE.h, ve_ev[("s0", c - 2)])
                        MST.group(("mg", c))(
                            e, mg_buf[c % 2][:].rearrange("p (h v) -> p h v", h=H),
                            cc_out.ap().rearrange("(c h k) v -> c k h v", c=NCORES, h=H)[c])
            be(e, "gp")

        def pb_te(e):
            bs(e)
            for it in range(NIT):
                qd, rem = divmod(it, 4 * NTG)
                h, tg = divmod(rem, NTG)
                (RST if qd == 0 else WST).wait(e, ("wq", qd))
                if it >= 2:
                    e.wait_ge(SE.h, se_ev[("qbf", it - 2)])
                last = None
                for k in range(KC):
                    last = e.matmul(psQ[it % 2][:],
                                    wqb[qd % 2][:, k * 512 + h * 128: k * 512 + (h + 1) * 128],
                                    xn_fm[:, k * T + tg * 512: k * T + (tg + 1) * 512],
                                    start=(k == 0), stop=(k == KC - 1))
                PE.inc(last)
                e.wait_ge(SE.h, se_ev[("qbf", it)])
                if it >= 1:
                    e.wait_ge(VE.h, ve_ev[("fm", it - 1)])
                PE.inc(e.matmul(psRq[:], p_bf[:], qbf[it % 2][:], start=True, stop=True))
                (RST if qd == 0 else WST).wait(e, ("wg", qd))
                if it >= 2:
                    e.wait_ge(SE.h, se_ev[("gev", it - 2)])
                last = None
                for k in range(KC):
                    last = e.matmul(psG5[it % 2][:],
                                    wgb[qd % 2][:, k * 512 + h * 128: k * 512 + (h + 1) * 128],
                                    xn_fm[:, k * T + tg * 512: k * T + (tg + 1) * 512],
                                    start=(k == 0), stop=(k == KC - 1))
                PE.inc(last)
            be(e, "te")

        def pb_se(e):
            bs(e)
            for it in range(NIT):
                qd, rem = divmod(it, 4 * NTG)
                h, tg = divmod(rem, NTG)
                head = qd * 4 + h
                e.wait_ge(PE.h, pe_ev[("qproj", it)])
                if it >= 2:
                    e.wait_ge(PE.h, pe_ev[("qrot", it - 2)])
                    e.wait_ge(VE.h, ve_ev[("fm", it - 2)])
                SE.inc(e.activation(qbf[it % 2][:], psQ[it % 2][:], AF.Identity,
                                    bias=bqr_sb[:, head:head + 1]))
                e.wait_ge(PE.h, pe_ev[("gproj", it)])
                if it >= 2:
                    KSP.wait(e, ("spq", it - 2))
                SE.inc(e.activation(g_buf[it % 2][:], psG5[it % 2][:], AF.Silu,
                                    bias=bgr_sb[:, head:head + 1]))
            be(e, "se")

        def pb_ve(e):
            bs(e)
            e.wait_ge(LD.h, bLD_pb + 32)
            e.memset(s_f32[:], 0.0)
            for it in range(NIT):
                qd, rem = divmod(it, 4 * NTG)
                h, tg = divmod(rem, NTG)
                tsl = slice(tg * 512, (tg + 1) * 512)
                e.wait_ge(PE.h, pe_ev[("qrot", it)])
                if it >= 2:
                    KSP.wait(e, ("spq", it - 2))
                e.tensor_tensor(taq[:], qbf[it % 2][:], c1q[:, tsl], ALU.mult)
                e.tensor_tensor(tbq[:], psRq[:], c2q[:, tsl], ALU.mult)
                e.drain()
                VE.inc(e.tensor_tensor(fmq[it % 2][:], taq[:], tbq[:], ALU.add))
                for c in S0C.get(it, []):
                    MST.wait(e, ("mg", c))
                    e.drain()
                    last = None
                    for h in range(H):
                        hsl = slice(h * 128, (h + 1) * 128)
                        last = e.scalar_tensor_tensor(s_f32[:, hsl], mg_buf[c % 2][:, hsl],
                                                      wm_sb[:, c * H + h:c * H + h + 1],
                                                      s_f32[:, hsl], ALU.mult, ALU.add)
                    VE.inc(last)
                    if c == NCORES - 1:
                        e.drain()
                        VE.inc(e.tensor_copy(s_bfT[:], s_f32[:]))
            be(e, "ve")

        bLD_pb = LD.n
        block.sync(pb_sy); block.gpsimd(pb_gp); block.tensor(pb_te)
        block.scalar(pb_se); block.vector(pb_ve)
        assert PE.n == _pectr[0] and VE.n == _vectr[0] and SE.n == _sectr[0]
        pb.close()
        px.close()
        nphase[0] += 1


        p6 = ExitStack()
        sb6 = lambda n, shp, dt: p6.enter_context(nc.sbuf_tensor(n, shp, dt))
        ps6 = lambda n, shp, dt=F32: p6.enter_context(nc.psum_tensor(n, shp, dt))
        wo_bf = sb6("wo_bf", [128, KC * D], BF16)
        bo_sb = sb6("bo_sb", [128, D], F32)
        d2_sb = sb6("d2_sb", [128, H * 128], F32)
        grp1v = sb6("grp1v", [128, H], F32)
        chq = [sb6(f"chq{i}", [128, H * 128], BF16) for i in range(2)]
        chk = [sb6(f"chk{i}", [128, H * 128], BF16) for i in range(2)]
        chkg = [sb6(f"chkg{i}", [128, H * 128], BF16) for i in range(2)]
        chv = [sb6(f"chv{i}", [128, H * 128], BF16) for i in range(2)]
        chg = [sb6(f"chg{i}", [128, H * 128], BF16) for i in range(2)]
        a2_sb = [sb6(f"a2_{i}", [128, 512], BF16) for i in range(2)]
        ret1 = sb6("ret1", [128, 512], F32)
        ret_all = [sb6(f"ret_all{i}", [128, H * 128], F32) for i in range(2)]
        sq_all = sb6("sq_all", [128, H * 128], F32)
        gn_all = sb6("gn_all", [128, H * 128], BF16)
        prod_fm = sb6("prod_fm", [128, H * 128], BF16)
        s_buf = [sb6(f"s_buf{i}", [128, D], BF16) for i in range(2)]
        st16 = sb6("st16", [128, 16 * 6], F32)
        psS = [ps6(f"psS{i}", [128, 512]) for i in range(2)]
        psRr = ps6("psRr", [128, 512])
        psC = ps6("psC", [128, 512])
        psU6 = [ps6(f"psU6{i}", [128, 512]) for i in range(2)]
        psT6 = ps6("psT6", [128, 512], BF16)
        psW = ps6("psW", [128, 512])
        c_sum = slice(0, 16); c_ssq = slice(16, 32); c_mu = slice(32, 48)
        c_nv = slice(48, 64); c_rs = slice(64, 80); c_nmr = slice(80, 96)

        bLD6, bPE, bVE, bSE, bGP = LD.n, PE.n, VE.n, SE.n, GPC.n
        ve_sbf0_val = bVE
        pe_ev = {}; _pectr = [bPE]
        ve_ev = {}; _vectr = [bVE]
        se_ev = {}; _sectr = [bSE]
        gp_ev = {}; _gpctr = [bGP]
        def _mk(ctr, ev):
            def f(key):
                ctr[0] += 1
                ev[key] = ctr[0]
            return f
        _pe = _mk(_pectr, pe_ev); _ve = _mk(_vectr, ve_ev)
        _se = _mk(_sectr, se_ev); _gp = _mk(_gpctr, gp_ev)
        CORE_ORD = [("sc", 0), ("sc", 1), ("rt", 0), ("cx", 0), ("um", 0),
                    ("sc", 2), ("rt", 1), ("cx", 1), ("um", 1),
                    ("sc", 3), ("rt", 2), ("cx", 2), ("um", 2),
                    ("rt", 3), ("cx", 3), ("um", 3)]
        for n in range(NT):
            for op, hq in CORE_ORD:
                _pe((op, n, hq))
            if n >= 1:
                for tg4 in range(4):
                    _pe(("tp", n - 1, tg4))
                for dmg in range(4):
                    _pe(("wo", n - 1, dmg))
        for tg4 in range(4):
            _pe(("tp", NT - 1, tg4))
        for dmg in range(4):
            _pe(("wo", NT - 1, dmg))
        for n in range(NT):
            for hq in range(4):
                _ve(("a2", n, hq)); _ve(("retf", n, hq)); _ve(("supd", n, hq))
            if n >= 1:
                for tg4 in range(4):
                    _ve(("prod", n - 1, tg4))
            _ve(("red", n))
            if n >= 1:
                for dmg in range(4):
                    _ve(("sev", n - 1, dmg))
        for tg4 in range(4):
            _ve(("prod", NT - 1, tg4))
        for dmg in range(4):
            _ve(("sev", NT - 1, dmg))
        for n in range(NT):
            for hq in range(4):
                _se(("ev", n, hq))
            _se(("rstd", n))
            for h in range(H):
                _se(("gn", n, h))
        for n in range(NT):
            _gp(("sq", n)); _gp(("stats", n)); _gp(("nmr", n))

        def p6_sy(e):
            bs(e)
            LD.inc(e.dma_start(out=bo_sb[:], in_=bo_bc), 16)
            LD.inc(e.dma_start(out=d2_sb[:], in_=d2_t), 16)
            LD.inc(e.dma_start(out=grp1v[:], in_=grp1v_t), 16)
            def ch_load(n):
                g = RST.group(("ch", n))
                for spill, buf in [(q_sp, chq), (k_sp, chk), (kg_sp, chkg),
                                   (v_sp, chv), (g_sp, chg)]:
                    g(e, buf[n % 2][:].rearrange("p (h t) -> p h t", h=H), spill[n])
            ch_load(0)
            if NT > 1:
                ch_load(1)
            e.wait_ge(WOC.h, 48)
            g = WOS.group("wo")
            for k in range(KC):
                g(e, wo_bf[:, k * D:(k + 1) * D], wo_scr[k * 128:(k + 1) * 128, :])
            for n in range(NT):
                if n + 2 < NT:
                    e.wait_ge(PE.h, pe_ev[("um", n, 3)])
                    e.wait_ge(VE.h, ve_ev[("prod", n, 3)])
                    ch_load(n + 2)
                if n >= 1:
                    e.wait_ge(VE.h, ve_ev[("sev", n - 1, 3)])
                    SPST.group(("s", n - 1))(e, s_sp[n - 1], s_buf[(n - 1) % 2][:])
            e.wait_ge(VE.h, ve_ev[("sev", NT - 1, 3)])
            SPST.group(("s", NT - 1))(e, s_sp[NT - 1], s_buf[(NT - 1) % 2][:])
            be(e, "sy")

        def pk4(e, mk, start=True, stop=True):
            last = None
            for j in range(4):
                o, l, r = mk(j)
                last = e.matmul(o, l, r, start=(start and j == 0), stop=(stop and j == 3))
            return last

        def p6_te_core(e, n):
            def sl4(buf, hq, j):
                return buf[n % 2][:, (hq * 4 + j) * 128:(hq * 4 + j + 1) * 128]
            def mm_sc(hq):
                g = n * 4 + hq
                RST.wait(e, ("ch", n))
                if g >= 2:
                    pn, phq = divmod(g - 2, 4)
                    e.wait_ge(VE.h, ve_ev[("a2", pn, phq)])
                PE.inc(pk4(e, lambda j: (psS[hq % 2][:, j * 128:(j + 1) * 128],
                                         sl4(chk, hq, j), sl4(chq, hq, j))))
            def mm_rt(hq):
                g = n * 4 + hq
                e.wait_ge(VE.h, ve_ev[("a2", n, hq)])
                if g >= 1:
                    pn, phq = divmod(g - 1, 4)
                    e.wait_ge(SE.h, se_ev[("ev", pn, phq)])
                PE.inc(pk4(e, lambda j: (psRr[:, j * 128:(j + 1) * 128],
                                         a2_sb[hq % 2][:, j * 128:(j + 1) * 128],
                                         sl4(chv, hq, j))))
            def mm_cx(hq):
                g = n * 4 + hq
                if n == 0:
                    if hq == 0:
                        e.wait_ge(VE.h, ve_sbf0_val)
                else:
                    e.wait_ge(VE.h, ve_ev[("supd", n - 1, hq)])
                if g >= 1:
                    pn, phq = divmod(g - 1, 4)
                    e.wait_ge(VE.h, ve_ev[("retf", pn, phq)])
                PE.inc(pk4(e, lambda j: (psC[:, j * 128:(j + 1) * 128],
                                         sl4(chq, hq, j),
                                         s_bfT[:, (hq * 4 + j) * 128:(hq * 4 + j + 1) * 128])))
            def mm_um(hq):
                g = n * 4 + hq
                if g >= 2:
                    pn, phq = divmod(g - 2, 4)
                    e.wait_ge(VE.h, ve_ev[("supd", pn, phq)])
                PE.inc(pk4(e, lambda j: (psU6[hq % 2][:, j * 128:(j + 1) * 128],
                                         sl4(chkg, hq, j), sl4(chv, hq, j))))
            mm = {"sc": mm_sc, "rt": mm_rt, "cx": mm_cx, "um": mm_um}
            for op, hq in CORE_ORD:
                mm[op](hq)

        def p6_te_tail(e, n):
            for tg4 in range(4):
                last = None
                for j in range(4):
                    h = tg4 * 4 + j
                    e.wait_ge(SE.h, se_ev[("gn", n, h)])
                    if j == 0:
                        f = n * 4 + tg4
                        if f >= 1:
                            pn, pg = divmod(f - 1, 4)
                            e.wait_ge(VE.h, ve_ev[("prod", pn, pg)])
                    last = e.transpose(psT6[:, j * 128:(j + 1) * 128],
                                       gn_all[:, h * 128:(h + 1) * 128], id_bf[:])
                PE.inc(last)
            e.wait_ge(VE.h, ve_ev[("prod", n, 3)])
            for dmg in range(4):
                if n == 0 and dmg == 0:
                    WOS.wait(e, "wo")
                w = n * 4 + dmg
                if w >= 1:
                    pn, pd = divmod(w - 1, 4)
                    e.wait_ge(VE.h, ve_ev[("sev", pn, pd)])
                last = None
                for hv in range(KC):
                    last = e.matmul(psW[:],
                                    prod_fm[:, hv * 128:(hv + 1) * 128],
                                    wo_bf[:, hv * D + dmg * 512: hv * D + (dmg + 1) * 512],
                                    start=(hv == 0), stop=(hv == KC - 1))
                PE.inc(last)

        def p6_te(e):
            bs(e)
            for n in range(NT):
                p6_te_core(e, n)
                if n >= 1:
                    p6_te_tail(e, n - 1)
            p6_te_tail(e, NT - 1)
            be(e, "te")

        def p6_se(e):
            bs(e)
            for n in range(NT):
                for hq in range(4):
                    e.wait_ge(PE.h, pe_ev[("rt", n, hq)])
                    if n * 4 + hq >= 1:
                        g = n * 4 + hq - 1
                        pn, phq = divmod(g, 4)
                        e.wait_ge(VE.h, ve_ev[("retf", pn, phq)])
                    e.drain()
                    SE.inc(e.copy(ret1[:], psRr[:]))
                e.wait_ge(GPC.h, gp_ev[("stats", n)])
                e.drain()
                e.activation(st16[:, c_nv], st16[:, c_nv], AF.Ln, scale=-1.0)
                e.drain()
                SE.inc(e.activation(st16[:, c_rs], st16[:, c_nv], AF.Exp, scale=-0.5))
                e.wait_ge(GPC.h, gp_ev[("nmr", n)])
                e.drain()
                for h in range(H):
                    if n >= 1:
                        e.wait_ge(PE.h, pe_ev[("tp", n - 1, h // 4)])
                    SE.inc(e.activation(gn_all[:, h * 128:(h + 1) * 128],
                                        ret_all[n % 2][:, h * 128:(h + 1) * 128], AF.Identity,
                                        scale=st16[:, 64 + h:64 + h + 1],
                                        bias=st16[:, 80 + h:80 + h + 1]))
            be(e, "se")

        def p6_ve(e):
            bs(e)
            e.wait_ge(LD.h, bLD6 + 48)
            for n in range(NT):
                for hq in range(4):
                    hsl = slice(hq * 512, (hq + 1) * 512)
                    e.wait_ge(PE.h, pe_ev[("sc", n, hq)])
                    if n * 4 + hq >= 2:
                        pn, phq = divmod(n * 4 + hq - 2, 4)
                        e.wait_ge(PE.h, pe_ev[("rt", pn, phq)])
                    if n >= 2 and hq == 0:
                        e.wait_ge(SE.h, se_ev[("gn", n - 2, 15)])
                    VE.inc(e.tensor_tensor(a2_sb[hq % 2][:], psS[hq % 2][:], d2_sb[:, hsl], ALU.mult))
                    e.wait_ge(SE.h, se_ev[("ev", n, hq)])
                    e.wait_ge(PE.h, pe_ev[("cx", n, hq)])
                    last = None
                    for j in range(4):
                        h = hq * 4 + j
                        last = e.scalar_tensor_tensor(
                            ret_all[n % 2][:, h * 128:(h + 1) * 128],
                            psC[:, j * 128:(j + 1) * 128], grp1v[:, h:h + 1],
                            ret1[:, j * 128:(j + 1) * 128], ALU.mult, ALU.add)
                    VE.inc(last)
                    e.wait_ge(PE.h, pe_ev[("um", n, hq)])
                    for j in range(4):
                        h = hq * 4 + j
                        hsl2 = slice(h * 128, (h + 1) * 128)
                        e.scalar_tensor_tensor(s_f32[:, hsl2], s_f32[:, hsl2], GC[h],
                                               psU6[hq % 2][:, j * 128:(j + 1) * 128],
                                               ALU.mult, ALU.add)
                    e.drain()
                    VE.inc(e.tensor_copy(s_bfT[:, hq * 512:(hq + 1) * 512],
                                         s_f32[:, hq * 512:(hq + 1) * 512]))
                if n >= 1:
                    for tg4 in range(4):
                        e.wait_ge(PE.h, pe_ev[("tp", n - 1, tg4)])
                        if n >= 2 and tg4 == 0:
                            e.wait_ge(PE.h, pe_ev[("wo", n - 2, 3)])
                        VE.inc(e.tensor_tensor(prod_fm[:, tg4 * 512:(tg4 + 1) * 512],
                                               psT6[:],
                                               chg[(n - 1) % 2][:, tg4 * 512:(tg4 + 1) * 512],
                                               ALU.mult))
                e.drain()
                e.reduce_sum(st16[:, c_sum].rearrange("p (h o) -> p h o", o=1),
                             ret_all[n % 2][:].rearrange("p (h t) -> p h t", h=H), axis=AX.X)
                e.wait_ge(GPC.h, gp_ev[("sq", n)])
                VE.inc(e.reduce_sum(st16[:, c_ssq].rearrange("p (h o) -> p h o", o=1),
                                    sq_all[:].rearrange("p (h t) -> p h t", h=H), axis=AX.X))
                if n >= 1:
                    for dmg in range(4):
                        dsl = slice(dmg * 512, (dmg + 1) * 512)
                        e.wait_ge(PE.h, pe_ev[("wo", n - 1, dmg)])
                        if n >= 3 and dmg == 0:
                            SPST.wait(e, ("s", n - 3))
                        VE.inc(e.tensor_tensor(s_buf[(n - 1) % 2][:, dsl], psW[:],
                                               bo_sb[:, dsl], ALU.add))
            n = NT - 1
            for tg4 in range(4):
                e.wait_ge(PE.h, pe_ev[("tp", n, tg4)])
                if tg4 == 0 and NT >= 2:
                    e.wait_ge(PE.h, pe_ev[("wo", n - 1, 3)])
                VE.inc(e.tensor_tensor(prod_fm[:, tg4 * 512:(tg4 + 1) * 512], psT6[:],
                                       chg[n % 2][:, tg4 * 512:(tg4 + 1) * 512], ALU.mult))
            for dmg in range(4):
                dsl = slice(dmg * 512, (dmg + 1) * 512)
                e.wait_ge(PE.h, pe_ev[("wo", n, dmg)])
                if dmg == 0 and NT >= 3:
                    SPST.wait(e, ("s", NT - 3))
                VE.inc(e.tensor_tensor(s_buf[n % 2][:, dsl], psW[:],
                                       bo_sb[:, dsl], ALU.add))
            be(e, "ve")

        def p6_gp(e):
            bs(e)
            for n in range(NT):
                e.wait_ge(VE.h, ve_ev[("retf", n, 3)])
                e.drain()
                GPC.inc(e.tensor_tensor(sq_all[:], ret_all[n % 2][:], ret_all[n % 2][:], ALU.mult))
                e.wait_ge(VE.h, ve_ev[("red", n)])
                e.drain()
                e.tensor_scalar(st16[:, c_mu], st16[:, c_sum], 1.0 / 128, None, ALU.mult)
                e.tensor_scalar(st16[:, c_ssq], st16[:, c_ssq], 1.0 / 128, None, ALU.mult)
                e.drain()
                e.tensor_tensor(st16[:, c_nv], st16[:, c_mu], st16[:, c_mu], ALU.mult)
                e.drain()
                e.tensor_tensor(st16[:, c_nv], st16[:, c_nv], st16[:, c_ssq], ALU.subtract)
                e.drain()
                GPC.inc(e.tensor_scalar(st16[:, c_nv], st16[:, c_nv], EPS, None, ALU.subtract))
                e.wait_ge(SE.h, se_ev[("rstd", n)])
                e.drain()
                e.tensor_scalar(st16[:, c_nmr], st16[:, c_mu], -1.0, None, ALU.mult)
                e.drain()
                GPC.inc(e.tensor_tensor(st16[:, c_nmr], st16[:, c_nmr], st16[:, c_rs], ALU.mult))
            be(e, "gp")

        block.sync(p6_sy); block.gpsimd(p6_gp); block.tensor(p6_te)
        block.scalar(p6_se); block.vector(p6_ve)
        assert PE.n == _pectr[0] and VE.n == _vectr[0]
        assert SE.n == _sectr[0] and GPC.n == _gpctr[0]
        p6.close()
        p56.close()
        nphase[0] += 1


        p7ab = ExitStack()
        sb7s = lambda n, shp, dt: p7ab.enter_context(nc.sbuf_tensor(n, shp, dt, side="right"))
        dwb = [sb7s(f"dwb{i}", [128, IC * 512], BF16) for i in range(2)]
        p7a = ExitStack()
        sb7 = lambda n, shp, dt: p7a.enter_context(nc.sbuf_tensor(n, shp, dt))
        ps7 = lambda n, shp, dt=F32: p7a.enter_context(nc.psum_tensor(n, shp, dt))
        xn_fm2 = sb7("xn_fm2", [128, KC * T], BF16)
        gwb = [sb7(f"gwb{i}", [128, KC * 128], BF16) for i in range(2)]
        uwb = [sb7(f"uwb{i}", [128, KC * 128], BF16) for i in range(2)]
        ga_sb = [sb7(f"ga_sb{i}", [128, 512], F32) for i in range(2)]
        ua_sb = [sb7(f"ua_sb{i}", [128, 512], F32) for i in range(2)]
        hdn_buf = [sb7(f"hdn_buf{i}", [128, 512], BF16) for i in range(2)]
        psGa = [ps7(f"psGa{i}", [128, 512]) for i in range(2)]
        psUa = [ps7(f"psUa{i}", [128, 512]) for i in range(2)]

        bPE, bVE, bSE = PE.n, VE.n, SE.n
        NF = IC * NTG
        pe_gm = lambda f: bPE + f * 2 + 1
        pe_um7 = lambda f: bPE + f * 2 + 2
        se_ga = lambda f: bSE + f * 2 + 1
        se_ua = lambda f: bSE + f * 2 + 2
        ve_hd = lambda f: bVE + f + 1

        def dn_quarter_load(e, q):
            g = DST.group(("dn", q))
            for ic in range(IC):
                g(e, dwb[q % 2][:, ic * 512:(ic + 1) * 512],
                  downT[ic * 128:(ic + 1) * 128, q * 512:(q + 1) * 512])

        def p7a_sy(e):
            bs(e)
            SPST.wait(e, "xn")
            RST.group("xn2")(e, xn_fm2[:].rearrange("p (k t) -> p k t", k=KC),
                             xn_sp.rearrange("k p t -> p k t"))
            for f in range(NF):
                ic, tg = divmod(f, NTG)
                e.wait_ge(VE.h, ve_hd(f))
                SPST.group(("h", f))(
                    e, hdn_sp[tg * 4:(tg + 1) * 4, :, ic, :].rearrange("b i t -> i b t"),
                    hdn_buf[f % 2][:].rearrange("p (b t) -> p b t", b=4))
            be(e, "sy")

        def p7a_gp(e):
            bs(e)
            for ic in range(IC):
                if ic >= 2:
                    e.wait_ge(PE.h, pe_um7((ic - 1) * NTG - 1))
                g = WST.group(("gu", ic))
                g(e, gwb[ic % 2][:].rearrange("p (k m) -> p k m", k=KC),
                  gateT[:, ic * 128:(ic + 1) * 128].rearrange("(k p) i -> p k i", p=128))
                g(e, uwb[ic % 2][:].rearrange("p (k m) -> p k m", k=KC),
                  upT[:, ic * 128:(ic + 1) * 128].rearrange("(k p) i -> p k i", p=128))
                if ic == IC // 2:
                    dn_quarter_load(e, 0)
                if ic == IC - 1:
                    dn_quarter_load(e, 1)
            be(e, "gp")

        def p7a_te(e):
            bs(e)
            RST.wait(e, "xn2")
            for f in range(NF):
                ic, tg = divmod(f, NTG)
                WST.wait(e, ("gu", ic))
                if f >= 2:
                    e.wait_ge(SE.h, se_ga(f - 2))
                last = None
                for k in range(KC):
                    last = e.matmul(psGa[f % 2][:], gwb[ic % 2][:, k * 128:(k + 1) * 128],
                                    xn_fm2[:, k * T + tg * 512: k * T + (tg + 1) * 512],
                                    start=(k == 0), stop=(k == KC - 1))
                PE.inc(last)
                if f >= 2:
                    e.wait_ge(SE.h, se_ua(f - 2))
                last = None
                for k in range(KC):
                    last = e.matmul(psUa[f % 2][:], uwb[ic % 2][:, k * 128:(k + 1) * 128],
                                    xn_fm2[:, k * T + tg * 512: k * T + (tg + 1) * 512],
                                    start=(k == 0), stop=(k == KC - 1))
                PE.inc(last)
            be(e, "te")

        def p7a_se(e):
            bs(e)
            for f in range(NF):
                ic, tg = divmod(f, NTG)
                e.wait_ge(PE.h, pe_gm(f))
                if f >= 2:
                    e.wait_ge(VE.h, ve_hd(f - 2))
                SE.inc(e.activation(ga_sb[f % 2][:], psGa[f % 2][:], AF.Silu,
                                    scale=gss_sb[:, ic:ic + 1]))
                e.wait_ge(PE.h, pe_um7(f))
                SE.inc(e.activation(ua_sb[f % 2][:], psUa[f % 2][:], AF.Identity,
                                    scale=uss_sb[:, ic:ic + 1]))
            be(e, "se")

        def p7a_ve(e):
            bs(e)
            for f in range(NF):
                e.wait_ge(SE.h, se_ua(f))
                if f >= 2:
                    SPST.wait(e, ("h", f - 2))
                VE.inc(e.tensor_tensor(hdn_buf[f % 2][:], ga_sb[f % 2][:], ua_sb[f % 2][:], ALU.mult))
            be(e, "ve")

        block.sync(p7a_sy); block.gpsimd(p7a_gp); block.tensor(p7a_te)
        block.scalar(p7a_se); block.vector(p7a_ve)
        assert PE.n == bPE + NF * 2 and SE.n == bSE + NF * 2 and VE.n == bVE + NF
        p7a.close()
        nphase[0] += 1

        p7b = ExitStack()
        sb8 = lambda n, shp, dt: p7b.enter_context(nc.sbuf_tensor(n, shp, dt))
        ps8 = lambda n, shp, dt=F32: p7b.enter_context(nc.psum_tensor(n, shp, dt))
        dss_sb = sb8("dss_sb", [128, D], F32)
        htb = [sb8(f"htb{i}", [128, IC * 128], BF16) for i in range(2)]
        x_t7 = [sb8(f"x_t7{i}", [128, 512], F32) for i in range(2)]
        s_t7 = [sb8(f"s_t7{i}", [128, 512], BF16) for i in range(2)]
        f1_sb = [sb8(f"f1_sb{i}", [128, 512], F32) for i in range(2)]
        f2_sb = [sb8(f"f2_sb{i}", [128, 512], F32) for i in range(2)]
        ob = [sb8(f"ob{i}", [128, 512], F32) for i in range(2)]
        psD = [ps8(f"psD{i}", [128, 512]) for i in range(2)]

        bLD, bPE, bVE, bSE = LD.n, PE.n, VE.n, SE.n
        NB = 4 * NT
        pe_dm = lambda p: bPE + p + 1
        ve_f1 = lambda p: bVE + p * 2 + 1
        ve_fin = lambda p: bVE + p * 2 + 2
        se_f2 = lambda p: bSE + p + 1

        def p7b_sy(e):
            bs(e)
            LD.inc(e.dma_start(out=dss_sb[:], in_=dss_bc), 16)
            for p in range(NB):
                q, tb = divmod(p, NT)
                if p >= 2:
                    e.wait_ge(PE.h, pe_dm(p - 2))
                    e.wait_ge(VE.h, ve_fin(p - 2))
                g = RST.group(("l", p))
                g(e, htb[p % 2][:].rearrange("p (c t) -> p c t", c=IC), hdn_sp[tb])
                g(e, x_t7[p % 2][:], x_in[tb * 128:(tb + 1) * 128, q * 512:(q + 1) * 512])
                g(e, s_t7[p % 2][:], s_sp[tb][:, q * 512:(q + 1) * 512])
                if p >= 1:
                    e.wait_ge(VE.h, ve_fin(p - 1))
                    pv = p - 1
                    qv, tv = divmod(pv, NT)
                    SPST.group(("o", pv))(
                        e, out[tv * 128:(tv + 1) * 128, qv * 512:(qv + 1) * 512], ob[pv % 2][:])
            e.wait_ge(VE.h, ve_fin(NB - 1))
            SPST.group(("o", NB - 1))(
                e, out[(NT - 1) * 128:NT * 128, 3 * 512:4 * 512], ob[(NB - 1) % 2][:])
            be(e, "sy")

        def p7b_gp(e):
            bs(e)
            for q in range(2, 4):
                e.wait_ge(PE.h, pe_dm((q - 1) * NT - 1))
                dn_quarter_load(e, q)
            be(e, "gp")

        def p7b_te(e):
            bs(e)
            for p in range(NB):
                q, tb = divmod(p, NT)
                DST.wait(e, ("dn", q))
                RST.wait(e, ("l", p))
                if p >= 2:
                    e.wait_ge(VE.h, ve_f1(p - 2))
                last = None
                for ic in range(IC):
                    last = e.matmul(psD[p % 2][:], htb[p % 2][:, ic * 128:(ic + 1) * 128],
                                    dwb[q % 2][:, ic * 512:(ic + 1) * 512],
                                    start=(ic == 0), stop=(ic == IC - 1))
                PE.inc(last)
            be(e, "te")

        def p7b_se(e):
            bs(e)
            for p in range(NB):
                e.wait_ge(VE.h, ve_f1(p))
                if p >= 1:
                    e.wait_ge(VE.h, ve_fin(p - 1))
                SE.inc(e.activation(f2_sb[p % 2][:], f1_sb[p % 2][:], AF.Silu))
            be(e, "se")

        def p7b_ve(e):
            bs(e)
            e.wait_ge(LD.h, bLD + 16)
            for p in range(NB):
                q, tb = divmod(p, NT)
                RST.wait(e, ("l", p))
                dsl = slice(q * 512, (q + 1) * 512)
                e.wait_ge(PE.h, pe_dm(p))
                if p >= 2:
                    e.wait_ge(SE.h, se_f2(p - 2))
                VE.inc(e.tensor_tensor(f1_sb[p % 2][:], psD[p % 2][:],
                                       dss_sb[:, dsl], ALU.mult))
                e.wait_ge(SE.h, se_f2(p))
                if p >= 2:
                    SPST.wait(e, ("o", p - 2))
                e.drain()
                e.tensor_tensor(f2_sb[p % 2][:], f2_sb[p % 2][:], s_t7[p % 2][:], ALU.mult)
                e.drain()
                VE.inc(e.tensor_tensor(ob[p % 2][:], f2_sb[p % 2][:], x_t7[p % 2][:], ALU.add))
            be(e, "ve")

        block.sync(p7b_sy); block.gpsimd(p7b_gp); block.tensor(p7b_te)
        block.scalar(p7b_se); block.vector(p7b_ve)
        assert PE.n == bPE + NB and VE.n == bVE + NB * 2 and SE.n == bSE + NB
        p7b.close()
        p7ab.close()
        nphase[0] += 1


    ctx.close()
    return nc



def _rope_tables(span_start, T):
    pos = np.arange(span_start, span_start + T, dtype=np.float64)
    j = np.arange(64, dtype=np.float64)
    inv_freq = 1.0 / THETA ** (2.0 * j / 128.0)
    freqs = pos[:, None] * inv_freq[None, :]
    cos = np.cos(freqs); sin = np.sin(freqs)
    sbase = (2.0 * j + 0.4 * 128.0) / (1.4 * 128.0)
    power = (pos - S // 2) / SCALE_BASE
    scale = sbase[None, :] ** power[:, None]
    ex = lambda a: np.repeat(a, 2, axis=1)
    cosf, sinf, scf = ex(cos), ex(sin), ex(scale)
    mk = lambda a: np.ascontiguousarray(a.T.astype(np.float32))
    return mk(cosf * scf), mk(sinf * scf), mk(cosf / scf), mk(sinf / scf)


def _const_tables():
    r = np.arange(128)
    d2 = np.zeros((128, H * 128), np.float32)
    grp1 = np.zeros((128, H * 128), np.float32)
    gcm = np.zeros((128, H), np.float32)
    for h, g in enumerate(GAMMAS):
        pm = r[None, :] - r[:, None]
        d2[:, h * 128:(h + 1) * 128] = np.where(pm >= 0, np.float64(g) ** np.maximum(pm, 0), 0.0)
        grp1[:, h * 128:(h + 1) * 128] = (np.float64(g) ** (r + 1.0))[:, None]
        gcm[:, h] = np.float64(g) ** (127.0 - r)
    p_rot = np.zeros((128, 128), np.float32)
    for i in range(64):
        p_rot[2 * i + 1, 2 * i] = -1.0
        p_rot[2 * i, 2 * i + 1] = 1.0
    return d2, grp1, gcm, p_rot, np.eye(128, dtype=np.float32)


def _wm_table(core, gs=SPAN):
    b, sp = core // NSPAN, core % NSPAN
    wm = np.zeros((NCORES * H,), np.float64)
    for cp in range(NCORES):
        bp, spp = cp // NSPAN, cp % NSPAN
        if bp == b and spp < sp:
            for h, g in enumerate(GAMMAS):
                wm[cp * H + h] = np.float64(g) ** (gs * (sp - spp - 1))
    return np.ascontiguousarray(np.broadcast_to(wm.astype(np.float32), (128, NCORES * H)))


def _rs(v):
    return np.ascontiguousarray(np.asarray(v, np.float32).reshape(-1, 128).T)


def _bc(v):
    v = np.asarray(v, np.float32)
    return np.ascontiguousarray(np.broadcast_to(v, (128, v.shape[-1])))


def make_in_maps(inputs, T=SPAN, geom_span=None):
    gs = SPAN if geom_span is None else geom_span
    x = np.ascontiguousarray(np.asarray(inputs["L0_attn_x"], np.float32))
    cf = lambda a: np.ascontiguousarray(np.asarray(a, np.float32))
    shared = {
        "Wq": cf(inputs["L0_attn_Wq"]), "Wk": cf(inputs["L0_attn_Wk"]),
        "Wv": cf(inputs["L0_attn_Wv"]), "Wg": cf(inputs["L0_attn_Wg"]),
        "Wo": cf(inputs["L0_attn_Wo"]),
        "gateT": cf(np.asarray(inputs["L0_ffn_gate_w"], np.float32).T),
        "upT": cf(np.asarray(inputs["L0_ffn_up_w"], np.float32).T),
        "downT": cf(np.asarray(inputs["L0_ffn_down_w"], np.float32).T),
        "bqr": _rs(inputs["L0_attn_bq"]), "bkr": _rs(inputs["L0_attn_bk"]),
        "bgr": _rs(inputs["L0_attn_bg"]),
        "bv_bc": _bc(inputs["L0_attn_bv"]), "bo_bc": _bc(inputs["L0_attn_bo"]),
        "gss": _rs(inputs["L0_ffn_gate_s"]), "uss": _rs(inputs["L0_ffn_up_s"]),
        "dss_bc": _bc(inputs["L0_ffn_down_s"]),
    }
    d2, grp1, gcm, p_rot, ident = _const_tables()
    r_ = np.arange(128)
    grp1v = np.stack([np.float64(g) ** (r_ + 1.0) for g in GAMMAS], axis=1).astype(np.float32)
    shared.update({"d2_t": d2, "grp1_bc": grp1, "grp1v_t": np.ascontiguousarray(grp1v),
                   "gcm_t": gcm, "p_rot": p_rot, "ident": ident})
    rope = [_rope_tables(sp * gs, T) for sp in range(NSPAN)]
    in_maps = []
    for c in range(NCORES):
        b, sp = c // NSPAN, c % NSPAN
        cs, ss_, ics, iss = rope[sp]
        im = dict(shared)
        im["x"] = np.ascontiguousarray(x[b, sp * gs: sp * gs + T])
        im["cs_t"] = cs; im["ss_t"] = ss_; im["ics_t"] = ics; im["iss_t"] = iss
        im["wm_bc"] = _wm_table(c, gs)
        in_maps.append(im)
    return in_maps


def kernel(**inputs):
    nc = build(T=SPAN, debug=False)
    in_maps = make_in_maps(inputs, T=SPAN)
    res = run_bass_kernel_spmd(nc, in_maps, core_ids=list(range(NCORES)))
    out = np.empty((B, S, D), np.float32)
    for c in range(NCORES):
        b, sp = c // NSPAN, c % NSPAN
        out[b, sp * SPAN:(sp + 1) * SPAN] = res.results[c]["out"]
    return out


# version 17
# speedup vs baseline: 1.0235x; 1.0206x over previous
import numpy as np
import ml_dtypes
from contextlib import ExitStack

import concourse.bass as bass
import concourse.mybir as mybir
from concourse.bass_utils import run_bass_kernel_spmd

F32 = mybir.dt.float32
BF16 = mybir.dt.bfloat16
AF = mybir.ActivationFunctionType
ALU = mybir.AluOpType
AX = mybir.AxisListType

B = 2
S = 8192
D = 2048
H = 16
INTER = 5632
NCORES = 8
NSPAN = 4
SPAN = S // NSPAN
EPS = 1e-5
THETA = 10000.0
SCALE_BASE = 512.0
GAMMAS = [1.0 - 2.0 ** (-5.0 - h) for h in range(H)]
GC = [float(np.float32(np.float64(g) ** 128)) for g in GAMMAS]
KC = D // 128
IC = INTER // 128


class Sem:
    def __init__(self, nc, ctx, name):
        self.h = ctx.enter_context(nc.semaphore(name))
        self.n = 0

    def inc(self, instr, k=1):
        instr.then_inc(self.h, k)
        self.n += k
        return self.n


class DStream:

    def __init__(self, nc, ctx, name):
        self.s = [Sem(nc, ctx, name + "a"), Sem(nc, ctx, name + "b")]
        self.rec = {}
        self.gi = 0

    def group(self, key=None):
        p = self.gi % 2
        self.gi += 1
        if key is None:
            key = self.gi - 1
        sem = self.s[p]

        def emit(e, out, in_):
            v = sem.inc(e.dma_start(out=out, in_=in_), 16)
            self.rec[key] = (sem, v)
        return emit

    def wait(self, e, key):
        sem, v = self.rec[key]
        e.wait_ge(sem.h, v)

    def drain_all(self, e):
        for sem in self.s:
            if sem.n:
                e.wait_ge(sem.h, sem.n)


def build(T=SPAN, debug=False):
    NT = T // 128
    NTG = T // 512
    TPH = NT // 2
    nc = bass.Bass("TRN2", target_bir_lowering=False)

    dram = lambda n, shp, dt=F32, kind="ExternalInput": nc.dram_tensor(n, shp, dt, kind=kind).ap()
    sk = "ExternalOutput" if debug else "Internal"

    x_in = dram("x", [T, D])
    Wq = dram("Wq", [D, D]); Wk = dram("Wk", [D, D]); Wv = dram("Wv", [D, D])
    Wg = dram("Wg", [D, D]); Wo = dram("Wo", [D, D])
    gateT = dram("gateT", [D, INTER]); upT = dram("upT", [D, INTER]); downT = dram("downT", [INTER, D])
    bqr = dram("bqr", [128, 16]); bkr = dram("bkr", [128, 16]); bgr = dram("bgr", [128, 16])
    bv_bc = dram("bv_bc", [128, D]); bo_bc = dram("bo_bc", [128, D])
    gss = dram("gss", [128, IC]); uss = dram("uss", [128, IC]); dss_bc = dram("dss_bc", [128, D])
    cs_t = dram("cs_t", [128, T]); ss_t = dram("ss_t", [128, T])
    ics_t = dram("ics_t", [128, T]); iss_t = dram("iss_t", [128, T])
    d2_t = dram("d2_t", [128, H * 128])
    grp1_bc = dram("grp1_bc", [128, H * 128])
    grp1v_t = dram("grp1v_t", [128, H])
    gcm_t = dram("gcm_t", [128, H])
    wm_bc = dram("wm_bc", [128, NCORES * H])
    p_in = dram("p_rot", [128, 128]); id_in = dram("ident", [128, 128])

    out = dram("out", [T, D], F32, kind="ExternalOutput")

    q_sp = dram("q_sp", [NT, 128, H, 128], BF16, kind=sk)
    k_sp = dram("k_sp", [NT, 128, H, 128], BF16, kind=sk)
    kg_sp = dram("kg_sp", [NT, 128, H, 128], BF16, kind=sk)
    v_sp = dram("v_sp", [NT, 128, H, 128], BF16, kind=sk)
    g_sp = dram("g_sp", [NT, 128, H, 128], BF16, kind=sk)
    xn_sp = dram("xn_sp", [KC, 128, T], BF16, kind=sk)
    s_sp = dram("s_sp", [NT, 128, D], BF16, kind=sk)
    hdn_sp = dram("hdn_sp", [NT, 128, IC, 128], BF16, kind=sk)
    cc_in = nc.dram_tensor("cc_in", [H * 128, 128], F32, kind="Internal")
    cc_out = nc.dram_tensor("cc_out", [NCORES * H * 128, 128], F32, kind="Internal")

    ctx = ExitStack()
    sb = lambda n, shp, dt: ctx.enter_context(nc.sbuf_tensor(n, shp, dt))

    LD = Sem(nc, ctx, "LD")
    SPST = DStream(nc, ctx, "SP")
    KSP = DStream(nc, ctx, "KP")
    RST = DStream(nc, ctx, "RS")
    WST = DStream(nc, ctx, "WS")
    DST = DStream(nc, ctx, "DS")
    MST = DStream(nc, ctx, "MS")
    CCS = Sem(nc, ctx, "CCS")
    GPC = Sem(nc, ctx, "GPC")
    PE = Sem(nc, ctx, "PE")
    VE = Sem(nc, ctx, "VE")
    SE = Sem(nc, ctx, "SE")
    ENGS = ("sy", "gp", "te", "se", "ve")
    BAR = {e: Sem(nc, ctx, f"B_{e}") for e in ENGS}
    nphase = [0]

    def bs(e):
        if nphase[0] > 0:
            for s_ in BAR.values():
                e.wait_ge(s_.h, nphase[0])

    def be(e, which):
        if which == "sy":
            if LD.n:
                e.wait_ge(LD.h, LD.n)
            SPST.drain_all(e)
            KSP.drain_all(e)
            RST.drain_all(e)
        elif which == "gp":
            WST.drain_all(e)
            DST.drain_all(e)
            MST.drain_all(e)
            if GPC.n:
                e.wait_ge(GPC.h, GPC.n)
        else:
            st = {"te": PE, "se": SE, "ve": VE}[which]
            if st.n > 0:
                e.wait_ge(st.h, st.n)
        BAR[which].inc(e.nop(), 1)

    bqr_sb = sb("bqr_sb", [128, 16], F32); bkr_sb = sb("bkr_sb", [128, 16], F32)
    bgr_sb = sb("bgr_sb", [128, 16], F32)
    gcm_sb = sb("gcm_sb", [128, H], F32)
    gss_sb = sb("gss_sb", [128, IC], F32); uss_sb = sb("uss_sb", [128, IC], F32)
    wm_sb = sb("wm_sb", [128, NCORES * H], F32)
    p_bf = sb("p_bf", [128, 128], BF16); id_bf = sb("id_bf", [128, 128], BF16)
    m_sb = sb("m_sb", [128, H * 128], F32)
    px = ExitStack()
    xn_fm = px.enter_context(nc.sbuf_tensor("xn_fm", [128, KC * T], BF16))

    with nc.Block() as block:
        def p0_sy(e):
            for dst, src in [(bqr_sb, bqr), (bkr_sb, bkr), (bgr_sb, bgr), (gcm_sb, gcm_t),
                             (gss_sb, gss), (uss_sb, uss), (wm_sb, wm_bc)]:
                LD.inc(e.dma_start(out=dst[:], in_=src), 16)
            be(e, "sy")
        def p0_gp(e):
            g = WST.group("p0")
            g(e, p_bf[:], p_in)
            g(e, id_bf[:], id_in)
            be(e, "gp")
        block.sync(p0_sy); block.gpsimd(p0_gp)
        block.tensor(lambda e: be(e, "te")); block.scalar(lambda e: be(e, "se"))
        block.vector(lambda e: be(e, "ve"))
        nphase[0] += 1

        pa_scope = ExitStack()
        sbA = lambda n, shp, dt: pa_scope.enter_context(nc.sbuf_tensor(n, shp, dt, side="right"))
        wvb = [sbA(f"wvb{i}", [128, KC * 512], BF16) for i in range(2)]
        wkb = [sbA(f"wkb{i}", [128, KC * 512], BF16) for i in range(2)]

        p1 = ExitStack()
        sb1 = lambda n, shp, dt: p1.enter_context(nc.sbuf_tensor(n, shp, dt))
        ps1 = lambda n, shp, dt=F32: p1.enter_context(nc.psum_tensor(n, shp, dt))
        NGRP = 4
        TPH = max(NT // NGRP, 1)
        NGRP = NT // TPH
        x_b = [sb1(f"x_b{i}", [128, D], F32) for i in range(TPH)]
        xn_tok = [sb1(f"xn_tok{i}", [128, D], BF16) for i in range(2)]
        sq_sc = sb1("sq_sc", [128, D], F32)
        st_sum = sb1("st_sum", [128, NT], F32); st_ssq = sb1("st_ssq", [128, NT], F32)
        st_mu = sb1("st_mu", [128, NT], F32); st_rs = sb1("st_rs", [128, NT], F32)
        st_nv = sb1("st_nv", [128, NT], F32)
        ps_tp = [ps1(f"ps_tp{i}", [128, 128], BF16) for i in range(2)]

        bLD, bPE, bVE, bSE = LD.n, PE.n, VE.n, SE.n
        ld_half = lambda hh: bLD + (hh + 1) * TPH * 16
        se_sq = lambda t: bSE + (t // TPH) * (TPH + 1) + (t % TPH) + 1
        se_rstd = lambda hh: bSE + hh * (TPH + 1) + TPH + 1
        ve_negv = lambda hh: bVE + hh * (1 + TPH * 17) + 1
        ve_xn = lambda t: ve_negv(t // TPH) + (t % TPH) * 17 + 1
        ve_evac = lambda t, j: ve_xn(t) + j + 1
        pe_tp1 = lambda t, j: bPE + (t * 16 + j) + 1

        def p1_sy(e):
            bs(e)
            for t in range(NT):
                if t >= TPH:
                    e.wait_ge(VE.h, ve_xn(t - TPH))
                    e.wait_ge(SE.h, se_sq(t - TPH))
                LD.inc(e.dma_start(out=x_b[t % TPH][:], in_=x_in[t * 128:(t + 1) * 128, :]), 16)
            be(e, "sy")

        def p1_gp(e):
            bs(e)
            WST.group(("wv", 0))(
                e, wvb[0][:].rearrange("p (k m) -> p k m", k=KC),
                Wv[:, 0:512].rearrange("(k p) m -> p k m", p=128))
            WST.group(("wk", 0))(
                e, wkb[0][:].rearrange("p (k m) -> p k m", k=KC),
                Wk[:, 0:512].rearrange("(k p) m -> p k m", p=128))
            be(e, "gp")

        def p1_te(e):
            bs(e)
            for t in range(NT):
                e.wait_ge(VE.h, ve_xn(t))
                for j in range(16):
                    f = t * 16 + j
                    if f >= 2:
                        pt, pj = divmod(f - 2, 16)
                        e.wait_ge(VE.h, ve_evac(pt, pj))
                    PE.inc(e.transpose(ps_tp[f % 2][:], xn_tok[t % 2][:, j * 128:(j + 1) * 128],
                                       id_bf[:]))
            be(e, "te")

        def p1_se(e):
            bs(e)
            for hh in range(NGRP):
                e.wait_ge(LD.h, ld_half(hh))
                for i in range(TPH):
                    t = hh * TPH + i
                    e.drain()
                    SE.inc(e.activation(sq_sc[:], x_b[i][:], AF.Square,
                                        accum_out=st_ssq[:, t:t + 1]))
                sl = slice(hh * TPH, hh * TPH + TPH)
                e.wait_ge(VE.h, ve_negv(hh))
                e.activation(st_nv[:, sl], st_nv[:, sl], AF.Ln, scale=-1.0)
                e.drain()
                SE.inc(e.activation(st_rs[:, sl], st_nv[:, sl], AF.Exp, scale=-0.5))
            be(e, "se")

        def p1_ve(e):
            bs(e)
            for hh in range(NGRP):
                sl = slice(hh * TPH, hh * TPH + TPH)
                e.wait_ge(LD.h, ld_half(hh))
                for i in range(TPH):
                    t = hh * TPH + i
                    e.reduce_sum(st_sum[:, t:t + 1], x_b[i][:], axis=AX.X)
                e.wait_ge(SE.h, se_sq(hh * TPH + TPH - 1))
                e.drain()
                e.tensor_scalar(st_mu[:, sl], st_sum[:, sl], 1.0 / D, None, ALU.mult)
                e.tensor_scalar(st_ssq[:, sl], st_ssq[:, sl], 1.0 / D, None, ALU.mult)
                e.drain()
                e.tensor_tensor(st_nv[:, sl], st_mu[:, sl], st_mu[:, sl], ALU.mult)
                e.drain()
                e.tensor_tensor(st_nv[:, sl], st_nv[:, sl], st_ssq[:, sl], ALU.subtract)
                e.drain()
                VE.inc(e.tensor_scalar(st_nv[:, sl], st_nv[:, sl], EPS, None, ALU.subtract))
                for i in range(TPH):
                    t = hh * TPH + i
                    e.wait_ge(SE.h, se_rstd(hh))
                    e.drain()
                    VE.inc(e.tensor_scalar(xn_tok[t % 2][:], x_b[i][:], st_mu[:, t:t + 1],
                                           st_rs[:, t:t + 1], ALU.subtract, ALU.mult))
                    for j in range(16):
                        e.wait_ge(PE.h, pe_tp1(t, j))
                        VE.inc(e.tensor_copy(xn_fm[:, j * T + t * 128: j * T + (t + 1) * 128],
                                             ps_tp[(t * 16 + j) % 2][:]))
            be(e, "ve")

        block.sync(p1_sy); block.gpsimd(p1_gp); block.tensor(p1_te)
        block.scalar(p1_se); block.vector(p1_ve)
        assert SE.n == bSE + NGRP * (TPH + 1) and VE.n == bVE + NGRP * (1 + TPH * 17)
        assert PE.n == bPE + NT * 16 and LD.n == bLD + NT * 16
        p1.close()
        nphase[0] += 1

        pa = ExitStack()
        sbp = lambda n, shp, dt: pa.enter_context(nc.sbuf_tensor(n, shp, dt))
        psp = lambda n, shp, dt=F32: pa.enter_context(nc.psum_tensor(n, shp, dt))
        c1_sb = sbp("c1_sb", [128, T], F32); c2_sb = sbp("c2_sb", [128, T], F32)
        bv_sb = sbp("bv_sb", [128, D], F32)
        vqb = [sbp(f"vqb{i}", [128, NT * 512], BF16) for i in range(2)]
        kbf = [sbp(f"kbf{i}", [128, 512], BF16) for i in range(2)]
        ta = sbp("ta", [128, 512], F32); tb_ = sbp("tb", [128, 512], F32)
        fmb = [sbp(f"fmb{i}", [128, 512], BF16) for i in range(2)]
        kgb = [sbp(f"kgb{i}", [128, 512], BF16) for i in range(2)]
        psV = [psp(f"psV{i}", [128, 512]) for i in range(2)]
        psK = [psp(f"psK{i}", [128, 512]) for i in range(2)]
        psR = psp("psR", [128, 512])
        psT = psp("psT", [128, 512], BF16)
        psU = psp("psU", [128, 512])

        NIT = 16 * NTG
        bPE, bVE, bSE = PE.n, VE.n, SE.n
        pe_ev = {}; _pectr = [bPE]
        ve_ev = {}; _vectr = [bVE]
        se_ev = {}; _sectr = [bSE]
        def _mk(ctr, ev):
            def f(key):
                ctr[0] += 1
                ev[key] = ctr[0]
            return f
        _pe = _mk(_pectr, pe_ev); _ve = _mk(_vectr, ve_ev); _se = _mk(_sectr, se_ev)
        SHIFT = 16 * NTG // 4
        SHIFT = NT
        for i in range(NIT + SHIFT):
            if i < NIT:
                _pe(("v", i))
            if i >= SHIFT:
                it = i - SHIFT
                _pe(("kproj", it)); _pe(("krot", it)); _pe(("ktp", it)); _pe(("kum", it))
        for i in range(NIT + SHIFT):
            if i < NIT:
                _ve(("vev", i))
            if i >= SHIFT:
                it = i - SHIFT
                _ve(("fm", it)); _ve(("kg", it)); _ve(("mu", it))
        for it in range(NIT):
            _se(("kbf", it))

        def pa_sy(e):
            bs(e)
            LD.inc(e.dma_start(out=c1_sb[:], in_=ics_t), 16)
            LD.inc(e.dma_start(out=c2_sb[:], in_=iss_t), 16)
            LD.inc(e.dma_start(out=bv_sb[:], in_=bv_bc), 16)
            for i in range(NIT + SHIFT):
                if i < NIT:
                    vg, t = divmod(i, NT)
                    e.wait_ge(VE.h, ve_ev[("vev", i)])
                    SPST.group(("v", i))(
                        e, v_sp[t, :, 4 * vg:4 * vg + 4, :],
                        vqb[vg % 2][:, t * 512:(t + 1) * 512])
                if i >= SHIFT:
                    it = i - SHIFT
                    qd, rem = divmod(it, 4 * NTG)
                    h, tg = divmod(rem, NTG)
                    head = qd * 4 + h
                    g = KSP.group(("sp", it))
                    e.wait_ge(VE.h, ve_ev[("fm", it)])
                    g(e, k_sp[tg * 4:(tg + 1) * 4, :, head, :].rearrange("n k t -> k n t"),
                      fmb[it % 2][:].rearrange("p (n t) -> p n t", n=4))
                    e.wait_ge(VE.h, ve_ev[("kg", it)])
                    g(e, kg_sp[tg * 4:(tg + 1) * 4, :, head, :].rearrange("n t k -> t n k"),
                      kgb[it % 2][:].rearrange("p (n k) -> p n k", n=4))
            e.wait_ge(VE.h, ve_ev[("mu", NIT - 1)])
            LD.inc(e.dma_start(out=cc_in.ap().rearrange("(h k) v -> k h v", h=H),
                               in_=m_sb[:].rearrange("p (h v) -> p h v", h=H)), 16)
            be(e, "sy")

        ld_m_total = [None]

        def pa_gp(e):
            bs(e)
            for q in range(1, 4):
                e.wait_ge(PE.h, pe_ev[("v", (q - 1) * NT)])
                WST.group(("wv", q))(
                    e, wvb[q % 2][:].rearrange("p (k m) -> p k m", k=KC),
                    Wv[:, q * 512:(q + 1) * 512].rearrange("(k p) m -> p k m", p=128))
                e.wait_ge(PE.h, pe_ev[("kproj", max((q - 1) * 4 * NTG, 0))])
                WST.group(("wk", q))(
                    e, wkb[q % 2][:].rearrange("p (k m) -> p k m", k=KC),
                    Wk[:, q * 512:(q + 1) * 512].rearrange("(k p) m -> p k m", p=128))
            e.wait_ge(LD.h, ld_m_total[0])
            CCS.inc(e.collective_compute(
                "AllGather", ALU.bypass, replica_groups=[list(range(NCORES))],
                ins=[cc_in.ap().opt()], outs=[cc_out.ap().opt()]), 1)
            be(e, "gp")

        def pa_te(e):
            bs(e)
            for i in range(NIT + SHIFT):
                if i < NIT:
                    vg, t = divmod(i, NT)
                    WST.wait(e, ("wv", vg))
                    if i >= 2:
                        e.wait_ge(VE.h, ve_ev[("vev", i - 2)])
                    last = None
                    for k in range(KC):
                        last = e.matmul(psV[i % 2][:],
                                        xn_fm[:, k * T + t * 128: k * T + (t + 1) * 128],
                                        wvb[vg % 2][:, k * 512:(k + 1) * 512],
                                        start=(k == 0), stop=(k == KC - 1))
                    PE.inc(last)
                if i >= SHIFT:
                    it = i - SHIFT
                    qd, rem = divmod(it, 4 * NTG)
                    h, tg = divmod(rem, NTG)
                    WST.wait(e, ("wk", qd))
                    if it >= 2:
                        e.wait_ge(SE.h, se_ev[("kbf", it - 2)])
                    last = None
                    for k in range(KC):
                        last = e.matmul(psK[it % 2][:],
                                        wkb[qd % 2][:, k * 512 + h * 128: k * 512 + (h + 1) * 128],
                                        xn_fm[:, k * T + tg * 512: k * T + (tg + 1) * 512],
                                        start=(k == 0), stop=(k == KC - 1))
                    PE.inc(last)
                    e.wait_ge(SE.h, se_ev[("kbf", it)])
                    if it >= 1:
                        e.wait_ge(VE.h, ve_ev[("fm", it - 1)])
                    PE.inc(e.matmul(psR[:], p_bf[:], kbf[it % 2][:], start=True, stop=True))
                    e.wait_ge(VE.h, ve_ev[("fm", it)])
                    if it >= 1:
                        e.wait_ge(VE.h, ve_ev[("kg", it - 1)])
                    last = None
                    for j in range(4):
                        last = e.transpose(psT[:, j * 128:(j + 1) * 128],
                                           fmb[it % 2][:, j * 128:(j + 1) * 128], id_bf[:])
                    PE.inc(last)
                    e.wait_ge(VE.h, ve_ev[("kg", it)])
                    e.wait_ge(VE.h, ve_ev[("vev", (qd + 1) * NT - 1)])
                    if it >= 1:
                        e.wait_ge(VE.h, ve_ev[("mu", it - 1)])
                    last = None
                    for j in range(4):
                        n_ = tg * 4 + j
                        last = e.matmul(psU[:, j * 128:(j + 1) * 128],
                                        kgb[it % 2][:, j * 128:(j + 1) * 128],
                                        vqb[qd % 2][:, n_ * 512 + h * 128: n_ * 512 + (h + 1) * 128],
                                        start=(j == 0), stop=(j == 3))
                    PE.inc(last)
            be(e, "te")

        def pa_se(e):
            bs(e)
            for it in range(NIT):
                qd, rem = divmod(it, 4 * NTG)
                h, tg = divmod(rem, NTG)
                head = qd * 4 + h
                e.wait_ge(PE.h, pe_ev[("kproj", it)])
                if it >= 2:
                    e.wait_ge(PE.h, pe_ev[("krot", it - 2)])
                    e.wait_ge(VE.h, ve_ev[("fm", it - 2)])
                SE.inc(e.activation(kbf[it % 2][:], psK[it % 2][:], AF.Identity,
                                    bias=bkr_sb[:, head:head + 1]))
            be(e, "se")

        def pa_ve(e):
            bs(e)
            e.memset(m_sb[:], 0.0)
            e.wait_ge(LD.h, bLD_pa + 48)
            for i in range(NIT + SHIFT):
                if i < NIT:
                    vg, t = divmod(i, NT)
                    e.wait_ge(PE.h, pe_ev[("v", i)])
                    if i >= 2:
                        SPST.wait(e, ("v", i - 2))
                    if i >= 2 * NT and t == 0:
                        e.wait_ge(PE.h, pe_ev[("kum", (vg - 1) * 4 * NTG - 1)])
                    VE.inc(e.tensor_tensor(vqb[vg % 2][:, t * 512:(t + 1) * 512], psV[i % 2][:],
                                           bv_sb[:, vg * 512:(vg + 1) * 512], ALU.add))
                if i >= SHIFT:
                    it = i - SHIFT
                    qd, rem = divmod(it, 4 * NTG)
                    h, tg = divmod(rem, NTG)
                    head = qd * 4 + h
                    tsl = slice(tg * 512, (tg + 1) * 512)
                    e.wait_ge(PE.h, pe_ev[("krot", it)])
                    if it >= 2:
                        KSP.wait(e, ("sp", it - 2))
                        e.wait_ge(PE.h, pe_ev[("ktp", it - 2)])
                    e.tensor_tensor(ta[:], kbf[it % 2][:], c1_sb[:, tsl], ALU.mult)
                    e.tensor_tensor(tb_[:], psR[:], c2_sb[:, tsl], ALU.mult)
                    e.drain()
                    VE.inc(e.tensor_tensor(fmb[it % 2][:], ta[:], tb_[:], ALU.add))
                    e.wait_ge(PE.h, pe_ev[("ktp", it)])
                    if it >= 2:
                        e.wait_ge(PE.h, pe_ev[("kum", it - 2)])
                    VE.inc(e.tensor_scalar(kgb[it % 2][:], psT[:], gcm_sb[:, head:head + 1],
                                           None, ALU.mult))
                    e.wait_ge(PE.h, pe_ev[("kum", it)])
                    last = None
                    for j in range(4):
                        if j > 0:
                            e.drain()
                        hsl = slice(head * 128, (head + 1) * 128)
                        last = e.scalar_tensor_tensor(m_sb[:, hsl], m_sb[:, hsl], GC[head],
                                                      psU[:, j * 128:(j + 1) * 128],
                                                      ALU.mult, ALU.add)
                    VE.inc(last)
            be(e, "ve")

        bLD_pa = LD.n
        block.sync(pa_sy)
        ld_m_total[0] = LD.n
        block.gpsimd(pa_gp); block.tensor(pa_te)
        block.scalar(pa_se); block.vector(pa_ve)
        assert PE.n == _pectr[0] and VE.n == _vectr[0] and SE.n == _sectr[0]
        pa.close()
        pa_scope.close()
        nphase[0] += 1

        p56 = ExitStack()
        sb56 = lambda n, shp, dt: p56.enter_context(nc.sbuf_tensor(n, shp, dt, side="right"))
        s_f32 = sb56("s_f32", [128, H * 128], F32)
        s_bfT = sb56("s_bfT", [128, H * 128], BF16)

        pb = ExitStack()
        sbq = lambda n, shp, dt: pb.enter_context(nc.sbuf_tensor(n, shp, dt))
        psq = lambda n, shp, dt=F32: pb.enter_context(nc.psum_tensor(n, shp, dt))
        wqb = [sbq(f"wqb{i}", [128, KC * 512], BF16) for i in range(2)]
        wgb = [sbq(f"wgb{i}", [128, KC * 512], BF16) for i in range(2)]
        mg_buf = [sbq(f"mg{i}", [128, H * 128], F32) for i in range(2)]
        c1q = sbq("c1q", [128, T], F32); c2q = sbq("c2q", [128, T], F32)
        qbf = [sbq(f"qbf{i}", [128, 512], BF16) for i in range(2)]
        taq = sbq("taq", [128, 512], F32); tbq = sbq("tbq", [128, 512], F32)
        fmq = [sbq(f"fmq{i}", [128, 512], BF16) for i in range(2)]
        g_buf = [sbq(f"g_buf{i}", [128, 512], BF16) for i in range(2)]
        psQ = [psq(f"psQ{i}", [128, 512]) for i in range(2)]
        psRq = psq("psRq", [128, 512])
        psG5 = [psq(f"psG5{i}", [128, 512]) for i in range(2)]

        bPE, bVE, bSE = PE.n, VE.n, SE.n
        pe_ev = {}; _pectr = [bPE]
        ve_ev = {}; _vectr = [bVE]
        se_ev = {}; _sectr = [bSE]
        _pe = _mk(_pectr, pe_ev); _ve = _mk(_vectr, ve_ev); _se = _mk(_sectr, se_ev)
        for it in range(NIT):
            _pe(("qproj", it)); _pe(("qrot", it)); _pe(("gproj", it))
        S0C = {}
        for c in range(NCORES):
            S0C.setdefault(4 * NTG + (c * 4 * NTG) // NCORES, []).append(c)
        for it in range(NIT):
            _ve(("fm", it))
            for c in S0C.get(it, []):
                _ve(("s0", c))
                if c == NCORES - 1:
                    _ve(("sbf0",))
        for it in range(NIT):
            _se(("qbf", it)); _se(("gev", it))

        def pb_sy(e):
            bs(e)
            LD.inc(e.dma_start(out=c1q[:], in_=cs_t), 16)
            LD.inc(e.dma_start(out=c2q[:], in_=ss_t), 16)
            SPST.group("xn")(e, xn_sp.rearrange("k p t -> p k t"),
                             xn_fm[:].rearrange("p (k t) -> p k t", k=KC))
            for it in range(NIT):
                qd, rem = divmod(it, 4 * NTG)
                h, tg = divmod(rem, NTG)
                head = qd * 4 + h
                g = KSP.group(("spq", it))
                e.wait_ge(VE.h, ve_ev[("fm", it)])
                g(e, q_sp[tg * 4:(tg + 1) * 4, :, head, :].rearrange("n k t -> k n t"),
                  fmq[it % 2][:].rearrange("p (n t) -> p n t", n=4))
                e.wait_ge(SE.h, se_ev[("gev", it)])
                g(e, g_sp[tg * 4:(tg + 1) * 4, :, head, :].rearrange("n g t -> g n t"),
                  g_buf[it % 2][:].rearrange("p (n t) -> p n t", n=4))
            be(e, "sy")

        def pb_gp(e):
            bs(e)
            for q in range(4):
                if q >= 1:
                    e.wait_ge(PE.h, pe_ev[("qproj", (q - 1) * 4 * NTG)])
                WST.group(("wq", q))(
                    e, wqb[q % 2][:].rearrange("p (k m) -> p k m", k=KC),
                    Wq[:, q * 512:(q + 1) * 512].rearrange("(k p) m -> p k m", p=128))
                if q >= 1:
                    e.wait_ge(PE.h, pe_ev[("gproj", (q - 1) * 4 * NTG)])
                WST.group(("wg", q))(
                    e, wgb[q % 2][:].rearrange("p (k m) -> p k m", k=KC),
                    Wg[:, q * 512:(q + 1) * 512].rearrange("(k p) m -> p k m", p=128))
                if q == 1:
                    e.wait_ge(CCS.h, 1)
                    for c in range(NCORES):
                        if c >= 2:
                            e.wait_ge(VE.h, ve_ev[("s0", c - 2)])
                        MST.group(("mg", c))(
                            e, mg_buf[c % 2][:].rearrange("p (h v) -> p h v", h=H),
                            cc_out.ap().rearrange("(c h k) v -> c k h v", c=NCORES, h=H)[c])
            be(e, "gp")

        def pb_te(e):
            bs(e)
            for it in range(NIT):
                qd, rem = divmod(it, 4 * NTG)
                h, tg = divmod(rem, NTG)
                WST.wait(e, ("wq", qd))
                if it >= 2:
                    e.wait_ge(SE.h, se_ev[("qbf", it - 2)])
                last = None
                for k in range(KC):
                    last = e.matmul(psQ[it % 2][:],
                                    wqb[qd % 2][:, k * 512 + h * 128: k * 512 + (h + 1) * 128],
                                    xn_fm[:, k * T + tg * 512: k * T + (tg + 1) * 512],
                                    start=(k == 0), stop=(k == KC - 1))
                PE.inc(last)
                e.wait_ge(SE.h, se_ev[("qbf", it)])
                if it >= 1:
                    e.wait_ge(VE.h, ve_ev[("fm", it - 1)])
                PE.inc(e.matmul(psRq[:], p_bf[:], qbf[it % 2][:], start=True, stop=True))
                WST.wait(e, ("wg", qd))
                if it >= 2:
                    e.wait_ge(SE.h, se_ev[("gev", it - 2)])
                last = None
                for k in range(KC):
                    last = e.matmul(psG5[it % 2][:],
                                    wgb[qd % 2][:, k * 512 + h * 128: k * 512 + (h + 1) * 128],
                                    xn_fm[:, k * T + tg * 512: k * T + (tg + 1) * 512],
                                    start=(k == 0), stop=(k == KC - 1))
                PE.inc(last)
            be(e, "te")

        def pb_se(e):
            bs(e)
            for it in range(NIT):
                qd, rem = divmod(it, 4 * NTG)
                h, tg = divmod(rem, NTG)
                head = qd * 4 + h
                e.wait_ge(PE.h, pe_ev[("qproj", it)])
                if it >= 2:
                    e.wait_ge(PE.h, pe_ev[("qrot", it - 2)])
                    e.wait_ge(VE.h, ve_ev[("fm", it - 2)])
                SE.inc(e.activation(qbf[it % 2][:], psQ[it % 2][:], AF.Identity,
                                    bias=bqr_sb[:, head:head + 1]))
                e.wait_ge(PE.h, pe_ev[("gproj", it)])
                if it >= 2:
                    KSP.wait(e, ("spq", it - 2))
                SE.inc(e.activation(g_buf[it % 2][:], psG5[it % 2][:], AF.Silu,
                                    bias=bgr_sb[:, head:head + 1]))
            be(e, "se")

        def pb_ve(e):
            bs(e)
            e.wait_ge(LD.h, bLD_pb + 32)
            e.memset(s_f32[:], 0.0)
            for it in range(NIT):
                qd, rem = divmod(it, 4 * NTG)
                h, tg = divmod(rem, NTG)
                tsl = slice(tg * 512, (tg + 1) * 512)
                e.wait_ge(PE.h, pe_ev[("qrot", it)])
                if it >= 2:
                    KSP.wait(e, ("spq", it - 2))
                e.tensor_tensor(taq[:], qbf[it % 2][:], c1q[:, tsl], ALU.mult)
                e.tensor_tensor(tbq[:], psRq[:], c2q[:, tsl], ALU.mult)
                e.drain()
                VE.inc(e.tensor_tensor(fmq[it % 2][:], taq[:], tbq[:], ALU.add))
                for c in S0C.get(it, []):
                    MST.wait(e, ("mg", c))
                    e.drain()
                    last = None
                    for h in range(H):
                        hsl = slice(h * 128, (h + 1) * 128)
                        last = e.scalar_tensor_tensor(s_f32[:, hsl], mg_buf[c % 2][:, hsl],
                                                      wm_sb[:, c * H + h:c * H + h + 1],
                                                      s_f32[:, hsl], ALU.mult, ALU.add)
                    VE.inc(last)
                    if c == NCORES - 1:
                        e.drain()
                        VE.inc(e.tensor_copy(s_bfT[:], s_f32[:]))
            be(e, "ve")

        bLD_pb = LD.n
        block.sync(pb_sy); block.gpsimd(pb_gp); block.tensor(pb_te)
        block.scalar(pb_se); block.vector(pb_ve)
        assert PE.n == _pectr[0] and VE.n == _vectr[0] and SE.n == _sectr[0]
        pb.close()
        px.close()
        nphase[0] += 1


        p6 = ExitStack()
        sb6 = lambda n, shp, dt: p6.enter_context(nc.sbuf_tensor(n, shp, dt))
        ps6 = lambda n, shp, dt=F32: p6.enter_context(nc.psum_tensor(n, shp, dt))
        wo_bf = sb6("wo_bf", [128, KC * D], BF16)
        bo_sb = sb6("bo_sb", [128, D], F32)
        d2_sb = sb6("d2_sb", [128, H * 128], F32)
        grp1v = sb6("grp1v", [128, H], F32)
        chq = [sb6(f"chq{i}", [128, H * 128], BF16) for i in range(2)]
        chk = [sb6(f"chk{i}", [128, H * 128], BF16) for i in range(2)]
        chkg = [sb6(f"chkg{i}", [128, H * 128], BF16) for i in range(2)]
        chv = [sb6(f"chv{i}", [128, H * 128], BF16) for i in range(2)]
        chg = [sb6(f"chg{i}", [128, H * 128], BF16) for i in range(2)]
        a2_sb = [sb6(f"a2_{i}", [128, 512], BF16) for i in range(2)]
        ret1 = sb6("ret1", [128, 512], F32)
        ret_all = [sb6(f"ret_all{i}", [128, H * 128], F32) for i in range(2)]
        sq_all = sb6("sq_all", [128, H * 128], F32)
        gn_all = sb6("gn_all", [128, H * 128], BF16)
        prod_fm = sb6("prod_fm", [128, H * 128], BF16)
        s_buf = [sb6(f"s_buf{i}", [128, D], BF16) for i in range(2)]
        st16 = sb6("st16", [128, 16 * 6], F32)
        psS = [ps6(f"psS{i}", [128, 512]) for i in range(2)]
        psRr = ps6("psRr", [128, 512])
        psC = ps6("psC", [128, 512])
        psU6 = [ps6(f"psU6{i}", [128, 512]) for i in range(2)]
        psT6 = ps6("psT6", [128, 512], BF16)
        psW = ps6("psW", [128, 512])
        c_sum = slice(0, 16); c_ssq = slice(16, 32); c_mu = slice(32, 48)
        c_nv = slice(48, 64); c_rs = slice(64, 80); c_nmr = slice(80, 96)

        bLD6, bPE, bVE, bSE, bGP = LD.n, PE.n, VE.n, SE.n, GPC.n
        ve_sbf0_val = bVE
        pe_ev = {}; _pectr = [bPE]
        ve_ev = {}; _vectr = [bVE]
        se_ev = {}; _sectr = [bSE]
        gp_ev = {}; _gpctr = [bGP]
        def _mk(ctr, ev):
            def f(key):
                ctr[0] += 1
                ev[key] = ctr[0]
            return f
        _pe = _mk(_pectr, pe_ev); _ve = _mk(_vectr, ve_ev)
        _se = _mk(_sectr, se_ev); _gp = _mk(_gpctr, gp_ev)
        CORE_ORD = [("sc", 0), ("sc", 1), ("rt", 0), ("cx", 0), ("um", 0),
                    ("sc", 2), ("rt", 1), ("cx", 1), ("um", 1),
                    ("sc", 3), ("rt", 2), ("cx", 2), ("um", 2),
                    ("rt", 3), ("cx", 3), ("um", 3)]
        for n in range(NT):
            for op, hq in CORE_ORD:
                _pe((op, n, hq))
            if n >= 1:
                for tg4 in range(4):
                    _pe(("tp", n - 1, tg4))
                for dmg in range(4):
                    _pe(("wo", n - 1, dmg))
        for tg4 in range(4):
            _pe(("tp", NT - 1, tg4))
        for dmg in range(4):
            _pe(("wo", NT - 1, dmg))
        for n in range(NT):
            for hq in range(4):
                _ve(("a2", n, hq)); _ve(("retf", n, hq)); _ve(("supd", n, hq))
            if n >= 1:
                for tg4 in range(4):
                    _ve(("prod", n - 1, tg4))
            _ve(("red", n))
            if n >= 1:
                for dmg in range(4):
                    _ve(("sev", n - 1, dmg))
        for tg4 in range(4):
            _ve(("prod", NT - 1, tg4))
        for dmg in range(4):
            _ve(("sev", NT - 1, dmg))
        for n in range(NT):
            for hq in range(4):
                _se(("ev", n, hq))
            _se(("rstd", n))
            for h in range(H):
                _se(("gn", n, h))
        for n in range(NT):
            _gp(("sq", n)); _gp(("stats", n)); _gp(("nmr", n))

        def p6_sy(e):
            bs(e)
            LD.inc(e.dma_start(out=bo_sb[:], in_=bo_bc), 16)
            LD.inc(e.dma_start(out=d2_sb[:], in_=d2_t), 16)
            LD.inc(e.dma_start(out=grp1v[:], in_=grp1v_t), 16)
            def ch_load(n):
                g = RST.group(("ch", n))
                for spill, buf in [(q_sp, chq), (k_sp, chk), (kg_sp, chkg),
                                   (v_sp, chv), (g_sp, chg)]:
                    g(e, buf[n % 2][:].rearrange("p (h t) -> p h t", h=H), spill[n])
            ch_load(0)
            if NT > 1:
                ch_load(1)
            for n in range(NT):
                if n + 2 < NT:
                    e.wait_ge(PE.h, pe_ev[("um", n, 3)])
                    e.wait_ge(VE.h, ve_ev[("prod", n, 3)])
                    ch_load(n + 2)
                if n >= 1:
                    e.wait_ge(VE.h, ve_ev[("sev", n - 1, 3)])
                    SPST.group(("s", n - 1))(e, s_sp[n - 1], s_buf[(n - 1) % 2][:])
            e.wait_ge(VE.h, ve_ev[("sev", NT - 1, 3)])
            SPST.group(("s", NT - 1))(e, s_sp[NT - 1], s_buf[(NT - 1) % 2][:])
            be(e, "sy")

        def pk4(e, mk, start=True, stop=True):
            last = None
            for j in range(4):
                o, l, r = mk(j)
                last = e.matmul(o, l, r, start=(start and j == 0), stop=(stop and j == 3))
            return last

        def p6_te_core(e, n):
            def sl4(buf, hq, j):
                return buf[n % 2][:, (hq * 4 + j) * 128:(hq * 4 + j + 1) * 128]
            def mm_sc(hq):
                g = n * 4 + hq
                RST.wait(e, ("ch", n))
                if g >= 2:
                    pn, phq = divmod(g - 2, 4)
                    e.wait_ge(VE.h, ve_ev[("a2", pn, phq)])
                PE.inc(pk4(e, lambda j: (psS[hq % 2][:, j * 128:(j + 1) * 128],
                                         sl4(chk, hq, j), sl4(chq, hq, j))))
            def mm_rt(hq):
                g = n * 4 + hq
                e.wait_ge(VE.h, ve_ev[("a2", n, hq)])
                if g >= 1:
                    pn, phq = divmod(g - 1, 4)
                    e.wait_ge(SE.h, se_ev[("ev", pn, phq)])
                PE.inc(pk4(e, lambda j: (psRr[:, j * 128:(j + 1) * 128],
                                         a2_sb[hq % 2][:, j * 128:(j + 1) * 128],
                                         sl4(chv, hq, j))))
            def mm_cx(hq):
                g = n * 4 + hq
                if n == 0:
                    if hq == 0:
                        e.wait_ge(VE.h, ve_sbf0_val)
                else:
                    e.wait_ge(VE.h, ve_ev[("supd", n - 1, hq)])
                if g >= 1:
                    pn, phq = divmod(g - 1, 4)
                    e.wait_ge(VE.h, ve_ev[("retf", pn, phq)])
                PE.inc(pk4(e, lambda j: (psC[:, j * 128:(j + 1) * 128],
                                         sl4(chq, hq, j),
                                         s_bfT[:, (hq * 4 + j) * 128:(hq * 4 + j + 1) * 128])))
            def mm_um(hq):
                g = n * 4 + hq
                if g >= 2:
                    pn, phq = divmod(g - 2, 4)
                    e.wait_ge(VE.h, ve_ev[("supd", pn, phq)])
                PE.inc(pk4(e, lambda j: (psU6[hq % 2][:, j * 128:(j + 1) * 128],
                                         sl4(chkg, hq, j), sl4(chv, hq, j))))
            mm = {"sc": mm_sc, "rt": mm_rt, "cx": mm_cx, "um": mm_um}
            for op, hq in CORE_ORD:
                mm[op](hq)

        def p6_te_tail(e, n):
            for tg4 in range(4):
                last = None
                for j in range(4):
                    h = tg4 * 4 + j
                    e.wait_ge(SE.h, se_ev[("gn", n, h)])
                    if j == 0:
                        f = n * 4 + tg4
                        if f >= 1:
                            pn, pg = divmod(f - 1, 4)
                            e.wait_ge(VE.h, ve_ev[("prod", pn, pg)])
                    last = e.transpose(psT6[:, j * 128:(j + 1) * 128],
                                       gn_all[:, h * 128:(h + 1) * 128], id_bf[:])
                PE.inc(last)
            e.wait_ge(VE.h, ve_ev[("prod", n, 3)])
            for dmg in range(4):
                if n == 0 and dmg == 0:
                    WST.wait(e, "wo")
                w = n * 4 + dmg
                if w >= 1:
                    pn, pd = divmod(w - 1, 4)
                    e.wait_ge(VE.h, ve_ev[("sev", pn, pd)])
                last = None
                for hv in range(KC):
                    last = e.matmul(psW[:],
                                    prod_fm[:, hv * 128:(hv + 1) * 128],
                                    wo_bf[:, hv * D + dmg * 512: hv * D + (dmg + 1) * 512],
                                    start=(hv == 0), stop=(hv == KC - 1))
                PE.inc(last)

        def p6_te(e):
            bs(e)
            for n in range(NT):
                p6_te_core(e, n)
                if n >= 1:
                    p6_te_tail(e, n - 1)
            p6_te_tail(e, NT - 1)
            be(e, "te")

        def p6_se(e):
            bs(e)
            for n in range(NT):
                for hq in range(4):
                    e.wait_ge(PE.h, pe_ev[("rt", n, hq)])
                    if n * 4 + hq >= 1:
                        g = n * 4 + hq - 1
                        pn, phq = divmod(g, 4)
                        e.wait_ge(VE.h, ve_ev[("retf", pn, phq)])
                    e.drain()
                    SE.inc(e.copy(ret1[:], psRr[:]))
                e.wait_ge(GPC.h, gp_ev[("stats", n)])
                e.drain()
                e.activation(st16[:, c_nv], st16[:, c_nv], AF.Ln, scale=-1.0)
                e.drain()
                SE.inc(e.activation(st16[:, c_rs], st16[:, c_nv], AF.Exp, scale=-0.5))
                e.wait_ge(GPC.h, gp_ev[("nmr", n)])
                e.drain()
                for h in range(H):
                    if n >= 1:
                        e.wait_ge(PE.h, pe_ev[("tp", n - 1, h // 4)])
                    SE.inc(e.activation(gn_all[:, h * 128:(h + 1) * 128],
                                        ret_all[n % 2][:, h * 128:(h + 1) * 128], AF.Identity,
                                        scale=st16[:, 64 + h:64 + h + 1],
                                        bias=st16[:, 80 + h:80 + h + 1]))
            be(e, "se")

        def p6_ve(e):
            bs(e)
            e.wait_ge(LD.h, bLD6 + 48)
            for n in range(NT):
                for hq in range(4):
                    hsl = slice(hq * 512, (hq + 1) * 512)
                    e.wait_ge(PE.h, pe_ev[("sc", n, hq)])
                    if n * 4 + hq >= 2:
                        pn, phq = divmod(n * 4 + hq - 2, 4)
                        e.wait_ge(PE.h, pe_ev[("rt", pn, phq)])
                    if n >= 2 and hq == 0:
                        e.wait_ge(SE.h, se_ev[("gn", n - 2, 15)])
                    VE.inc(e.tensor_tensor(a2_sb[hq % 2][:], psS[hq % 2][:], d2_sb[:, hsl], ALU.mult))
                    e.wait_ge(SE.h, se_ev[("ev", n, hq)])
                    e.wait_ge(PE.h, pe_ev[("cx", n, hq)])
                    last = None
                    for j in range(4):
                        h = hq * 4 + j
                        last = e.scalar_tensor_tensor(
                            ret_all[n % 2][:, h * 128:(h + 1) * 128],
                            psC[:, j * 128:(j + 1) * 128], grp1v[:, h:h + 1],
                            ret1[:, j * 128:(j + 1) * 128], ALU.mult, ALU.add)
                    VE.inc(last)
                    e.wait_ge(PE.h, pe_ev[("um", n, hq)])
                    for j in range(4):
                        h = hq * 4 + j
                        hsl2 = slice(h * 128, (h + 1) * 128)
                        e.scalar_tensor_tensor(s_f32[:, hsl2], s_f32[:, hsl2], GC[h],
                                               psU6[hq % 2][:, j * 128:(j + 1) * 128],
                                               ALU.mult, ALU.add)
                    e.drain()
                    VE.inc(e.tensor_copy(s_bfT[:, hq * 512:(hq + 1) * 512],
                                         s_f32[:, hq * 512:(hq + 1) * 512]))
                if n >= 1:
                    for tg4 in range(4):
                        e.wait_ge(PE.h, pe_ev[("tp", n - 1, tg4)])
                        if n >= 2 and tg4 == 0:
                            e.wait_ge(PE.h, pe_ev[("wo", n - 2, 3)])
                        VE.inc(e.tensor_tensor(prod_fm[:, tg4 * 512:(tg4 + 1) * 512],
                                               psT6[:],
                                               chg[(n - 1) % 2][:, tg4 * 512:(tg4 + 1) * 512],
                                               ALU.mult))
                e.drain()
                e.reduce_sum(st16[:, c_sum].rearrange("p (h o) -> p h o", o=1),
                             ret_all[n % 2][:].rearrange("p (h t) -> p h t", h=H), axis=AX.X)
                e.wait_ge(GPC.h, gp_ev[("sq", n)])
                VE.inc(e.reduce_sum(st16[:, c_ssq].rearrange("p (h o) -> p h o", o=1),
                                    sq_all[:].rearrange("p (h t) -> p h t", h=H), axis=AX.X))
                if n >= 1:
                    for dmg in range(4):
                        dsl = slice(dmg * 512, (dmg + 1) * 512)
                        e.wait_ge(PE.h, pe_ev[("wo", n - 1, dmg)])
                        if n >= 3 and dmg == 0:
                            SPST.wait(e, ("s", n - 3))
                        VE.inc(e.tensor_tensor(s_buf[(n - 1) % 2][:, dsl], psW[:],
                                               bo_sb[:, dsl], ALU.add))
            n = NT - 1
            for tg4 in range(4):
                e.wait_ge(PE.h, pe_ev[("tp", n, tg4)])
                if tg4 == 0 and NT >= 2:
                    e.wait_ge(PE.h, pe_ev[("wo", n - 1, 3)])
                VE.inc(e.tensor_tensor(prod_fm[:, tg4 * 512:(tg4 + 1) * 512], psT6[:],
                                       chg[n % 2][:, tg4 * 512:(tg4 + 1) * 512], ALU.mult))
            for dmg in range(4):
                dsl = slice(dmg * 512, (dmg + 1) * 512)
                e.wait_ge(PE.h, pe_ev[("wo", n, dmg)])
                if dmg == 0 and NT >= 3:
                    SPST.wait(e, ("s", NT - 3))
                VE.inc(e.tensor_tensor(s_buf[n % 2][:, dsl], psW[:],
                                       bo_sb[:, dsl], ALU.add))
            be(e, "ve")

        def p6_gp(e):
            bs(e)
            g = WST.group("wo")
            for k in range(KC):
                g(e, wo_bf[:, k * D:(k + 1) * D],
                  Wo[k * 128:(k + 1) * 128, :])
            for n in range(NT):
                e.wait_ge(VE.h, ve_ev[("retf", n, 3)])
                e.drain()
                GPC.inc(e.tensor_tensor(sq_all[:], ret_all[n % 2][:], ret_all[n % 2][:], ALU.mult))
                e.wait_ge(VE.h, ve_ev[("red", n)])
                e.drain()
                e.tensor_scalar(st16[:, c_mu], st16[:, c_sum], 1.0 / 128, None, ALU.mult)
                e.tensor_scalar(st16[:, c_ssq], st16[:, c_ssq], 1.0 / 128, None, ALU.mult)
                e.drain()
                e.tensor_tensor(st16[:, c_nv], st16[:, c_mu], st16[:, c_mu], ALU.mult)
                e.drain()
                e.tensor_tensor(st16[:, c_nv], st16[:, c_nv], st16[:, c_ssq], ALU.subtract)
                e.drain()
                GPC.inc(e.tensor_scalar(st16[:, c_nv], st16[:, c_nv], EPS, None, ALU.subtract))
                e.wait_ge(SE.h, se_ev[("rstd", n)])
                e.drain()
                e.tensor_scalar(st16[:, c_nmr], st16[:, c_mu], -1.0, None, ALU.mult)
                e.drain()
                GPC.inc(e.tensor_tensor(st16[:, c_nmr], st16[:, c_nmr], st16[:, c_rs], ALU.mult))
            be(e, "gp")

        block.sync(p6_sy); block.gpsimd(p6_gp); block.tensor(p6_te)
        block.scalar(p6_se); block.vector(p6_ve)
        assert PE.n == _pectr[0] and VE.n == _vectr[0]
        assert SE.n == _sectr[0] and GPC.n == _gpctr[0]
        p6.close()
        p56.close()
        nphase[0] += 1


        p7ab = ExitStack()
        sb7s = lambda n, shp, dt: p7ab.enter_context(nc.sbuf_tensor(n, shp, dt, side="right"))
        dwb = [sb7s(f"dwb{i}", [128, IC * 512], BF16) for i in range(2)]
        p7a = ExitStack()
        sb7 = lambda n, shp, dt: p7a.enter_context(nc.sbuf_tensor(n, shp, dt))
        ps7 = lambda n, shp, dt=F32: p7a.enter_context(nc.psum_tensor(n, shp, dt))
        xn_fm2 = sb7("xn_fm2", [128, KC * T], BF16)
        gwb = [sb7(f"gwb{i}", [128, KC * 128], BF16) for i in range(2)]
        uwb = [sb7(f"uwb{i}", [128, KC * 128], BF16) for i in range(2)]
        ga_sb = [sb7(f"ga_sb{i}", [128, 512], F32) for i in range(2)]
        ua_sb = [sb7(f"ua_sb{i}", [128, 512], F32) for i in range(2)]
        hdn_buf = [sb7(f"hdn_buf{i}", [128, 512], BF16) for i in range(2)]
        psGa = [ps7(f"psGa{i}", [128, 512]) for i in range(2)]
        psUa = [ps7(f"psUa{i}", [128, 512]) for i in range(2)]

        bPE, bVE, bSE = PE.n, VE.n, SE.n
        NF = IC * NTG
        pe_gm = lambda f: bPE + f * 2 + 1
        pe_um7 = lambda f: bPE + f * 2 + 2
        se_ga = lambda f: bSE + f * 2 + 1
        se_ua = lambda f: bSE + f * 2 + 2
        ve_hd = lambda f: bVE + f + 1

        def dn_quarter_load(e, q):
            g = DST.group(("dn", q))
            for ic in range(IC):
                g(e, dwb[q % 2][:, ic * 512:(ic + 1) * 512],
                  downT[ic * 128:(ic + 1) * 128, q * 512:(q + 1) * 512])

        def p7a_sy(e):
            bs(e)
            SPST.wait(e, "xn")
            RST.group("xn2")(e, xn_fm2[:].rearrange("p (k t) -> p k t", k=KC),
                             xn_sp.rearrange("k p t -> p k t"))
            for f in range(NF):
                ic, tg = divmod(f, NTG)
                e.wait_ge(VE.h, ve_hd(f))
                SPST.group(("h", f))(
                    e, hdn_sp[tg * 4:(tg + 1) * 4, :, ic, :].rearrange("b i t -> i b t"),
                    hdn_buf[f % 2][:].rearrange("p (b t) -> p b t", b=4))
            be(e, "sy")

        def p7a_gp(e):
            bs(e)
            for ic in range(IC):
                if ic >= 2:
                    e.wait_ge(PE.h, pe_um7((ic - 1) * NTG - 1))
                g = WST.group(("gu", ic))
                g(e, gwb[ic % 2][:].rearrange("p (k m) -> p k m", k=KC),
                  gateT[:, ic * 128:(ic + 1) * 128].rearrange("(k p) i -> p k i", p=128))
                g(e, uwb[ic % 2][:].rearrange("p (k m) -> p k m", k=KC),
                  upT[:, ic * 128:(ic + 1) * 128].rearrange("(k p) i -> p k i", p=128))
                if ic == IC // 2:
                    dn_quarter_load(e, 0)
                if ic == IC - 1:
                    dn_quarter_load(e, 1)
            be(e, "gp")

        def p7a_te(e):
            bs(e)
            RST.wait(e, "xn2")
            for f in range(NF):
                ic, tg = divmod(f, NTG)
                WST.wait(e, ("gu", ic))
                if f >= 2:
                    e.wait_ge(SE.h, se_ga(f - 2))
                last = None
                for k in range(KC):
                    last = e.matmul(psGa[f % 2][:], gwb[ic % 2][:, k * 128:(k + 1) * 128],
                                    xn_fm2[:, k * T + tg * 512: k * T + (tg + 1) * 512],
                                    start=(k == 0), stop=(k == KC - 1))
                PE.inc(last)
                if f >= 2:
                    e.wait_ge(SE.h, se_ua(f - 2))
                last = None
                for k in range(KC):
                    last = e.matmul(psUa[f % 2][:], uwb[ic % 2][:, k * 128:(k + 1) * 128],
                                    xn_fm2[:, k * T + tg * 512: k * T + (tg + 1) * 512],
                                    start=(k == 0), stop=(k == KC - 1))
                PE.inc(last)
            be(e, "te")

        def p7a_se(e):
            bs(e)
            for f in range(NF):
                ic, tg = divmod(f, NTG)
                e.wait_ge(PE.h, pe_gm(f))
                if f >= 2:
                    e.wait_ge(VE.h, ve_hd(f - 2))
                SE.inc(e.activation(ga_sb[f % 2][:], psGa[f % 2][:], AF.Silu,
                                    scale=gss_sb[:, ic:ic + 1]))
                e.wait_ge(PE.h, pe_um7(f))
                SE.inc(e.activation(ua_sb[f % 2][:], psUa[f % 2][:], AF.Identity,
                                    scale=uss_sb[:, ic:ic + 1]))
            be(e, "se")

        def p7a_ve(e):
            bs(e)
            for f in range(NF):
                e.wait_ge(SE.h, se_ua(f))
                if f >= 2:
                    SPST.wait(e, ("h", f - 2))
                VE.inc(e.tensor_tensor(hdn_buf[f % 2][:], ga_sb[f % 2][:], ua_sb[f % 2][:], ALU.mult))
            be(e, "ve")

        block.sync(p7a_sy); block.gpsimd(p7a_gp); block.tensor(p7a_te)
        block.scalar(p7a_se); block.vector(p7a_ve)
        assert PE.n == bPE + NF * 2 and SE.n == bSE + NF * 2 and VE.n == bVE + NF
        p7a.close()
        nphase[0] += 1

        p7b = ExitStack()
        sb8 = lambda n, shp, dt: p7b.enter_context(nc.sbuf_tensor(n, shp, dt))
        ps8 = lambda n, shp, dt=F32: p7b.enter_context(nc.psum_tensor(n, shp, dt))
        dss_sb = sb8("dss_sb", [128, D], F32)
        htb = [sb8(f"htb{i}", [128, IC * 128], BF16) for i in range(2)]
        x_t7 = [sb8(f"x_t7{i}", [128, 512], F32) for i in range(2)]
        s_t7 = [sb8(f"s_t7{i}", [128, 512], BF16) for i in range(2)]
        f1_sb = [sb8(f"f1_sb{i}", [128, 512], F32) for i in range(2)]
        f2_sb = [sb8(f"f2_sb{i}", [128, 512], F32) for i in range(2)]
        ob = [sb8(f"ob{i}", [128, 512], F32) for i in range(2)]
        psD = [ps8(f"psD{i}", [128, 512]) for i in range(2)]

        bLD, bPE, bVE, bSE = LD.n, PE.n, VE.n, SE.n
        NB = 4 * NT
        pe_dm = lambda p: bPE + p + 1
        ve_f1 = lambda p: bVE + p * 2 + 1
        ve_fin = lambda p: bVE + p * 2 + 2
        se_f2 = lambda p: bSE + p + 1

        def p7b_sy(e):
            bs(e)
            LD.inc(e.dma_start(out=dss_sb[:], in_=dss_bc), 16)
            for p in range(NB):
                q, tb = divmod(p, NT)
                if p >= 2:
                    e.wait_ge(PE.h, pe_dm(p - 2))
                    e.wait_ge(VE.h, ve_fin(p - 2))
                g = RST.group(("l", p))
                g(e, htb[p % 2][:].rearrange("p (c t) -> p c t", c=IC), hdn_sp[tb])
                g(e, x_t7[p % 2][:], x_in[tb * 128:(tb + 1) * 128, q * 512:(q + 1) * 512])
                g(e, s_t7[p % 2][:], s_sp[tb][:, q * 512:(q + 1) * 512])
                if p >= 1:
                    e.wait_ge(VE.h, ve_fin(p - 1))
                    pv = p - 1
                    qv, tv = divmod(pv, NT)
                    SPST.group(("o", pv))(
                        e, out[tv * 128:(tv + 1) * 128, qv * 512:(qv + 1) * 512], ob[pv % 2][:])
            e.wait_ge(VE.h, ve_fin(NB - 1))
            SPST.group(("o", NB - 1))(
                e, out[(NT - 1) * 128:NT * 128, 3 * 512:4 * 512], ob[(NB - 1) % 2][:])
            be(e, "sy")

        def p7b_gp(e):
            bs(e)
            for q in range(2, 4):
                e.wait_ge(PE.h, pe_dm((q - 1) * NT - 1))
                dn_quarter_load(e, q)
            be(e, "gp")

        def p7b_te(e):
            bs(e)
            for p in range(NB):
                q, tb = divmod(p, NT)
                DST.wait(e, ("dn", q))
                RST.wait(e, ("l", p))
                if p >= 2:
                    e.wait_ge(VE.h, ve_f1(p - 2))
                last = None
                for ic in range(IC):
                    last = e.matmul(psD[p % 2][:], htb[p % 2][:, ic * 128:(ic + 1) * 128],
                                    dwb[q % 2][:, ic * 512:(ic + 1) * 512],
                                    start=(ic == 0), stop=(ic == IC - 1))
                PE.inc(last)
            be(e, "te")

        def p7b_se(e):
            bs(e)
            for p in range(NB):
                e.wait_ge(VE.h, ve_f1(p))
                if p >= 1:
                    e.wait_ge(VE.h, ve_fin(p - 1))
                SE.inc(e.activation(f2_sb[p % 2][:], f1_sb[p % 2][:], AF.Silu))
            be(e, "se")

        def p7b_ve(e):
            bs(e)
            e.wait_ge(LD.h, bLD + 16)
            for p in range(NB):
                q, tb = divmod(p, NT)
                RST.wait(e, ("l", p))
                dsl = slice(q * 512, (q + 1) * 512)
                e.wait_ge(PE.h, pe_dm(p))
                if p >= 2:
                    e.wait_ge(SE.h, se_f2(p - 2))
                VE.inc(e.tensor_tensor(f1_sb[p % 2][:], psD[p % 2][:],
                                       dss_sb[:, dsl], ALU.mult))
                e.wait_ge(SE.h, se_f2(p))
                if p >= 2:
                    SPST.wait(e, ("o", p - 2))
                e.drain()
                e.tensor_tensor(f2_sb[p % 2][:], f2_sb[p % 2][:], s_t7[p % 2][:], ALU.mult)
                e.drain()
                VE.inc(e.tensor_tensor(ob[p % 2][:], f2_sb[p % 2][:], x_t7[p % 2][:], ALU.add))
            be(e, "ve")

        block.sync(p7b_sy); block.gpsimd(p7b_gp); block.tensor(p7b_te)
        block.scalar(p7b_se); block.vector(p7b_ve)
        assert PE.n == bPE + NB and VE.n == bVE + NB * 2 and SE.n == bSE + NB
        p7b.close()
        p7ab.close()
        nphase[0] += 1


    ctx.close()
    return nc



def _rope_tables(span_start, T):
    pos = np.arange(span_start, span_start + T, dtype=np.float64)
    j = np.arange(64, dtype=np.float64)
    inv_freq = 1.0 / THETA ** (2.0 * j / 128.0)
    freqs = pos[:, None] * inv_freq[None, :]
    cos = np.cos(freqs); sin = np.sin(freqs)
    sbase = (2.0 * j + 0.4 * 128.0) / (1.4 * 128.0)
    power = (pos - S // 2) / SCALE_BASE
    scale = sbase[None, :] ** power[:, None]
    ex = lambda a: np.repeat(a, 2, axis=1)
    cosf, sinf, scf = ex(cos), ex(sin), ex(scale)
    mk = lambda a: np.ascontiguousarray(a.T.astype(np.float32))
    return mk(cosf * scf), mk(sinf * scf), mk(cosf / scf), mk(sinf / scf)


def _const_tables():
    r = np.arange(128)
    d2 = np.zeros((128, H * 128), np.float32)
    grp1 = np.zeros((128, H * 128), np.float32)
    gcm = np.zeros((128, H), np.float32)
    for h, g in enumerate(GAMMAS):
        pm = r[None, :] - r[:, None]
        d2[:, h * 128:(h + 1) * 128] = np.where(pm >= 0, np.float64(g) ** np.maximum(pm, 0), 0.0)
        grp1[:, h * 128:(h + 1) * 128] = (np.float64(g) ** (r + 1.0))[:, None]
        gcm[:, h] = np.float64(g) ** (127.0 - r)
    p_rot = np.zeros((128, 128), np.float32)
    for i in range(64):
        p_rot[2 * i + 1, 2 * i] = -1.0
        p_rot[2 * i, 2 * i + 1] = 1.0
    return d2, grp1, gcm, p_rot, np.eye(128, dtype=np.float32)


def _wm_table(core, gs=SPAN):
    b, sp = core // NSPAN, core % NSPAN
    wm = np.zeros((NCORES * H,), np.float64)
    for cp in range(NCORES):
        bp, spp = cp // NSPAN, cp % NSPAN
        if bp == b and spp < sp:
            for h, g in enumerate(GAMMAS):
                wm[cp * H + h] = np.float64(g) ** (gs * (sp - spp - 1))
    return np.ascontiguousarray(np.broadcast_to(wm.astype(np.float32), (128, NCORES * H)))


def _rs(v):
    return np.ascontiguousarray(np.asarray(v, np.float32).reshape(-1, 128).T)


def _bc(v):
    v = np.asarray(v, np.float32)
    return np.ascontiguousarray(np.broadcast_to(v, (128, v.shape[-1])))


def make_in_maps(inputs, T=SPAN, geom_span=None):
    gs = SPAN if geom_span is None else geom_span
    x = np.ascontiguousarray(np.asarray(inputs["L0_attn_x"], np.float32))
    cf = lambda a: np.ascontiguousarray(np.asarray(a, np.float32))
    shared = {
        "Wq": cf(inputs["L0_attn_Wq"]), "Wk": cf(inputs["L0_attn_Wk"]),
        "Wv": cf(inputs["L0_attn_Wv"]), "Wg": cf(inputs["L0_attn_Wg"]),
        "Wo": cf(inputs["L0_attn_Wo"]),
        "gateT": cf(np.asarray(inputs["L0_ffn_gate_w"], np.float32).T),
        "upT": cf(np.asarray(inputs["L0_ffn_up_w"], np.float32).T),
        "downT": cf(np.asarray(inputs["L0_ffn_down_w"], np.float32).T),
        "bqr": _rs(inputs["L0_attn_bq"]), "bkr": _rs(inputs["L0_attn_bk"]),
        "bgr": _rs(inputs["L0_attn_bg"]),
        "bv_bc": _bc(inputs["L0_attn_bv"]), "bo_bc": _bc(inputs["L0_attn_bo"]),
        "gss": _rs(inputs["L0_ffn_gate_s"]), "uss": _rs(inputs["L0_ffn_up_s"]),
        "dss_bc": _bc(inputs["L0_ffn_down_s"]),
    }
    d2, grp1, gcm, p_rot, ident = _const_tables()
    r_ = np.arange(128)
    grp1v = np.stack([np.float64(g) ** (r_ + 1.0) for g in GAMMAS], axis=1).astype(np.float32)
    shared.update({"d2_t": d2, "grp1_bc": grp1, "grp1v_t": np.ascontiguousarray(grp1v),
                   "gcm_t": gcm, "p_rot": p_rot, "ident": ident})
    rope = [_rope_tables(sp * gs, T) for sp in range(NSPAN)]
    in_maps = []
    for c in range(NCORES):
        b, sp = c // NSPAN, c % NSPAN
        cs, ss_, ics, iss = rope[sp]
        im = dict(shared)
        im["x"] = np.ascontiguousarray(x[b, sp * gs: sp * gs + T])
        im["cs_t"] = cs; im["ss_t"] = ss_; im["ics_t"] = ics; im["iss_t"] = iss
        im["wm_bc"] = _wm_table(c, gs)
        in_maps.append(im)
    return in_maps


def kernel(**inputs):
    nc = build(T=SPAN, debug=False)
    in_maps = make_in_maps(inputs, T=SPAN)
    res = run_bass_kernel_spmd(nc, in_maps, core_ids=list(range(NCORES)))
    out = np.empty((B, S, D), np.float32)
    for c in range(NCORES):
        b, sp = c // NSPAN, c % NSPAN
        out[b, sp * SPAN:(sp + 1) * SPAN] = res.results[c]["out"]
    return out


# version 18
# speedup vs baseline: 1.0267x; 1.0032x over previous
import numpy as np
import ml_dtypes
from contextlib import ExitStack

import concourse.bass as bass
import concourse.mybir as mybir
from concourse.bass_utils import run_bass_kernel_spmd

F32 = mybir.dt.float32
BF16 = mybir.dt.bfloat16
AF = mybir.ActivationFunctionType
ALU = mybir.AluOpType
AX = mybir.AxisListType

B = 2
S = 8192
D = 2048
H = 16
INTER = 5632
NCORES = 8
NSPAN = 4
SPAN = S // NSPAN
EPS = 1e-5
THETA = 10000.0
SCALE_BASE = 512.0
GAMMAS = [1.0 - 2.0 ** (-5.0 - h) for h in range(H)]
GC = [float(np.float32(np.float64(g) ** 128)) for g in GAMMAS]
KC = D // 128
IC = INTER // 128


class Sem:
    def __init__(self, nc, ctx, name):
        self.h = ctx.enter_context(nc.semaphore(name))
        self.n = 0

    def inc(self, instr, k=1):
        instr.then_inc(self.h, k)
        self.n += k
        return self.n


class DStream:

    def __init__(self, nc, ctx, name):
        self.s = [Sem(nc, ctx, name + "a"), Sem(nc, ctx, name + "b")]
        self.rec = {}
        self.gi = 0

    def group(self, key=None):
        p = self.gi % 2
        self.gi += 1
        if key is None:
            key = self.gi - 1
        sem = self.s[p]

        def emit(e, out, in_):
            v = sem.inc(e.dma_start(out=out, in_=in_), 16)
            self.rec[key] = (sem, v)
        return emit

    def wait(self, e, key):
        sem, v = self.rec[key]
        e.wait_ge(sem.h, v)

    def drain_all(self, e):
        for sem in self.s:
            if sem.n:
                e.wait_ge(sem.h, sem.n)


def build(T=SPAN, debug=False):
    NT = T // 128
    NTG = T // 512
    TPH = NT // 2
    nc = bass.Bass("TRN2", target_bir_lowering=False)

    dram = lambda n, shp, dt=F32, kind="ExternalInput": nc.dram_tensor(n, shp, dt, kind=kind).ap()
    sk = "ExternalOutput" if debug else "Internal"

    x_in = dram("x", [T, D])
    Wq = dram("Wq", [D, D]); Wk = dram("Wk", [D, D]); Wv = dram("Wv", [D, D])
    Wg = dram("Wg", [D, D]); Wo = dram("Wo", [D, D])
    gateT = dram("gateT", [D, INTER]); upT = dram("upT", [D, INTER]); downT = dram("downT", [INTER, D])
    bqr = dram("bqr", [128, 16]); bkr = dram("bkr", [128, 16]); bgr = dram("bgr", [128, 16])
    bv_bc = dram("bv_bc", [128, D]); bo_bc = dram("bo_bc", [128, D])
    gss = dram("gss", [128, IC]); uss = dram("uss", [128, IC]); dss_bc = dram("dss_bc", [128, D])
    cs_t = dram("cs_t", [128, T]); ss_t = dram("ss_t", [128, T])
    ics_t = dram("ics_t", [128, T]); iss_t = dram("iss_t", [128, T])
    d2_t = dram("d2_t", [128, H * 128])
    grp1_bc = dram("grp1_bc", [128, H * 128])
    grp1v_t = dram("grp1v_t", [128, H])
    gcm_t = dram("gcm_t", [128, H])
    wm_bc = dram("wm_bc", [128, NCORES * H])
    p_in = dram("p_rot", [128, 128]); id_in = dram("ident", [128, 128])

    out = dram("out", [T, D], F32, kind="ExternalOutput")

    q_sp = dram("q_sp", [NT, 128, H, 128], BF16, kind=sk)
    k_sp = dram("k_sp", [NT, 128, H, 128], BF16, kind=sk)
    kg_sp = dram("kg_sp", [NT, 128, H, 128], BF16, kind=sk)
    v_sp = dram("v_sp", [NT, 128, H, 128], BF16, kind=sk)
    g_sp = dram("g_sp", [NT, 128, H, 128], BF16, kind=sk)
    xn_sp = dram("xn_sp", [KC, 128, T], BF16, kind=sk)
    s_sp = dram("s_sp", [NT, 128, D], BF16, kind=sk)
    hdn_sp = dram("hdn_sp", [NT, 128, IC, 128], BF16, kind=sk)
    cc_in = nc.dram_tensor("cc_in", [H * 128, 128], F32, kind="Internal")
    cc_out = nc.dram_tensor("cc_out", [NCORES * H * 128, 128], F32, kind="Internal")

    ctx = ExitStack()
    sb = lambda n, shp, dt: ctx.enter_context(nc.sbuf_tensor(n, shp, dt))

    LD = Sem(nc, ctx, "LD")
    SPST = DStream(nc, ctx, "SP")
    KSP = DStream(nc, ctx, "KP")
    RST = DStream(nc, ctx, "RS")
    WST = DStream(nc, ctx, "WS")
    DST = DStream(nc, ctx, "DS")
    MST = DStream(nc, ctx, "MS")
    CCS = Sem(nc, ctx, "CCS")
    GPC = Sem(nc, ctx, "GPC")
    PE = Sem(nc, ctx, "PE")
    VE = Sem(nc, ctx, "VE")
    SE = Sem(nc, ctx, "SE")
    ENGS = ("sy", "gp", "te", "se", "ve")
    BAR = {e: Sem(nc, ctx, f"B_{e}") for e in ENGS}
    nphase = [0]

    def bs(e):
        if nphase[0] > 0:
            for s_ in BAR.values():
                e.wait_ge(s_.h, nphase[0])

    def be(e, which):
        if which == "sy":
            if LD.n:
                e.wait_ge(LD.h, LD.n)
            SPST.drain_all(e)
            KSP.drain_all(e)
            RST.drain_all(e)
        elif which == "gp":
            WST.drain_all(e)
            DST.drain_all(e)
            MST.drain_all(e)
            if GPC.n:
                e.wait_ge(GPC.h, GPC.n)
        else:
            st = {"te": PE, "se": SE, "ve": VE}[which]
            if st.n > 0:
                e.wait_ge(st.h, st.n)
        BAR[which].inc(e.nop(), 1)

    bqr_sb = sb("bqr_sb", [128, 16], F32); bkr_sb = sb("bkr_sb", [128, 16], F32)
    bgr_sb = sb("bgr_sb", [128, 16], F32)
    gcm_sb = sb("gcm_sb", [128, H], F32)
    gss_sb = sb("gss_sb", [128, IC], F32); uss_sb = sb("uss_sb", [128, IC], F32)
    wm_sb = sb("wm_sb", [128, NCORES * H], F32)
    p_bf = sb("p_bf", [128, 128], BF16); id_bf = sb("id_bf", [128, 128], BF16)
    m_sb = sb("m_sb", [128, H * 128], F32)
    px = ExitStack()
    xn_fm = px.enter_context(nc.sbuf_tensor("xn_fm", [128, KC * T], BF16))

    with nc.Block() as block:
        def p0_sy(e):
            for dst, src in [(bqr_sb, bqr), (bkr_sb, bkr), (bgr_sb, bgr), (gcm_sb, gcm_t),
                             (gss_sb, gss), (uss_sb, uss), (wm_sb, wm_bc)]:
                LD.inc(e.dma_start(out=dst[:], in_=src), 16)
            be(e, "sy")
        def p0_gp(e):
            g = WST.group("p0")
            g(e, p_bf[:], p_in)
            g(e, id_bf[:], id_in)
            be(e, "gp")
        block.sync(p0_sy); block.gpsimd(p0_gp)
        block.tensor(lambda e: be(e, "te")); block.scalar(lambda e: be(e, "se"))
        block.vector(lambda e: be(e, "ve"))
        nphase[0] += 1

        pa_scope = ExitStack()
        sbA = lambda n, shp, dt: pa_scope.enter_context(nc.sbuf_tensor(n, shp, dt, side="right"))
        wvb = [sbA(f"wvb{i}", [128, KC * 512], BF16) for i in range(2)]
        wkb = [sbA(f"wkb{i}", [128, KC * 512], BF16) for i in range(2)]

        p1 = ExitStack()
        sb1 = lambda n, shp, dt: p1.enter_context(nc.sbuf_tensor(n, shp, dt))
        ps1 = lambda n, shp, dt=F32: p1.enter_context(nc.psum_tensor(n, shp, dt))
        NGRP = 4
        TPH = max(NT // NGRP, 1)
        NGRP = NT // TPH
        x_b = [sb1(f"x_b{i}", [128, D], F32) for i in range(TPH)]
        xn_tok = [sb1(f"xn_tok{i}", [128, D], BF16) for i in range(2)]
        sq_sc = sb1("sq_sc", [128, D], F32)
        st_sum = sb1("st_sum", [128, NT], F32); st_ssq = sb1("st_ssq", [128, NT], F32)
        st_mu = sb1("st_mu", [128, NT], F32); st_rs = sb1("st_rs", [128, NT], F32)
        st_nv = sb1("st_nv", [128, NT], F32)
        ps_tp = [ps1(f"ps_tp{i}", [128, 128], BF16) for i in range(2)]

        bLD, bPE, bVE, bSE = LD.n, PE.n, VE.n, SE.n
        ld_half = lambda hh: bLD + (hh + 1) * TPH * 16
        se_sq = lambda t: bSE + (t // TPH) * (TPH + 1) + (t % TPH) + 1
        se_rstd = lambda hh: bSE + hh * (TPH + 1) + TPH + 1
        ve_negv = lambda hh: bVE + hh * (1 + TPH * 17) + 1
        ve_xn = lambda t: ve_negv(t // TPH) + (t % TPH) * 17 + 1
        ve_evac = lambda t, j: ve_xn(t) + j + 1
        pe_tp1 = lambda t, j: bPE + (t * 16 + j) + 1

        def p1_sy(e):
            bs(e)
            for t in range(NT):
                if t >= TPH:
                    e.wait_ge(VE.h, ve_xn(t - TPH))
                    e.wait_ge(SE.h, se_sq(t - TPH))
                LD.inc(e.dma_start(out=x_b[t % TPH][:], in_=x_in[t * 128:(t + 1) * 128, :]), 16)
            be(e, "sy")

        def p1_gp(e):
            bs(e)
            WST.group(("wv", 0))(
                e, wvb[0][:].rearrange("p (k m) -> p k m", k=KC),
                Wv[:, 0:512].rearrange("(k p) m -> p k m", p=128))
            WST.group(("wk", 0))(
                e, wkb[0][:].rearrange("p (k m) -> p k m", k=KC),
                Wk[:, 0:512].rearrange("(k p) m -> p k m", p=128))
            be(e, "gp")

        def p1_te(e):
            bs(e)
            for t in range(NT):
                e.wait_ge(VE.h, ve_xn(t))
                for j in range(16):
                    f = t * 16 + j
                    if f >= 2:
                        pt, pj = divmod(f - 2, 16)
                        e.wait_ge(VE.h, ve_evac(pt, pj))
                    PE.inc(e.transpose(ps_tp[f % 2][:], xn_tok[t % 2][:, j * 128:(j + 1) * 128],
                                       id_bf[:]))
            be(e, "te")

        def p1_se(e):
            bs(e)
            for hh in range(NGRP):
                e.wait_ge(LD.h, ld_half(hh))
                for i in range(TPH):
                    t = hh * TPH + i
                    e.drain()
                    SE.inc(e.activation(sq_sc[:], x_b[i][:], AF.Square,
                                        accum_out=st_ssq[:, t:t + 1]))
                sl = slice(hh * TPH, hh * TPH + TPH)
                e.wait_ge(VE.h, ve_negv(hh))
                e.activation(st_nv[:, sl], st_nv[:, sl], AF.Ln, scale=-1.0)
                e.drain()
                SE.inc(e.activation(st_rs[:, sl], st_nv[:, sl], AF.Exp, scale=-0.5))
            be(e, "se")

        def p1_ve(e):
            bs(e)
            for hh in range(NGRP):
                sl = slice(hh * TPH, hh * TPH + TPH)
                e.wait_ge(LD.h, ld_half(hh))
                for i in range(TPH):
                    t = hh * TPH + i
                    e.reduce_sum(st_sum[:, t:t + 1], x_b[i][:], axis=AX.X)
                e.wait_ge(SE.h, se_sq(hh * TPH + TPH - 1))
                e.drain()
                e.tensor_scalar(st_mu[:, sl], st_sum[:, sl], 1.0 / D, None, ALU.mult)
                e.tensor_scalar(st_ssq[:, sl], st_ssq[:, sl], 1.0 / D, None, ALU.mult)
                e.drain()
                e.tensor_tensor(st_nv[:, sl], st_mu[:, sl], st_mu[:, sl], ALU.mult)
                e.drain()
                e.tensor_tensor(st_nv[:, sl], st_nv[:, sl], st_ssq[:, sl], ALU.subtract)
                e.drain()
                VE.inc(e.tensor_scalar(st_nv[:, sl], st_nv[:, sl], EPS, None, ALU.subtract))
                for i in range(TPH):
                    t = hh * TPH + i
                    e.wait_ge(SE.h, se_rstd(hh))
                    e.drain()
                    VE.inc(e.tensor_scalar(xn_tok[t % 2][:], x_b[i][:], st_mu[:, t:t + 1],
                                           st_rs[:, t:t + 1], ALU.subtract, ALU.mult))
                    for j in range(16):
                        e.wait_ge(PE.h, pe_tp1(t, j))
                        VE.inc(e.tensor_copy(xn_fm[:, j * T + t * 128: j * T + (t + 1) * 128],
                                             ps_tp[(t * 16 + j) % 2][:]))
            be(e, "ve")

        block.sync(p1_sy); block.gpsimd(p1_gp); block.tensor(p1_te)
        block.scalar(p1_se); block.vector(p1_ve)
        assert SE.n == bSE + NGRP * (TPH + 1) and VE.n == bVE + NGRP * (1 + TPH * 17)
        assert PE.n == bPE + NT * 16 and LD.n == bLD + NT * 16
        p1.close()
        nphase[0] += 1

        pa = ExitStack()
        sbp = lambda n, shp, dt: pa.enter_context(nc.sbuf_tensor(n, shp, dt))
        psp = lambda n, shp, dt=F32: pa.enter_context(nc.psum_tensor(n, shp, dt))
        c1_sb = sbp("c1_sb", [128, T], F32); c2_sb = sbp("c2_sb", [128, T], F32)
        bv_sb = sbp("bv_sb", [128, D], F32)
        vqb = [sbp(f"vqb{i}", [128, NT * 512], BF16) for i in range(2)]
        kbf = [sbp(f"kbf{i}", [128, 512], BF16) for i in range(2)]
        ta = sbp("ta", [128, 512], F32); tb_ = sbp("tb", [128, 512], F32)
        fmb = [sbp(f"fmb{i}", [128, 512], BF16) for i in range(2)]
        kgb = [sbp(f"kgb{i}", [128, 512], BF16) for i in range(2)]
        psV = [psp(f"psV{i}", [128, 512]) for i in range(2)]
        psK = [psp(f"psK{i}", [128, 512]) for i in range(2)]
        psR = psp("psR", [128, 512])
        psT = psp("psT", [128, 512], BF16)
        psU = psp("psU", [128, 512])

        NIT = 16 * NTG
        bPE, bVE, bSE = PE.n, VE.n, SE.n
        pe_ev = {}; _pectr = [bPE]
        ve_ev = {}; _vectr = [bVE]
        se_ev = {}; _sectr = [bSE]
        def _mk(ctr, ev):
            def f(key):
                ctr[0] += 1
                ev[key] = ctr[0]
            return f
        _pe = _mk(_pectr, pe_ev); _ve = _mk(_vectr, ve_ev); _se = _mk(_sectr, se_ev)
        SHIFT = 16 * NTG // 4
        SHIFT = NT
        for i in range(NIT + SHIFT):
            if i < NIT:
                _pe(("v", i))
            if i >= SHIFT:
                it = i - SHIFT
                _pe(("kproj", it)); _pe(("krot", it)); _pe(("ktp", it)); _pe(("kum", it))
        for i in range(NIT + SHIFT):
            if i < NIT:
                _ve(("vev", i))
            if i >= SHIFT:
                it = i - SHIFT
                _ve(("fm", it)); _ve(("kg", it)); _ve(("mu", it))
        for it in range(NIT):
            _se(("kbf", it))

        def pa_sy(e):
            bs(e)
            LD.inc(e.dma_start(out=c1_sb[:], in_=ics_t), 16)
            LD.inc(e.dma_start(out=c2_sb[:], in_=iss_t), 16)
            LD.inc(e.dma_start(out=bv_sb[:], in_=bv_bc), 16)
            for i in range(NIT + SHIFT):
                if i < NIT:
                    vg, t = divmod(i, NT)
                    e.wait_ge(VE.h, ve_ev[("vev", i)])
                    SPST.group(("v", i))(
                        e, v_sp[t, :, 4 * vg:4 * vg + 4, :],
                        vqb[vg % 2][:, t * 512:(t + 1) * 512])
                if i >= SHIFT:
                    it = i - SHIFT
                    qd, rem = divmod(it, 4 * NTG)
                    h, tg = divmod(rem, NTG)
                    head = qd * 4 + h
                    g = KSP.group(("sp", it))
                    e.wait_ge(VE.h, ve_ev[("fm", it)])
                    g(e, k_sp[tg * 4:(tg + 1) * 4, :, head, :].rearrange("n k t -> k n t"),
                      fmb[it % 2][:].rearrange("p (n t) -> p n t", n=4))
                    e.wait_ge(VE.h, ve_ev[("kg", it)])
                    g(e, kg_sp[tg * 4:(tg + 1) * 4, :, head, :].rearrange("n t k -> t n k"),
                      kgb[it % 2][:].rearrange("p (n k) -> p n k", n=4))
            e.wait_ge(VE.h, ve_ev[("mu", NIT - 1)])
            LD.inc(e.dma_start(out=cc_in.ap().rearrange("(h k) v -> k h v", h=H),
                               in_=m_sb[:].rearrange("p (h v) -> p h v", h=H)), 16)
            be(e, "sy")

        ld_m_total = [None]

        def pa_gp(e):
            bs(e)
            for q in range(1, 4):
                e.wait_ge(PE.h, pe_ev[("v", (q - 1) * NT)])
                WST.group(("wv", q))(
                    e, wvb[q % 2][:].rearrange("p (k m) -> p k m", k=KC),
                    Wv[:, q * 512:(q + 1) * 512].rearrange("(k p) m -> p k m", p=128))
                e.wait_ge(PE.h, pe_ev[("kproj", max((q - 1) * 4 * NTG, 0))])
                WST.group(("wk", q))(
                    e, wkb[q % 2][:].rearrange("p (k m) -> p k m", k=KC),
                    Wk[:, q * 512:(q + 1) * 512].rearrange("(k p) m -> p k m", p=128))
            e.wait_ge(LD.h, ld_m_total[0])
            CCS.inc(e.collective_compute(
                "AllGather", ALU.bypass, replica_groups=[list(range(NCORES))],
                ins=[cc_in.ap().opt()], outs=[cc_out.ap().opt()]), 1)
            be(e, "gp")

        def pa_te(e):
            bs(e)
            for i in range(NIT + SHIFT):
                if i < NIT:
                    vg, t = divmod(i, NT)
                    WST.wait(e, ("wv", vg))
                    if i >= 2:
                        e.wait_ge(VE.h, ve_ev[("vev", i - 2)])
                    last = None
                    for k in range(KC):
                        last = e.matmul(psV[i % 2][:],
                                        xn_fm[:, k * T + t * 128: k * T + (t + 1) * 128],
                                        wvb[vg % 2][:, k * 512:(k + 1) * 512],
                                        start=(k == 0), stop=(k == KC - 1))
                    PE.inc(last)
                if i >= SHIFT:
                    it = i - SHIFT
                    qd, rem = divmod(it, 4 * NTG)
                    h, tg = divmod(rem, NTG)
                    WST.wait(e, ("wk", qd))
                    if it >= 2:
                        e.wait_ge(SE.h, se_ev[("kbf", it - 2)])
                    last = None
                    for k in range(KC):
                        last = e.matmul(psK[it % 2][:],
                                        wkb[qd % 2][:, k * 512 + h * 128: k * 512 + (h + 1) * 128],
                                        xn_fm[:, k * T + tg * 512: k * T + (tg + 1) * 512],
                                        start=(k == 0), stop=(k == KC - 1))
                    PE.inc(last)
                    e.wait_ge(SE.h, se_ev[("kbf", it)])
                    if it >= 1:
                        e.wait_ge(VE.h, ve_ev[("fm", it - 1)])
                    PE.inc(e.matmul(psR[:], p_bf[:], kbf[it % 2][:], start=True, stop=True))
                    e.wait_ge(VE.h, ve_ev[("fm", it)])
                    if it >= 1:
                        e.wait_ge(VE.h, ve_ev[("kg", it - 1)])
                    last = None
                    for j in range(4):
                        last = e.transpose(psT[:, j * 128:(j + 1) * 128],
                                           fmb[it % 2][:, j * 128:(j + 1) * 128], id_bf[:])
                    PE.inc(last)
                    e.wait_ge(VE.h, ve_ev[("kg", it)])
                    e.wait_ge(VE.h, ve_ev[("vev", (qd + 1) * NT - 1)])
                    if it >= 1:
                        e.wait_ge(VE.h, ve_ev[("mu", it - 1)])
                    last = None
                    for j in range(4):
                        n_ = tg * 4 + j
                        last = e.matmul(psU[:, j * 128:(j + 1) * 128],
                                        kgb[it % 2][:, j * 128:(j + 1) * 128],
                                        vqb[qd % 2][:, n_ * 512 + h * 128: n_ * 512 + (h + 1) * 128],
                                        start=(j == 0), stop=(j == 3))
                    PE.inc(last)
            be(e, "te")

        def pa_se(e):
            bs(e)
            for it in range(NIT):
                qd, rem = divmod(it, 4 * NTG)
                h, tg = divmod(rem, NTG)
                head = qd * 4 + h
                e.wait_ge(PE.h, pe_ev[("kproj", it)])
                if it >= 2:
                    e.wait_ge(PE.h, pe_ev[("krot", it - 2)])
                    e.wait_ge(VE.h, ve_ev[("fm", it - 2)])
                SE.inc(e.activation(kbf[it % 2][:], psK[it % 2][:], AF.Identity,
                                    bias=bkr_sb[:, head:head + 1]))
            be(e, "se")

        def pa_ve(e):
            bs(e)
            e.memset(m_sb[:], 0.0)
            e.wait_ge(LD.h, bLD_pa + 48)
            for i in range(NIT + SHIFT):
                if i < NIT:
                    vg, t = divmod(i, NT)
                    e.wait_ge(PE.h, pe_ev[("v", i)])
                    if i >= 2:
                        SPST.wait(e, ("v", i - 2))
                    if i >= 2 * NT and t == 0:
                        e.wait_ge(PE.h, pe_ev[("kum", (vg - 1) * 4 * NTG - 1)])
                    VE.inc(e.tensor_tensor(vqb[vg % 2][:, t * 512:(t + 1) * 512], psV[i % 2][:],
                                           bv_sb[:, vg * 512:(vg + 1) * 512], ALU.add))
                if i >= SHIFT:
                    it = i - SHIFT
                    qd, rem = divmod(it, 4 * NTG)
                    h, tg = divmod(rem, NTG)
                    head = qd * 4 + h
                    tsl = slice(tg * 512, (tg + 1) * 512)
                    e.wait_ge(PE.h, pe_ev[("krot", it)])
                    if it >= 2:
                        KSP.wait(e, ("sp", it - 2))
                        e.wait_ge(PE.h, pe_ev[("ktp", it - 2)])
                    e.tensor_tensor(ta[:], kbf[it % 2][:], c1_sb[:, tsl], ALU.mult)
                    e.tensor_tensor(tb_[:], psR[:], c2_sb[:, tsl], ALU.mult)
                    e.drain()
                    VE.inc(e.tensor_tensor(fmb[it % 2][:], ta[:], tb_[:], ALU.add))
                    e.wait_ge(PE.h, pe_ev[("ktp", it)])
                    if it >= 2:
                        e.wait_ge(PE.h, pe_ev[("kum", it - 2)])
                    VE.inc(e.tensor_scalar(kgb[it % 2][:], psT[:], gcm_sb[:, head:head + 1],
                                           None, ALU.mult))
                    e.wait_ge(PE.h, pe_ev[("kum", it)])
                    last = None
                    for j in range(4):
                        if j > 0:
                            e.drain()
                        hsl = slice(head * 128, (head + 1) * 128)
                        last = e.scalar_tensor_tensor(m_sb[:, hsl], m_sb[:, hsl], GC[head],
                                                      psU[:, j * 128:(j + 1) * 128],
                                                      ALU.mult, ALU.add)
                    VE.inc(last)
            be(e, "ve")

        bLD_pa = LD.n
        block.sync(pa_sy)
        ld_m_total[0] = LD.n
        block.gpsimd(pa_gp); block.tensor(pa_te)
        block.scalar(pa_se); block.vector(pa_ve)
        assert PE.n == _pectr[0] and VE.n == _vectr[0] and SE.n == _sectr[0]
        pa.close()
        pa_scope.close()
        nphase[0] += 1

        p56 = ExitStack()
        sb56 = lambda n, shp, dt: p56.enter_context(nc.sbuf_tensor(n, shp, dt, side="right"))
        s_f32 = sb56("s_f32", [128, H * 128], F32)
        s_bfT = sb56("s_bfT", [128, H * 128], BF16)

        pb = ExitStack()
        sbq = lambda n, shp, dt: pb.enter_context(nc.sbuf_tensor(n, shp, dt))
        psq = lambda n, shp, dt=F32: pb.enter_context(nc.psum_tensor(n, shp, dt))
        wqb = [sbq(f"wqb{i}", [128, KC * 512], BF16) for i in range(2)]
        wgb = [sbq(f"wgb{i}", [128, KC * 512], BF16) for i in range(2)]
        mg_buf = [sbq(f"mg{i}", [128, H * 128], F32) for i in range(2)]
        c1q = sbq("c1q", [128, T], F32); c2q = sbq("c2q", [128, T], F32)
        qbf = [sbq(f"qbf{i}", [128, 512], BF16) for i in range(2)]
        taq = sbq("taq", [128, 512], F32); tbq = sbq("tbq", [128, 512], F32)
        fmq = [sbq(f"fmq{i}", [128, 512], BF16) for i in range(2)]
        g_buf = [sbq(f"g_buf{i}", [128, 512], BF16) for i in range(2)]
        psQ = [psq(f"psQ{i}", [128, 512]) for i in range(2)]
        psRq = psq("psRq", [128, 512])
        psG5 = [psq(f"psG5{i}", [128, 512]) for i in range(2)]

        bPE, bVE, bSE = PE.n, VE.n, SE.n
        pe_ev = {}; _pectr = [bPE]
        ve_ev = {}; _vectr = [bVE]
        se_ev = {}; _sectr = [bSE]
        _pe = _mk(_pectr, pe_ev); _ve = _mk(_vectr, ve_ev); _se = _mk(_sectr, se_ev)
        for it in range(NIT):
            _pe(("qproj", it)); _pe(("qrot", it)); _pe(("gproj", it))
        S0C = {}
        for c in range(NCORES):
            S0C.setdefault(4 * NTG + (c * 4 * NTG) // NCORES, []).append(c)
        for it in range(NIT):
            _ve(("fm", it))
            for c in S0C.get(it, []):
                _ve(("s0", c))
                if c == NCORES - 1:
                    _ve(("sbf0",))
        for it in range(NIT):
            _se(("qbf", it)); _se(("gev", it))

        def pb_sy(e):
            bs(e)
            LD.inc(e.dma_start(out=c1q[:], in_=cs_t), 16)
            LD.inc(e.dma_start(out=c2q[:], in_=ss_t), 16)
            SPST.group("xn")(e, xn_sp.rearrange("k p t -> p k t"),
                             xn_fm[:].rearrange("p (k t) -> p k t", k=KC))
            for it in range(NIT):
                qd, rem = divmod(it, 4 * NTG)
                h, tg = divmod(rem, NTG)
                head = qd * 4 + h
                g = KSP.group(("spq", it))
                e.wait_ge(VE.h, ve_ev[("fm", it)])
                g(e, q_sp[tg * 4:(tg + 1) * 4, :, head, :].rearrange("n k t -> k n t"),
                  fmq[it % 2][:].rearrange("p (n t) -> p n t", n=4))
                e.wait_ge(SE.h, se_ev[("gev", it)])
                g(e, g_sp[tg * 4:(tg + 1) * 4, :, head, :].rearrange("n g t -> g n t"),
                  g_buf[it % 2][:].rearrange("p (n t) -> p n t", n=4))
            be(e, "sy")

        def pb_gp(e):
            bs(e)
            for q in range(4):
                if q >= 1:
                    e.wait_ge(PE.h, pe_ev[("qproj", (q - 1) * 4 * NTG)])
                WST.group(("wq", q))(
                    e, wqb[q % 2][:].rearrange("p (k m) -> p k m", k=KC),
                    Wq[:, q * 512:(q + 1) * 512].rearrange("(k p) m -> p k m", p=128))
                if q >= 1:
                    e.wait_ge(PE.h, pe_ev[("gproj", (q - 1) * 4 * NTG)])
                WST.group(("wg", q))(
                    e, wgb[q % 2][:].rearrange("p (k m) -> p k m", k=KC),
                    Wg[:, q * 512:(q + 1) * 512].rearrange("(k p) m -> p k m", p=128))
                if q == 1:
                    e.wait_ge(CCS.h, 1)
                    for c in range(NCORES):
                        if c >= 2:
                            e.wait_ge(VE.h, ve_ev[("s0", c - 2)])
                        MST.group(("mg", c))(
                            e, mg_buf[c % 2][:].rearrange("p (h v) -> p h v", h=H),
                            cc_out.ap().rearrange("(c h k) v -> c k h v", c=NCORES, h=H)[c])
            be(e, "gp")

        def pb_te(e):
            bs(e)
            for it in range(NIT):
                qd, rem = divmod(it, 4 * NTG)
                h, tg = divmod(rem, NTG)
                WST.wait(e, ("wq", qd))
                if it >= 2:
                    e.wait_ge(SE.h, se_ev[("qbf", it - 2)])
                last = None
                for k in range(KC):
                    last = e.matmul(psQ[it % 2][:],
                                    wqb[qd % 2][:, k * 512 + h * 128: k * 512 + (h + 1) * 128],
                                    xn_fm[:, k * T + tg * 512: k * T + (tg + 1) * 512],
                                    start=(k == 0), stop=(k == KC - 1))
                PE.inc(last)
                e.wait_ge(SE.h, se_ev[("qbf", it)])
                if it >= 1:
                    e.wait_ge(VE.h, ve_ev[("fm", it - 1)])
                PE.inc(e.matmul(psRq[:], p_bf[:], qbf[it % 2][:], start=True, stop=True))
                WST.wait(e, ("wg", qd))
                if it >= 2:
                    e.wait_ge(SE.h, se_ev[("gev", it - 2)])
                last = None
                for k in range(KC):
                    last = e.matmul(psG5[it % 2][:],
                                    wgb[qd % 2][:, k * 512 + h * 128: k * 512 + (h + 1) * 128],
                                    xn_fm[:, k * T + tg * 512: k * T + (tg + 1) * 512],
                                    start=(k == 0), stop=(k == KC - 1))
                PE.inc(last)
            be(e, "te")

        def pb_se(e):
            bs(e)
            for it in range(NIT):
                qd, rem = divmod(it, 4 * NTG)
                h, tg = divmod(rem, NTG)
                head = qd * 4 + h
                e.wait_ge(PE.h, pe_ev[("qproj", it)])
                if it >= 2:
                    e.wait_ge(PE.h, pe_ev[("qrot", it - 2)])
                    e.wait_ge(VE.h, ve_ev[("fm", it - 2)])
                SE.inc(e.activation(qbf[it % 2][:], psQ[it % 2][:], AF.Identity,
                                    bias=bqr_sb[:, head:head + 1]))
                e.wait_ge(PE.h, pe_ev[("gproj", it)])
                if it >= 2:
                    KSP.wait(e, ("spq", it - 2))
                SE.inc(e.activation(g_buf[it % 2][:], psG5[it % 2][:], AF.Silu,
                                    bias=bgr_sb[:, head:head + 1]))
            be(e, "se")

        def pb_ve(e):
            bs(e)
            e.wait_ge(LD.h, bLD_pb + 32)
            e.memset(s_f32[:], 0.0)
            for it in range(NIT):
                qd, rem = divmod(it, 4 * NTG)
                h, tg = divmod(rem, NTG)
                tsl = slice(tg * 512, (tg + 1) * 512)
                e.wait_ge(PE.h, pe_ev[("qrot", it)])
                if it >= 2:
                    KSP.wait(e, ("spq", it - 2))
                e.tensor_tensor(taq[:], qbf[it % 2][:], c1q[:, tsl], ALU.mult)
                e.tensor_tensor(tbq[:], psRq[:], c2q[:, tsl], ALU.mult)
                e.drain()
                VE.inc(e.tensor_tensor(fmq[it % 2][:], taq[:], tbq[:], ALU.add))
                for c in S0C.get(it, []):
                    MST.wait(e, ("mg", c))
                    e.drain()
                    last = None
                    for h in range(H):
                        hsl = slice(h * 128, (h + 1) * 128)
                        last = e.scalar_tensor_tensor(s_f32[:, hsl], mg_buf[c % 2][:, hsl],
                                                      wm_sb[:, c * H + h:c * H + h + 1],
                                                      s_f32[:, hsl], ALU.mult, ALU.add)
                    VE.inc(last)
                    if c == NCORES - 1:
                        e.drain()
                        VE.inc(e.tensor_copy(s_bfT[:], s_f32[:]))
            be(e, "ve")

        bLD_pb = LD.n
        block.sync(pb_sy); block.gpsimd(pb_gp); block.tensor(pb_te)
        block.scalar(pb_se); block.vector(pb_ve)
        assert PE.n == _pectr[0] and VE.n == _vectr[0] and SE.n == _sectr[0]
        pb.close()
        px.close()
        nphase[0] += 1


        p6 = ExitStack()
        sb6 = lambda n, shp, dt: p6.enter_context(nc.sbuf_tensor(n, shp, dt))
        ps6 = lambda n, shp, dt=F32: p6.enter_context(nc.psum_tensor(n, shp, dt))
        wo_bf = sb6("wo_bf", [128, KC * D], BF16)
        bo_sb = sb6("bo_sb", [128, D], F32)
        d2_sb = sb6("d2_sb", [128, H * 128], F32)
        grp1v = sb6("grp1v", [128, H], F32)
        chq = [sb6(f"chq{i}", [128, H * 128], BF16) for i in range(2)]
        chk = [sb6(f"chk{i}", [128, H * 128], BF16) for i in range(2)]
        chkg = [sb6(f"chkg{i}", [128, H * 128], BF16) for i in range(2)]
        chv = [sb6(f"chv{i}", [128, H * 128], BF16) for i in range(2)]
        chg = [sb6(f"chg{i}", [128, H * 128], BF16) for i in range(2)]
        a2_sb = [sb6(f"a2_{i}", [128, 512], BF16) for i in range(2)]
        ret1 = sb6("ret1", [128, 512], F32)
        ret_all = [sb6(f"ret_all{i}", [128, H * 128], F32) for i in range(2)]
        sq_all = sb6("sq_all", [128, H * 128], F32)
        gn_all = sb6("gn_all", [128, H * 128], BF16)
        prod_fm = sb6("prod_fm", [128, H * 128], BF16)
        s_buf = [sb6(f"s_buf{i}", [128, D], BF16) for i in range(2)]
        st16 = sb6("st16", [128, 16 * 6], F32)
        psS = [ps6(f"psS{i}", [128, 512]) for i in range(2)]
        psRr = ps6("psRr", [128, 512])
        psC = ps6("psC", [128, 512])
        psU6 = [ps6(f"psU6{i}", [128, 512]) for i in range(2)]
        psT6 = ps6("psT6", [128, 512], BF16)
        psW = ps6("psW", [128, 512])
        c_sum = slice(0, 16); c_ssq = slice(16, 32); c_mu = slice(32, 48)
        c_nv = slice(48, 64); c_rs = slice(64, 80); c_nmr = slice(80, 96)

        bLD6, bPE, bVE, bSE, bGP = LD.n, PE.n, VE.n, SE.n, GPC.n
        ve_sbf0_val = bVE
        pe_ev = {}; _pectr = [bPE]
        ve_ev = {}; _vectr = [bVE]
        se_ev = {}; _sectr = [bSE]
        gp_ev = {}; _gpctr = [bGP]
        def _mk(ctr, ev):
            def f(key):
                ctr[0] += 1
                ev[key] = ctr[0]
            return f
        _pe = _mk(_pectr, pe_ev); _ve = _mk(_vectr, ve_ev)
        _se = _mk(_sectr, se_ev); _gp = _mk(_gpctr, gp_ev)
        CORE_ORD = [("sc", 0), ("sc", 1), ("rt", 0), ("cx", 0), ("um", 0),
                    ("sc", 2), ("rt", 1), ("cx", 1), ("um", 1),
                    ("sc", 3), ("rt", 2), ("cx", 2), ("um", 2),
                    ("rt", 3), ("cx", 3), ("um", 3)]
        for n in range(NT):
            for op, hq in CORE_ORD:
                _pe((op, n, hq))
            if n >= 1:
                for tg4 in range(4):
                    _pe(("tp", n - 1, tg4))
                for dmg in range(4):
                    _pe(("wo", n - 1, dmg))
        for tg4 in range(4):
            _pe(("tp", NT - 1, tg4))
        for dmg in range(4):
            _pe(("wo", NT - 1, dmg))
        for n in range(NT):
            for hq in range(4):
                _ve(("a2", n, hq)); _ve(("retf", n, hq)); _ve(("supd", n, hq))
            if n >= 1:
                for tg4 in range(4):
                    _ve(("prod", n - 1, tg4))
            _ve(("red", n))
            if n >= 1:
                for dmg in range(4):
                    _ve(("sev", n - 1, dmg))
        for tg4 in range(4):
            _ve(("prod", NT - 1, tg4))
        for dmg in range(4):
            _ve(("sev", NT - 1, dmg))
        for n in range(NT):
            for hq in range(4):
                _se(("ev", n, hq)); _se(("sbf", n, hq))
            _se(("rstd", n))
            for h in range(H):
                _se(("gn", n, h))
        for n in range(NT):
            _gp(("sq", n)); _gp(("stats", n)); _gp(("nmr", n))

        def p6_sy(e):
            bs(e)
            LD.inc(e.dma_start(out=bo_sb[:], in_=bo_bc), 16)
            LD.inc(e.dma_start(out=d2_sb[:], in_=d2_t), 16)
            LD.inc(e.dma_start(out=grp1v[:], in_=grp1v_t), 16)
            def ch_load(n):
                g = RST.group(("ch", n))
                for spill, buf in [(q_sp, chq), (k_sp, chk), (kg_sp, chkg),
                                   (v_sp, chv), (g_sp, chg)]:
                    g(e, buf[n % 2][:].rearrange("p (h t) -> p h t", h=H), spill[n])
            ch_load(0)
            if NT > 1:
                ch_load(1)
            for n in range(NT):
                if n + 2 < NT:
                    e.wait_ge(PE.h, pe_ev[("um", n, 3)])
                    e.wait_ge(VE.h, ve_ev[("prod", n, 3)])
                    ch_load(n + 2)
                if n >= 1:
                    e.wait_ge(VE.h, ve_ev[("sev", n - 1, 3)])
                    SPST.group(("s", n - 1))(e, s_sp[n - 1], s_buf[(n - 1) % 2][:])
            e.wait_ge(VE.h, ve_ev[("sev", NT - 1, 3)])
            SPST.group(("s", NT - 1))(e, s_sp[NT - 1], s_buf[(NT - 1) % 2][:])
            be(e, "sy")

        def pk4(e, mk, start=True, stop=True):
            last = None
            for j in range(4):
                o, l, r = mk(j)
                last = e.matmul(o, l, r, start=(start and j == 0), stop=(stop and j == 3))
            return last

        def p6_te_core(e, n):
            def sl4(buf, hq, j):
                return buf[n % 2][:, (hq * 4 + j) * 128:(hq * 4 + j + 1) * 128]
            def mm_sc(hq):
                g = n * 4 + hq
                RST.wait(e, ("ch", n))
                if g >= 2:
                    pn, phq = divmod(g - 2, 4)
                    e.wait_ge(VE.h, ve_ev[("a2", pn, phq)])
                PE.inc(pk4(e, lambda j: (psS[hq % 2][:, j * 128:(j + 1) * 128],
                                         sl4(chk, hq, j), sl4(chq, hq, j))))
            def mm_rt(hq):
                g = n * 4 + hq
                e.wait_ge(VE.h, ve_ev[("a2", n, hq)])
                if g >= 1:
                    pn, phq = divmod(g - 1, 4)
                    e.wait_ge(SE.h, se_ev[("ev", pn, phq)])
                PE.inc(pk4(e, lambda j: (psRr[:, j * 128:(j + 1) * 128],
                                         a2_sb[hq % 2][:, j * 128:(j + 1) * 128],
                                         sl4(chv, hq, j))))
            def mm_cx(hq):
                g = n * 4 + hq
                if n == 0:
                    if hq == 0:
                        e.wait_ge(VE.h, ve_sbf0_val)
                else:
                    e.wait_ge(SE.h, se_ev[("sbf", n - 1, hq)])
                if g >= 1:
                    pn, phq = divmod(g - 1, 4)
                    e.wait_ge(VE.h, ve_ev[("retf", pn, phq)])
                PE.inc(pk4(e, lambda j: (psC[:, j * 128:(j + 1) * 128],
                                         sl4(chq, hq, j),
                                         s_bfT[:, (hq * 4 + j) * 128:(hq * 4 + j + 1) * 128])))
            def mm_um(hq):
                g = n * 4 + hq
                if g >= 2:
                    pn, phq = divmod(g - 2, 4)
                    e.wait_ge(VE.h, ve_ev[("supd", pn, phq)])
                PE.inc(pk4(e, lambda j: (psU6[hq % 2][:, j * 128:(j + 1) * 128],
                                         sl4(chkg, hq, j), sl4(chv, hq, j))))
            mm = {"sc": mm_sc, "rt": mm_rt, "cx": mm_cx, "um": mm_um}
            for op, hq in CORE_ORD:
                mm[op](hq)

        def p6_te_tail(e, n):
            for tg4 in range(4):
                last = None
                for j in range(4):
                    h = tg4 * 4 + j
                    e.wait_ge(SE.h, se_ev[("gn", n, h)])
                    if j == 0:
                        f = n * 4 + tg4
                        if f >= 1:
                            pn, pg = divmod(f - 1, 4)
                            e.wait_ge(VE.h, ve_ev[("prod", pn, pg)])
                    last = e.transpose(psT6[:, j * 128:(j + 1) * 128],
                                       gn_all[:, h * 128:(h + 1) * 128], id_bf[:])
                PE.inc(last)
            e.wait_ge(VE.h, ve_ev[("prod", n, 3)])
            for dmg in range(4):
                if n == 0 and dmg == 0:
                    WST.wait(e, "wo")
                w = n * 4 + dmg
                if w >= 1:
                    pn, pd = divmod(w - 1, 4)
                    e.wait_ge(VE.h, ve_ev[("sev", pn, pd)])
                last = None
                for hv in range(KC):
                    last = e.matmul(psW[:],
                                    prod_fm[:, hv * 128:(hv + 1) * 128],
                                    wo_bf[:, hv * D + dmg * 512: hv * D + (dmg + 1) * 512],
                                    start=(hv == 0), stop=(hv == KC - 1))
                PE.inc(last)

        def p6_te(e):
            bs(e)
            for n in range(NT):
                p6_te_core(e, n)
                if n >= 1:
                    p6_te_tail(e, n - 1)
            p6_te_tail(e, NT - 1)
            be(e, "te")

        def p6_se(e):
            bs(e)
            for n in range(NT):
                for hq in range(4):
                    e.wait_ge(PE.h, pe_ev[("rt", n, hq)])
                    if n * 4 + hq >= 1:
                        g = n * 4 + hq - 1
                        pn, phq = divmod(g, 4)
                        e.wait_ge(VE.h, ve_ev[("retf", pn, phq)])
                    e.drain()
                    SE.inc(e.copy(ret1[:], psRr[:]))
                    e.wait_ge(VE.h, ve_ev[("supd", n, hq)])
                    SE.inc(e.activation(s_bfT[:, hq * 512:(hq + 1) * 512],
                                        s_f32[:, hq * 512:(hq + 1) * 512], AF.Identity))
                e.wait_ge(GPC.h, gp_ev[("stats", n)])
                e.drain()
                e.activation(st16[:, c_nv], st16[:, c_nv], AF.Ln, scale=-1.0)
                e.drain()
                SE.inc(e.activation(st16[:, c_rs], st16[:, c_nv], AF.Exp, scale=-0.5))
                e.wait_ge(GPC.h, gp_ev[("nmr", n)])
                e.drain()
                for h in range(H):
                    if n >= 1:
                        e.wait_ge(PE.h, pe_ev[("tp", n - 1, h // 4)])
                    SE.inc(e.activation(gn_all[:, h * 128:(h + 1) * 128],
                                        ret_all[n % 2][:, h * 128:(h + 1) * 128], AF.Identity,
                                        scale=st16[:, 64 + h:64 + h + 1],
                                        bias=st16[:, 80 + h:80 + h + 1]))
            be(e, "se")

        def p6_ve(e):
            bs(e)
            e.wait_ge(LD.h, bLD6 + 48)
            for n in range(NT):
                for hq in range(4):
                    hsl = slice(hq * 512, (hq + 1) * 512)
                    e.wait_ge(PE.h, pe_ev[("sc", n, hq)])
                    if n * 4 + hq >= 2:
                        pn, phq = divmod(n * 4 + hq - 2, 4)
                        e.wait_ge(PE.h, pe_ev[("rt", pn, phq)])
                    if n >= 2 and hq == 0:
                        e.wait_ge(SE.h, se_ev[("gn", n - 2, 15)])
                    VE.inc(e.tensor_tensor(a2_sb[hq % 2][:], psS[hq % 2][:], d2_sb[:, hsl], ALU.mult))
                    e.wait_ge(SE.h, se_ev[("ev", n, hq)])
                    e.wait_ge(PE.h, pe_ev[("cx", n, hq)])
                    last = None
                    for j in range(4):
                        h = hq * 4 + j
                        last = e.scalar_tensor_tensor(
                            ret_all[n % 2][:, h * 128:(h + 1) * 128],
                            psC[:, j * 128:(j + 1) * 128], grp1v[:, h:h + 1],
                            ret1[:, j * 128:(j + 1) * 128], ALU.mult, ALU.add)
                    VE.inc(last)
                    e.wait_ge(PE.h, pe_ev[("um", n, hq)])
                    if n >= 1:
                        e.wait_ge(SE.h, se_ev[("sbf", n - 1, hq)])
                    last = None
                    for j in range(4):
                        h = hq * 4 + j
                        hsl2 = slice(h * 128, (h + 1) * 128)
                        last = e.scalar_tensor_tensor(s_f32[:, hsl2], s_f32[:, hsl2], GC[h],
                                                      psU6[hq % 2][:, j * 128:(j + 1) * 128],
                                                      ALU.mult, ALU.add)
                    VE.inc(last)
                if n >= 1:
                    for tg4 in range(4):
                        e.wait_ge(PE.h, pe_ev[("tp", n - 1, tg4)])
                        if n >= 2 and tg4 == 0:
                            e.wait_ge(PE.h, pe_ev[("wo", n - 2, 3)])
                        VE.inc(e.tensor_tensor(prod_fm[:, tg4 * 512:(tg4 + 1) * 512],
                                               psT6[:],
                                               chg[(n - 1) % 2][:, tg4 * 512:(tg4 + 1) * 512],
                                               ALU.mult))
                e.drain()
                e.reduce_sum(st16[:, c_sum].rearrange("p (h o) -> p h o", o=1),
                             ret_all[n % 2][:].rearrange("p (h t) -> p h t", h=H), axis=AX.X)
                e.wait_ge(GPC.h, gp_ev[("sq", n)])
                VE.inc(e.reduce_sum(st16[:, c_ssq].rearrange("p (h o) -> p h o", o=1),
                                    sq_all[:].rearrange("p (h t) -> p h t", h=H), axis=AX.X))
                if n >= 1:
                    for dmg in range(4):
                        dsl = slice(dmg * 512, (dmg + 1) * 512)
                        e.wait_ge(PE.h, pe_ev[("wo", n - 1, dmg)])
                        if n >= 3 and dmg == 0:
                            SPST.wait(e, ("s", n - 3))
                        VE.inc(e.tensor_tensor(s_buf[(n - 1) % 2][:, dsl], psW[:],
                                               bo_sb[:, dsl], ALU.add))
            n = NT - 1
            for tg4 in range(4):
                e.wait_ge(PE.h, pe_ev[("tp", n, tg4)])
                if tg4 == 0 and NT >= 2:
                    e.wait_ge(PE.h, pe_ev[("wo", n - 1, 3)])
                VE.inc(e.tensor_tensor(prod_fm[:, tg4 * 512:(tg4 + 1) * 512], psT6[:],
                                       chg[n % 2][:, tg4 * 512:(tg4 + 1) * 512], ALU.mult))
            for dmg in range(4):
                dsl = slice(dmg * 512, (dmg + 1) * 512)
                e.wait_ge(PE.h, pe_ev[("wo", n, dmg)])
                if dmg == 0 and NT >= 3:
                    SPST.wait(e, ("s", NT - 3))
                VE.inc(e.tensor_tensor(s_buf[n % 2][:, dsl], psW[:],
                                       bo_sb[:, dsl], ALU.add))
            be(e, "ve")

        def p6_gp(e):
            bs(e)
            g = WST.group("wo")
            for k in range(KC):
                g(e, wo_bf[:, k * D:(k + 1) * D],
                  Wo[k * 128:(k + 1) * 128, :])
            for n in range(NT):
                e.wait_ge(VE.h, ve_ev[("retf", n, 3)])
                e.drain()
                GPC.inc(e.tensor_tensor(sq_all[:], ret_all[n % 2][:], ret_all[n % 2][:], ALU.mult))
                e.wait_ge(VE.h, ve_ev[("red", n)])
                e.drain()
                e.tensor_scalar(st16[:, c_mu], st16[:, c_sum], 1.0 / 128, None, ALU.mult)
                e.tensor_scalar(st16[:, c_ssq], st16[:, c_ssq], 1.0 / 128, None, ALU.mult)
                e.drain()
                e.tensor_tensor(st16[:, c_nv], st16[:, c_mu], st16[:, c_mu], ALU.mult)
                e.drain()
                e.tensor_tensor(st16[:, c_nv], st16[:, c_nv], st16[:, c_ssq], ALU.subtract)
                e.drain()
                GPC.inc(e.tensor_scalar(st16[:, c_nv], st16[:, c_nv], EPS, None, ALU.subtract))
                e.wait_ge(SE.h, se_ev[("rstd", n)])
                e.drain()
                e.tensor_scalar(st16[:, c_nmr], st16[:, c_mu], -1.0, None, ALU.mult)
                e.drain()
                GPC.inc(e.tensor_tensor(st16[:, c_nmr], st16[:, c_nmr], st16[:, c_rs], ALU.mult))
            be(e, "gp")

        block.sync(p6_sy); block.gpsimd(p6_gp); block.tensor(p6_te)
        block.scalar(p6_se); block.vector(p6_ve)
        assert PE.n == _pectr[0] and VE.n == _vectr[0]
        assert SE.n == _sectr[0] and GPC.n == _gpctr[0]
        p6.close()
        p56.close()
        nphase[0] += 1


        p7ab = ExitStack()
        sb7s = lambda n, shp, dt: p7ab.enter_context(nc.sbuf_tensor(n, shp, dt, side="right"))
        dwb = [sb7s(f"dwb{i}", [128, IC * 512], BF16) for i in range(2)]
        p7a = ExitStack()
        sb7 = lambda n, shp, dt: p7a.enter_context(nc.sbuf_tensor(n, shp, dt))
        ps7 = lambda n, shp, dt=F32: p7a.enter_context(nc.psum_tensor(n, shp, dt))
        xn_fm2 = sb7("xn_fm2", [128, KC * T], BF16)
        gwb = [sb7(f"gwb{i}", [128, KC * 128], BF16) for i in range(2)]
        uwb = [sb7(f"uwb{i}", [128, KC * 128], BF16) for i in range(2)]
        ga_sb = [sb7(f"ga_sb{i}", [128, 512], F32) for i in range(2)]
        ua_sb = [sb7(f"ua_sb{i}", [128, 512], F32) for i in range(2)]
        hdn_buf = [sb7(f"hdn_buf{i}", [128, 512], BF16) for i in range(2)]
        psGa = [ps7(f"psGa{i}", [128, 512]) for i in range(2)]
        psUa = [ps7(f"psUa{i}", [128, 512]) for i in range(2)]

        bPE, bVE, bSE = PE.n, VE.n, SE.n
        NF = IC * NTG
        pe_gm = lambda f: bPE + f * 2 + 1
        pe_um7 = lambda f: bPE + f * 2 + 2
        se_ga = lambda f: bSE + f * 2 + 1
        se_ua = lambda f: bSE + f * 2 + 2
        ve_hd = lambda f: bVE + f + 1

        def dn_quarter_load(e, q):
            g = DST.group(("dn", q))
            for ic in range(IC):
                g(e, dwb[q % 2][:, ic * 512:(ic + 1) * 512],
                  downT[ic * 128:(ic + 1) * 128, q * 512:(q + 1) * 512])

        def p7a_sy(e):
            bs(e)
            SPST.wait(e, "xn")
            RST.group("xn2")(e, xn_fm2[:].rearrange("p (k t) -> p k t", k=KC),
                             xn_sp.rearrange("k p t -> p k t"))
            for f in range(NF):
                ic, tg = divmod(f, NTG)
                e.wait_ge(VE.h, ve_hd(f))
                SPST.group(("h", f))(
                    e, hdn_sp[tg * 4:(tg + 1) * 4, :, ic, :].rearrange("b i t -> i b t"),
                    hdn_buf[f % 2][:].rearrange("p (b t) -> p b t", b=4))
            be(e, "sy")

        def p7a_gp(e):
            bs(e)
            for ic in range(IC):
                if ic >= 2:
                    e.wait_ge(PE.h, pe_um7((ic - 1) * NTG - 1))
                g = WST.group(("gu", ic))
                g(e, gwb[ic % 2][:].rearrange("p (k m) -> p k m", k=KC),
                  gateT[:, ic * 128:(ic + 1) * 128].rearrange("(k p) i -> p k i", p=128))
                g(e, uwb[ic % 2][:].rearrange("p (k m) -> p k m", k=KC),
                  upT[:, ic * 128:(ic + 1) * 128].rearrange("(k p) i -> p k i", p=128))
                if ic == IC // 2:
                    dn_quarter_load(e, 0)
                if ic == IC - 1:
                    dn_quarter_load(e, 1)
            be(e, "gp")

        def p7a_te(e):
            bs(e)
            RST.wait(e, "xn2")
            for f in range(NF):
                ic, tg = divmod(f, NTG)
                WST.wait(e, ("gu", ic))
                if f >= 2:
                    e.wait_ge(SE.h, se_ga(f - 2))
                last = None
                for k in range(KC):
                    last = e.matmul(psGa[f % 2][:], gwb[ic % 2][:, k * 128:(k + 1) * 128],
                                    xn_fm2[:, k * T + tg * 512: k * T + (tg + 1) * 512],
                                    start=(k == 0), stop=(k == KC - 1))
                PE.inc(last)
                if f >= 2:
                    e.wait_ge(SE.h, se_ua(f - 2))
                last = None
                for k in range(KC):
                    last = e.matmul(psUa[f % 2][:], uwb[ic % 2][:, k * 128:(k + 1) * 128],
                                    xn_fm2[:, k * T + tg * 512: k * T + (tg + 1) * 512],
                                    start=(k == 0), stop=(k == KC - 1))
                PE.inc(last)
            be(e, "te")

        def p7a_se(e):
            bs(e)
            for f in range(NF):
                ic, tg = divmod(f, NTG)
                e.wait_ge(PE.h, pe_gm(f))
                if f >= 2:
                    e.wait_ge(VE.h, ve_hd(f - 2))
                SE.inc(e.activation(ga_sb[f % 2][:], psGa[f % 2][:], AF.Silu,
                                    scale=gss_sb[:, ic:ic + 1]))
                e.wait_ge(PE.h, pe_um7(f))
                SE.inc(e.activation(ua_sb[f % 2][:], psUa[f % 2][:], AF.Identity,
                                    scale=uss_sb[:, ic:ic + 1]))
            be(e, "se")

        def p7a_ve(e):
            bs(e)
            for f in range(NF):
                e.wait_ge(SE.h, se_ua(f))
                if f >= 2:
                    SPST.wait(e, ("h", f - 2))
                VE.inc(e.tensor_tensor(hdn_buf[f % 2][:], ga_sb[f % 2][:], ua_sb[f % 2][:], ALU.mult))
            be(e, "ve")

        block.sync(p7a_sy); block.gpsimd(p7a_gp); block.tensor(p7a_te)
        block.scalar(p7a_se); block.vector(p7a_ve)
        assert PE.n == bPE + NF * 2 and SE.n == bSE + NF * 2 and VE.n == bVE + NF
        p7a.close()
        nphase[0] += 1

        p7b = ExitStack()
        sb8 = lambda n, shp, dt: p7b.enter_context(nc.sbuf_tensor(n, shp, dt))
        ps8 = lambda n, shp, dt=F32: p7b.enter_context(nc.psum_tensor(n, shp, dt))
        dss_sb = sb8("dss_sb", [128, D], F32)
        htb = [sb8(f"htb{i}", [128, IC * 128], BF16) for i in range(2)]
        x_t7 = [sb8(f"x_t7{i}", [128, 512], F32) for i in range(2)]
        s_t7 = [sb8(f"s_t7{i}", [128, 512], BF16) for i in range(2)]
        f1_sb = [sb8(f"f1_sb{i}", [128, 512], F32) for i in range(2)]
        f2_sb = [sb8(f"f2_sb{i}", [128, 512], F32) for i in range(2)]
        ob = [sb8(f"ob{i}", [128, 512], F32) for i in range(2)]
        psD = [ps8(f"psD{i}", [128, 512]) for i in range(2)]

        bLD, bPE, bVE, bSE = LD.n, PE.n, VE.n, SE.n
        NB = 4 * NT
        pe_dm = lambda p: bPE + p + 1
        ve_f1 = lambda p: bVE + p * 2 + 1
        ve_fin = lambda p: bVE + p * 2 + 2
        se_f2 = lambda p: bSE + p + 1

        def p7b_sy(e):
            bs(e)
            LD.inc(e.dma_start(out=dss_sb[:], in_=dss_bc), 16)
            for p in range(NB):
                q, tb = divmod(p, NT)
                if p >= 2:
                    e.wait_ge(PE.h, pe_dm(p - 2))
                    e.wait_ge(VE.h, ve_fin(p - 2))
                g = RST.group(("l", p))
                g(e, htb[p % 2][:].rearrange("p (c t) -> p c t", c=IC), hdn_sp[tb])
                g(e, x_t7[p % 2][:], x_in[tb * 128:(tb + 1) * 128, q * 512:(q + 1) * 512])
                g(e, s_t7[p % 2][:], s_sp[tb][:, q * 512:(q + 1) * 512])
                if p >= 1:
                    e.wait_ge(VE.h, ve_fin(p - 1))
                    pv = p - 1
                    qv, tv = divmod(pv, NT)
                    SPST.group(("o", pv))(
                        e, out[tv * 128:(tv + 1) * 128, qv * 512:(qv + 1) * 512], ob[pv % 2][:])
            e.wait_ge(VE.h, ve_fin(NB - 1))
            SPST.group(("o", NB - 1))(
                e, out[(NT - 1) * 128:NT * 128, 3 * 512:4 * 512], ob[(NB - 1) % 2][:])
            be(e, "sy")

        def p7b_gp(e):
            bs(e)
            for q in range(2, 4):
                e.wait_ge(PE.h, pe_dm((q - 1) * NT - 1))
                dn_quarter_load(e, q)
            be(e, "gp")

        def p7b_te(e):
            bs(e)
            for p in range(NB):
                q, tb = divmod(p, NT)
                DST.wait(e, ("dn", q))
                RST.wait(e, ("l", p))
                if p >= 2:
                    e.wait_ge(VE.h, ve_f1(p - 2))
                last = None
                for ic in range(IC):
                    last = e.matmul(psD[p % 2][:], htb[p % 2][:, ic * 128:(ic + 1) * 128],
                                    dwb[q % 2][:, ic * 512:(ic + 1) * 512],
                                    start=(ic == 0), stop=(ic == IC - 1))
                PE.inc(last)
            be(e, "te")

        def p7b_se(e):
            bs(e)
            for p in range(NB):
                e.wait_ge(VE.h, ve_f1(p))
                if p >= 1:
                    e.wait_ge(VE.h, ve_fin(p - 1))
                SE.inc(e.activation(f2_sb[p % 2][:], f1_sb[p % 2][:], AF.Silu))
            be(e, "se")

        def p7b_ve(e):
            bs(e)
            e.wait_ge(LD.h, bLD + 16)
            for p in range(NB):
                q, tb = divmod(p, NT)
                RST.wait(e, ("l", p))
                dsl = slice(q * 512, (q + 1) * 512)
                e.wait_ge(PE.h, pe_dm(p))
                if p >= 2:
                    e.wait_ge(SE.h, se_f2(p - 2))
                VE.inc(e.tensor_tensor(f1_sb[p % 2][:], psD[p % 2][:],
                                       dss_sb[:, dsl], ALU.mult))
                e.wait_ge(SE.h, se_f2(p))
                if p >= 2:
                    SPST.wait(e, ("o", p - 2))
                e.drain()
                e.tensor_tensor(f2_sb[p % 2][:], f2_sb[p % 2][:], s_t7[p % 2][:], ALU.mult)
                e.drain()
                VE.inc(e.tensor_tensor(ob[p % 2][:], f2_sb[p % 2][:], x_t7[p % 2][:], ALU.add))
            be(e, "ve")

        block.sync(p7b_sy); block.gpsimd(p7b_gp); block.tensor(p7b_te)
        block.scalar(p7b_se); block.vector(p7b_ve)
        assert PE.n == bPE + NB and VE.n == bVE + NB * 2 and SE.n == bSE + NB
        p7b.close()
        p7ab.close()
        nphase[0] += 1


    ctx.close()
    return nc



def _rope_tables(span_start, T):
    pos = np.arange(span_start, span_start + T, dtype=np.float64)
    j = np.arange(64, dtype=np.float64)
    inv_freq = 1.0 / THETA ** (2.0 * j / 128.0)
    freqs = pos[:, None] * inv_freq[None, :]
    cos = np.cos(freqs); sin = np.sin(freqs)
    sbase = (2.0 * j + 0.4 * 128.0) / (1.4 * 128.0)
    power = (pos - S // 2) / SCALE_BASE
    scale = sbase[None, :] ** power[:, None]
    ex = lambda a: np.repeat(a, 2, axis=1)
    cosf, sinf, scf = ex(cos), ex(sin), ex(scale)
    mk = lambda a: np.ascontiguousarray(a.T.astype(np.float32))
    return mk(cosf * scf), mk(sinf * scf), mk(cosf / scf), mk(sinf / scf)


def _const_tables():
    r = np.arange(128)
    d2 = np.zeros((128, H * 128), np.float32)
    grp1 = np.zeros((128, H * 128), np.float32)
    gcm = np.zeros((128, H), np.float32)
    for h, g in enumerate(GAMMAS):
        pm = r[None, :] - r[:, None]
        d2[:, h * 128:(h + 1) * 128] = np.where(pm >= 0, np.float64(g) ** np.maximum(pm, 0), 0.0)
        grp1[:, h * 128:(h + 1) * 128] = (np.float64(g) ** (r + 1.0))[:, None]
        gcm[:, h] = np.float64(g) ** (127.0 - r)
    p_rot = np.zeros((128, 128), np.float32)
    for i in range(64):
        p_rot[2 * i + 1, 2 * i] = -1.0
        p_rot[2 * i, 2 * i + 1] = 1.0
    return d2, grp1, gcm, p_rot, np.eye(128, dtype=np.float32)


def _wm_table(core, gs=SPAN):
    b, sp = core // NSPAN, core % NSPAN
    wm = np.zeros((NCORES * H,), np.float64)
    for cp in range(NCORES):
        bp, spp = cp // NSPAN, cp % NSPAN
        if bp == b and spp < sp:
            for h, g in enumerate(GAMMAS):
                wm[cp * H + h] = np.float64(g) ** (gs * (sp - spp - 1))
    return np.ascontiguousarray(np.broadcast_to(wm.astype(np.float32), (128, NCORES * H)))


def _rs(v):
    return np.ascontiguousarray(np.asarray(v, np.float32).reshape(-1, 128).T)


def _bc(v):
    v = np.asarray(v, np.float32)
    return np.ascontiguousarray(np.broadcast_to(v, (128, v.shape[-1])))


def make_in_maps(inputs, T=SPAN, geom_span=None):
    gs = SPAN if geom_span is None else geom_span
    x = np.ascontiguousarray(np.asarray(inputs["L0_attn_x"], np.float32))
    cf = lambda a: np.ascontiguousarray(np.asarray(a, np.float32))
    shared = {
        "Wq": cf(inputs["L0_attn_Wq"]), "Wk": cf(inputs["L0_attn_Wk"]),
        "Wv": cf(inputs["L0_attn_Wv"]), "Wg": cf(inputs["L0_attn_Wg"]),
        "Wo": cf(inputs["L0_attn_Wo"]),
        "gateT": cf(np.asarray(inputs["L0_ffn_gate_w"], np.float32).T),
        "upT": cf(np.asarray(inputs["L0_ffn_up_w"], np.float32).T),
        "downT": cf(np.asarray(inputs["L0_ffn_down_w"], np.float32).T),
        "bqr": _rs(inputs["L0_attn_bq"]), "bkr": _rs(inputs["L0_attn_bk"]),
        "bgr": _rs(inputs["L0_attn_bg"]),
        "bv_bc": _bc(inputs["L0_attn_bv"]), "bo_bc": _bc(inputs["L0_attn_bo"]),
        "gss": _rs(inputs["L0_ffn_gate_s"]), "uss": _rs(inputs["L0_ffn_up_s"]),
        "dss_bc": _bc(inputs["L0_ffn_down_s"]),
    }
    d2, grp1, gcm, p_rot, ident = _const_tables()
    r_ = np.arange(128)
    grp1v = np.stack([np.float64(g) ** (r_ + 1.0) for g in GAMMAS], axis=1).astype(np.float32)
    shared.update({"d2_t": d2, "grp1_bc": grp1, "grp1v_t": np.ascontiguousarray(grp1v),
                   "gcm_t": gcm, "p_rot": p_rot, "ident": ident})
    rope = [_rope_tables(sp * gs, T) for sp in range(NSPAN)]
    in_maps = []
    for c in range(NCORES):
        b, sp = c // NSPAN, c % NSPAN
        cs, ss_, ics, iss = rope[sp]
        im = dict(shared)
        im["x"] = np.ascontiguousarray(x[b, sp * gs: sp * gs + T])
        im["cs_t"] = cs; im["ss_t"] = ss_; im["ics_t"] = ics; im["iss_t"] = iss
        im["wm_bc"] = _wm_table(c, gs)
        in_maps.append(im)
    return in_maps


def kernel(**inputs):
    nc = build(T=SPAN, debug=False)
    in_maps = make_in_maps(inputs, T=SPAN)
    res = run_bass_kernel_spmd(nc, in_maps, core_ids=list(range(NCORES)))
    out = np.empty((B, S, D), np.float32)
    for c in range(NCORES):
        b, sp = c // NSPAN, c % NSPAN
        out[b, sp * SPAN:(sp + 1) * SPAN] = res.results[c]["out"]
    return out


# version 19
# speedup vs baseline: 1.0398x; 1.0127x over previous
import numpy as np
import ml_dtypes
from contextlib import ExitStack

import concourse.bass as bass
import concourse.mybir as mybir
from concourse.bass_utils import run_bass_kernel_spmd

F32 = mybir.dt.float32
BF16 = mybir.dt.bfloat16
AF = mybir.ActivationFunctionType
ALU = mybir.AluOpType
AX = mybir.AxisListType

B = 2
S = 8192
D = 2048
H = 16
INTER = 5632
NCORES = 8
NSPAN = 4
SPAN = S // NSPAN
EPS = 1e-5
THETA = 10000.0
SCALE_BASE = 512.0
GAMMAS = [1.0 - 2.0 ** (-5.0 - h) for h in range(H)]
GC = [float(np.float32(np.float64(g) ** 128)) for g in GAMMAS]
KC = D // 128
IC = INTER // 128


class Sem:
    def __init__(self, nc, ctx, name):
        self.h = ctx.enter_context(nc.semaphore(name))
        self.n = 0

    def inc(self, instr, k=1):
        instr.then_inc(self.h, k)
        self.n += k
        return self.n


class DStream:

    def __init__(self, nc, ctx, name):
        self.s = [Sem(nc, ctx, name + "a"), Sem(nc, ctx, name + "b")]
        self.rec = {}
        self.gi = 0

    def group(self, key=None):
        p = self.gi % 2
        self.gi += 1
        if key is None:
            key = self.gi - 1
        sem = self.s[p]

        def emit(e, out, in_):
            v = sem.inc(e.dma_start(out=out, in_=in_), 16)
            self.rec[key] = (sem, v)
        return emit

    def wait(self, e, key):
        sem, v = self.rec[key]
        e.wait_ge(sem.h, v)

    def drain_all(self, e):
        for sem in self.s:
            if sem.n:
                e.wait_ge(sem.h, sem.n)


def build(T=SPAN, debug=False):
    NT = T // 128
    NTG = T // 512
    TPH = NT // 2
    nc = bass.Bass("TRN2", target_bir_lowering=False)

    dram = lambda n, shp, dt=F32, kind="ExternalInput": nc.dram_tensor(n, shp, dt, kind=kind).ap()
    sk = "ExternalOutput" if debug else "Internal"

    x_in = dram("x", [T, D])
    Wq = dram("Wq", [D, D]); Wk = dram("Wk", [D, D]); Wv = dram("Wv", [D, D])
    Wg = dram("Wg", [D, D]); Wo = dram("Wo", [D, D])
    gateT = dram("gateT", [D, INTER]); upT = dram("upT", [D, INTER]); downT = dram("downT", [INTER, D])
    bqr = dram("bqr", [128, 16]); bkr = dram("bkr", [128, 16]); bgr = dram("bgr", [128, 16])
    bv_bc = dram("bv_bc", [128, D]); bo_bc = dram("bo_bc", [128, D])
    gss = dram("gss", [128, IC]); uss = dram("uss", [128, IC]); dss_bc = dram("dss_bc", [128, D])
    cs_t = dram("cs_t", [128, T]); ss_t = dram("ss_t", [128, T])
    ics_t = dram("ics_t", [128, T]); iss_t = dram("iss_t", [128, T])
    d2_t = dram("d2_t", [128, H * 128])
    grp1_bc = dram("grp1_bc", [128, H * 128])
    grp1v_t = dram("grp1v_t", [128, H])
    gcm_t = dram("gcm_t", [128, H])
    wm_bc = dram("wm_bc", [128, NCORES * H])
    p_in = dram("p_rot", [128, 128]); id_in = dram("ident", [128, 128])

    out = dram("out", [T, D], F32, kind="ExternalOutput")

    q_sp = dram("q_sp", [NT, 128, H, 128], BF16, kind=sk)
    k_sp = dram("k_sp", [NT, 128, H, 128], BF16, kind=sk)
    kg_sp = dram("kg_sp", [NT, 128, H, 128], BF16, kind=sk)
    v_sp = dram("v_sp", [NT, 128, H, 128], BF16, kind=sk)
    g_sp = dram("g_sp", [NT, 128, H, 128], BF16, kind=sk)
    xn_sp = dram("xn_sp", [KC, 128, T], BF16, kind=sk)
    s_sp = dram("s_sp", [NT, 128, D], BF16, kind=sk)
    hdn_sp = dram("hdn_sp", [NT, 128, IC, 128], BF16, kind=sk)
    cc_in = nc.dram_tensor("cc_in", [H * 128, 128], F32, kind="Internal")
    cc_out = nc.dram_tensor("cc_out", [NCORES * H * 128, 128], F32, kind="Internal")

    ctx = ExitStack()
    sb = lambda n, shp, dt: ctx.enter_context(nc.sbuf_tensor(n, shp, dt))

    LD = Sem(nc, ctx, "LD")
    SPST = DStream(nc, ctx, "SP")
    KSP = DStream(nc, ctx, "KP")
    RST = DStream(nc, ctx, "RS")
    WST = DStream(nc, ctx, "WS")
    DST = DStream(nc, ctx, "DS")
    MST = DStream(nc, ctx, "MS")
    CCS = Sem(nc, ctx, "CCS")
    GPC = Sem(nc, ctx, "GPC")
    PE = Sem(nc, ctx, "PE")
    VE = Sem(nc, ctx, "VE")
    SE = Sem(nc, ctx, "SE")
    ENGS = ("sy", "gp", "te", "se", "ve")
    BAR = {e: Sem(nc, ctx, f"B_{e}") for e in ENGS}
    nphase = [0]

    def bs(e):
        if nphase[0] > 0:
            for s_ in BAR.values():
                e.wait_ge(s_.h, nphase[0])

    def be(e, which):
        if which == "sy":
            if LD.n:
                e.wait_ge(LD.h, LD.n)
            SPST.drain_all(e)
            KSP.drain_all(e)
            RST.drain_all(e)
        elif which == "gp":
            WST.drain_all(e)
            DST.drain_all(e)
            MST.drain_all(e)
            if GPC.n:
                e.wait_ge(GPC.h, GPC.n)
        else:
            st = {"te": PE, "se": SE, "ve": VE}[which]
            if st.n > 0:
                e.wait_ge(st.h, st.n)
        BAR[which].inc(e.nop(), 1)

    bqr_sb = sb("bqr_sb", [128, 16], F32); bkr_sb = sb("bkr_sb", [128, 16], F32)
    bgr_sb = sb("bgr_sb", [128, 16], F32)
    gcm_sb = sb("gcm_sb", [128, H], F32)
    gss_sb = sb("gss_sb", [128, IC], F32); uss_sb = sb("uss_sb", [128, IC], F32)
    wm_sb = sb("wm_sb", [128, NCORES * H], F32)
    p_bf = sb("p_bf", [128, 128], BF16); id_bf = sb("id_bf", [128, 128], BF16)
    m_sb = sb("m_sb", [128, H * 128], F32)
    px = ExitStack()
    xn_fm = px.enter_context(nc.sbuf_tensor("xn_fm", [128, KC * T], BF16))

    with nc.Block() as block:
        def p0_sy(e):
            for dst, src in [(bqr_sb, bqr), (bkr_sb, bkr), (bgr_sb, bgr), (gcm_sb, gcm_t),
                             (gss_sb, gss), (uss_sb, uss), (wm_sb, wm_bc)]:
                LD.inc(e.dma_start(out=dst[:], in_=src), 16)
            be(e, "sy")
        def p0_gp(e):
            g = WST.group("p0")
            g(e, p_bf[:], p_in)
            g(e, id_bf[:], id_in)
            be(e, "gp")
        block.sync(p0_sy); block.gpsimd(p0_gp)
        block.tensor(lambda e: be(e, "te")); block.scalar(lambda e: be(e, "se"))
        block.vector(lambda e: be(e, "ve"))
        nphase[0] += 1

        pa_scope = ExitStack()
        sbA = lambda n, shp, dt: pa_scope.enter_context(nc.sbuf_tensor(n, shp, dt, side="right"))
        wvb = [sbA(f"wvb{i}", [128, KC * 512], BF16) for i in range(2)]
        wkb = [sbA(f"wkb{i}", [128, KC * 512], BF16) for i in range(2)]

        p1 = ExitStack()
        sb1 = lambda n, shp, dt: p1.enter_context(nc.sbuf_tensor(n, shp, dt))
        ps1 = lambda n, shp, dt=F32: p1.enter_context(nc.psum_tensor(n, shp, dt))
        NGRP = 4
        TPH = max(NT // NGRP, 1)
        NGRP = NT // TPH
        x_b = [sb1(f"x_b{i}", [128, D], F32) for i in range(TPH)]
        xn_tok = [sb1(f"xn_tok{i}", [128, D], BF16) for i in range(2)]
        sq_sc = sb1("sq_sc", [128, D], F32)
        st_sum = sb1("st_sum", [128, NT], F32); st_ssq = sb1("st_ssq", [128, NT], F32)
        st_mu = sb1("st_mu", [128, NT], F32); st_rs = sb1("st_rs", [128, NT], F32)
        st_nv = sb1("st_nv", [128, NT], F32)
        ps_tp = [ps1(f"ps_tp{i}", [128, 128], BF16) for i in range(2)]

        bLD, bPE, bVE, bSE = LD.n, PE.n, VE.n, SE.n
        ld_half = lambda hh: bLD + (hh + 1) * TPH * 16
        se_sq = lambda t: bSE + (t // TPH) * (TPH + 1) + (t % TPH) + 1
        se_rstd = lambda hh: bSE + hh * (TPH + 1) + TPH + 1
        ve_negv = lambda hh: bVE + hh * (1 + TPH * 17) + 1
        ve_xn = lambda t: ve_negv(t // TPH) + (t % TPH) * 17 + 1
        ve_evac = lambda t, j: ve_xn(t) + j + 1
        pe_tp1 = lambda t, j: bPE + (t * 16 + j) + 1

        def p1_sy(e):
            bs(e)
            for t in range(NT):
                if t >= TPH:
                    e.wait_ge(VE.h, ve_xn(t - TPH))
                    e.wait_ge(SE.h, se_sq(t - TPH))
                LD.inc(e.dma_start(out=x_b[t % TPH][:], in_=x_in[t * 128:(t + 1) * 128, :]), 16)
            be(e, "sy")

        def p1_gp(e):
            bs(e)
            WST.group(("wv", 0))(
                e, wvb[0][:].rearrange("p (k m) -> p k m", k=KC),
                Wv[:, 0:512].rearrange("(k p) m -> p k m", p=128))
            WST.group(("wk", 0))(
                e, wkb[0][:].rearrange("p (k m) -> p k m", k=KC),
                Wk[:, 0:512].rearrange("(k p) m -> p k m", p=128))
            be(e, "gp")

        def p1_te(e):
            bs(e)
            for t in range(NT):
                e.wait_ge(VE.h, ve_xn(t))
                for j in range(16):
                    f = t * 16 + j
                    if f >= 2:
                        pt, pj = divmod(f - 2, 16)
                        e.wait_ge(VE.h, ve_evac(pt, pj))
                    PE.inc(e.transpose(ps_tp[f % 2][:], xn_tok[t % 2][:, j * 128:(j + 1) * 128],
                                       id_bf[:]))
            be(e, "te")

        def p1_se(e):
            bs(e)
            for hh in range(NGRP):
                e.wait_ge(LD.h, ld_half(hh))
                for i in range(TPH):
                    t = hh * TPH + i
                    e.drain()
                    SE.inc(e.activation(sq_sc[:], x_b[i][:], AF.Square,
                                        accum_out=st_ssq[:, t:t + 1]))
                sl = slice(hh * TPH, hh * TPH + TPH)
                e.wait_ge(VE.h, ve_negv(hh))
                e.activation(st_nv[:, sl], st_nv[:, sl], AF.Ln, scale=-1.0)
                e.drain()
                SE.inc(e.activation(st_rs[:, sl], st_nv[:, sl], AF.Exp, scale=-0.5))
            be(e, "se")

        def p1_ve(e):
            bs(e)
            for hh in range(NGRP):
                sl = slice(hh * TPH, hh * TPH + TPH)
                e.wait_ge(LD.h, ld_half(hh))
                for i in range(TPH):
                    t = hh * TPH + i
                    e.reduce_sum(st_sum[:, t:t + 1], x_b[i][:], axis=AX.X)
                e.wait_ge(SE.h, se_sq(hh * TPH + TPH - 1))
                e.drain()
                e.tensor_scalar(st_mu[:, sl], st_sum[:, sl], 1.0 / D, None, ALU.mult)
                e.tensor_scalar(st_ssq[:, sl], st_ssq[:, sl], 1.0 / D, None, ALU.mult)
                e.drain()
                e.tensor_tensor(st_nv[:, sl], st_mu[:, sl], st_mu[:, sl], ALU.mult)
                e.drain()
                e.tensor_tensor(st_nv[:, sl], st_nv[:, sl], st_ssq[:, sl], ALU.subtract)
                e.drain()
                VE.inc(e.tensor_scalar(st_nv[:, sl], st_nv[:, sl], EPS, None, ALU.subtract))
                for i in range(TPH):
                    t = hh * TPH + i
                    e.wait_ge(SE.h, se_rstd(hh))
                    e.drain()
                    VE.inc(e.tensor_scalar(xn_tok[t % 2][:], x_b[i][:], st_mu[:, t:t + 1],
                                           st_rs[:, t:t + 1], ALU.subtract, ALU.mult))
                    for j in range(16):
                        e.wait_ge(PE.h, pe_tp1(t, j))
                        VE.inc(e.tensor_copy(xn_fm[:, j * T + t * 128: j * T + (t + 1) * 128],
                                             ps_tp[(t * 16 + j) % 2][:]))
            be(e, "ve")

        block.sync(p1_sy); block.gpsimd(p1_gp); block.tensor(p1_te)
        block.scalar(p1_se); block.vector(p1_ve)
        assert SE.n == bSE + NGRP * (TPH + 1) and VE.n == bVE + NGRP * (1 + TPH * 17)
        assert PE.n == bPE + NT * 16 and LD.n == bLD + NT * 16
        p1.close()
        nphase[0] += 1

        pa = ExitStack()
        sbp = lambda n, shp, dt: pa.enter_context(nc.sbuf_tensor(n, shp, dt))
        psp = lambda n, shp, dt=F32: pa.enter_context(nc.psum_tensor(n, shp, dt))
        c1_sb = sbp("c1_sb", [128, T], F32); c2_sb = sbp("c2_sb", [128, T], F32)
        bv_sb = sbp("bv_sb", [128, D], F32)
        vqb = [sbp(f"vqb{i}", [128, NT * 512], BF16) for i in range(2)]
        kbf = [sbp(f"kbf{i}", [128, 512], BF16) for i in range(2)]
        ta = sbp("ta", [128, 512], F32); tb_ = sbp("tb", [128, 512], F32)
        fmb = [sbp(f"fmb{i}", [128, 512], BF16) for i in range(2)]
        kgb = [sbp(f"kgb{i}", [128, 512], BF16) for i in range(2)]
        psV = [psp(f"psV{i}", [128, 512]) for i in range(2)]
        psK = [psp(f"psK{i}", [128, 512]) for i in range(2)]
        psR = psp("psR", [128, 512])
        psT = psp("psT", [128, 512], BF16)
        psU = psp("psU", [128, 512])

        NIT = 16 * NTG
        bPE, bVE, bSE = PE.n, VE.n, SE.n
        pe_ev = {}; _pectr = [bPE]
        ve_ev = {}; _vectr = [bVE]
        se_ev = {}; _sectr = [bSE]
        def _mk(ctr, ev):
            def f(key):
                ctr[0] += 1
                ev[key] = ctr[0]
            return f
        _pe = _mk(_pectr, pe_ev); _ve = _mk(_vectr, ve_ev); _se = _mk(_sectr, se_ev)
        SHIFT = 16 * NTG // 4
        SHIFT = NT
        for i in range(NIT + SHIFT):
            if i < NIT:
                _pe(("v", i))
            if i >= SHIFT:
                it = i - SHIFT
                _pe(("kproj", it))
                if it >= 1:
                    _pe(("krot", it - 1)); _pe(("ktp", it - 1)); _pe(("kum", it - 1))
        _pe(("krot", NIT - 1)); _pe(("ktp", NIT - 1)); _pe(("kum", NIT - 1))
        for i in range(NIT + SHIFT):
            if i < NIT:
                _ve(("vev", i))
            if i >= SHIFT:
                it = i - SHIFT
                _ve(("fm", it)); _ve(("kg", it)); _ve(("mu", it))
        for it in range(NIT):
            _se(("kbf", it))

        def pa_sy(e):
            bs(e)
            LD.inc(e.dma_start(out=c1_sb[:], in_=ics_t), 16)
            LD.inc(e.dma_start(out=c2_sb[:], in_=iss_t), 16)
            LD.inc(e.dma_start(out=bv_sb[:], in_=bv_bc), 16)
            for i in range(NIT + SHIFT):
                if i < NIT:
                    vg, t = divmod(i, NT)
                    e.wait_ge(VE.h, ve_ev[("vev", i)])
                    SPST.group(("v", i))(
                        e, v_sp[t, :, 4 * vg:4 * vg + 4, :],
                        vqb[vg % 2][:, t * 512:(t + 1) * 512])
                if i >= SHIFT:
                    it = i - SHIFT
                    qd, rem = divmod(it, 4 * NTG)
                    h, tg = divmod(rem, NTG)
                    head = qd * 4 + h
                    g = KSP.group(("sp", it))
                    e.wait_ge(VE.h, ve_ev[("fm", it)])
                    g(e, k_sp[tg * 4:(tg + 1) * 4, :, head, :].rearrange("n k t -> k n t"),
                      fmb[it % 2][:].rearrange("p (n t) -> p n t", n=4))
                    e.wait_ge(VE.h, ve_ev[("kg", it)])
                    g(e, kg_sp[tg * 4:(tg + 1) * 4, :, head, :].rearrange("n t k -> t n k"),
                      kgb[it % 2][:].rearrange("p (n k) -> p n k", n=4))
            e.wait_ge(VE.h, ve_ev[("mu", NIT - 1)])
            LD.inc(e.dma_start(out=cc_in.ap().rearrange("(h k) v -> k h v", h=H),
                               in_=m_sb[:].rearrange("p (h v) -> p h v", h=H)), 16)
            be(e, "sy")

        ld_m_total = [None]

        def pa_gp(e):
            bs(e)
            for q in range(1, 4):
                e.wait_ge(PE.h, pe_ev[("v", (q - 1) * NT)])
                WST.group(("wv", q))(
                    e, wvb[q % 2][:].rearrange("p (k m) -> p k m", k=KC),
                    Wv[:, q * 512:(q + 1) * 512].rearrange("(k p) m -> p k m", p=128))
                e.wait_ge(PE.h, pe_ev[("kproj", max((q - 1) * 4 * NTG, 0))])
                WST.group(("wk", q))(
                    e, wkb[q % 2][:].rearrange("p (k m) -> p k m", k=KC),
                    Wk[:, q * 512:(q + 1) * 512].rearrange("(k p) m -> p k m", p=128))
            e.wait_ge(LD.h, ld_m_total[0])
            CCS.inc(e.collective_compute(
                "AllGather", ALU.bypass, replica_groups=[list(range(NCORES))],
                ins=[cc_in.ap().opt()], outs=[cc_out.ap().opt()]), 1)
            be(e, "gp")

        def k_tail(e, it):
            qd, rem = divmod(it, 4 * NTG)
            h, tg = divmod(rem, NTG)
            e.wait_ge(SE.h, se_ev[("kbf", it)])
            if it >= 1:
                e.wait_ge(VE.h, ve_ev[("fm", it - 1)])
            PE.inc(e.matmul(psR[:], p_bf[:], kbf[it % 2][:], start=True, stop=True))
            e.wait_ge(VE.h, ve_ev[("fm", it)])
            if it >= 1:
                e.wait_ge(VE.h, ve_ev[("kg", it - 1)])
            last = None
            for j in range(4):
                last = e.transpose(psT[:, j * 128:(j + 1) * 128],
                                   fmb[it % 2][:, j * 128:(j + 1) * 128], id_bf[:])
            PE.inc(last)
            e.wait_ge(VE.h, ve_ev[("kg", it)])
            e.wait_ge(VE.h, ve_ev[("vev", (qd + 1) * NT - 1)])
            if it >= 1:
                e.wait_ge(VE.h, ve_ev[("mu", it - 1)])
            last = None
            for j in range(4):
                n_ = tg * 4 + j
                last = e.matmul(psU[:, j * 128:(j + 1) * 128],
                                kgb[it % 2][:, j * 128:(j + 1) * 128],
                                vqb[qd % 2][:, n_ * 512 + h * 128: n_ * 512 + (h + 1) * 128],
                                start=(j == 0), stop=(j == 3))
            PE.inc(last)

        def pa_te(e):
            bs(e)
            for i in range(NIT + SHIFT):
                if i < NIT:
                    vg, t = divmod(i, NT)
                    WST.wait(e, ("wv", vg))
                    if i >= 2:
                        e.wait_ge(VE.h, ve_ev[("vev", i - 2)])
                    last = None
                    for k in range(KC):
                        last = e.matmul(psV[i % 2][:],
                                        xn_fm[:, k * T + t * 128: k * T + (t + 1) * 128],
                                        wvb[vg % 2][:, k * 512:(k + 1) * 512],
                                        start=(k == 0), stop=(k == KC - 1))
                    PE.inc(last)
                if i >= SHIFT:
                    it = i - SHIFT
                    qd, rem = divmod(it, 4 * NTG)
                    h, tg = divmod(rem, NTG)
                    WST.wait(e, ("wk", qd))
                    if it >= 2:
                        e.wait_ge(SE.h, se_ev[("kbf", it - 2)])
                    last = None
                    for k in range(KC):
                        last = e.matmul(psK[it % 2][:],
                                        wkb[qd % 2][:, k * 512 + h * 128: k * 512 + (h + 1) * 128],
                                        xn_fm[:, k * T + tg * 512: k * T + (tg + 1) * 512],
                                        start=(k == 0), stop=(k == KC - 1))
                    PE.inc(last)
                    if it >= 1:
                        k_tail(e, it - 1)
            k_tail(e, NIT - 1)
            be(e, "te")

        def pa_se(e):
            bs(e)
            for it in range(NIT):
                qd, rem = divmod(it, 4 * NTG)
                h, tg = divmod(rem, NTG)
                head = qd * 4 + h
                e.wait_ge(PE.h, pe_ev[("kproj", it)])
                if it >= 2:
                    e.wait_ge(PE.h, pe_ev[("krot", it - 2)])
                    e.wait_ge(VE.h, ve_ev[("fm", it - 2)])
                SE.inc(e.activation(kbf[it % 2][:], psK[it % 2][:], AF.Identity,
                                    bias=bkr_sb[:, head:head + 1]))
            be(e, "se")

        def pa_ve(e):
            bs(e)
            e.memset(m_sb[:], 0.0)
            e.wait_ge(LD.h, bLD_pa + 48)
            for i in range(NIT + SHIFT):
                if i < NIT:
                    vg, t = divmod(i, NT)
                    e.wait_ge(PE.h, pe_ev[("v", i)])
                    if i >= 2:
                        SPST.wait(e, ("v", i - 2))
                    if i >= 2 * NT and t == 0:
                        e.wait_ge(PE.h, pe_ev[("kum", (vg - 1) * 4 * NTG - 1)])
                    VE.inc(e.tensor_tensor(vqb[vg % 2][:, t * 512:(t + 1) * 512], psV[i % 2][:],
                                           bv_sb[:, vg * 512:(vg + 1) * 512], ALU.add))
                if i >= SHIFT:
                    it = i - SHIFT
                    qd, rem = divmod(it, 4 * NTG)
                    h, tg = divmod(rem, NTG)
                    head = qd * 4 + h
                    tsl = slice(tg * 512, (tg + 1) * 512)
                    e.wait_ge(PE.h, pe_ev[("krot", it)])
                    if it >= 2:
                        KSP.wait(e, ("sp", it - 2))
                        e.wait_ge(PE.h, pe_ev[("ktp", it - 2)])
                    e.tensor_tensor(ta[:], kbf[it % 2][:], c1_sb[:, tsl], ALU.mult)
                    e.tensor_tensor(tb_[:], psR[:], c2_sb[:, tsl], ALU.mult)
                    e.drain()
                    VE.inc(e.tensor_tensor(fmb[it % 2][:], ta[:], tb_[:], ALU.add))
                    e.wait_ge(PE.h, pe_ev[("ktp", it)])
                    if it >= 2:
                        e.wait_ge(PE.h, pe_ev[("kum", it - 2)])
                    VE.inc(e.tensor_scalar(kgb[it % 2][:], psT[:], gcm_sb[:, head:head + 1],
                                           None, ALU.mult))
                    e.wait_ge(PE.h, pe_ev[("kum", it)])
                    last = None
                    for j in range(4):
                        if j > 0:
                            e.drain()
                        hsl = slice(head * 128, (head + 1) * 128)
                        last = e.scalar_tensor_tensor(m_sb[:, hsl], m_sb[:, hsl], GC[head],
                                                      psU[:, j * 128:(j + 1) * 128],
                                                      ALU.mult, ALU.add)
                    VE.inc(last)
            be(e, "ve")

        bLD_pa = LD.n
        block.sync(pa_sy)
        ld_m_total[0] = LD.n
        block.gpsimd(pa_gp); block.tensor(pa_te)
        block.scalar(pa_se); block.vector(pa_ve)
        assert PE.n == _pectr[0] and VE.n == _vectr[0] and SE.n == _sectr[0]
        pa.close()
        pa_scope.close()
        nphase[0] += 1

        p56 = ExitStack()
        sb56 = lambda n, shp, dt: p56.enter_context(nc.sbuf_tensor(n, shp, dt, side="right"))
        s_f32 = sb56("s_f32", [128, H * 128], F32)
        s_bfT = sb56("s_bfT", [128, H * 128], BF16)

        pb = ExitStack()
        sbq = lambda n, shp, dt: pb.enter_context(nc.sbuf_tensor(n, shp, dt))
        psq = lambda n, shp, dt=F32: pb.enter_context(nc.psum_tensor(n, shp, dt))
        wqb = [sbq(f"wqb{i}", [128, KC * 512], BF16) for i in range(2)]
        wgb = [sbq(f"wgb{i}", [128, KC * 512], BF16) for i in range(2)]
        mg_buf = [sbq(f"mg{i}", [128, H * 128], F32) for i in range(2)]
        c1q = sbq("c1q", [128, T], F32); c2q = sbq("c2q", [128, T], F32)
        qbf = [sbq(f"qbf{i}", [128, 512], BF16) for i in range(2)]
        taq = sbq("taq", [128, 512], F32); tbq = sbq("tbq", [128, 512], F32)
        fmq = [sbq(f"fmq{i}", [128, 512], BF16) for i in range(2)]
        g_buf = [sbq(f"g_buf{i}", [128, 512], BF16) for i in range(2)]
        psQ = [psq(f"psQ{i}", [128, 512]) for i in range(2)]
        psRq = psq("psRq", [128, 512])
        psG5 = [psq(f"psG5{i}", [128, 512]) for i in range(2)]

        bPE, bVE, bSE = PE.n, VE.n, SE.n
        pe_ev = {}; _pectr = [bPE]
        ve_ev = {}; _vectr = [bVE]
        se_ev = {}; _sectr = [bSE]
        _pe = _mk(_pectr, pe_ev); _ve = _mk(_vectr, ve_ev); _se = _mk(_sectr, se_ev)
        for it in range(NIT):
            _pe(("qproj", it)); _pe(("qrot", it)); _pe(("gproj", it))
        S0C = {}
        for c in range(NCORES):
            S0C.setdefault(4 * NTG + (c * 4 * NTG) // NCORES, []).append(c)
        for it in range(NIT):
            _ve(("fm", it))
            for c in S0C.get(it, []):
                _ve(("s0", c))
                if c == NCORES - 1:
                    _ve(("sbf0",))
        for it in range(NIT):
            _se(("qbf", it)); _se(("gev", it))

        def pb_sy(e):
            bs(e)
            LD.inc(e.dma_start(out=c1q[:], in_=cs_t), 16)
            LD.inc(e.dma_start(out=c2q[:], in_=ss_t), 16)
            for it in range(NIT):
                qd, rem = divmod(it, 4 * NTG)
                h, tg = divmod(rem, NTG)
                head = qd * 4 + h
                g = KSP.group(("spq", it))
                e.wait_ge(VE.h, ve_ev[("fm", it)])
                g(e, q_sp[tg * 4:(tg + 1) * 4, :, head, :].rearrange("n k t -> k n t"),
                  fmq[it % 2][:].rearrange("p (n t) -> p n t", n=4))
                e.wait_ge(SE.h, se_ev[("gev", it)])
                g(e, g_sp[tg * 4:(tg + 1) * 4, :, head, :].rearrange("n g t -> g n t"),
                  g_buf[it % 2][:].rearrange("p (n t) -> p n t", n=4))
            SPST.group("xn")(e, xn_sp.rearrange("k p t -> p k t"),
                             xn_fm[:].rearrange("p (k t) -> p k t", k=KC))
            be(e, "sy")

        def pb_gp(e):
            bs(e)
            for q in range(4):
                if q >= 1:
                    e.wait_ge(PE.h, pe_ev[("qproj", (q - 1) * 4 * NTG)])
                WST.group(("wq", q))(
                    e, wqb[q % 2][:].rearrange("p (k m) -> p k m", k=KC),
                    Wq[:, q * 512:(q + 1) * 512].rearrange("(k p) m -> p k m", p=128))
                if q >= 1:
                    e.wait_ge(PE.h, pe_ev[("gproj", (q - 1) * 4 * NTG)])
                WST.group(("wg", q))(
                    e, wgb[q % 2][:].rearrange("p (k m) -> p k m", k=KC),
                    Wg[:, q * 512:(q + 1) * 512].rearrange("(k p) m -> p k m", p=128))
                if q == 1:
                    e.wait_ge(CCS.h, 1)
                    for c in range(NCORES):
                        if c >= 2:
                            e.wait_ge(VE.h, ve_ev[("s0", c - 2)])
                        MST.group(("mg", c))(
                            e, mg_buf[c % 2][:].rearrange("p (h v) -> p h v", h=H),
                            cc_out.ap().rearrange("(c h k) v -> c k h v", c=NCORES, h=H)[c])
            be(e, "gp")

        def pb_te(e):
            bs(e)
            for it in range(NIT):
                qd, rem = divmod(it, 4 * NTG)
                h, tg = divmod(rem, NTG)
                WST.wait(e, ("wq", qd))
                if it >= 2:
                    e.wait_ge(SE.h, se_ev[("qbf", it - 2)])
                last = None
                for k in range(KC):
                    last = e.matmul(psQ[it % 2][:],
                                    wqb[qd % 2][:, k * 512 + h * 128: k * 512 + (h + 1) * 128],
                                    xn_fm[:, k * T + tg * 512: k * T + (tg + 1) * 512],
                                    start=(k == 0), stop=(k == KC - 1))
                PE.inc(last)
                e.wait_ge(SE.h, se_ev[("qbf", it)])
                if it >= 1:
                    e.wait_ge(VE.h, ve_ev[("fm", it - 1)])
                PE.inc(e.matmul(psRq[:], p_bf[:], qbf[it % 2][:], start=True, stop=True))
                WST.wait(e, ("wg", qd))
                if it >= 2:
                    e.wait_ge(SE.h, se_ev[("gev", it - 2)])
                last = None
                for k in range(KC):
                    last = e.matmul(psG5[it % 2][:],
                                    wgb[qd % 2][:, k * 512 + h * 128: k * 512 + (h + 1) * 128],
                                    xn_fm[:, k * T + tg * 512: k * T + (tg + 1) * 512],
                                    start=(k == 0), stop=(k == KC - 1))
                PE.inc(last)
            be(e, "te")

        def pb_se(e):
            bs(e)
            for it in range(NIT):
                qd, rem = divmod(it, 4 * NTG)
                h, tg = divmod(rem, NTG)
                head = qd * 4 + h
                e.wait_ge(PE.h, pe_ev[("qproj", it)])
                if it >= 2:
                    e.wait_ge(PE.h, pe_ev[("qrot", it - 2)])
                    e.wait_ge(VE.h, ve_ev[("fm", it - 2)])
                SE.inc(e.activation(qbf[it % 2][:], psQ[it % 2][:], AF.Identity,
                                    bias=bqr_sb[:, head:head + 1]))
                e.wait_ge(PE.h, pe_ev[("gproj", it)])
                if it >= 2:
                    KSP.wait(e, ("spq", it - 2))
                SE.inc(e.activation(g_buf[it % 2][:], psG5[it % 2][:], AF.Silu,
                                    bias=bgr_sb[:, head:head + 1]))
            be(e, "se")

        def pb_ve(e):
            bs(e)
            e.wait_ge(LD.h, bLD_pb + 32)
            e.memset(s_f32[:], 0.0)
            for it in range(NIT):
                qd, rem = divmod(it, 4 * NTG)
                h, tg = divmod(rem, NTG)
                tsl = slice(tg * 512, (tg + 1) * 512)
                e.wait_ge(PE.h, pe_ev[("qrot", it)])
                if it >= 2:
                    KSP.wait(e, ("spq", it - 2))
                e.tensor_tensor(taq[:], qbf[it % 2][:], c1q[:, tsl], ALU.mult)
                e.tensor_tensor(tbq[:], psRq[:], c2q[:, tsl], ALU.mult)
                e.drain()
                VE.inc(e.tensor_tensor(fmq[it % 2][:], taq[:], tbq[:], ALU.add))
                for c in S0C.get(it, []):
                    MST.wait(e, ("mg", c))
                    e.drain()
                    last = None
                    for h in range(H):
                        hsl = slice(h * 128, (h + 1) * 128)
                        last = e.scalar_tensor_tensor(s_f32[:, hsl], mg_buf[c % 2][:, hsl],
                                                      wm_sb[:, c * H + h:c * H + h + 1],
                                                      s_f32[:, hsl], ALU.mult, ALU.add)
                    VE.inc(last)
                    if c == NCORES - 1:
                        e.drain()
                        VE.inc(e.tensor_copy(s_bfT[:], s_f32[:]))
            be(e, "ve")

        bLD_pb = LD.n
        block.sync(pb_sy); block.gpsimd(pb_gp); block.tensor(pb_te)
        block.scalar(pb_se); block.vector(pb_ve)
        assert PE.n == _pectr[0] and VE.n == _vectr[0] and SE.n == _sectr[0]
        pb.close()
        px.close()
        nphase[0] += 1


        p6 = ExitStack()
        sb6 = lambda n, shp, dt: p6.enter_context(nc.sbuf_tensor(n, shp, dt))
        ps6 = lambda n, shp, dt=F32: p6.enter_context(nc.psum_tensor(n, shp, dt))
        wo_bf = sb6("wo_bf", [128, KC * D], BF16)
        bo_sb = sb6("bo_sb", [128, D], F32)
        d2_sb = sb6("d2_sb", [128, H * 128], F32)
        grp1v = sb6("grp1v", [128, H], F32)
        chq = [sb6(f"chq{i}", [128, H * 128], BF16) for i in range(2)]
        chk = [sb6(f"chk{i}", [128, H * 128], BF16) for i in range(2)]
        chkg = [sb6(f"chkg{i}", [128, H * 128], BF16) for i in range(2)]
        chv = [sb6(f"chv{i}", [128, H * 128], BF16) for i in range(2)]
        chg = [sb6(f"chg{i}", [128, H * 128], BF16) for i in range(2)]
        a2_sb = [sb6(f"a2_{i}", [128, 512], BF16) for i in range(2)]
        ret1 = sb6("ret1", [128, 512], F32)
        ret_all = [sb6(f"ret_all{i}", [128, H * 128], F32) for i in range(2)]
        sq_all = sb6("sq_all", [128, H * 128], F32)
        gn_all = sb6("gn_all", [128, H * 128], BF16)
        prod_fm = sb6("prod_fm", [128, H * 128], BF16)
        s_buf = [sb6(f"s_buf{i}", [128, D], BF16) for i in range(2)]
        st16 = sb6("st16", [128, 16 * 6], F32)
        psS = [ps6(f"psS{i}", [128, 512]) for i in range(2)]
        psRr = ps6("psRr", [128, 512])
        psC = ps6("psC", [128, 512])
        psU6 = [ps6(f"psU6{i}", [128, 512]) for i in range(2)]
        psT6 = ps6("psT6", [128, 512], BF16)
        psW = ps6("psW", [128, 512])
        c_sum = slice(0, 16); c_ssq = slice(16, 32); c_mu = slice(32, 48)
        c_nv = slice(48, 64); c_rs = slice(64, 80); c_nmr = slice(80, 96)

        bLD6, bPE, bVE, bSE, bGP = LD.n, PE.n, VE.n, SE.n, GPC.n
        ve_sbf0_val = bVE
        pe_ev = {}; _pectr = [bPE]
        ve_ev = {}; _vectr = [bVE]
        se_ev = {}; _sectr = [bSE]
        gp_ev = {}; _gpctr = [bGP]
        def _mk(ctr, ev):
            def f(key):
                ctr[0] += 1
                ev[key] = ctr[0]
            return f
        _pe = _mk(_pectr, pe_ev); _ve = _mk(_vectr, ve_ev)
        _se = _mk(_sectr, se_ev); _gp = _mk(_gpctr, gp_ev)
        for n in range(NT):
            for hq in range(4):
                _pe(("sc", n, hq)); _pe(("rt", n, hq)); _pe(("cx", n, hq)); _pe(("um", n, hq))
            if n >= 1:
                for tg4 in range(4):
                    _pe(("tp", n - 1, tg4))
                for dmg in range(4):
                    _pe(("wo", n - 1, dmg))
        for tg4 in range(4):
            _pe(("tp", NT - 1, tg4))
        for dmg in range(4):
            _pe(("wo", NT - 1, dmg))
        for n in range(NT):
            for hq in range(4):
                _ve(("a2", n, hq)); _ve(("retf", n, hq)); _ve(("supd", n, hq))
            if n >= 1:
                for tg4 in range(4):
                    _ve(("prod", n - 1, tg4))
            _ve(("red", n))
            if n >= 1:
                for dmg in range(4):
                    _ve(("sev", n - 1, dmg))
        for tg4 in range(4):
            _ve(("prod", NT - 1, tg4))
        for dmg in range(4):
            _ve(("sev", NT - 1, dmg))
        for n in range(NT):
            for hq in range(4):
                _se(("ev", n, hq))
            _se(("rstd", n))
            for h in range(H):
                _se(("gn", n, h))
        for n in range(NT):
            _gp(("sq", n)); _gp(("stats", n)); _gp(("nmr", n))

        def p6_sy(e):
            bs(e)
            LD.inc(e.dma_start(out=bo_sb[:], in_=bo_bc), 16)
            LD.inc(e.dma_start(out=d2_sb[:], in_=d2_t), 16)
            LD.inc(e.dma_start(out=grp1v[:], in_=grp1v_t), 16)
            def ch_load(n):
                g = RST.group(("ch", n))
                for spill, buf in [(q_sp, chq), (k_sp, chk), (kg_sp, chkg),
                                   (v_sp, chv), (g_sp, chg)]:
                    g(e, buf[n % 2][:].rearrange("p (h t) -> p h t", h=H), spill[n])
            ch_load(0)
            if NT > 1:
                ch_load(1)
            for n in range(NT):
                if n + 2 < NT:
                    e.wait_ge(PE.h, pe_ev[("um", n, 3)])
                    e.wait_ge(VE.h, ve_ev[("prod", n, 3)])
                    ch_load(n + 2)
                if n >= 1:
                    e.wait_ge(VE.h, ve_ev[("sev", n - 1, 3)])
                    SPST.group(("s", n - 1))(e, s_sp[n - 1], s_buf[(n - 1) % 2][:])
            e.wait_ge(VE.h, ve_ev[("sev", NT - 1, 3)])
            SPST.group(("s", NT - 1))(e, s_sp[NT - 1], s_buf[(NT - 1) % 2][:])
            be(e, "sy")

        def pk4(e, mk, start=True, stop=True):
            last = None
            for j in range(4):
                o, l, r = mk(j)
                last = e.matmul(o, l, r, start=(start and j == 0), stop=(stop and j == 3))
            return last

        def p6_te_core(e, n):
            def sl4(buf, hq, j):
                return buf[n % 2][:, (hq * 4 + j) * 128:(hq * 4 + j + 1) * 128]
            def mm_sc(hq):
                g = n * 4 + hq
                RST.wait(e, ("ch", n))
                if g >= 2:
                    pn, phq = divmod(g - 2, 4)
                    e.wait_ge(VE.h, ve_ev[("a2", pn, phq)])
                PE.inc(pk4(e, lambda j: (psS[hq % 2][:, j * 128:(j + 1) * 128],
                                         sl4(chk, hq, j), sl4(chq, hq, j))))
            def mm_rt(hq):
                g = n * 4 + hq
                e.wait_ge(VE.h, ve_ev[("a2", n, hq)])
                if g >= 1:
                    pn, phq = divmod(g - 1, 4)
                    e.wait_ge(SE.h, se_ev[("ev", pn, phq)])
                PE.inc(pk4(e, lambda j: (psRr[:, j * 128:(j + 1) * 128],
                                         a2_sb[hq % 2][:, j * 128:(j + 1) * 128],
                                         sl4(chv, hq, j))))
            def mm_cx(hq):
                g = n * 4 + hq
                if n == 0:
                    if hq == 0:
                        e.wait_ge(VE.h, ve_sbf0_val)
                else:
                    e.wait_ge(VE.h, ve_ev[("supd", n - 1, hq)])
                if g >= 1:
                    pn, phq = divmod(g - 1, 4)
                    e.wait_ge(VE.h, ve_ev[("retf", pn, phq)])
                PE.inc(pk4(e, lambda j: (psC[:, j * 128:(j + 1) * 128],
                                         sl4(chq, hq, j),
                                         s_bfT[:, (hq * 4 + j) * 128:(hq * 4 + j + 1) * 128])))
            def mm_um(hq):
                g = n * 4 + hq
                if g >= 2:
                    pn, phq = divmod(g - 2, 4)
                    e.wait_ge(VE.h, ve_ev[("supd", pn, phq)])
                PE.inc(pk4(e, lambda j: (psU6[hq % 2][:, j * 128:(j + 1) * 128],
                                         sl4(chkg, hq, j), sl4(chv, hq, j))))
            mm_sc(0); mm_rt(0); mm_cx(0); mm_um(0)
            mm_sc(1); mm_rt(1); mm_cx(1); mm_um(1)
            mm_sc(2); mm_rt(2); mm_cx(2); mm_um(2)
            mm_sc(3); mm_rt(3); mm_cx(3); mm_um(3)

        def p6_te_tail(e, n):
            for tg4 in range(4):
                last = None
                for j in range(4):
                    h = tg4 * 4 + j
                    e.wait_ge(SE.h, se_ev[("gn", n, h)])
                    if j == 0:
                        f = n * 4 + tg4
                        if f >= 1:
                            pn, pg = divmod(f - 1, 4)
                            e.wait_ge(VE.h, ve_ev[("prod", pn, pg)])
                    last = e.transpose(psT6[:, j * 128:(j + 1) * 128],
                                       gn_all[:, h * 128:(h + 1) * 128], id_bf[:])
                PE.inc(last)
            e.wait_ge(VE.h, ve_ev[("prod", n, 3)])
            for dmg in range(4):
                if n == 0:
                    (WST if dmg < 2 else DST).wait(e, ("wo", dmg))
                w = n * 4 + dmg
                if w >= 1:
                    pn, pd = divmod(w - 1, 4)
                    e.wait_ge(VE.h, ve_ev[("sev", pn, pd)])
                last = None
                for hv in range(KC):
                    last = e.matmul(psW[:],
                                    prod_fm[:, hv * 128:(hv + 1) * 128],
                                    wo_bf[:, hv * D + dmg * 512: hv * D + (dmg + 1) * 512],
                                    start=(hv == 0), stop=(hv == KC - 1))
                PE.inc(last)

        def p6_te(e):
            bs(e)
            for n in range(NT):
                p6_te_core(e, n)
                if n >= 1:
                    p6_te_tail(e, n - 1)
            p6_te_tail(e, NT - 1)
            be(e, "te")

        def p6_se(e):
            bs(e)
            for n in range(NT):
                for hq in range(4):
                    e.wait_ge(PE.h, pe_ev[("rt", n, hq)])
                    if n * 4 + hq >= 1:
                        g = n * 4 + hq - 1
                        pn, phq = divmod(g, 4)
                        e.wait_ge(VE.h, ve_ev[("retf", pn, phq)])
                    e.drain()
                    SE.inc(e.copy(ret1[:], psRr[:]))
                e.wait_ge(GPC.h, gp_ev[("stats", n)])
                e.drain()
                e.activation(st16[:, c_nv], st16[:, c_nv], AF.Ln, scale=-1.0)
                e.drain()
                SE.inc(e.activation(st16[:, c_rs], st16[:, c_nv], AF.Exp, scale=-0.5))
                e.wait_ge(GPC.h, gp_ev[("nmr", n)])
                e.drain()
                for h in range(H):
                    if n >= 1:
                        e.wait_ge(PE.h, pe_ev[("tp", n - 1, h // 4)])
                    SE.inc(e.activation(gn_all[:, h * 128:(h + 1) * 128],
                                        ret_all[n % 2][:, h * 128:(h + 1) * 128], AF.Identity,
                                        scale=st16[:, 64 + h:64 + h + 1],
                                        bias=st16[:, 80 + h:80 + h + 1]))
            be(e, "se")

        def p6_ve(e):
            bs(e)
            e.wait_ge(LD.h, bLD6 + 48)
            for n in range(NT):
                for hq in range(4):
                    hsl = slice(hq * 512, (hq + 1) * 512)
                    e.wait_ge(PE.h, pe_ev[("sc", n, hq)])
                    if n * 4 + hq >= 2:
                        pn, phq = divmod(n * 4 + hq - 2, 4)
                        e.wait_ge(PE.h, pe_ev[("rt", pn, phq)])
                    if n >= 2 and hq == 0:
                        e.wait_ge(SE.h, se_ev[("gn", n - 2, 15)])
                    VE.inc(e.tensor_tensor(a2_sb[hq % 2][:], psS[hq % 2][:], d2_sb[:, hsl], ALU.mult))
                    e.wait_ge(SE.h, se_ev[("ev", n, hq)])
                    e.wait_ge(PE.h, pe_ev[("cx", n, hq)])
                    last = None
                    for j in range(4):
                        h = hq * 4 + j
                        last = e.scalar_tensor_tensor(
                            ret_all[n % 2][:, h * 128:(h + 1) * 128],
                            psC[:, j * 128:(j + 1) * 128], grp1v[:, h:h + 1],
                            ret1[:, j * 128:(j + 1) * 128], ALU.mult, ALU.add)
                    VE.inc(last)
                    e.wait_ge(PE.h, pe_ev[("um", n, hq)])
                    for j in range(4):
                        h = hq * 4 + j
                        hsl2 = slice(h * 128, (h + 1) * 128)
                        e.scalar_tensor_tensor(s_f32[:, hsl2], s_f32[:, hsl2], GC[h],
                                               psU6[hq % 2][:, j * 128:(j + 1) * 128],
                                               ALU.mult, ALU.add)
                    e.drain()
                    VE.inc(e.tensor_copy(s_bfT[:, hq * 512:(hq + 1) * 512],
                                         s_f32[:, hq * 512:(hq + 1) * 512]))
                if n >= 1:
                    for tg4 in range(4):
                        e.wait_ge(PE.h, pe_ev[("tp", n - 1, tg4)])
                        if n >= 2 and tg4 == 0:
                            e.wait_ge(PE.h, pe_ev[("wo", n - 2, 3)])
                        VE.inc(e.tensor_tensor(prod_fm[:, tg4 * 512:(tg4 + 1) * 512],
                                               psT6[:],
                                               chg[(n - 1) % 2][:, tg4 * 512:(tg4 + 1) * 512],
                                               ALU.mult))
                e.drain()
                e.reduce_sum(st16[:, c_sum].rearrange("p (h o) -> p h o", o=1),
                             ret_all[n % 2][:].rearrange("p (h t) -> p h t", h=H), axis=AX.X)
                e.wait_ge(GPC.h, gp_ev[("sq", n)])
                VE.inc(e.reduce_sum(st16[:, c_ssq].rearrange("p (h o) -> p h o", o=1),
                                    sq_all[:].rearrange("p (h t) -> p h t", h=H), axis=AX.X))
                if n >= 1:
                    for dmg in range(4):
                        dsl = slice(dmg * 512, (dmg + 1) * 512)
                        e.wait_ge(PE.h, pe_ev[("wo", n - 1, dmg)])
                        if n >= 3 and dmg == 0:
                            SPST.wait(e, ("s", n - 3))
                        VE.inc(e.tensor_tensor(s_buf[(n - 1) % 2][:, dsl], psW[:],
                                               bo_sb[:, dsl], ALU.add))
            n = NT - 1
            for tg4 in range(4):
                e.wait_ge(PE.h, pe_ev[("tp", n, tg4)])
                if tg4 == 0 and NT >= 2:
                    e.wait_ge(PE.h, pe_ev[("wo", n - 1, 3)])
                VE.inc(e.tensor_tensor(prod_fm[:, tg4 * 512:(tg4 + 1) * 512], psT6[:],
                                       chg[n % 2][:, tg4 * 512:(tg4 + 1) * 512], ALU.mult))
            for dmg in range(4):
                dsl = slice(dmg * 512, (dmg + 1) * 512)
                e.wait_ge(PE.h, pe_ev[("wo", n, dmg)])
                if dmg == 0 and NT >= 3:
                    SPST.wait(e, ("s", NT - 3))
                VE.inc(e.tensor_tensor(s_buf[n % 2][:, dsl], psW[:],
                                       bo_sb[:, dsl], ALU.add))
            be(e, "ve")

        def p6_gp(e):
            bs(e)
            for dmg in range(4):
                g = (WST if dmg < 2 else DST).group(("wo", dmg))
                for k in range(KC):
                    g(e, wo_bf[:, k * D + dmg * 512: k * D + (dmg + 1) * 512],
                      Wo[k * 128:(k + 1) * 128, dmg * 512:(dmg + 1) * 512])
            for n in range(NT):
                e.wait_ge(VE.h, ve_ev[("retf", n, 3)])
                e.drain()
                GPC.inc(e.tensor_tensor(sq_all[:], ret_all[n % 2][:], ret_all[n % 2][:], ALU.mult))
                e.wait_ge(VE.h, ve_ev[("red", n)])
                e.drain()
                e.tensor_scalar(st16[:, c_mu], st16[:, c_sum], 1.0 / 128, None, ALU.mult)
                e.tensor_scalar(st16[:, c_ssq], st16[:, c_ssq], 1.0 / 128, None, ALU.mult)
                e.drain()
                e.tensor_tensor(st16[:, c_nv], st16[:, c_mu], st16[:, c_mu], ALU.mult)
                e.drain()
                e.tensor_tensor(st16[:, c_nv], st16[:, c_nv], st16[:, c_ssq], ALU.subtract)
                e.drain()
                GPC.inc(e.tensor_scalar(st16[:, c_nv], st16[:, c_nv], EPS, None, ALU.subtract))
                e.wait_ge(SE.h, se_ev[("rstd", n)])
                e.drain()
                e.tensor_scalar(st16[:, c_nmr], st16[:, c_mu], -1.0, None, ALU.mult)
                e.drain()
                GPC.inc(e.tensor_tensor(st16[:, c_nmr], st16[:, c_nmr], st16[:, c_rs], ALU.mult))
            be(e, "gp")

        block.sync(p6_sy); block.gpsimd(p6_gp); block.tensor(p6_te)
        block.scalar(p6_se); block.vector(p6_ve)
        assert PE.n == _pectr[0] and VE.n == _vectr[0]
        assert SE.n == _sectr[0] and GPC.n == _gpctr[0]
        p6.close()
        p56.close()
        nphase[0] += 1


        p7ab = ExitStack()
        sb7s = lambda n, shp, dt: p7ab.enter_context(nc.sbuf_tensor(n, shp, dt, side="right"))
        dwb = [sb7s(f"dwb{i}", [128, IC * 512], BF16) for i in range(2)]
        p7a = ExitStack()
        sb7 = lambda n, shp, dt: p7a.enter_context(nc.sbuf_tensor(n, shp, dt))
        ps7 = lambda n, shp, dt=F32: p7a.enter_context(nc.psum_tensor(n, shp, dt))
        xn_fm2 = sb7("xn_fm2", [128, KC * T], BF16)
        gwb = [sb7(f"gwb{i}", [128, KC * 128], BF16) for i in range(2)]
        uwb = [sb7(f"uwb{i}", [128, KC * 128], BF16) for i in range(2)]
        ga_sb = [sb7(f"ga_sb{i}", [128, 512], F32) for i in range(2)]
        ua_sb = [sb7(f"ua_sb{i}", [128, 512], F32) for i in range(2)]
        hdn_buf = [sb7(f"hdn_buf{i}", [128, 512], BF16) for i in range(2)]
        psGa = [ps7(f"psGa{i}", [128, 512]) for i in range(2)]
        psUa = [ps7(f"psUa{i}", [128, 512]) for i in range(2)]

        bPE, bVE, bSE = PE.n, VE.n, SE.n
        NF = IC * NTG
        pe_gm = lambda f: bPE + f * 2 + 1
        pe_um7 = lambda f: bPE + f * 2 + 2
        se_ga = lambda f: bSE + f * 2 + 1
        se_ua = lambda f: bSE + f * 2 + 2
        ve_hd = lambda f: bVE + f + 1

        def dn_quarter_load(e, q):
            g = DST.group(("dn", q))
            for ic in range(IC):
                g(e, dwb[q % 2][:, ic * 512:(ic + 1) * 512],
                  downT[ic * 128:(ic + 1) * 128, q * 512:(q + 1) * 512])

        def p7a_sy(e):
            bs(e)
            SPST.wait(e, "xn")
            RST.group("xn2")(e, xn_fm2[:].rearrange("p (k t) -> p k t", k=KC),
                             xn_sp.rearrange("k p t -> p k t"))
            for f in range(NF):
                ic, tg = divmod(f, NTG)
                e.wait_ge(VE.h, ve_hd(f))
                SPST.group(("h", f))(
                    e, hdn_sp[tg * 4:(tg + 1) * 4, :, ic, :].rearrange("b i t -> i b t"),
                    hdn_buf[f % 2][:].rearrange("p (b t) -> p b t", b=4))
            be(e, "sy")

        def p7a_gp(e):
            bs(e)
            for ic in range(IC):
                if ic >= 2:
                    e.wait_ge(PE.h, pe_um7((ic - 1) * NTG - 1))
                g = WST.group(("gu", ic))
                g(e, gwb[ic % 2][:].rearrange("p (k m) -> p k m", k=KC),
                  gateT[:, ic * 128:(ic + 1) * 128].rearrange("(k p) i -> p k i", p=128))
                g(e, uwb[ic % 2][:].rearrange("p (k m) -> p k m", k=KC),
                  upT[:, ic * 128:(ic + 1) * 128].rearrange("(k p) i -> p k i", p=128))
                if ic == IC // 2:
                    dn_quarter_load(e, 0)
                if ic == IC - 1:
                    dn_quarter_load(e, 1)
            be(e, "gp")

        def p7a_te(e):
            bs(e)
            RST.wait(e, "xn2")
            for f in range(NF):
                ic, tg = divmod(f, NTG)
                WST.wait(e, ("gu", ic))
                if f >= 2:
                    e.wait_ge(SE.h, se_ga(f - 2))
                last = None
                for k in range(KC):
                    last = e.matmul(psGa[f % 2][:], gwb[ic % 2][:, k * 128:(k + 1) * 128],
                                    xn_fm2[:, k * T + tg * 512: k * T + (tg + 1) * 512],
                                    start=(k == 0), stop=(k == KC - 1))
                PE.inc(last)
                if f >= 2:
                    e.wait_ge(SE.h, se_ua(f - 2))
                last = None
                for k in range(KC):
                    last = e.matmul(psUa[f % 2][:], uwb[ic % 2][:, k * 128:(k + 1) * 128],
                                    xn_fm2[:, k * T + tg * 512: k * T + (tg + 1) * 512],
                                    start=(k == 0), stop=(k == KC - 1))
                PE.inc(last)
            be(e, "te")

        def p7a_se(e):
            bs(e)
            for f in range(NF):
                ic, tg = divmod(f, NTG)
                e.wait_ge(PE.h, pe_gm(f))
                if f >= 2:
                    e.wait_ge(VE.h, ve_hd(f - 2))
                SE.inc(e.activation(ga_sb[f % 2][:], psGa[f % 2][:], AF.Silu,
                                    scale=gss_sb[:, ic:ic + 1]))
                e.wait_ge(PE.h, pe_um7(f))
                SE.inc(e.activation(ua_sb[f % 2][:], psUa[f % 2][:], AF.Identity,
                                    scale=uss_sb[:, ic:ic + 1]))
            be(e, "se")

        def p7a_ve(e):
            bs(e)
            for f in range(NF):
                e.wait_ge(SE.h, se_ua(f))
                if f >= 2:
                    SPST.wait(e, ("h", f - 2))
                VE.inc(e.tensor_tensor(hdn_buf[f % 2][:], ga_sb[f % 2][:], ua_sb[f % 2][:], ALU.mult))
            be(e, "ve")

        block.sync(p7a_sy); block.gpsimd(p7a_gp); block.tensor(p7a_te)
        block.scalar(p7a_se); block.vector(p7a_ve)
        assert PE.n == bPE + NF * 2 and SE.n == bSE + NF * 2 and VE.n == bVE + NF
        p7a.close()
        nphase[0] += 1

        p7b = ExitStack()
        sb8 = lambda n, shp, dt: p7b.enter_context(nc.sbuf_tensor(n, shp, dt))
        ps8 = lambda n, shp, dt=F32: p7b.enter_context(nc.psum_tensor(n, shp, dt))
        dss_sb = sb8("dss_sb", [128, D], F32)
        htb = [sb8(f"htb{i}", [128, IC * 128], BF16) for i in range(2)]
        x_t7 = [sb8(f"x_t7{i}", [128, 512], F32) for i in range(2)]
        s_t7 = [sb8(f"s_t7{i}", [128, 512], BF16) for i in range(2)]
        f1_sb = [sb8(f"f1_sb{i}", [128, 512], F32) for i in range(2)]
        f2_sb = [sb8(f"f2_sb{i}", [128, 512], F32) for i in range(2)]
        ob = [sb8(f"ob{i}", [128, 512], F32) for i in range(2)]
        psD = [ps8(f"psD{i}", [128, 512]) for i in range(2)]

        bLD, bPE, bVE, bSE = LD.n, PE.n, VE.n, SE.n
        NB = 4 * NT
        pe_dm = lambda p: bPE + p + 1
        ve_f1 = lambda p: bVE + p * 2 + 1
        ve_fin = lambda p: bVE + p * 2 + 2
        se_f2 = lambda p: bSE + p + 1

        def p7b_sy(e):
            bs(e)
            LD.inc(e.dma_start(out=dss_sb[:], in_=dss_bc), 16)
            for p in range(NB):
                q, tb = divmod(p, NT)
                if p >= 2:
                    e.wait_ge(PE.h, pe_dm(p - 2))
                    e.wait_ge(VE.h, ve_fin(p - 2))
                g = RST.group(("l", p))
                g(e, htb[p % 2][:].rearrange("p (c t) -> p c t", c=IC), hdn_sp[tb])
                g(e, x_t7[p % 2][:], x_in[tb * 128:(tb + 1) * 128, q * 512:(q + 1) * 512])
                g(e, s_t7[p % 2][:], s_sp[tb][:, q * 512:(q + 1) * 512])
                if p >= 1:
                    e.wait_ge(VE.h, ve_fin(p - 1))
                    pv = p - 1
                    qv, tv = divmod(pv, NT)
                    SPST.group(("o", pv))(
                        e, out[tv * 128:(tv + 1) * 128, qv * 512:(qv + 1) * 512], ob[pv % 2][:])
            e.wait_ge(VE.h, ve_fin(NB - 1))
            SPST.group(("o", NB - 1))(
                e, out[(NT - 1) * 128:NT * 128, 3 * 512:4 * 512], ob[(NB - 1) % 2][:])
            be(e, "sy")

        def p7b_gp(e):
            bs(e)
            for q in range(2, 4):
                e.wait_ge(PE.h, pe_dm((q - 1) * NT - 1))
                dn_quarter_load(e, q)
            be(e, "gp")

        def p7b_te(e):
            bs(e)
            for p in range(NB):
                q, tb = divmod(p, NT)
                DST.wait(e, ("dn", q))
                RST.wait(e, ("l", p))
                if p >= 2:
                    e.wait_ge(VE.h, ve_f1(p - 2))
                last = None
                for ic in range(IC):
                    last = e.matmul(psD[p % 2][:], htb[p % 2][:, ic * 128:(ic + 1) * 128],
                                    dwb[q % 2][:, ic * 512:(ic + 1) * 512],
                                    start=(ic == 0), stop=(ic == IC - 1))
                PE.inc(last)
            be(e, "te")

        def p7b_se(e):
            bs(e)
            for p in range(NB):
                e.wait_ge(VE.h, ve_f1(p))
                if p >= 1:
                    e.wait_ge(VE.h, ve_fin(p - 1))
                SE.inc(e.activation(f2_sb[p % 2][:], f1_sb[p % 2][:], AF.Silu))
            be(e, "se")

        def p7b_ve(e):
            bs(e)
            e.wait_ge(LD.h, bLD + 16)
            for p in range(NB):
                q, tb = divmod(p, NT)
                RST.wait(e, ("l", p))
                dsl = slice(q * 512, (q + 1) * 512)
                e.wait_ge(PE.h, pe_dm(p))
                if p >= 2:
                    e.wait_ge(SE.h, se_f2(p - 2))
                VE.inc(e.tensor_tensor(f1_sb[p % 2][:], psD[p % 2][:],
                                       dss_sb[:, dsl], ALU.mult))
                e.wait_ge(SE.h, se_f2(p))
                if p >= 2:
                    SPST.wait(e, ("o", p - 2))
                e.drain()
                e.tensor_tensor(f2_sb[p % 2][:], f2_sb[p % 2][:], s_t7[p % 2][:], ALU.mult)
                e.drain()
                VE.inc(e.tensor_tensor(ob[p % 2][:], f2_sb[p % 2][:], x_t7[p % 2][:], ALU.add))
            be(e, "ve")

        block.sync(p7b_sy); block.gpsimd(p7b_gp); block.tensor(p7b_te)
        block.scalar(p7b_se); block.vector(p7b_ve)
        assert PE.n == bPE + NB and VE.n == bVE + NB * 2 and SE.n == bSE + NB
        p7b.close()
        p7ab.close()
        nphase[0] += 1


    ctx.close()
    return nc



def _rope_tables(span_start, T):
    pos = np.arange(span_start, span_start + T, dtype=np.float64)
    j = np.arange(64, dtype=np.float64)
    inv_freq = 1.0 / THETA ** (2.0 * j / 128.0)
    freqs = pos[:, None] * inv_freq[None, :]
    cos = np.cos(freqs); sin = np.sin(freqs)
    sbase = (2.0 * j + 0.4 * 128.0) / (1.4 * 128.0)
    power = (pos - S // 2) / SCALE_BASE
    scale = sbase[None, :] ** power[:, None]
    ex = lambda a: np.repeat(a, 2, axis=1)
    cosf, sinf, scf = ex(cos), ex(sin), ex(scale)
    mk = lambda a: np.ascontiguousarray(a.T.astype(np.float32))
    return mk(cosf * scf), mk(sinf * scf), mk(cosf / scf), mk(sinf / scf)


def _const_tables():
    r = np.arange(128)
    d2 = np.zeros((128, H * 128), np.float32)
    grp1 = np.zeros((128, H * 128), np.float32)
    gcm = np.zeros((128, H), np.float32)
    for h, g in enumerate(GAMMAS):
        pm = r[None, :] - r[:, None]
        d2[:, h * 128:(h + 1) * 128] = np.where(pm >= 0, np.float64(g) ** np.maximum(pm, 0), 0.0)
        grp1[:, h * 128:(h + 1) * 128] = (np.float64(g) ** (r + 1.0))[:, None]
        gcm[:, h] = np.float64(g) ** (127.0 - r)
    p_rot = np.zeros((128, 128), np.float32)
    for i in range(64):
        p_rot[2 * i + 1, 2 * i] = -1.0
        p_rot[2 * i, 2 * i + 1] = 1.0
    return d2, grp1, gcm, p_rot, np.eye(128, dtype=np.float32)


def _wm_table(core, gs=SPAN):
    b, sp = core // NSPAN, core % NSPAN
    wm = np.zeros((NCORES * H,), np.float64)
    for cp in range(NCORES):
        bp, spp = cp // NSPAN, cp % NSPAN
        if bp == b and spp < sp:
            for h, g in enumerate(GAMMAS):
                wm[cp * H + h] = np.float64(g) ** (gs * (sp - spp - 1))
    return np.ascontiguousarray(np.broadcast_to(wm.astype(np.float32), (128, NCORES * H)))


def _rs(v):
    return np.ascontiguousarray(np.asarray(v, np.float32).reshape(-1, 128).T)


def _bc(v):
    v = np.asarray(v, np.float32)
    return np.ascontiguousarray(np.broadcast_to(v, (128, v.shape[-1])))


def make_in_maps(inputs, T=SPAN, geom_span=None):
    gs = SPAN if geom_span is None else geom_span
    x = np.ascontiguousarray(np.asarray(inputs["L0_attn_x"], np.float32))
    cf = lambda a: np.ascontiguousarray(np.asarray(a, np.float32))
    shared = {
        "Wq": cf(inputs["L0_attn_Wq"]), "Wk": cf(inputs["L0_attn_Wk"]),
        "Wv": cf(inputs["L0_attn_Wv"]), "Wg": cf(inputs["L0_attn_Wg"]),
        "Wo": cf(inputs["L0_attn_Wo"]),
        "gateT": cf(np.asarray(inputs["L0_ffn_gate_w"], np.float32).T),
        "upT": cf(np.asarray(inputs["L0_ffn_up_w"], np.float32).T),
        "downT": cf(np.asarray(inputs["L0_ffn_down_w"], np.float32).T),
        "bqr": _rs(inputs["L0_attn_bq"]), "bkr": _rs(inputs["L0_attn_bk"]),
        "bgr": _rs(inputs["L0_attn_bg"]),
        "bv_bc": _bc(inputs["L0_attn_bv"]), "bo_bc": _bc(inputs["L0_attn_bo"]),
        "gss": _rs(inputs["L0_ffn_gate_s"]), "uss": _rs(inputs["L0_ffn_up_s"]),
        "dss_bc": _bc(inputs["L0_ffn_down_s"]),
    }
    d2, grp1, gcm, p_rot, ident = _const_tables()
    r_ = np.arange(128)
    grp1v = np.stack([np.float64(g) ** (r_ + 1.0) for g in GAMMAS], axis=1).astype(np.float32)
    shared.update({"d2_t": d2, "grp1_bc": grp1, "grp1v_t": np.ascontiguousarray(grp1v),
                   "gcm_t": gcm, "p_rot": p_rot, "ident": ident})
    rope = [_rope_tables(sp * gs, T) for sp in range(NSPAN)]
    in_maps = []
    for c in range(NCORES):
        b, sp = c // NSPAN, c % NSPAN
        cs, ss_, ics, iss = rope[sp]
        im = dict(shared)
        im["x"] = np.ascontiguousarray(x[b, sp * gs: sp * gs + T])
        im["cs_t"] = cs; im["ss_t"] = ss_; im["ics_t"] = ics; im["iss_t"] = iss
        im["wm_bc"] = _wm_table(c, gs)
        in_maps.append(im)
    return in_maps


def kernel(**inputs):
    nc = build(T=SPAN, debug=False)
    in_maps = make_in_maps(inputs, T=SPAN)
    res = run_bass_kernel_spmd(nc, in_maps, core_ids=list(range(NCORES)))
    out = np.empty((B, S, D), np.float32)
    for c in range(NCORES):
        b, sp = c // NSPAN, c % NSPAN
        out[b, sp * SPAN:(sp + 1) * SPAN] = res.results[c]["out"]
    return out


# version 20 (final)
# speedup vs baseline: 1.0476x; 1.0075x over previous
import numpy as np
import ml_dtypes
from contextlib import ExitStack

import concourse.bass as bass
import concourse.mybir as mybir
from concourse.bass_utils import run_bass_kernel_spmd

F32 = mybir.dt.float32
BF16 = mybir.dt.bfloat16
AF = mybir.ActivationFunctionType
ALU = mybir.AluOpType
AX = mybir.AxisListType

B = 2
S = 8192
D = 2048
H = 16
INTER = 5632
NCORES = 8
NSPAN = 4
SPAN = S // NSPAN
EPS = 1e-5
THETA = 10000.0
SCALE_BASE = 512.0
GAMMAS = [1.0 - 2.0 ** (-5.0 - h) for h in range(H)]
GC = [float(np.float32(np.float64(g) ** 128)) for g in GAMMAS]
KC = D // 128
IC = INTER // 128


class Sem:
    def __init__(self, nc, ctx, name):
        self.h = ctx.enter_context(nc.semaphore(name))
        self.n = 0

    def inc(self, instr, k=1):
        instr.then_inc(self.h, k)
        self.n += k
        return self.n


class DStream:

    def __init__(self, nc, ctx, name):
        self.s = [Sem(nc, ctx, name + "a"), Sem(nc, ctx, name + "b")]
        self.rec = {}
        self.gi = 0

    def group(self, key=None):
        p = self.gi % 2
        self.gi += 1
        if key is None:
            key = self.gi - 1
        sem = self.s[p]

        def emit(e, out, in_):
            v = sem.inc(e.dma_start(out=out, in_=in_), 16)
            self.rec[key] = (sem, v)
        return emit

    def wait(self, e, key):
        sem, v = self.rec[key]
        e.wait_ge(sem.h, v)

    def drain_all(self, e):
        for sem in self.s:
            if sem.n:
                e.wait_ge(sem.h, sem.n)


def build(T=SPAN, debug=False):
    NT = T // 128
    NTG = T // 512
    TPH = NT // 2
    nc = bass.Bass("TRN2", target_bir_lowering=False)

    dram = lambda n, shp, dt=F32, kind="ExternalInput": nc.dram_tensor(n, shp, dt, kind=kind).ap()
    sk = "ExternalOutput" if debug else "Internal"

    x_in = dram("x", [T, D])
    Wq = dram("Wq", [D, D]); Wk = dram("Wk", [D, D]); Wv = dram("Wv", [D, D])
    Wg = dram("Wg", [D, D]); Wo = dram("Wo", [D, D])
    gateT = dram("gateT", [D, INTER]); upT = dram("upT", [D, INTER]); downT = dram("downT", [INTER, D])
    bqr = dram("bqr", [128, 16]); bkr = dram("bkr", [128, 16]); bgr = dram("bgr", [128, 16])
    bv_bc = dram("bv_bc", [128, D]); bo_bc = dram("bo_bc", [128, D])
    gss = dram("gss", [128, IC]); uss = dram("uss", [128, IC]); dss_bc = dram("dss_bc", [128, D])
    cs_t = dram("cs_t", [128, T]); ss_t = dram("ss_t", [128, T])
    ics_t = dram("ics_t", [128, T]); iss_t = dram("iss_t", [128, T])
    d2_t = dram("d2_t", [128, H * 128])
    grp1_bc = dram("grp1_bc", [128, H * 128])
    grp1v_t = dram("grp1v_t", [128, H])
    gcm_t = dram("gcm_t", [128, H])
    wm_bc = dram("wm_bc", [128, NCORES * H])
    p_in = dram("p_rot", [128, 128]); id_in = dram("ident", [128, 128])

    out = dram("out", [T, D], F32, kind="ExternalOutput")

    q_sp = dram("q_sp", [NT, 128, H, 128], BF16, kind=sk)
    k_sp = dram("k_sp", [NT, 128, H, 128], BF16, kind=sk)
    kg_sp = dram("kg_sp", [NT, 128, H, 128], BF16, kind=sk)
    v_sp = dram("v_sp", [NT, 128, H, 128], BF16, kind=sk)
    g_sp = dram("g_sp", [NT, 128, H, 128], BF16, kind=sk)
    xn_sp = dram("xn_sp", [KC, 128, T], BF16, kind=sk)
    s_sp = dram("s_sp", [NT, 128, D], BF16, kind=sk)
    hdn_sp = dram("hdn_sp", [NT, 128, IC, 128], BF16, kind=sk)
    cc_in = nc.dram_tensor("cc_in", [H * 128, 128], F32, kind="Internal")
    cc_out = nc.dram_tensor("cc_out", [NCORES * H * 128, 128], F32, kind="Internal")

    ctx = ExitStack()
    sb = lambda n, shp, dt: ctx.enter_context(nc.sbuf_tensor(n, shp, dt))

    LD = Sem(nc, ctx, "LD")
    SPST = DStream(nc, ctx, "SP")
    KSP = DStream(nc, ctx, "KP")
    RST = DStream(nc, ctx, "RS")
    WST = DStream(nc, ctx, "WS")
    DST = DStream(nc, ctx, "DS")
    MST = DStream(nc, ctx, "MS")
    CCS = Sem(nc, ctx, "CCS")
    GPC = Sem(nc, ctx, "GPC")
    PE = Sem(nc, ctx, "PE")
    VE = Sem(nc, ctx, "VE")
    SE = Sem(nc, ctx, "SE")
    ENGS = ("sy", "gp", "te", "se", "ve")
    BAR = {e: Sem(nc, ctx, f"B_{e}") for e in ENGS}
    nphase = [0]

    def bs(e):
        if nphase[0] > 0:
            for s_ in BAR.values():
                e.wait_ge(s_.h, nphase[0])

    def be(e, which):
        if which == "sy":
            if LD.n:
                e.wait_ge(LD.h, LD.n)
            SPST.drain_all(e)
            KSP.drain_all(e)
            RST.drain_all(e)
        elif which == "gp":
            WST.drain_all(e)
            DST.drain_all(e)
            MST.drain_all(e)
            if GPC.n:
                e.wait_ge(GPC.h, GPC.n)
        else:
            st = {"te": PE, "se": SE, "ve": VE}[which]
            if st.n > 0:
                e.wait_ge(st.h, st.n)
        BAR[which].inc(e.nop(), 1)

    bqr_sb = sb("bqr_sb", [128, 16], F32); bkr_sb = sb("bkr_sb", [128, 16], F32)
    bgr_sb = sb("bgr_sb", [128, 16], F32)
    gcm_sb = sb("gcm_sb", [128, H], F32)
    gss_sb = sb("gss_sb", [128, IC], F32); uss_sb = sb("uss_sb", [128, IC], F32)
    wm_sb = sb("wm_sb", [128, NCORES * H], F32)
    p_bf = sb("p_bf", [128, 128], BF16); id_bf = sb("id_bf", [128, 128], BF16)
    m_sb = sb("m_sb", [128, H * 128], F32)
    px = ExitStack()
    xn_fm = px.enter_context(nc.sbuf_tensor("xn_fm", [128, KC * T], BF16))

    with nc.Block() as block:
        def p0_sy(e):
            for dst, src in [(bqr_sb, bqr), (bkr_sb, bkr), (bgr_sb, bgr), (gcm_sb, gcm_t),
                             (gss_sb, gss), (uss_sb, uss), (wm_sb, wm_bc)]:
                LD.inc(e.dma_start(out=dst[:], in_=src), 16)
            be(e, "sy")
        def p0_gp(e):
            g = WST.group("p0")
            g(e, p_bf[:], p_in)
            g(e, id_bf[:], id_in)
            be(e, "gp")
        block.sync(p0_sy); block.gpsimd(p0_gp)
        block.tensor(lambda e: be(e, "te")); block.scalar(lambda e: be(e, "se"))
        block.vector(lambda e: be(e, "ve"))
        nphase[0] += 1

        pa_scope = ExitStack()
        sbA = lambda n, shp, dt: pa_scope.enter_context(nc.sbuf_tensor(n, shp, dt, side="right"))
        wvb = [sbA(f"wvb{i}", [128, KC * 512], BF16) for i in range(2)]
        wkb = [sbA(f"wkb{i}", [128, KC * 512], BF16) for i in range(2)]

        p1 = ExitStack()
        sb1 = lambda n, shp, dt: p1.enter_context(nc.sbuf_tensor(n, shp, dt))
        ps1 = lambda n, shp, dt=F32: p1.enter_context(nc.psum_tensor(n, shp, dt))
        NGRP = 4
        TPH = max(NT // NGRP, 1)
        NGRP = NT // TPH
        x_b = [sb1(f"x_b{i}", [128, D], F32) for i in range(TPH)]
        xn_tok = [sb1(f"xn_tok{i}", [128, D], BF16) for i in range(2)]
        sq_sc = sb1("sq_sc", [128, D], F32)
        st_sum = sb1("st_sum", [128, NT], F32); st_ssq = sb1("st_ssq", [128, NT], F32)
        st_mu = sb1("st_mu", [128, NT], F32); st_rs = sb1("st_rs", [128, NT], F32)
        st_nv = sb1("st_nv", [128, NT], F32)
        ps_tp = [ps1(f"ps_tp{i}", [128, 128], BF16) for i in range(2)]

        bLD, bPE, bVE, bSE = LD.n, PE.n, VE.n, SE.n
        ld_half = lambda hh: bLD + (hh + 1) * TPH * 16
        se_sq = lambda t: bSE + (t // TPH) * (TPH + 1) + (t % TPH) + 1
        se_rstd = lambda hh: bSE + hh * (TPH + 1) + TPH + 1
        ve_negv = lambda hh: bVE + hh * (1 + TPH * 17) + 1
        ve_xn = lambda t: ve_negv(t // TPH) + (t % TPH) * 17 + 1
        ve_evac = lambda t, j: ve_xn(t) + j + 1
        pe_tp1 = lambda t, j: bPE + (t * 16 + j) + 1

        def p1_sy(e):
            bs(e)
            for t in range(NT):
                if t >= TPH:
                    e.wait_ge(VE.h, ve_xn(t - TPH))
                    e.wait_ge(SE.h, se_sq(t - TPH))
                LD.inc(e.dma_start(out=x_b[t % TPH][:], in_=x_in[t * 128:(t + 1) * 128, :]), 16)
            be(e, "sy")

        def p1_gp(e):
            bs(e)
            WST.group(("wv", 0))(
                e, wvb[0][:].rearrange("p (k m) -> p k m", k=KC),
                Wv[:, 0:512].rearrange("(k p) m -> p k m", p=128))
            WST.group(("wk", 0))(
                e, wkb[0][:].rearrange("p (k m) -> p k m", k=KC),
                Wk[:, 0:512].rearrange("(k p) m -> p k m", p=128))
            be(e, "gp")

        def p1_te(e):
            bs(e)
            for t in range(NT):
                e.wait_ge(VE.h, ve_xn(t))
                for j in range(16):
                    f = t * 16 + j
                    if f >= 2:
                        pt, pj = divmod(f - 2, 16)
                        e.wait_ge(VE.h, ve_evac(pt, pj))
                    PE.inc(e.transpose(ps_tp[f % 2][:], xn_tok[t % 2][:, j * 128:(j + 1) * 128],
                                       id_bf[:]))
            be(e, "te")

        def p1_se(e):
            bs(e)
            for hh in range(NGRP):
                e.wait_ge(LD.h, ld_half(hh))
                for i in range(TPH):
                    t = hh * TPH + i
                    e.drain()
                    SE.inc(e.activation(sq_sc[:], x_b[i][:], AF.Square,
                                        accum_out=st_ssq[:, t:t + 1]))
                sl = slice(hh * TPH, hh * TPH + TPH)
                e.wait_ge(VE.h, ve_negv(hh))
                e.activation(st_nv[:, sl], st_nv[:, sl], AF.Ln, scale=-1.0)
                e.drain()
                SE.inc(e.activation(st_rs[:, sl], st_nv[:, sl], AF.Exp, scale=-0.5))
            be(e, "se")

        def p1_ve(e):
            bs(e)
            for hh in range(NGRP):
                sl = slice(hh * TPH, hh * TPH + TPH)
                e.wait_ge(LD.h, ld_half(hh))
                for i in range(TPH):
                    t = hh * TPH + i
                    e.reduce_sum(st_sum[:, t:t + 1], x_b[i][:], axis=AX.X)
                e.wait_ge(SE.h, se_sq(hh * TPH + TPH - 1))
                e.drain()
                e.tensor_scalar(st_mu[:, sl], st_sum[:, sl], 1.0 / D, None, ALU.mult)
                e.tensor_scalar(st_ssq[:, sl], st_ssq[:, sl], 1.0 / D, None, ALU.mult)
                e.drain()
                e.tensor_tensor(st_nv[:, sl], st_mu[:, sl], st_mu[:, sl], ALU.mult)
                e.drain()
                e.tensor_tensor(st_nv[:, sl], st_nv[:, sl], st_ssq[:, sl], ALU.subtract)
                e.drain()
                VE.inc(e.tensor_scalar(st_nv[:, sl], st_nv[:, sl], EPS, None, ALU.subtract))
                for i in range(TPH):
                    t = hh * TPH + i
                    e.wait_ge(SE.h, se_rstd(hh))
                    e.drain()
                    VE.inc(e.tensor_scalar(xn_tok[t % 2][:], x_b[i][:], st_mu[:, t:t + 1],
                                           st_rs[:, t:t + 1], ALU.subtract, ALU.mult))
                    for j in range(16):
                        e.wait_ge(PE.h, pe_tp1(t, j))
                        VE.inc(e.tensor_copy(xn_fm[:, j * T + t * 128: j * T + (t + 1) * 128],
                                             ps_tp[(t * 16 + j) % 2][:]))
            be(e, "ve")

        block.sync(p1_sy); block.gpsimd(p1_gp); block.tensor(p1_te)
        block.scalar(p1_se); block.vector(p1_ve)
        assert SE.n == bSE + NGRP * (TPH + 1) and VE.n == bVE + NGRP * (1 + TPH * 17)
        assert PE.n == bPE + NT * 16 and LD.n == bLD + NT * 16
        p1.close()
        nphase[0] += 1

        pa = ExitStack()
        sbp = lambda n, shp, dt: pa.enter_context(nc.sbuf_tensor(n, shp, dt))
        psp = lambda n, shp, dt=F32: pa.enter_context(nc.psum_tensor(n, shp, dt))
        c1_sb = sbp("c1_sb", [128, T], F32); c2_sb = sbp("c2_sb", [128, T], F32)
        bv_sb = sbp("bv_sb", [128, D], F32)
        vqb = [sbp(f"vqb{i}", [128, NT * 512], BF16) for i in range(2)]
        kbf = [sbp(f"kbf{i}", [128, 512], BF16) for i in range(2)]
        ta = sbp("ta", [128, 512], F32); tb_ = sbp("tb", [128, 512], F32)
        fmb = [sbp(f"fmb{i}", [128, 512], BF16) for i in range(2)]
        kgb = [sbp(f"kgb{i}", [128, 512], BF16) for i in range(2)]
        psV = [psp(f"psV{i}", [128, 512]) for i in range(2)]
        psK = [psp(f"psK{i}", [128, 512]) for i in range(2)]
        psR = psp("psR", [128, 512])
        psT = psp("psT", [128, 512], BF16)
        psU = psp("psU", [128, 512])

        NIT = 16 * NTG
        bPE, bVE, bSE = PE.n, VE.n, SE.n
        pe_ev = {}; _pectr = [bPE]
        ve_ev = {}; _vectr = [bVE]
        se_ev = {}; _sectr = [bSE]
        def _mk(ctr, ev):
            def f(key):
                ctr[0] += 1
                ev[key] = ctr[0]
            return f
        _pe = _mk(_pectr, pe_ev); _ve = _mk(_vectr, ve_ev); _se = _mk(_sectr, se_ev)
        SHIFT = 16 * NTG // 4
        SHIFT = NT
        for i in range(NIT + SHIFT):
            if i < NIT:
                _pe(("v", i))
            if i >= SHIFT:
                it = i - SHIFT
                _pe(("kproj", it))
                if it >= 1:
                    _pe(("krot", it - 1)); _pe(("ktp", it - 1)); _pe(("kum", it - 1))
        _pe(("krot", NIT - 1)); _pe(("ktp", NIT - 1)); _pe(("kum", NIT - 1))
        for i in range(NIT + SHIFT):
            if i < NIT:
                _ve(("vev", i))
            if i >= SHIFT:
                it = i - SHIFT
                _ve(("fm", it)); _ve(("kg", it)); _ve(("mu", it))
        for it in range(NIT):
            _se(("kbf", it))

        def pa_sy(e):
            bs(e)
            LD.inc(e.dma_start(out=c1_sb[:], in_=ics_t), 16)
            LD.inc(e.dma_start(out=c2_sb[:], in_=iss_t), 16)
            LD.inc(e.dma_start(out=bv_sb[:], in_=bv_bc), 16)
            for i in range(NIT + SHIFT):
                if i < NIT:
                    vg, t = divmod(i, NT)
                    e.wait_ge(VE.h, ve_ev[("vev", i)])
                    SPST.group(("v", i))(
                        e, v_sp[t, :, 4 * vg:4 * vg + 4, :],
                        vqb[vg % 2][:, t * 512:(t + 1) * 512])
                if i >= SHIFT:
                    it = i - SHIFT
                    qd, rem = divmod(it, 4 * NTG)
                    h, tg = divmod(rem, NTG)
                    head = qd * 4 + h
                    g = KSP.group(("sp", it))
                    e.wait_ge(VE.h, ve_ev[("fm", it)])
                    g(e, k_sp[tg * 4:(tg + 1) * 4, :, head, :].rearrange("n k t -> k n t"),
                      fmb[it % 2][:].rearrange("p (n t) -> p n t", n=4))
                    e.wait_ge(VE.h, ve_ev[("kg", it)])
                    g(e, kg_sp[tg * 4:(tg + 1) * 4, :, head, :].rearrange("n t k -> t n k"),
                      kgb[it % 2][:].rearrange("p (n k) -> p n k", n=4))
            e.wait_ge(VE.h, ve_ev[("mu", NIT - 1)])
            LD.inc(e.dma_start(out=cc_in.ap().rearrange("(h k) v -> k h v", h=H),
                               in_=m_sb[:].rearrange("p (h v) -> p h v", h=H)), 16)
            be(e, "sy")

        ld_m_total = [None]

        def pa_gp(e):
            bs(e)
            for q in range(1, 4):
                e.wait_ge(PE.h, pe_ev[("v", (q - 1) * NT)])
                WST.group(("wv", q))(
                    e, wvb[q % 2][:].rearrange("p (k m) -> p k m", k=KC),
                    Wv[:, q * 512:(q + 1) * 512].rearrange("(k p) m -> p k m", p=128))
                e.wait_ge(PE.h, pe_ev[("kproj", max((q - 1) * 4 * NTG, 0))])
                WST.group(("wk", q))(
                    e, wkb[q % 2][:].rearrange("p (k m) -> p k m", k=KC),
                    Wk[:, q * 512:(q + 1) * 512].rearrange("(k p) m -> p k m", p=128))
            e.wait_ge(LD.h, ld_m_total[0])
            CCS.inc(e.collective_compute(
                "AllGather", ALU.bypass, replica_groups=[list(range(NCORES))],
                ins=[cc_in.ap().opt()], outs=[cc_out.ap().opt()]), 1)
            be(e, "gp")

        def k_tail(e, it):
            qd, rem = divmod(it, 4 * NTG)
            h, tg = divmod(rem, NTG)
            e.wait_ge(SE.h, se_ev[("kbf", it)])
            if it >= 1:
                e.wait_ge(VE.h, ve_ev[("fm", it - 1)])
            PE.inc(e.matmul(psR[:], p_bf[:], kbf[it % 2][:], start=True, stop=True))
            e.wait_ge(VE.h, ve_ev[("fm", it)])
            if it >= 1:
                e.wait_ge(VE.h, ve_ev[("kg", it - 1)])
            last = None
            for j in range(4):
                last = e.transpose(psT[:, j * 128:(j + 1) * 128],
                                   fmb[it % 2][:, j * 128:(j + 1) * 128], id_bf[:])
            PE.inc(last)
            e.wait_ge(VE.h, ve_ev[("kg", it)])
            e.wait_ge(VE.h, ve_ev[("vev", (qd + 1) * NT - 1)])
            if it >= 1:
                e.wait_ge(VE.h, ve_ev[("mu", it - 1)])
            last = None
            for j in range(4):
                n_ = tg * 4 + j
                last = e.matmul(psU[:, j * 128:(j + 1) * 128],
                                kgb[it % 2][:, j * 128:(j + 1) * 128],
                                vqb[qd % 2][:, n_ * 512 + h * 128: n_ * 512 + (h + 1) * 128],
                                start=(j == 0), stop=(j == 3))
            PE.inc(last)

        def pa_te(e):
            bs(e)
            for i in range(NIT + SHIFT):
                if i < NIT:
                    vg, t = divmod(i, NT)
                    WST.wait(e, ("wv", vg))
                    if i >= 2:
                        e.wait_ge(VE.h, ve_ev[("vev", i - 2)])
                    last = None
                    for k in range(KC):
                        last = e.matmul(psV[i % 2][:],
                                        xn_fm[:, k * T + t * 128: k * T + (t + 1) * 128],
                                        wvb[vg % 2][:, k * 512:(k + 1) * 512],
                                        start=(k == 0), stop=(k == KC - 1))
                    PE.inc(last)
                if i >= SHIFT:
                    it = i - SHIFT
                    qd, rem = divmod(it, 4 * NTG)
                    h, tg = divmod(rem, NTG)
                    WST.wait(e, ("wk", qd))
                    if it >= 2:
                        e.wait_ge(SE.h, se_ev[("kbf", it - 2)])
                    last = None
                    for k in range(KC):
                        last = e.matmul(psK[it % 2][:],
                                        wkb[qd % 2][:, k * 512 + h * 128: k * 512 + (h + 1) * 128],
                                        xn_fm[:, k * T + tg * 512: k * T + (tg + 1) * 512],
                                        start=(k == 0), stop=(k == KC - 1))
                    PE.inc(last)
                    if it >= 1:
                        k_tail(e, it - 1)
            k_tail(e, NIT - 1)
            be(e, "te")

        def pa_se(e):
            bs(e)
            for it in range(NIT):
                qd, rem = divmod(it, 4 * NTG)
                h, tg = divmod(rem, NTG)
                head = qd * 4 + h
                e.wait_ge(PE.h, pe_ev[("kproj", it)])
                if it >= 2:
                    e.wait_ge(PE.h, pe_ev[("krot", it - 2)])
                    e.wait_ge(VE.h, ve_ev[("fm", it - 2)])
                SE.inc(e.activation(kbf[it % 2][:], psK[it % 2][:], AF.Identity,
                                    bias=bkr_sb[:, head:head + 1]))
            be(e, "se")

        def pa_ve(e):
            bs(e)
            e.memset(m_sb[:], 0.0)
            e.wait_ge(LD.h, bLD_pa + 48)
            for i in range(NIT + SHIFT):
                if i < NIT:
                    vg, t = divmod(i, NT)
                    e.wait_ge(PE.h, pe_ev[("v", i)])
                    if i >= 2:
                        SPST.wait(e, ("v", i - 2))
                    if i >= 2 * NT and t == 0:
                        e.wait_ge(PE.h, pe_ev[("kum", (vg - 1) * 4 * NTG - 1)])
                    VE.inc(e.tensor_tensor(vqb[vg % 2][:, t * 512:(t + 1) * 512], psV[i % 2][:],
                                           bv_sb[:, vg * 512:(vg + 1) * 512], ALU.add))
                if i >= SHIFT:
                    it = i - SHIFT
                    qd, rem = divmod(it, 4 * NTG)
                    h, tg = divmod(rem, NTG)
                    head = qd * 4 + h
                    tsl = slice(tg * 512, (tg + 1) * 512)
                    e.wait_ge(PE.h, pe_ev[("krot", it)])
                    if it >= 2:
                        KSP.wait(e, ("sp", it - 2))
                        e.wait_ge(PE.h, pe_ev[("ktp", it - 2)])
                    e.tensor_tensor(ta[:], kbf[it % 2][:], c1_sb[:, tsl], ALU.mult)
                    e.tensor_tensor(tb_[:], psR[:], c2_sb[:, tsl], ALU.mult)
                    e.drain()
                    VE.inc(e.tensor_tensor(fmb[it % 2][:], ta[:], tb_[:], ALU.add))
                    e.wait_ge(PE.h, pe_ev[("ktp", it)])
                    if it >= 2:
                        e.wait_ge(PE.h, pe_ev[("kum", it - 2)])
                    VE.inc(e.tensor_scalar(kgb[it % 2][:], psT[:], gcm_sb[:, head:head + 1],
                                           None, ALU.mult))
                    e.wait_ge(PE.h, pe_ev[("kum", it)])
                    last = None
                    for j in range(4):
                        if j > 0:
                            e.drain()
                        hsl = slice(head * 128, (head + 1) * 128)
                        last = e.scalar_tensor_tensor(m_sb[:, hsl], m_sb[:, hsl], GC[head],
                                                      psU[:, j * 128:(j + 1) * 128],
                                                      ALU.mult, ALU.add)
                    VE.inc(last)
            be(e, "ve")

        bLD_pa = LD.n
        block.sync(pa_sy)
        ld_m_total[0] = LD.n
        block.gpsimd(pa_gp); block.tensor(pa_te)
        block.scalar(pa_se); block.vector(pa_ve)
        assert PE.n == _pectr[0] and VE.n == _vectr[0] and SE.n == _sectr[0]
        pa.close()
        pa_scope.close()
        nphase[0] += 1

        p56 = ExitStack()
        sb56 = lambda n, shp, dt: p56.enter_context(nc.sbuf_tensor(n, shp, dt, side="right"))
        s_f32 = sb56("s_f32", [128, H * 128], F32)
        s_bfT = sb56("s_bfT", [128, H * 128], BF16)

        pb = ExitStack()
        sbq = lambda n, shp, dt: pb.enter_context(nc.sbuf_tensor(n, shp, dt))
        psq = lambda n, shp, dt=F32: pb.enter_context(nc.psum_tensor(n, shp, dt))
        wqb = [sbq(f"wqb{i}", [128, KC * 512], BF16) for i in range(2)]
        wgb = [sbq(f"wgb{i}", [128, KC * 512], BF16) for i in range(2)]
        mg_buf = [sbq(f"mg{i}", [128, H * 128], F32) for i in range(2)]
        c1q = sbq("c1q", [128, T], F32); c2q = sbq("c2q", [128, T], F32)
        qbf = [sbq(f"qbf{i}", [128, 512], BF16) for i in range(2)]
        taq = sbq("taq", [128, 512], F32); tbq = sbq("tbq", [128, 512], F32)
        fmq = [sbq(f"fmq{i}", [128, 512], BF16) for i in range(2)]
        g_buf = [sbq(f"g_buf{i}", [128, 512], BF16) for i in range(2)]
        psQ = [psq(f"psQ{i}", [128, 512]) for i in range(2)]
        psRq = psq("psRq", [128, 512])
        psG5 = [psq(f"psG5{i}", [128, 512]) for i in range(2)]

        bPE, bVE, bSE = PE.n, VE.n, SE.n
        pe_ev = {}; _pectr = [bPE]
        ve_ev = {}; _vectr = [bVE]
        se_ev = {}; _sectr = [bSE]
        _pe = _mk(_pectr, pe_ev); _ve = _mk(_vectr, ve_ev); _se = _mk(_sectr, se_ev)
        for it in range(NIT):
            _pe(("qproj", it))
            if it >= 1:
                _pe(("qrot", it - 1))
            _pe(("gproj", it))
        _pe(("qrot", NIT - 1))
        S0C = {}
        for c in range(NCORES):
            S0C.setdefault(4 * NTG + (c * 4 * NTG) // NCORES, []).append(c)
        for it in range(NIT):
            _ve(("fm", it))
            for c in S0C.get(it, []):
                _ve(("s0", c))
                if c == NCORES - 1:
                    _ve(("sbf0",))
        for it in range(NIT):
            _se(("qbf", it)); _se(("gev", it))

        def pb_sy(e):
            bs(e)
            LD.inc(e.dma_start(out=c1q[:], in_=cs_t), 16)
            LD.inc(e.dma_start(out=c2q[:], in_=ss_t), 16)
            for it in range(NIT):
                qd, rem = divmod(it, 4 * NTG)
                h, tg = divmod(rem, NTG)
                head = qd * 4 + h
                g = KSP.group(("spq", it))
                e.wait_ge(VE.h, ve_ev[("fm", it)])
                g(e, q_sp[tg * 4:(tg + 1) * 4, :, head, :].rearrange("n k t -> k n t"),
                  fmq[it % 2][:].rearrange("p (n t) -> p n t", n=4))
                e.wait_ge(SE.h, se_ev[("gev", it)])
                g(e, g_sp[tg * 4:(tg + 1) * 4, :, head, :].rearrange("n g t -> g n t"),
                  g_buf[it % 2][:].rearrange("p (n t) -> p n t", n=4))
            SPST.group("xn")(e, xn_sp.rearrange("k p t -> p k t"),
                             xn_fm[:].rearrange("p (k t) -> p k t", k=KC))
            be(e, "sy")

        def pb_gp(e):
            bs(e)
            for q in range(4):
                if q >= 1:
                    e.wait_ge(PE.h, pe_ev[("qproj", (q - 1) * 4 * NTG)])
                WST.group(("wq", q))(
                    e, wqb[q % 2][:].rearrange("p (k m) -> p k m", k=KC),
                    Wq[:, q * 512:(q + 1) * 512].rearrange("(k p) m -> p k m", p=128))
                if q >= 1:
                    e.wait_ge(PE.h, pe_ev[("gproj", (q - 1) * 4 * NTG)])
                WST.group(("wg", q))(
                    e, wgb[q % 2][:].rearrange("p (k m) -> p k m", k=KC),
                    Wg[:, q * 512:(q + 1) * 512].rearrange("(k p) m -> p k m", p=128))
                if q == 1:
                    e.wait_ge(CCS.h, 1)
                    for c in range(NCORES):
                        if c >= 2:
                            e.wait_ge(VE.h, ve_ev[("s0", c - 2)])
                        MST.group(("mg", c))(
                            e, mg_buf[c % 2][:].rearrange("p (h v) -> p h v", h=H),
                            cc_out.ap().rearrange("(c h k) v -> c k h v", c=NCORES, h=H)[c])
            be(e, "gp")

        def q_rot(e, it):
            e.wait_ge(SE.h, se_ev[("qbf", it)])
            if it >= 1:
                e.wait_ge(VE.h, ve_ev[("fm", it - 1)])
            PE.inc(e.matmul(psRq[:], p_bf[:], qbf[it % 2][:], start=True, stop=True))

        def pb_te(e):
            bs(e)
            for it in range(NIT):
                qd, rem = divmod(it, 4 * NTG)
                h, tg = divmod(rem, NTG)
                WST.wait(e, ("wq", qd))
                if it >= 2:
                    e.wait_ge(SE.h, se_ev[("qbf", it - 2)])
                last = None
                for k in range(KC):
                    last = e.matmul(psQ[it % 2][:],
                                    wqb[qd % 2][:, k * 512 + h * 128: k * 512 + (h + 1) * 128],
                                    xn_fm[:, k * T + tg * 512: k * T + (tg + 1) * 512],
                                    start=(k == 0), stop=(k == KC - 1))
                PE.inc(last)
                if it >= 1:
                    q_rot(e, it - 1)
                WST.wait(e, ("wg", qd))
                if it >= 2:
                    e.wait_ge(SE.h, se_ev[("gev", it - 2)])
                last = None
                for k in range(KC):
                    last = e.matmul(psG5[it % 2][:],
                                    wgb[qd % 2][:, k * 512 + h * 128: k * 512 + (h + 1) * 128],
                                    xn_fm[:, k * T + tg * 512: k * T + (tg + 1) * 512],
                                    start=(k == 0), stop=(k == KC - 1))
                PE.inc(last)
            q_rot(e, NIT - 1)
            be(e, "te")

        def pb_se(e):
            bs(e)
            for it in range(NIT):
                qd, rem = divmod(it, 4 * NTG)
                h, tg = divmod(rem, NTG)
                head = qd * 4 + h
                e.wait_ge(PE.h, pe_ev[("qproj", it)])
                if it >= 2:
                    e.wait_ge(PE.h, pe_ev[("qrot", it - 2)])
                    e.wait_ge(VE.h, ve_ev[("fm", it - 2)])
                SE.inc(e.activation(qbf[it % 2][:], psQ[it % 2][:], AF.Identity,
                                    bias=bqr_sb[:, head:head + 1]))
                e.wait_ge(PE.h, pe_ev[("gproj", it)])
                if it >= 2:
                    KSP.wait(e, ("spq", it - 2))
                SE.inc(e.activation(g_buf[it % 2][:], psG5[it % 2][:], AF.Silu,
                                    bias=bgr_sb[:, head:head + 1]))
            be(e, "se")

        def pb_ve(e):
            bs(e)
            e.wait_ge(LD.h, bLD_pb + 32)
            e.memset(s_f32[:], 0.0)
            for it in range(NIT):
                qd, rem = divmod(it, 4 * NTG)
                h, tg = divmod(rem, NTG)
                tsl = slice(tg * 512, (tg + 1) * 512)
                e.wait_ge(PE.h, pe_ev[("qrot", it)])
                if it >= 2:
                    KSP.wait(e, ("spq", it - 2))
                e.tensor_tensor(taq[:], qbf[it % 2][:], c1q[:, tsl], ALU.mult)
                e.tensor_tensor(tbq[:], psRq[:], c2q[:, tsl], ALU.mult)
                e.drain()
                VE.inc(e.tensor_tensor(fmq[it % 2][:], taq[:], tbq[:], ALU.add))
                for c in S0C.get(it, []):
                    MST.wait(e, ("mg", c))
                    e.drain()
                    last = None
                    for h in range(H):
                        hsl = slice(h * 128, (h + 1) * 128)
                        last = e.scalar_tensor_tensor(s_f32[:, hsl], mg_buf[c % 2][:, hsl],
                                                      wm_sb[:, c * H + h:c * H + h + 1],
                                                      s_f32[:, hsl], ALU.mult, ALU.add)
                    VE.inc(last)
                    if c == NCORES - 1:
                        e.drain()
                        VE.inc(e.tensor_copy(s_bfT[:], s_f32[:]))
            be(e, "ve")

        bLD_pb = LD.n
        block.sync(pb_sy); block.gpsimd(pb_gp); block.tensor(pb_te)
        block.scalar(pb_se); block.vector(pb_ve)
        assert PE.n == _pectr[0] and VE.n == _vectr[0] and SE.n == _sectr[0]
        pb.close()
        px.close()
        nphase[0] += 1


        p6 = ExitStack()
        sb6 = lambda n, shp, dt: p6.enter_context(nc.sbuf_tensor(n, shp, dt))
        ps6 = lambda n, shp, dt=F32: p6.enter_context(nc.psum_tensor(n, shp, dt))
        wo_bf = sb6("wo_bf", [128, KC * D], BF16)
        bo_sb = sb6("bo_sb", [128, D], F32)
        d2_sb = sb6("d2_sb", [128, H * 128], F32)
        grp1v = sb6("grp1v", [128, H], F32)
        chq = [sb6(f"chq{i}", [128, H * 128], BF16) for i in range(2)]
        chk = [sb6(f"chk{i}", [128, H * 128], BF16) for i in range(2)]
        chkg = [sb6(f"chkg{i}", [128, H * 128], BF16) for i in range(2)]
        chv = [sb6(f"chv{i}", [128, H * 128], BF16) for i in range(2)]
        chg = [sb6(f"chg{i}", [128, H * 128], BF16) for i in range(2)]
        a2_sb = [sb6(f"a2_{i}", [128, 512], BF16) for i in range(2)]
        ret1 = sb6("ret1", [128, 512], F32)
        ret_all = [sb6(f"ret_all{i}", [128, H * 128], F32) for i in range(2)]
        sq_all = sb6("sq_all", [128, H * 128], F32)
        gn_all = sb6("gn_all", [128, H * 128], BF16)
        prod_fm = sb6("prod_fm", [128, H * 128], BF16)
        s_buf = [sb6(f"s_buf{i}", [128, D], BF16) for i in range(2)]
        st16 = sb6("st16", [128, 16 * 6], F32)
        psS = [ps6(f"psS{i}", [128, 512]) for i in range(2)]
        psRr = ps6("psRr", [128, 512])
        psC = ps6("psC", [128, 512])
        psU6 = [ps6(f"psU6{i}", [128, 512]) for i in range(2)]
        psT6 = ps6("psT6", [128, 512], BF16)
        psW = ps6("psW", [128, 512])
        c_sum = slice(0, 16); c_ssq = slice(16, 32); c_mu = slice(32, 48)
        c_nv = slice(48, 64); c_rs = slice(64, 80); c_nmr = slice(80, 96)

        bLD6, bPE, bVE, bSE, bGP = LD.n, PE.n, VE.n, SE.n, GPC.n
        ve_sbf0_val = bVE
        pe_ev = {}; _pectr = [bPE]
        ve_ev = {}; _vectr = [bVE]
        se_ev = {}; _sectr = [bSE]
        gp_ev = {}; _gpctr = [bGP]
        def _mk(ctr, ev):
            def f(key):
                ctr[0] += 1
                ev[key] = ctr[0]
            return f
        _pe = _mk(_pectr, pe_ev); _ve = _mk(_vectr, ve_ev)
        _se = _mk(_sectr, se_ev); _gp = _mk(_gpctr, gp_ev)
        for n in range(NT):
            for hq in range(4):
                _pe(("sc", n, hq)); _pe(("rt", n, hq)); _pe(("cx", n, hq)); _pe(("um", n, hq))
            if n >= 1:
                for tg4 in range(4):
                    _pe(("tp", n - 1, tg4))
                for dmg in range(4):
                    _pe(("wo", n - 1, dmg))
        for tg4 in range(4):
            _pe(("tp", NT - 1, tg4))
        for dmg in range(4):
            _pe(("wo", NT - 1, dmg))
        for n in range(NT):
            for hq in range(4):
                _ve(("a2", n, hq)); _ve(("retf", n, hq)); _ve(("supd", n, hq))
            if n >= 1:
                for tg4 in range(4):
                    _ve(("prod", n - 1, tg4))
            _ve(("red", n))
            if n >= 1:
                for dmg in range(4):
                    _ve(("sev", n - 1, dmg))
        for tg4 in range(4):
            _ve(("prod", NT - 1, tg4))
        for dmg in range(4):
            _ve(("sev", NT - 1, dmg))
        for n in range(NT):
            for hq in range(4):
                _se(("ev", n, hq))
            _se(("rstd", n))
            for h in range(H):
                _se(("gn", n, h))
        for n in range(NT):
            _gp(("sq", n)); _gp(("stats", n)); _gp(("nmr", n))

        def p6_sy(e):
            bs(e)
            LD.inc(e.dma_start(out=bo_sb[:], in_=bo_bc), 16)
            LD.inc(e.dma_start(out=d2_sb[:], in_=d2_t), 16)
            LD.inc(e.dma_start(out=grp1v[:], in_=grp1v_t), 16)
            def ch_load(n):
                g = RST.group(("ch", n))
                for spill, buf in [(q_sp, chq), (k_sp, chk), (kg_sp, chkg),
                                   (v_sp, chv), (g_sp, chg)]:
                    g(e, buf[n % 2][:].rearrange("p (h t) -> p h t", h=H), spill[n])
            ch_load(0)
            if NT > 1:
                ch_load(1)
            for n in range(NT):
                if n + 2 < NT:
                    e.wait_ge(PE.h, pe_ev[("um", n, 3)])
                    e.wait_ge(VE.h, ve_ev[("prod", n, 3)])
                    ch_load(n + 2)
                if n >= 1:
                    e.wait_ge(VE.h, ve_ev[("sev", n - 1, 3)])
                    SPST.group(("s", n - 1))(e, s_sp[n - 1], s_buf[(n - 1) % 2][:])
            e.wait_ge(VE.h, ve_ev[("sev", NT - 1, 3)])
            SPST.group(("s", NT - 1))(e, s_sp[NT - 1], s_buf[(NT - 1) % 2][:])
            be(e, "sy")

        def pk4(e, mk, start=True, stop=True):
            last = None
            for j in range(4):
                o, l, r = mk(j)
                last = e.matmul(o, l, r, start=(start and j == 0), stop=(stop and j == 3))
            return last

        def p6_te_core(e, n):
            def sl4(buf, hq, j):
                return buf[n % 2][:, (hq * 4 + j) * 128:(hq * 4 + j + 1) * 128]
            def mm_sc(hq):
                g = n * 4 + hq
                RST.wait(e, ("ch", n))
                if g >= 2:
                    pn, phq = divmod(g - 2, 4)
                    e.wait_ge(VE.h, ve_ev[("a2", pn, phq)])
                PE.inc(pk4(e, lambda j: (psS[hq % 2][:, j * 128:(j + 1) * 128],
                                         sl4(chk, hq, j), sl4(chq, hq, j))))
            def mm_rt(hq):
                g = n * 4 + hq
                e.wait_ge(VE.h, ve_ev[("a2", n, hq)])
                if g >= 1:
                    pn, phq = divmod(g - 1, 4)
                    e.wait_ge(SE.h, se_ev[("ev", pn, phq)])
                PE.inc(pk4(e, lambda j: (psRr[:, j * 128:(j + 1) * 128],
                                         a2_sb[hq % 2][:, j * 128:(j + 1) * 128],
                                         sl4(chv, hq, j))))
            def mm_cx(hq):
                g = n * 4 + hq
                if n == 0:
                    if hq == 0:
                        e.wait_ge(VE.h, ve_sbf0_val)
                else:
                    e.wait_ge(VE.h, ve_ev[("supd", n - 1, hq)])
                if g >= 1:
                    pn, phq = divmod(g - 1, 4)
                    e.wait_ge(VE.h, ve_ev[("retf", pn, phq)])
                PE.inc(pk4(e, lambda j: (psC[:, j * 128:(j + 1) * 128],
                                         sl4(chq, hq, j),
                                         s_bfT[:, (hq * 4 + j) * 128:(hq * 4 + j + 1) * 128])))
            def mm_um(hq):
                g = n * 4 + hq
                if g >= 2:
                    pn, phq = divmod(g - 2, 4)
                    e.wait_ge(VE.h, ve_ev[("supd", pn, phq)])
                PE.inc(pk4(e, lambda j: (psU6[hq % 2][:, j * 128:(j + 1) * 128],
                                         sl4(chkg, hq, j), sl4(chv, hq, j))))
            mm_sc(0); mm_rt(0); mm_cx(0); mm_um(0)
            mm_sc(1); mm_rt(1); mm_cx(1); mm_um(1)
            mm_sc(2); mm_rt(2); mm_cx(2); mm_um(2)
            mm_sc(3); mm_rt(3); mm_cx(3); mm_um(3)

        def p6_te_tail(e, n):
            for tg4 in range(4):
                last = None
                for j in range(4):
                    h = tg4 * 4 + j
                    e.wait_ge(SE.h, se_ev[("gn", n, h)])
                    if j == 0:
                        f = n * 4 + tg4
                        if f >= 1:
                            pn, pg = divmod(f - 1, 4)
                            e.wait_ge(VE.h, ve_ev[("prod", pn, pg)])
                    last = e.transpose(psT6[:, j * 128:(j + 1) * 128],
                                       gn_all[:, h * 128:(h + 1) * 128], id_bf[:])
                PE.inc(last)
            e.wait_ge(VE.h, ve_ev[("prod", n, 3)])
            for dmg in range(4):
                if n == 0:
                    (WST if dmg < 2 else DST).wait(e, ("wo", dmg))
                w = n * 4 + dmg
                if w >= 1:
                    pn, pd = divmod(w - 1, 4)
                    e.wait_ge(VE.h, ve_ev[("sev", pn, pd)])
                last = None
                for hv in range(KC):
                    last = e.matmul(psW[:],
                                    prod_fm[:, hv * 128:(hv + 1) * 128],
                                    wo_bf[:, hv * D + dmg * 512: hv * D + (dmg + 1) * 512],
                                    start=(hv == 0), stop=(hv == KC - 1))
                PE.inc(last)

        def p6_te(e):
            bs(e)
            for n in range(NT):
                p6_te_core(e, n)
                if n >= 1:
                    p6_te_tail(e, n - 1)
            p6_te_tail(e, NT - 1)
            be(e, "te")

        def p6_se(e):
            bs(e)
            for n in range(NT):
                for hq in range(4):
                    e.wait_ge(PE.h, pe_ev[("rt", n, hq)])
                    if n * 4 + hq >= 1:
                        g = n * 4 + hq - 1
                        pn, phq = divmod(g, 4)
                        e.wait_ge(VE.h, ve_ev[("retf", pn, phq)])
                    e.drain()
                    SE.inc(e.copy(ret1[:], psRr[:]))
                e.wait_ge(GPC.h, gp_ev[("stats", n)])
                e.drain()
                e.activation(st16[:, c_nv], st16[:, c_nv], AF.Ln, scale=-1.0)
                e.drain()
                SE.inc(e.activation(st16[:, c_rs], st16[:, c_nv], AF.Exp, scale=-0.5))
                e.wait_ge(GPC.h, gp_ev[("nmr", n)])
                e.drain()
                for h in range(H):
                    if n >= 1:
                        e.wait_ge(PE.h, pe_ev[("tp", n - 1, h // 4)])
                    SE.inc(e.activation(gn_all[:, h * 128:(h + 1) * 128],
                                        ret_all[n % 2][:, h * 128:(h + 1) * 128], AF.Identity,
                                        scale=st16[:, 64 + h:64 + h + 1],
                                        bias=st16[:, 80 + h:80 + h + 1]))
            be(e, "se")

        def p6_ve(e):
            bs(e)
            e.wait_ge(LD.h, bLD6 + 48)
            for n in range(NT):
                for hq in range(4):
                    hsl = slice(hq * 512, (hq + 1) * 512)
                    e.wait_ge(PE.h, pe_ev[("sc", n, hq)])
                    if n * 4 + hq >= 2:
                        pn, phq = divmod(n * 4 + hq - 2, 4)
                        e.wait_ge(PE.h, pe_ev[("rt", pn, phq)])
                    if n >= 2 and hq == 0:
                        e.wait_ge(SE.h, se_ev[("gn", n - 2, 15)])
                    VE.inc(e.tensor_tensor(a2_sb[hq % 2][:], psS[hq % 2][:], d2_sb[:, hsl], ALU.mult))
                    e.wait_ge(SE.h, se_ev[("ev", n, hq)])
                    e.wait_ge(PE.h, pe_ev[("cx", n, hq)])
                    last = None
                    for j in range(4):
                        h = hq * 4 + j
                        last = e.scalar_tensor_tensor(
                            ret_all[n % 2][:, h * 128:(h + 1) * 128],
                            psC[:, j * 128:(j + 1) * 128], grp1v[:, h:h + 1],
                            ret1[:, j * 128:(j + 1) * 128], ALU.mult, ALU.add)
                    VE.inc(last)
                    e.wait_ge(PE.h, pe_ev[("um", n, hq)])
                    for j in range(4):
                        h = hq * 4 + j
                        hsl2 = slice(h * 128, (h + 1) * 128)
                        e.scalar_tensor_tensor(s_f32[:, hsl2], s_f32[:, hsl2], GC[h],
                                               psU6[hq % 2][:, j * 128:(j + 1) * 128],
                                               ALU.mult, ALU.add)
                    e.drain()
                    VE.inc(e.tensor_copy(s_bfT[:, hq * 512:(hq + 1) * 512],
                                         s_f32[:, hq * 512:(hq + 1) * 512]))
                if n >= 1:
                    for tg4 in range(4):
                        e.wait_ge(PE.h, pe_ev[("tp", n - 1, tg4)])
                        if n >= 2 and tg4 == 0:
                            e.wait_ge(PE.h, pe_ev[("wo", n - 2, 3)])
                        VE.inc(e.tensor_tensor(prod_fm[:, tg4 * 512:(tg4 + 1) * 512],
                                               psT6[:],
                                               chg[(n - 1) % 2][:, tg4 * 512:(tg4 + 1) * 512],
                                               ALU.mult))
                e.drain()
                e.reduce_sum(st16[:, c_sum].rearrange("p (h o) -> p h o", o=1),
                             ret_all[n % 2][:].rearrange("p (h t) -> p h t", h=H), axis=AX.X)
                e.wait_ge(GPC.h, gp_ev[("sq", n)])
                VE.inc(e.reduce_sum(st16[:, c_ssq].rearrange("p (h o) -> p h o", o=1),
                                    sq_all[:].rearrange("p (h t) -> p h t", h=H), axis=AX.X))
                if n >= 1:
                    for dmg in range(4):
                        dsl = slice(dmg * 512, (dmg + 1) * 512)
                        e.wait_ge(PE.h, pe_ev[("wo", n - 1, dmg)])
                        if n >= 3 and dmg == 0:
                            SPST.wait(e, ("s", n - 3))
                        VE.inc(e.tensor_tensor(s_buf[(n - 1) % 2][:, dsl], psW[:],
                                               bo_sb[:, dsl], ALU.add))
            n = NT - 1
            for tg4 in range(4):
                e.wait_ge(PE.h, pe_ev[("tp", n, tg4)])
                if tg4 == 0 and NT >= 2:
                    e.wait_ge(PE.h, pe_ev[("wo", n - 1, 3)])
                VE.inc(e.tensor_tensor(prod_fm[:, tg4 * 512:(tg4 + 1) * 512], psT6[:],
                                       chg[n % 2][:, tg4 * 512:(tg4 + 1) * 512], ALU.mult))
            for dmg in range(4):
                dsl = slice(dmg * 512, (dmg + 1) * 512)
                e.wait_ge(PE.h, pe_ev[("wo", n, dmg)])
                if dmg == 0 and NT >= 3:
                    SPST.wait(e, ("s", NT - 3))
                VE.inc(e.tensor_tensor(s_buf[n % 2][:, dsl], psW[:],
                                       bo_sb[:, dsl], ALU.add))
            be(e, "ve")

        def p6_gp(e):
            bs(e)
            for dmg in range(4):
                g = (WST if dmg < 2 else DST).group(("wo", dmg))
                for k in range(KC):
                    g(e, wo_bf[:, k * D + dmg * 512: k * D + (dmg + 1) * 512],
                      Wo[k * 128:(k + 1) * 128, dmg * 512:(dmg + 1) * 512])
            for n in range(NT):
                e.wait_ge(VE.h, ve_ev[("retf", n, 3)])
                e.drain()
                GPC.inc(e.tensor_tensor(sq_all[:], ret_all[n % 2][:], ret_all[n % 2][:], ALU.mult))
                e.wait_ge(VE.h, ve_ev[("red", n)])
                e.drain()
                e.tensor_scalar(st16[:, c_mu], st16[:, c_sum], 1.0 / 128, None, ALU.mult)
                e.tensor_scalar(st16[:, c_ssq], st16[:, c_ssq], 1.0 / 128, None, ALU.mult)
                e.drain()
                e.tensor_tensor(st16[:, c_nv], st16[:, c_mu], st16[:, c_mu], ALU.mult)
                e.drain()
                e.tensor_tensor(st16[:, c_nv], st16[:, c_nv], st16[:, c_ssq], ALU.subtract)
                e.drain()
                GPC.inc(e.tensor_scalar(st16[:, c_nv], st16[:, c_nv], EPS, None, ALU.subtract))
                e.wait_ge(SE.h, se_ev[("rstd", n)])
                e.drain()
                e.tensor_scalar(st16[:, c_nmr], st16[:, c_mu], -1.0, None, ALU.mult)
                e.drain()
                GPC.inc(e.tensor_tensor(st16[:, c_nmr], st16[:, c_nmr], st16[:, c_rs], ALU.mult))
            be(e, "gp")

        block.sync(p6_sy); block.gpsimd(p6_gp); block.tensor(p6_te)
        block.scalar(p6_se); block.vector(p6_ve)
        assert PE.n == _pectr[0] and VE.n == _vectr[0]
        assert SE.n == _sectr[0] and GPC.n == _gpctr[0]
        p6.close()
        p56.close()
        nphase[0] += 1


        p7ab = ExitStack()
        sb7s = lambda n, shp, dt: p7ab.enter_context(nc.sbuf_tensor(n, shp, dt, side="right"))
        dwb = [sb7s(f"dwb{i}", [128, IC * 512], BF16) for i in range(2)]
        p7a = ExitStack()
        sb7 = lambda n, shp, dt: p7a.enter_context(nc.sbuf_tensor(n, shp, dt))
        ps7 = lambda n, shp, dt=F32: p7a.enter_context(nc.psum_tensor(n, shp, dt))
        xn_fm2 = sb7("xn_fm2", [128, KC * T], BF16)
        gwb = [sb7(f"gwb{i}", [128, KC * 128], BF16) for i in range(2)]
        uwb = [sb7(f"uwb{i}", [128, KC * 128], BF16) for i in range(2)]
        ga_sb = [sb7(f"ga_sb{i}", [128, 512], F32) for i in range(2)]
        ua_sb = [sb7(f"ua_sb{i}", [128, 512], F32) for i in range(2)]
        hdn_buf = [sb7(f"hdn_buf{i}", [128, 512], BF16) for i in range(2)]
        psGa = [ps7(f"psGa{i}", [128, 512]) for i in range(2)]
        psUa = [ps7(f"psUa{i}", [128, 512]) for i in range(2)]

        bPE, bVE, bSE = PE.n, VE.n, SE.n
        NF = IC * NTG
        pe_gm = lambda f: bPE + f * 2 + 1
        pe_um7 = lambda f: bPE + f * 2 + 2
        se_ga = lambda f: bSE + f * 2 + 1
        se_ua = lambda f: bSE + f * 2 + 2
        ve_hd = lambda f: bVE + f + 1

        def dn_quarter_load(e, q):
            g = DST.group(("dn", q))
            for ic in range(IC):
                g(e, dwb[q % 2][:, ic * 512:(ic + 1) * 512],
                  downT[ic * 128:(ic + 1) * 128, q * 512:(q + 1) * 512])

        def p7a_sy(e):
            bs(e)
            SPST.wait(e, "xn")
            RST.group("xn2")(e, xn_fm2[:].rearrange("p (k t) -> p k t", k=KC),
                             xn_sp.rearrange("k p t -> p k t"))
            for f in range(NF):
                ic, tg = divmod(f, NTG)
                e.wait_ge(VE.h, ve_hd(f))
                SPST.group(("h", f))(
                    e, hdn_sp[tg * 4:(tg + 1) * 4, :, ic, :].rearrange("b i t -> i b t"),
                    hdn_buf[f % 2][:].rearrange("p (b t) -> p b t", b=4))
            be(e, "sy")

        def p7a_gp(e):
            bs(e)
            for ic in range(IC):
                if ic >= 2:
                    e.wait_ge(PE.h, pe_um7((ic - 1) * NTG - 1))
                g = WST.group(("gu", ic))
                g(e, gwb[ic % 2][:].rearrange("p (k m) -> p k m", k=KC),
                  gateT[:, ic * 128:(ic + 1) * 128].rearrange("(k p) i -> p k i", p=128))
                g(e, uwb[ic % 2][:].rearrange("p (k m) -> p k m", k=KC),
                  upT[:, ic * 128:(ic + 1) * 128].rearrange("(k p) i -> p k i", p=128))
                if ic == IC // 2:
                    dn_quarter_load(e, 0)
                if ic == IC - 1:
                    dn_quarter_load(e, 1)
            be(e, "gp")

        def p7a_te(e):
            bs(e)
            RST.wait(e, "xn2")
            for f in range(NF):
                ic, tg = divmod(f, NTG)
                WST.wait(e, ("gu", ic))
                if f >= 2:
                    e.wait_ge(SE.h, se_ga(f - 2))
                last = None
                for k in range(KC):
                    last = e.matmul(psGa[f % 2][:], gwb[ic % 2][:, k * 128:(k + 1) * 128],
                                    xn_fm2[:, k * T + tg * 512: k * T + (tg + 1) * 512],
                                    start=(k == 0), stop=(k == KC - 1))
                PE.inc(last)
                if f >= 2:
                    e.wait_ge(SE.h, se_ua(f - 2))
                last = None
                for k in range(KC):
                    last = e.matmul(psUa[f % 2][:], uwb[ic % 2][:, k * 128:(k + 1) * 128],
                                    xn_fm2[:, k * T + tg * 512: k * T + (tg + 1) * 512],
                                    start=(k == 0), stop=(k == KC - 1))
                PE.inc(last)
            be(e, "te")

        def p7a_se(e):
            bs(e)
            for f in range(NF):
                ic, tg = divmod(f, NTG)
                e.wait_ge(PE.h, pe_gm(f))
                if f >= 2:
                    e.wait_ge(VE.h, ve_hd(f - 2))
                SE.inc(e.activation(ga_sb[f % 2][:], psGa[f % 2][:], AF.Silu,
                                    scale=gss_sb[:, ic:ic + 1]))
                e.wait_ge(PE.h, pe_um7(f))
                SE.inc(e.activation(ua_sb[f % 2][:], psUa[f % 2][:], AF.Identity,
                                    scale=uss_sb[:, ic:ic + 1]))
            be(e, "se")

        def p7a_ve(e):
            bs(e)
            for f in range(NF):
                e.wait_ge(SE.h, se_ua(f))
                if f >= 2:
                    SPST.wait(e, ("h", f - 2))
                VE.inc(e.tensor_tensor(hdn_buf[f % 2][:], ga_sb[f % 2][:], ua_sb[f % 2][:], ALU.mult))
            be(e, "ve")

        block.sync(p7a_sy); block.gpsimd(p7a_gp); block.tensor(p7a_te)
        block.scalar(p7a_se); block.vector(p7a_ve)
        assert PE.n == bPE + NF * 2 and SE.n == bSE + NF * 2 and VE.n == bVE + NF
        p7a.close()
        nphase[0] += 1

        p7b = ExitStack()
        sb8 = lambda n, shp, dt: p7b.enter_context(nc.sbuf_tensor(n, shp, dt))
        ps8 = lambda n, shp, dt=F32: p7b.enter_context(nc.psum_tensor(n, shp, dt))
        dss_sb = sb8("dss_sb", [128, D], F32)
        htb = [sb8(f"htb{i}", [128, IC * 128], BF16) for i in range(2)]
        x_t7 = [sb8(f"x_t7{i}", [128, 512], F32) for i in range(2)]
        s_t7 = [sb8(f"s_t7{i}", [128, 512], BF16) for i in range(2)]
        f1_sb = [sb8(f"f1_sb{i}", [128, 512], F32) for i in range(2)]
        f2_sb = [sb8(f"f2_sb{i}", [128, 512], F32) for i in range(2)]
        ob = [sb8(f"ob{i}", [128, 512], F32) for i in range(2)]
        psD = [ps8(f"psD{i}", [128, 512]) for i in range(2)]

        bLD, bPE, bVE, bSE = LD.n, PE.n, VE.n, SE.n
        NB = 4 * NT
        pe_dm = lambda p: bPE + p + 1
        ve_f1 = lambda p: bVE + p * 2 + 1
        ve_fin = lambda p: bVE + p * 2 + 2
        se_f2 = lambda p: bSE + p + 1

        def p7b_sy(e):
            bs(e)
            LD.inc(e.dma_start(out=dss_sb[:], in_=dss_bc), 16)
            for p in range(NB):
                q, tb = divmod(p, NT)
                if p >= 2:
                    e.wait_ge(PE.h, pe_dm(p - 2))
                    e.wait_ge(VE.h, ve_fin(p - 2))
                g = RST.group(("l", p))
                g(e, htb[p % 2][:].rearrange("p (c t) -> p c t", c=IC), hdn_sp[tb])
                g(e, x_t7[p % 2][:], x_in[tb * 128:(tb + 1) * 128, q * 512:(q + 1) * 512])
                g(e, s_t7[p % 2][:], s_sp[tb][:, q * 512:(q + 1) * 512])
                if p >= 1:
                    e.wait_ge(VE.h, ve_fin(p - 1))
                    pv = p - 1
                    qv, tv = divmod(pv, NT)
                    SPST.group(("o", pv))(
                        e, out[tv * 128:(tv + 1) * 128, qv * 512:(qv + 1) * 512], ob[pv % 2][:])
            e.wait_ge(VE.h, ve_fin(NB - 1))
            SPST.group(("o", NB - 1))(
                e, out[(NT - 1) * 128:NT * 128, 3 * 512:4 * 512], ob[(NB - 1) % 2][:])
            be(e, "sy")

        def p7b_gp(e):
            bs(e)
            for q in range(2, 4):
                e.wait_ge(PE.h, pe_dm((q - 1) * NT - 1))
                dn_quarter_load(e, q)
            be(e, "gp")

        def p7b_te(e):
            bs(e)
            for p in range(NB):
                q, tb = divmod(p, NT)
                DST.wait(e, ("dn", q))
                RST.wait(e, ("l", p))
                if p >= 2:
                    e.wait_ge(VE.h, ve_f1(p - 2))
                last = None
                for ic in range(IC):
                    last = e.matmul(psD[p % 2][:], htb[p % 2][:, ic * 128:(ic + 1) * 128],
                                    dwb[q % 2][:, ic * 512:(ic + 1) * 512],
                                    start=(ic == 0), stop=(ic == IC - 1))
                PE.inc(last)
            be(e, "te")

        def p7b_se(e):
            bs(e)
            for p in range(NB):
                e.wait_ge(VE.h, ve_f1(p))
                if p >= 1:
                    e.wait_ge(VE.h, ve_fin(p - 1))
                SE.inc(e.activation(f2_sb[p % 2][:], f1_sb[p % 2][:], AF.Silu))
            be(e, "se")

        def p7b_ve(e):
            bs(e)
            e.wait_ge(LD.h, bLD + 16)
            for p in range(NB):
                q, tb = divmod(p, NT)
                RST.wait(e, ("l", p))
                dsl = slice(q * 512, (q + 1) * 512)
                e.wait_ge(PE.h, pe_dm(p))
                if p >= 2:
                    e.wait_ge(SE.h, se_f2(p - 2))
                VE.inc(e.tensor_tensor(f1_sb[p % 2][:], psD[p % 2][:],
                                       dss_sb[:, dsl], ALU.mult))
                e.wait_ge(SE.h, se_f2(p))
                if p >= 2:
                    SPST.wait(e, ("o", p - 2))
                e.drain()
                e.tensor_tensor(f2_sb[p % 2][:], f2_sb[p % 2][:], s_t7[p % 2][:], ALU.mult)
                e.drain()
                VE.inc(e.tensor_tensor(ob[p % 2][:], f2_sb[p % 2][:], x_t7[p % 2][:], ALU.add))
            be(e, "ve")

        block.sync(p7b_sy); block.gpsimd(p7b_gp); block.tensor(p7b_te)
        block.scalar(p7b_se); block.vector(p7b_ve)
        assert PE.n == bPE + NB and VE.n == bVE + NB * 2 and SE.n == bSE + NB
        p7b.close()
        p7ab.close()
        nphase[0] += 1


    ctx.close()
    return nc



def _rope_tables(span_start, T):
    pos = np.arange(span_start, span_start + T, dtype=np.float64)
    j = np.arange(64, dtype=np.float64)
    inv_freq = 1.0 / THETA ** (2.0 * j / 128.0)
    freqs = pos[:, None] * inv_freq[None, :]
    cos = np.cos(freqs); sin = np.sin(freqs)
    sbase = (2.0 * j + 0.4 * 128.0) / (1.4 * 128.0)
    power = (pos - S // 2) / SCALE_BASE
    scale = sbase[None, :] ** power[:, None]
    ex = lambda a: np.repeat(a, 2, axis=1)
    cosf, sinf, scf = ex(cos), ex(sin), ex(scale)
    mk = lambda a: np.ascontiguousarray(a.T.astype(np.float32))
    return mk(cosf * scf), mk(sinf * scf), mk(cosf / scf), mk(sinf / scf)


def _const_tables():
    r = np.arange(128)
    d2 = np.zeros((128, H * 128), np.float32)
    grp1 = np.zeros((128, H * 128), np.float32)
    gcm = np.zeros((128, H), np.float32)
    for h, g in enumerate(GAMMAS):
        pm = r[None, :] - r[:, None]
        d2[:, h * 128:(h + 1) * 128] = np.where(pm >= 0, np.float64(g) ** np.maximum(pm, 0), 0.0)
        grp1[:, h * 128:(h + 1) * 128] = (np.float64(g) ** (r + 1.0))[:, None]
        gcm[:, h] = np.float64(g) ** (127.0 - r)
    p_rot = np.zeros((128, 128), np.float32)
    for i in range(64):
        p_rot[2 * i + 1, 2 * i] = -1.0
        p_rot[2 * i, 2 * i + 1] = 1.0
    return d2, grp1, gcm, p_rot, np.eye(128, dtype=np.float32)


def _wm_table(core, gs=SPAN):
    b, sp = core // NSPAN, core % NSPAN
    wm = np.zeros((NCORES * H,), np.float64)
    for cp in range(NCORES):
        bp, spp = cp // NSPAN, cp % NSPAN
        if bp == b and spp < sp:
            for h, g in enumerate(GAMMAS):
                wm[cp * H + h] = np.float64(g) ** (gs * (sp - spp - 1))
    return np.ascontiguousarray(np.broadcast_to(wm.astype(np.float32), (128, NCORES * H)))


def _rs(v):
    return np.ascontiguousarray(np.asarray(v, np.float32).reshape(-1, 128).T)


def _bc(v):
    v = np.asarray(v, np.float32)
    return np.ascontiguousarray(np.broadcast_to(v, (128, v.shape[-1])))


def make_in_maps(inputs, T=SPAN, geom_span=None):
    gs = SPAN if geom_span is None else geom_span
    x = np.ascontiguousarray(np.asarray(inputs["L0_attn_x"], np.float32))
    cf = lambda a: np.ascontiguousarray(np.asarray(a, np.float32))
    shared = {
        "Wq": cf(inputs["L0_attn_Wq"]), "Wk": cf(inputs["L0_attn_Wk"]),
        "Wv": cf(inputs["L0_attn_Wv"]), "Wg": cf(inputs["L0_attn_Wg"]),
        "Wo": cf(inputs["L0_attn_Wo"]),
        "gateT": cf(np.asarray(inputs["L0_ffn_gate_w"], np.float32).T),
        "upT": cf(np.asarray(inputs["L0_ffn_up_w"], np.float32).T),
        "downT": cf(np.asarray(inputs["L0_ffn_down_w"], np.float32).T),
        "bqr": _rs(inputs["L0_attn_bq"]), "bkr": _rs(inputs["L0_attn_bk"]),
        "bgr": _rs(inputs["L0_attn_bg"]),
        "bv_bc": _bc(inputs["L0_attn_bv"]), "bo_bc": _bc(inputs["L0_attn_bo"]),
        "gss": _rs(inputs["L0_ffn_gate_s"]), "uss": _rs(inputs["L0_ffn_up_s"]),
        "dss_bc": _bc(inputs["L0_ffn_down_s"]),
    }
    d2, grp1, gcm, p_rot, ident = _const_tables()
    r_ = np.arange(128)
    grp1v = np.stack([np.float64(g) ** (r_ + 1.0) for g in GAMMAS], axis=1).astype(np.float32)
    shared.update({"d2_t": d2, "grp1_bc": grp1, "grp1v_t": np.ascontiguousarray(grp1v),
                   "gcm_t": gcm, "p_rot": p_rot, "ident": ident})
    rope = [_rope_tables(sp * gs, T) for sp in range(NSPAN)]
    in_maps = []
    for c in range(NCORES):
        b, sp = c // NSPAN, c % NSPAN
        cs, ss_, ics, iss = rope[sp]
        im = dict(shared)
        im["x"] = np.ascontiguousarray(x[b, sp * gs: sp * gs + T])
        im["cs_t"] = cs; im["ss_t"] = ss_; im["ics_t"] = ics; im["iss_t"] = iss
        im["wm_bc"] = _wm_table(c, gs)
        in_maps.append(im)
    return in_maps


def kernel(**inputs):
    nc = build(T=SPAN, debug=False)
    in_maps = make_in_maps(inputs, T=SPAN)
    res = run_bass_kernel_spmd(nc, in_maps, core_ids=list(range(NCORES)))
    out = np.empty((B, S, D), np.float32)
    for c in range(NCORES):
        b, sp = c // NSPAN, c % NSPAN
        out[b, sp * SPAN:(sp + 1) * SPAN] = res.results[c]["out"]
    return out
